# version 03
# baseline (speedup 1.0000x reference)
import numpy as np
import ml_dtypes
import concourse.bass as bass
import concourse.mybir as mybir
from concourse.bass_utils import run_bass_kernel_spmd
from concourse import library_config
from concourse.aot_env import allow_ici_all_to_all
from concourse.library_overlay import lower_extended_insts
from contextlib import ExitStack

F32 = mybir.dt.float32
BF16 = mybir.dt.bfloat16
I16 = mybir.dt.int16
U8 = mybir.dt.uint8
AF = mybir.ActivationFunctionType
ALU = mybir.AluOpType
AX = mybir.AxisListType

NCORE = 8
NT = 32
NCOL = 256
D = 1024
NEXP = 8
CAPN = 4096
CAP = 1024
DFF = 3072
DH = DFF // 2
DUMMY = NEXP * CAP

DEBUG = False
STOP_AFTER = "full"
BFNP = ml_dtypes.bfloat16


def host_constants(core):
    c = core
    H = {}
    H["ltri128"] = np.triu(np.ones((128, 128), np.float32), 1)
    q = np.arange(128)
    H["ltri16x8"] = ((q[:, None] % 8 == q[None, :] % 8)
                     & (q[:, None] // 8 < q[None, :] // 8)).astype(np.float32)
    H["ident_bf"] = np.eye(128, dtype=np.float32).astype(BFNP)
    H["ident_f32"] = np.eye(128, dtype=np.float32)
    H["onescol"] = np.ones((128, 1), np.float32)
    H["ones1_bf"] = np.ones((1, 128), np.float32).astype(BFNP)
    o16p = np.zeros((32, 128), np.float32); o16p[0:16, :] = 1.0
    H["ones16p"] = o16p
    H["ones64"] = np.ones((64, 128), np.float32)
    H["ones128sq"] = np.ones((128, 128), np.float32)
    so = np.zeros((128, 32), np.float32)
    for half in range(2):
        for e in range(8):
            so[((c + 8 * half) % 16) * 8 + e, half * 8 + e] = 1.0
    H["selown16"] = so
    m16 = np.zeros((16, 256), np.float32)
    for mloc in range(32):
        for e in range(8):
            m16[(mloc % 2) * 8 + e, mloc * 8 + e] = 1.0
    H["m16pat"] = m16
    lt16 = np.zeros((32, 32), np.float32)
    lt16[:16, :16] = np.triu(np.ones((16, 16), np.float32), 1)
    H["ltri16p"] = lt16
    m16b = np.zeros((16, 256), np.float32)
    for mloc in range(32):
        for e in range(8):
            m16b[mloc // 2, mloc * 8 + e] = 1.0
    H["m16bpat"] = m16b
    m8t = (np.arange(128)[None, :] % 8 == np.arange(8)[:, None]).astype(np.float32)
    H["mod8tp"] = np.concatenate([m8t, np.zeros((24, 128), np.float32)], 0)
    H["mod8p"] = np.concatenate([m8t.T, np.zeros((128, 24), np.float32)], 1)
    sc8 = np.zeros((128, 64), np.float32)
    for qq in range(128):
        sc8[qq, ((qq // 8) % 8) * 8 + (qq % 8)] = 1.0
    H["selc8"] = sc8
    cm = np.arange(64)
    ce_c, ce_e = cm // 8, cm % 8
    H["mask64x"] = ((ce_e[:, None] == ce_e[None, :]) & (ce_c[:, None] < ce_c[None, :])).astype(np.float32)
    H["mask64i"] = ((ce_e[:, None] == ce_e[None, :]) & (ce_c[:, None] <= ce_c[None, :])).astype(np.float32)
    H["i64"] = np.eye(64, dtype=np.float32)
    H["memask"] = np.tile((np.arange(64) % 8 == c).astype(np.float32)[None, :], (128, 1))
    H["memask_tot"] = np.tile((np.arange(64) == 56 + c).astype(np.float32)[None, :], (128, 1))
    mp = np.zeros((128, 256), np.float32)
    for mloc in range(32):
        for e in range(8):
            mp[(mloc % 16) * 8 + e, mloc * 8 + e] = 1.0
    H["m128pat"] = mp
    for g in range(8):
        fs = np.zeros((128, 32), np.float32)
        for mrow in range(32):
            fs[16 * g + (mrow % 16), mrow] = 1.0
        H[f"foldsel{g}"] = fs
    H["siota"] = (np.arange(32)[None, :] * 128 + np.arange(128)[:, None]).astype(np.float32)
    u32 = np.zeros((32, 16), np.float32)
    u32[:16, :] = (np.arange(16)[:, None] < np.arange(16)[None, :]).astype(np.float32)
    H["ultri32"] = u32
    H["eio99"] = np.tile((np.arange(8) - 99.0).astype(np.float32), (128, 32))
    return H




def build_nc(debug=DEBUG, stop_after=None):
    stop_after = stop_after or STOP_AFTER
    nc = bass.Bass("TRN2", target_bir_lowering=False)
    AP_IN = {}

    def din(name, shape, dtype=F32):
        AP_IN[name] = nc.dram_tensor(name, list(shape), dtype, kind="ExternalInput").ap()
        return AP_IN[name]

    xsh = din("xsh", (NT, 128, D))
    W1 = din("W1", (D, D))
    Wbcg = din("Wbcg", (D, 40))
    brow = din("brow", (1, 40))
    b1row = din("b1row", (1, D))
    cfc = din("cfc", (D, DFF))
    cpr = din("cpr", (DH, D))
    consts = host_constants(0)
    for k, v in consts.items():
        din(k, v.shape, BF16 if v.dtype == BFNP else F32)

    partial = nc.dram_tensor("partial", [NT, 128, D], F32, kind="Internal")
    send1 = nc.dram_tensor("send1", [DUMMY + 1, D], BF16, kind="Internal")
    recv1 = nc.dram_tensor("recv1", [DUMMY + 1, D], BF16, kind="Internal")
    send2 = nc.dram_tensor("send2", [DUMMY + 1, D], BF16, kind="Internal")
    recv2 = nc.dram_tensor("recv2", [DUMMY + 1, D], BF16, kind="Internal")
    ag_in = nc.dram_tensor("ag_in", [128, 128], F32, kind="Internal")
    ag_out = nc.dram_tensor("ag_out", [NCORE * 128, 128], F32, kind="Internal")

    out = nc.dram_tensor("out", [NT, 128, D], F32, kind="ExternalOutput").ap()
    dbg = {}
    if debug:
        for nm, sh in [("d_r1", (128, 128)), ("d_r2", (128, 128)),
                       ("d_r3", (128, 128)), ("d_r4", (128, 96))]:
            dbg[nm] = nc.dram_tensor(nm, list(sh), F32, kind="ExternalOutput").ap()

    with ExitStack() as ctx:
        e_ = ctx.enter_context

        def sb(name, shape, dtype):
            return e_(nc.sbuf_tensor(name, list(shape), dtype))

        CT = {k: sb("c_" + k, v.shape, BF16 if v.dtype == BFNP else F32)
              for k, v in consts.items()}
        xt = [sb("xt0", (128, D), F32), sb("xt1", (128, D), F32)]
        sa = None
        s0 = sb("s0", (128, D), F32)
        s1 = sb("s1", (128, D), F32)
        u1 = sb("u1", (128, 4096), BF16)
        s2 = u1[:, 0:2048].bitcast(F32)
        fmt = [u1[:, 2048:3072], u1[:, 3072:4096]]
        lnst = sb("lnst", (128, 16), F32)
        bc16 = sb("bc16", (128, 16), F32)
        bc40 = sb("bc40", (128, 40), F32)
        sval = sb("sval", (128, 1), F32)
        xn_tm = sb("xn_tm", (128, NT * D), BF16)
        wbcg_sb = sb("wbcg_sb", (128, 8 * 40), BF16)
        brow_sb = sb("brow_sb", (1, 40), BF16)
        b1_sb = sb("b1_sb", (1, D), BF16)
        logits = sb("logits", (128, NT * 8), F32)
        R = {k: sb("r_" + k, (128, NT), F32) for k in
             ["i1", "i2", "w1", "w2", "m1", "m2", "den", "v1", "v2",
              "vr1", "vr2", "sx1", "sx2", "wm1", "wm2", "slotv", "csum",
              "acc", "t1", "t2"]}
        tmp8 = sb("tmp8", (128, NT * 8), F32)
        ex8 = sb("ex8", (128, NT * 8), F32)
        cbig = sb("cbig", (128, NT * 8), F32)
        eio8 = sb("eio8", (128, NT * 8), F32)
        pk = sb("pk", (128, 128), F32)
        idx1g = sb("idx1g", (128, NCOL), F32)
        idx2g = sb("idx2g", (128, NCOL), F32)

        t16 = sb("t16", (128, 16), F32)
        a1 = sb("a1", (128, 16), F32)
        g16 = sb("g16", (128, 32), F32)
        g16t = sb("g16t", (32, 16), F32)
        g16x = sb("g16x", (128, 32), F32)
        g16xT = sb("g16xT", (32, 16), F32)
        o16 = sb("o16", (16, 16), F32)
        rhsA = sb("rhsA", (32, 256), F32)
        rhsB = sb("rhsB", (32, 256), F32)
        ownoff = sb("ownoff", (128, 256), F32)
        offgT = sb("offgT", (128, 16), F32)
        keptT = sb("keptT", (128, 16), F32)
        kscr = sb("kscr", (128, 16), F32)
        vc64 = sb("vc64", (64, 16), F32)
        vc64r = sb("vc64r", (64, 1), F32)
        cum64 = sb("cum64", (64, 1), F32)
        inc64 = sb("inc64", (64, 1), F32)
        d64 = sb("d64", (64, 64), F32)
        cumB = sb("cumB", (128, 64), F32)
        incB = sb("incB", (128, 64), F32)
        scr64 = sb("scr64", (128, 64), F32)
        mycum = sb("mycum", (128, 8), F32)
        myinc = sb("myinc", (128, 8), F32)
        myvc = sb("myvc", (128, 8), F32)
        mytot = sb("mytot", (128, 1), F32)
        oh1 = sb("oh1", (128, NT * 8), F32)
        oh2 = sb("oh2", (128, NT * 8), F32)
        ohs = sb("ohs", (128, NT * 8), F32)
        ecsU = logits
        ecsV = eio8
        ohv1 = tmp8
        ohv2 = ex8
        selbuf = cbig
        csrowV = idx1g
        a2t = sb("a2t", (128, 2), F32)
        a2x = sb("a2x", (128, 2), F32)
        tot0e = sb("tot0e", (32, 2), F32)
        tot0b = sb("tot0b", (128, 1), F32)
        rhs3 = sb("rhs3", (128, 256), F32)
        offs2B = sb("offs2B", (128, 256), F32)
        Lk0 = sb("Lk0", (128, 256), I16)
        Lk1 = sb("Lk1", (128, 256), I16)
        Lslot = sb("Lslot", (128, 256), I16)
        cfc_sb = sb("cfc_sb", (128, 8 * DFF), BF16)
        w1_sb = cfc_sb.tensor if hasattr(cfc_sb, "tensor") else cfc_sb
        w1_sb = cfc_sb
        cpr_sb = sb("cpr_sb", (128, 12 * D), BF16)
        G = cpr_sb[:, 0:4096].bitcast(F32)
        srow = cpr_sb[:, 4096:8192].bitcast(F32)
        hh = sb("hh", (128, 12 * 512), BF16)
        expo = u1
        sa = xt[0][:, 0:512]

        zeros_bf = hh
        pt8 = xt
        ot8 = [s0, s1]

        psumP = e_(nc.psum_tensor("psumP", [128, 2048], F32))
        p_z = e_(nc.psum_tensor("p_z", [128, D], F32))
        p_bc = e_(nc.psum_tensor("p_bc", [128, 64], F32))
        p_tr = psumP[:, 0:1024]
        p_trb = psumP[:, 0:512].bitcast(BF16)
        p_row = psumP[0:1, 0:2048]

        sem = {nm: e_(nc.semaphore(nm)) for nm in
               ["sC", "sX0", "sX1", "sP0", "sP1", "sW1", "sWF", "sAG", "sSC",
                "sEG0", "sEG1", "sEO0", "sEO1", "sOG0", "sOG1", "sPL0", "sPL1",
                "sST0", "sST1", "sZ", "sDBG", "T", "V", "A", "CC"]}
        cnt = {k: 0 for k in sem}
        plans = {k: [] for k in ["sync", "scalar", "vector", "tensor", "gpsimd"]}
        off = [False]

        def emit(eng, fn):
            if off[0]:
                return
            plans[eng].append(fn)

        def bump(key, n=1):
            cnt[key] += n
            return cnt[key]

        def wt(eng, key, val):
            emit(eng, lambda e, k=key, v=val: e.wait_ge(sem[k], v))

        def dma(eng, key, out_ap, in_ap, pre_wait=False):
            pre = cnt[key]
            if pre_wait and pre > 0:
                wt(eng, key, pre)
            tgt = bump(key, 16)
            emit(eng, lambda e, o=out_ap, i=in_ap, k=key: e.dma_start(out=o, in_=i).then_inc(sem[k], 16))
            return tgt

        def st(eng, key, fn):
            if eng in ("vector", "scalar"):
                return op(eng, fn)
            tgt = bump(key, 1)
            emit(eng, lambda e, f=fn, k=key: f(e).then_inc(sem[k], 1))
            return tgt

        def op(eng, fn):
            if eng in ("vector", "scalar"):
                key = "V" if eng == "vector" else "A"
                pre = cnt[key]
                tgt = bump(key, 1)

                def run(e, f=fn, pre=pre, key=key):
                    ins = f(e).then_inc(sem[key], 1)
                    if pre > 0:
                        ins._wait_ge(sem[key], pre)
                emit(eng, run)
                return tgt
            emit(eng, lambda e, f=fn: f(e))
            return None

        def r3(ap, inner=8):
            return ap.rearrange("p (c e) -> p c e", e=inner)

        def rcol(ap):
            return ap.rearrange("p (c o) -> p c o", o=1)

        for k in consts:
            dma("sync", "sC", CT[k][:], AP_IN[k])
        t_const = cnt["sC"]
        op("gpsimd", lambda e: e.load_library(library_config.mlp))
        for j in range(8):
            dma("gpsimd", "sW1", w1_sb[:, j * D:(j + 1) * D], W1[j * 128:(j + 1) * 128, :])
        dma("gpsimd", "sW1", wbcg_sb[:].rearrange("p (a b) -> p a b", a=8),
            Wbcg.rearrange("(a p) b -> p a b", p=128))
        dma("gpsimd", "sW1", brow_sb[:], brow)
        dma("gpsimd", "sW1", b1_sb[:], b1row)
        t_wsmall = cnt["sW1"]

        z_v = st("vector", "V", lambda e: e.memset(zeros_bf[:, 0:2048], 0.0))
        wt("scalar", "V", z_v)
        for base in range(0, DUMMY, 256):
            dma("scalar", "sZ", send1.ap()[base:base + 256, :].rearrange("(a p) d -> p a d", p=128),
                zeros_bf[:, 0:2048].rearrange("p (a d) -> p a d", d=D))
            dma("scalar", "sZ", send2.ap()[base:base + 256, :].rearrange("(a p) d -> p a d", p=128),
                zeros_bf[:, 0:2048].rearrange("p (a d) -> p a d", d=D))
        dma("scalar", "sZ", send1.ap()[DUMMY:DUMMY + 1, :], zeros_bf[0:1, 0:1024])
        dma("scalar", "sZ", send2.ap()[DUMMY:DUMMY + 1, :], zeros_bf[0:1, 0:1024])
        dma("scalar", "sZ", recv2.ap()[DUMMY:DUMMY + 1, :], zeros_bf[0:1, 0:1024])
        t_zero = cnt["sZ"]
        for e8 in range(8):
            op("vector", lambda e, i=e8: e.memset(
                r3(eio8[:])[:, :, i:i + 1], float(i)))
        op("vector", lambda e: e.memset(g16[:], 0.0))
        op("vector", lambda e: e.memset(g16t[:], 0.0))
        op("vector", lambda e: e.memset(g16x[:], 0.0))
        op("vector", lambda e: e.memset(g16xT[:], 0.0))
        op("vector", lambda e: e.memset(rhsA[:], 0.0))
        op("vector", lambda e: e.memset(rhsB[:], 0.0))
        op("vector", lambda e: e.memset(tot0e[:], 0.0))
        op("vector", lambda e: e.memset(Lk0[:], 0))
        op("vector", lambda e: e.memset(Lk1[:], 0))
        op("vector", lambda e: e.memset(Lslot[:], 0))

        wt("tensor", "sC", t_const)
        wt("tensor", "sW1", t_wsmall)
        wt("vector", "sC", t_const)
        tV_tile = [0] * NT
        for m in range(NT):
            xb = xt[m % 2]
            fm = fmt[m % 2]
            if m >= 2:
                wt("sync", "V", tV_tile[m - 2])
            kx = f"sX{m % 2}"
            t_x = dma("sync", kx, xb[:], xsh[m], pre_wait=True)
            wt("scalar", kx, t_x)
            t_sq = st("scalar", "A", lambda e, xb=xb: e.activation(out=s1[:], in_=xb[:], func=AF.Square))
            wt("vector", kx, t_x)
            op("vector", lambda e, xb=xb: e.reduce_sum(
                out=lnst[:, 0:1].rearrange("p (a b) -> p a b", b=1),
                in_=xb[:].rearrange("p (a b) -> p a b", a=1), axis=AX.X))
            wt("vector", "A", t_sq)
            op("vector", lambda e: e.reduce_sum(
                out=lnst[:, 1:2].rearrange("p (a b) -> p a b", b=1),
                in_=s1[:].rearrange("p (a b) -> p a b", a=1), axis=AX.X))
            op("vector", lambda e: e.tensor_scalar(out=lnst[:, 2:3], in0=lnst[:, 0:1],
                                                   scalar1=1.0 / D, scalar2=None, op0=ALU.mult))
            op("vector", lambda e: e.tensor_scalar(out=lnst[:, 3:4], in0=lnst[:, 1:2],
                                                   scalar1=1.0 / D, scalar2=None, op0=ALU.mult))
            op("vector", lambda e: e.tensor_tensor(out=lnst[:, 4:5], in0=lnst[:, 2:3],
                                                   in1=lnst[:, 2:3], op=ALU.mult))
            t_var = st("vector", "V", lambda e: e.scalar_tensor_tensor(
                out=lnst[:, 5:6], in0=lnst[:, 3:4], scalar=1e-5,
                in1=lnst[:, 4:5], op0=ALU.add, op1=ALU.subtract))
            wt("scalar", "V", t_var)
            t_ln = st("scalar", "A", lambda e: e.activation(out=lnst[:, 6:7], in_=lnst[:, 5:6],
                                                            func=AF.Ln))
            wt("vector", "A", t_ln)
            t_nh = st("vector", "V", lambda e: e.tensor_scalar(
                out=lnst[:, 6:7], in0=lnst[:, 6:7], scalar1=-0.5, scalar2=None, op0=ALU.mult))
            wt("scalar", "V", t_nh)
            t_rstd = st("scalar", "A", lambda e: e.activation(out=lnst[:, 7:8], in_=lnst[:, 6:7],
                                                              func=AF.Exp))
            wt("vector", "A", t_rstd)
            t_xn = st("vector", "V", lambda e, m=m, xb=xb: e.tensor_scalar(
                out=xn_tm[:, m * D:(m + 1) * D], in0=xb[:],
                scalar1=lnst[:, 2:3], scalar2=lnst[:, 7:8],
                op0=ALU.subtract, op1=ALU.mult))
            wt("tensor", "V", t_xn)
            if m >= 1:
                pass
            t_tr = None
            for j in range(8):
                fn = (lambda e, m=m, j=j: e.transpose(
                    p_trb[:, j * 128:(j + 1) * 128],
                    xn_tm[:, m * D + j * 128:m * D + (j + 1) * 128],
                    CT["ident_bf"][:]))
                if j == 7:
                    t_tr = st("tensor", "T", fn)
                else:
                    op("tensor", fn)
            wt("vector", "T", t_tr)
            t_fm = st("vector", "V", lambda e, fm=fm: e.tensor_copy(out=fm, in_=p_trb[:]))
            wt("tensor", "V", t_fm)
            t_zmm = None
            for o in range(2):
                for j in range(8):
                    op("tensor", lambda e, fm=fm, o=o, j=j: e.matmul(
                        p_z[:, o * 512:(o + 1) * 512], fm[:, j * 128:(j + 1) * 128],
                        w1_sb[:, j * D + o * 512:j * D + (o + 1) * 512],
                        start=(j == 0), stop=False))
                fnb = (lambda e, o=o: e.matmul(
                    p_z[:, o * 512:(o + 1) * 512], CT["ones1_bf"][:],
                    b1_sb[:, o * 512:(o + 1) * 512], start=False, stop=True))
                if o == 1:
                    t_zmm = st("tensor", "T", fnb)
                else:
                    op("tensor", fnb)
            for j in range(8):
                op("tensor", lambda e, fm=fm, j=j: e.matmul(
                    p_bc[:, 0:40], fm[:, j * 128:(j + 1) * 128],
                    wbcg_sb[:, j * 40:(j + 1) * 40], start=(j == 0), stop=False))
            t_bc = st("tensor", "T", lambda e: e.matmul(
                p_bc[:, 0:40], CT["ones1_bf"][:], brow_sb[:], start=False, stop=True))
            wt("scalar", "T", t_zmm)
            op("scalar", lambda e: e.activation(out=s1[:], in_=p_z[:], func=AF.Exp))
            t_l = st("scalar", "A", lambda e: e.activation(out=s2, in_=s1[:], func=AF.Ln, bias=1.0))
            wt("vector", "T", t_bc)
            op("vector", lambda e: e.tensor_copy(out=bc40[:], in_=p_bc[:, 0:40]))
            op("vector", lambda e: e.tensor_tensor(out=bc16[:], in0=bc40[:, 0:16],
                                                   in1=bc40[:, 16:32], op=ALU.mult))
            op("vector", lambda e: e.reduce_sum(
                out=sval[:].rearrange("p (a b) -> p a b", b=1),
                in_=bc16[:].rearrange("p (a b) -> p a b", a=1), axis=AX.X))
            t_lg = st("vector", "V", lambda e, m=m: e.tensor_copy(
                out=logits[:, m * 8:(m + 1) * 8], in_=bc40[:, 32:40]))
            wt("tensor", "A", t_l)
            wt("tensor", "V", t_lg)
            wt("vector", "A", t_l)
            op("vector", lambda e, m=m: e.scalar_tensor_tensor(
                out=s1[:], in0=xn_tm[:, m * D:(m + 1) * D], scalar=sval[:, 0:1],
                in1=s2, op0=ALU.mult, op1=ALU.mult))
            t_pr = st("vector", "V", lambda e, xb=xb: e.tensor_add(out=s0[:], in0=xb[:], in1=s1[:]))
            tV_tile[m] = t_pr
            wt("sync", "V", t_pr)
            kp = f"sP{m % 2}"
            t_pst = dma("sync", kp, partial.ap()[m], s0[:], pre_wait=True)
            if m < NT - 1:
                wt("vector", kp, t_pst)

        def early_out():
            wt("sync", "A", cnt["A"])
            wt("sync", "V", cnt["V"])
            wt("sync", "sP0", cnt["sP0"])
            wt("sync", "sP1", cnt["sP1"])
            for mloc in range(NT):
                kpl = f"sPL{mloc % 2}"
                t_pl = dma("sync", kpl, pt8[mloc % 2][:], partial.ap()[mloc], pre_wait=True)
                wt("sync", kpl, t_pl)
                kst = f"sST{mloc % 2}"
                dma("sync", kst, out[mloc], pt8[mloc % 2][:], pre_wait=True)
            wt("sync", "sST0", cnt["sST0"])
            wt("sync", "sST1", cnt["sST1"])

        if stop_after == "p1a":
            early_out()
            off[0] = True
        z3 = r3(logits[:])
        eio3 = r3(eio8[:])
        op("vector", lambda e: e.reduce_max(out=rcol(R["m1"][:]), in_=z3, axis=AX.X))
        op("vector", lambda e: e.tensor_tensor(
            out=r3(ohs[:]), in0=z3,
            in1=rcol(R["m1"][:]).broadcast_to((128, NT, 8)), op=ALU.is_equal))
        op("vector", lambda e: e.tensor_tensor(out=tmp8[:], in0=ohs[:], in1=CT["eio99"][:], op=ALU.mult))
        op("vector", lambda e: e.tensor_scalar(out=tmp8[:], in0=tmp8[:], scalar1=99.0,
                                               scalar2=None, op0=ALU.add))
        op("vector", lambda e: e.tensor_reduce(out=rcol(R["i1"][:]), in_=r3(tmp8[:]),
                                               axis=AX.X, op=ALU.min))
        op("vector", lambda e: e.tensor_tensor(
            out=r3(ohs[:]), in0=eio3,
            in1=rcol(R["i1"][:]).broadcast_to((128, NT, 8)), op=ALU.is_equal))
        op("vector", lambda e: e.scalar_tensor_tensor(out=tmp8[:], in0=ohs[:], scalar=-1e30,
                                                      in1=logits[:], op0=ALU.mult, op1=ALU.add))
        op("vector", lambda e: e.tensor_reduce(out=rcol(R["m2"][:]), in_=r3(tmp8[:]),
                                               axis=AX.X, op=ALU.max))
        op("vector", lambda e: e.tensor_tensor(
            out=r3(ohs[:]), in0=r3(tmp8[:]),
            in1=rcol(R["m2"][:]).broadcast_to((128, NT, 8)), op=ALU.is_equal))
        op("vector", lambda e: e.tensor_tensor(out=tmp8[:], in0=ohs[:], in1=CT["eio99"][:], op=ALU.mult))
        op("vector", lambda e: e.tensor_scalar(out=tmp8[:], in0=tmp8[:], scalar1=99.0,
                                               scalar2=None, op0=ALU.add))
        op("vector", lambda e: e.tensor_reduce(out=rcol(R["i2"][:]), in_=r3(tmp8[:]),
                                               axis=AX.X, op=ALU.min))
        t_sm0 = st("vector", "V", lambda e: e.tensor_tensor(
            out=r3(tmp8[:]), in0=z3,
            in1=rcol(R["m1"][:]).broadcast_to((128, NT, 8)), op=ALU.subtract))
        wt("scalar", "V", t_sm0)
        t_ex = st("scalar", "A", lambda e: e.activation(out=ex8[:], in_=tmp8[:], func=AF.Exp))
        wt("vector", "A", t_ex)
        op("vector", lambda e: e.tensor_reduce(out=rcol(R["den"][:]), in_=r3(ex8[:]),
                                               axis=AX.X, op=ALU.add))
        op("vector", lambda e: e.reciprocal(out=R["w1"][:], in_=R["den"][:]))
        t_d2 = st("vector", "V", lambda e: e.tensor_sub(out=R["m2"][:], in0=R["m2"][:], in1=R["m1"][:]))
        wt("scalar", "V", t_d2)
        t_e2 = st("scalar", "A", lambda e: e.activation(out=R["den"][:], in_=R["m2"][:], func=AF.Exp))
        wt("vector", "A", t_e2)
        op("vector", lambda e: e.tensor_mul(out=R["w2"][:], in0=R["den"][:], in1=R["w1"][:]))
        t_pk = None
        for i, k in enumerate(["i1", "i2", "w1", "w2"]):
            fn = (lambda e, i=i, k=k: e.tensor_copy(out=pk[:, i * 32:(i + 1) * 32], in_=R[k][:]))
            if i == 3:
                t_pk = st("vector", "V", fn)
            else:
                op("vector", fn)
        wt("gpsimd", "V", t_pk)
        t_agin = dma("gpsimd", "sAG", ag_in.ap(), pk[:])
        wt("gpsimd", "sAG", t_agin)
        t_ag = st("gpsimd", "CC", lambda e: e.collective_compute(
            "AllGather", ALU.bypass, replica_groups=[list(range(NCORE))],
            ins=[ag_in.ap().opt()], outs=[ag_out.ap().opt()]))
        wt("gpsimd", "CC", t_ag)
        for cc in range(NCORE):
            dma("gpsimd", "sAG", tmp8[:, cc * 32:(cc + 1) * 32],
                ag_out.ap()[cc * 128:(cc + 1) * 128, 0:32])
            t_idxg = dma("gpsimd", "sAG", ex8[:, cc * 32:(cc + 1) * 32],
                         ag_out.ap()[cc * 128:(cc + 1) * 128, 32:64])
        wt("vector", "sAG", t_idxg)
        op("vector", lambda e: e.tensor_copy(
            out=idx1g[:].rearrange("p (f cc) -> p f cc", cc=8),
            in_=tmp8[:].rearrange("p (cc f) -> p f cc", cc=8)))
        op("vector", lambda e: e.tensor_copy(
            out=idx2g[:].rearrange("p (f cc) -> p f cc", cc=8),
            in_=ex8[:].rearrange("p (cc f) -> p f cc", cc=8)))

        if stop_after == "p1b":
            early_out()
            off[0] = True
        for e8 in range(8):
            op("vector", lambda e, i=e8: e.tensor_scalar(
                out=r3(G, 8)[:, :, i:i + 1],
                in0=idx1g[:].rearrange("p (c o) -> p c o", o=1),
                scalar1=float(i), scalar2=None, op0=ALU.is_equal))
            op("vector", lambda e, i=e8: e.tensor_scalar(
                out=r3(srow, 8)[:, :, i:i + 1],
                in0=idx2g[:].rearrange("p (c o) -> p c o", o=1),
                scalar1=float(i), scalar2=None, op0=ALU.is_equal))
        t_G = st("vector", "V", lambda e: e.tensor_add(out=G, in0=G, in1=srow))
        if stop_after == "p3a":
            early_out(); off[0] = True
        t_sz = st("vector", "V", lambda e: e.memset(srow, 0.0))
        wt("tensor", "V", t_sz)
        t_cs = None
        for qq in range(4):
            fn = (lambda e, q=qq: e.matmul(p_row[:, q * 512:(q + 1) * 512], CT["onescol"][:],
                                           G[:, q * 512:(q + 1) * 512], start=True, stop=True))
            if qq == 3:
                t_cs = st("tensor", "T", fn)
            else:
                op("tensor", fn)
        wt("vector", "T", t_cs)
        t_srow = st("vector", "V", lambda e: e.tensor_copy(out=srow[0:1, :], in_=p_row[:]))
        wt("tensor", "V", t_srow)
        for rnd in range(2):
            t_trr = None
            for j8 in range(8):
                j = rnd * 8 + j8
                fn = (lambda e, j=j, j8=j8: e.matmul(
                    psumP[:, 1024 + j8 * 128:1024 + (j8 + 1) * 128],
                    srow[:, j * 128:(j + 1) * 128], CT["ident_f32"][:],
                    start=True, stop=True))
                if j8 == 7:
                    t_trr = st("tensor", "T", fn)
                else:
                    op("tensor", fn)
            wt("vector", "T", t_trr)
            t_t16 = None
            for j8 in range(8):
                j = rnd * 8 + j8
                fn = (lambda e, j=j, j8=j8: e.tensor_copy(
                    out=t16[:, j:j + 1], in_=psumP[:, 1024 + j8 * 128:1024 + j8 * 128 + 1]))
                if j8 == 7:
                    t_t16 = st("vector", "V", fn)
                else:
                    op("vector", fn)
            wt("tensor", "V", t_t16)
        if stop_after == "p3b":
            early_out(); off[0] = True
        op("tensor", lambda e: e.matmul(psumP[:, 1024:1040], CT["ltri16x8"][:], t16[:],
                                        start=True, stop=True))
        t_mm1 = st("tensor", "T", lambda e: e.matmul(psumP[0:32, 1040:1056], CT["mod8p"][:], t16[:],
                                                     start=True, stop=True))
        wt("vector", "T", t_mm1)
        op("vector", lambda e: e.tensor_copy(out=a1[:], in_=psumP[:, 1024:1040]))
        t_g16 = st("vector", "V", lambda e: e.tensor_copy(out=g16[0:8, 0:16], in_=psumP[0:8, 1040:1056]))
        wt("tensor", "V", t_g16)
        t_mm2 = st("tensor", "T", lambda e: e.matmul(psumP[0:32, 1056:1184], g16[:, 0:16].bitcast(F32) if False else g16[:], CT["ident_f32"][:], start=True, stop=True))
        wt("vector", "T", t_mm2)
        t_g16t = st("vector", "V", lambda e: e.tensor_copy(out=g16t[0:16, :], in_=psumP[0:16, 1056:1072]))
        wt("tensor", "V", t_g16t)
        t_mm3 = st("tensor", "T", lambda e: e.matmul(psumP[0:32, 1184:1192], CT["ltri16p"][:],
                                                     g16t[:, 0:8], start=True, stop=True))
        wt("vector", "T", t_mm3)
        t_g16x = st("vector", "V", lambda e: e.tensor_copy(out=g16x[0:16, 0:8], in_=psumP[0:16, 1184:1192]))
        if stop_after == "p3b2":
            early_out(); off[0] = True
        wt("tensor", "V", t_g16x)
        t_mm4 = st("tensor", "T", lambda e: e.matmul(psumP[0:32, 1192:1208], CT["selown16"][:],
                                                     a1[:], start=True, stop=True))
        wt("vector", "T", t_mm4)
        t_o16 = st("vector", "V", lambda e: e.tensor_copy(out=o16[:], in_=psumP[0:16, 1192:1208]))
        op("vector", lambda e: e.tensor_tensor(
            out=rhsA[0:16, :].rearrange("r (j h ee) -> r j h ee", h=2, ee=8),
            in0=o16[:].rearrange("r (j h ee) -> r j h ee", h=1, ee=1).broadcast_to((16, 16, 2, 8)),
            in1=CT["m16pat"][:].rearrange("r (j h ee) -> r j h ee", h=2, ee=8), op=ALU.mult))
        op("vector", lambda e: e.tensor_tensor(
            out=rhsB[0:16, :].rearrange("r (j h ee) -> r j h ee", h=2, ee=8),
            in0=g16x[0:16, 0:8].rearrange("r (j h ee) -> r j h ee", j=1, h=1).broadcast_to((16, 16, 2, 8)),
            in1=CT["m16bpat"][:].rearrange("r (j h ee) -> r j h ee", h=2, ee=8), op=ALU.mult))
        t_rhs = st("vector", "V", lambda e: e.tensor_add(out=rhsA[:], in0=rhsA[:], in1=rhsB[:]))
        wt("tensor", "V", t_rhs)
        t_mm5 = st("tensor", "T", lambda e: e.matmul(psumP[:, 1024:1280], CT["ones16p"][:],
                                                     rhsA[:], start=True, stop=True))
        wt("vector", "T", t_mm5)
        t_ownoff = st("vector", "V", lambda e: e.tensor_copy(out=ownoff[:], in_=psumP[:, 1024:1280]))
        if stop_after == "p3b3":
            early_out(); off[0] = True
        wt("tensor", "V", t_g16x)
        t_mm6 = st("tensor", "T", lambda e: e.matmul(psumP[0:16, 1280:1296], g16t[:],
                                                     CT["ultri32"][:], start=True, stop=True))
        wt("vector", "T", t_mm6)
        t_gxt = st("vector", "V", lambda e: e.tensor_copy(out=g16xT[0:8, :], in_=psumP[0:8, 1280:1296]))
        if stop_after == "p3b4a":
            early_out(); off[0] = True
        wt("tensor", "V", t_gxt)
        t_mm7 = st("tensor", "T", lambda e: e.matmul(psumP[:, 1408:1424], CT["mod8tp"][:],
                                                     g16xT[:], start=True, stop=True))
        wt("vector", "T", t_mm7)
        op("vector", lambda e: e.tensor_copy(out=offgT[:], in_=psumP[:, 1408:1424]))
        if stop_after == "p3b4b":
            early_out(); off[0] = True
        op("vector", lambda e: e.tensor_add(out=kscr[:], in0=a1[:], in1=offgT[:]))
        op("vector", lambda e: e.tensor_scalar(out=kscr[:], in0=kscr[:], scalar1=-1.0,
                                               scalar2=float(CAPN), op0=ALU.mult, op1=ALU.add))
        op("vector", lambda e: e.tensor_scalar(out=kscr[:], in0=kscr[:], scalar1=0.0,
                                               scalar2=None, op0=ALU.max))
        t_kept = st("vector", "V", lambda e: e.tensor_tensor(out=keptT[:], in0=kscr[:],
                                                             in1=t16[:], op=ALU.min))
        if stop_after == "p3b4":
            early_out(); off[0] = True
        wt("tensor", "V", t_kept)
        t_mm8 = st("tensor", "T", lambda e: e.matmul(psumP[0:64, 1424:1440], CT["selc8"][:],
                                                     keptT[:], start=True, stop=True))
        wt("vector", "T", t_mm8)
        op("vector", lambda e: e.tensor_copy(out=vc64[:], in_=psumP[0:64, 1424:1440]))
        t_vcr = st("vector", "V", lambda e: e.tensor_reduce(
            out=vc64r[:].rearrange("p (a b) -> p a b", b=1),
            in_=vc64[:].rearrange("p (a b) -> p a b", a=1), axis=AX.X, op=ALU.add))
        wt("tensor", "V", t_vcr)
        op("tensor", lambda e: e.matmul(psumP[0:64, 1440:1441], CT["mask64x"][:],
                                        vc64r[:], start=True, stop=True))
        t_mm9 = st("tensor", "T", lambda e: e.matmul(psumP[0:64, 1441:1442], CT["mask64i"][:],
                                                     vc64r[:], start=True, stop=True))
        wt("vector", "T", t_mm9)
        op("vector", lambda e: e.tensor_copy(out=cum64[:], in_=psumP[0:64, 1440:1441]))
        t_c64 = st("vector", "V", lambda e: e.tensor_copy(out=inc64[:], in_=psumP[0:64, 1441:1442]))
        if stop_after == "p3b5":
            early_out(); off[0] = True
        op("vector", lambda e: e.tensor_scalar(out=d64[:], in0=CT["i64"][:],
                                               scalar1=cum64[:, 0:1], scalar2=None, op0=ALU.mult))
        t_dg1 = st("vector", "V", lambda e: e.tensor_copy(out=scr64[0:64, :], in_=d64[:]))
        wt("tensor", "V", t_dg1)
        t_mmA = st("tensor", "T", lambda e: e.matmul(psumP[:, 1442:1506], CT["ones64"][:],
                                                     scr64[0:64, :], start=True, stop=True))
        wt("vector", "T", t_mmA)
        t_cb = st("vector", "V", lambda e: e.tensor_copy(out=cumB[:], in_=psumP[:, 1442:1506]))
        op("vector", lambda e: e.tensor_scalar(out=d64[:], in0=CT["i64"][:],
                                               scalar1=inc64[:, 0:1], scalar2=None, op0=ALU.mult))
        t_dg2 = st("vector", "V", lambda e: e.tensor_copy(out=scr64[0:64, :], in_=d64[:]))
        wt("tensor", "V", t_dg2)
        t_mmB = st("tensor", "T", lambda e: e.matmul(psumP[:, 1536:1600], CT["ones64"][:],
                                                     scr64[0:64, :], start=True, stop=True))
        wt("vector", "T", t_mmB)
        op("vector", lambda e: e.tensor_copy(out=incB[:], in_=psumP[:, 1536:1600]))
        op("vector", lambda e: e.tensor_tensor(out=scr64[:], in0=cumB[:], in1=CT["memask"][:], op=ALU.mult))
        op("vector", lambda e: e.tensor_reduce(
            out=mycum[:].rearrange("p (c o) -> p c o", o=1),
            in_=scr64[:].rearrange("p (c e) -> p c e", e=8), axis=AX.X, op=ALU.add))
        op("vector", lambda e: e.tensor_tensor(out=scr64[:], in0=incB[:], in1=CT["memask"][:], op=ALU.mult))
        op("vector", lambda e: e.tensor_reduce(
            out=myinc[:].rearrange("p (c o) -> p c o", o=1),
            in_=scr64[:].rearrange("p (c e) -> p c e", e=8), axis=AX.X, op=ALU.add))
        op("vector", lambda e: e.tensor_sub(out=myvc[:], in0=myinc[:], in1=mycum[:]))
        op("vector", lambda e: e.tensor_tensor(out=scr64[:], in0=incB[:], in1=CT["memask_tot"][:], op=ALU.mult))
        op("vector", lambda e: e.tensor_reduce(
            out=mytot[:].rearrange("p (a b) -> p a b", b=1),
            in_=scr64[:].rearrange("p (a b) -> p a b", a=1), axis=AX.X, op=ALU.add))
        if stop_after == "p3c":
            early_out(); off[0] = True
        for e8 in range(8):
            op("vector", lambda e, i=e8: e.tensor_scalar(
                out=r3(oh1[:])[:, :, i:i + 1], in0=rcol(R["i1"][:]),
                scalar1=float(i), scalar2=None, op0=ALU.is_equal))
            op("vector", lambda e, i=e8: e.tensor_scalar(
                out=r3(oh2[:])[:, :, i:i + 1], in0=rcol(R["i2"][:]),
                scalar1=float(i), scalar2=None, op0=ALU.is_equal))
        t_ohs = st("vector", "V", lambda e: e.tensor_add(out=ohs[:], in0=oh1[:], in1=oh2[:]))
        wt("tensor", "V", t_ohs)
        t_mmC = st("tensor", "T", lambda e: e.matmul(psumP[:, 1024:1280], CT["ltri128"][:],
                                                     ohs[:], start=True, stop=True))
        wt("vector", "T", t_mmC)
        op("vector", lambda e: e.tensor_copy(out=ecsU[:], in_=psumP[:, 1024:1280]))
        op("vector", lambda e: e.tensor_tensor(out=selbuf[:], in0=oh1[:], in1=ecsU[:], op=ALU.mult))
        op("vector", lambda e: e.tensor_reduce(out=rcol(R["vr1"][:]), in_=r3(selbuf[:]),
                                               axis=AX.X, op=ALU.add))
        op("vector", lambda e: e.tensor_tensor(out=selbuf[:], in0=oh1[:], in1=ownoff[:], op=ALU.mult))
        op("vector", lambda e: e.tensor_reduce(out=rcol(R["t1"][:]), in_=r3(selbuf[:]),
                                               axis=AX.X, op=ALU.add))
        op("vector", lambda e: e.tensor_add(out=R["t1"][:], in0=R["t1"][:], in1=R["vr1"][:]))
        op("vector", lambda e: e.tensor_scalar(out=R["v1"][:], in0=R["t1"][:],
                                               scalar1=float(CAPN), scalar2=None, op0=ALU.is_lt))
        op("vector", lambda e: e.tensor_add(out=selbuf[:], in0=ecsU[:], in1=oh1[:]))
        op("vector", lambda e: e.tensor_tensor(out=selbuf[:], in0=oh2[:], in1=selbuf[:], op=ALU.mult))
        op("vector", lambda e: e.tensor_reduce(out=rcol(R["vr2"][:]), in_=r3(selbuf[:]),
                                               axis=AX.X, op=ALU.add))
        op("vector", lambda e: e.tensor_tensor(out=selbuf[:], in0=oh2[:], in1=ownoff[:], op=ALU.mult))
        op("vector", lambda e: e.tensor_reduce(out=rcol(R["t2"][:]), in_=r3(selbuf[:]),
                                               axis=AX.X, op=ALU.add))
        op("vector", lambda e: e.tensor_add(out=R["t2"][:], in0=R["t2"][:], in1=R["vr2"][:]))
        op("vector", lambda e: e.tensor_scalar(out=R["v2"][:], in0=R["t2"][:],
                                               scalar1=float(CAPN), scalar2=None, op0=ALU.is_lt))
        op("vector", lambda e: e.tensor_tensor(
            out=r3(ohv1[:]), in0=r3(oh1[:]),
            in1=rcol(R["v1"][:]).broadcast_to((128, NT, 8)), op=ALU.mult))
        op("vector", lambda e: e.tensor_tensor(
            out=r3(ohv2[:]), in0=r3(oh2[:]),
            in1=rcol(R["v2"][:]).broadcast_to((128, NT, 8)), op=ALU.mult))
        t_ohv = st("vector", "V", lambda e: e.tensor_add(out=ohs[:], in0=ohv1[:], in1=ohv2[:]))
        wt("tensor", "V", t_ohv)
        t_mmD = st("tensor", "T", lambda e: e.matmul(psumP[:, 1280:1536], CT["ltri128"][:],
                                                     ohs[:], start=True, stop=True))
        wt("vector", "T", t_mmD)
        op("vector", lambda e: e.tensor_copy(out=ecsV[:], in_=psumP[:, 1280:1536]))
        op("vector", lambda e: e.tensor_tensor(out=selbuf[:], in0=ohv1[:], in1=ecsV[:], op=ALU.mult))
        op("vector", lambda e: e.tensor_reduce(out=rcol(R["vr1"][:]), in_=r3(selbuf[:]),
                                               axis=AX.X, op=ALU.add))
        op("vector", lambda e: e.tensor_add(out=selbuf[:], in0=ecsV[:], in1=ohv1[:]))
        op("vector", lambda e: e.tensor_tensor(out=selbuf[:], in0=ohv2[:], in1=selbuf[:], op=ALU.mult))
        op("vector", lambda e: e.tensor_reduce(out=rcol(R["vr2"][:]), in_=r3(selbuf[:]),
                                               axis=AX.X, op=ALU.add))
        t_csz = st("vector", "V", lambda e: e.memset(csrowV[:], 0.0))
        wt("tensor", "V", max(t_ohv, t_csz))
        t_mmE = st("tensor", "T", lambda e: e.matmul(psumP[0:1, 1536:1792], CT["onescol"][:],
                                                     ohs[:], start=True, stop=True))
        wt("vector", "T", t_mmE)
        t_csr = st("vector", "V", lambda e: e.tensor_copy(out=csrowV[0:1, :], in_=psumP[0:1, 1536:1792]))
        wt("tensor", "V", t_csr)
        t_tr2 = None
        for half in range(2):
            fn = (lambda e, half=half: e.matmul(
                psumP[:, 1536 + half * 128:1536 + (half + 1) * 128],
                csrowV[:, half * 128:(half + 1) * 128], CT["ident_f32"][:],
                start=True, stop=True))
            if half == 1:
                t_tr2 = st("tensor", "T", fn)
            else:
                op("tensor", fn)
        wt("vector", "T", t_tr2)
        op("vector", lambda e: e.tensor_copy(out=a2t[:, 0:1], in_=psumP[:, 1536:1537]))
        t_a2t = st("vector", "V", lambda e: e.tensor_copy(out=a2t[:, 1:2], in_=psumP[:, 1664:1665]))
        wt("tensor", "V", t_a2t)
        op("tensor", lambda e: e.matmul(psumP[:, 1792:1794], CT["ltri16x8"][:], a2t[:],
                                        start=True, stop=True))
        t_mmF = st("tensor", "T", lambda e: e.matmul(psumP[0:32, 1794:1796], CT["mod8p"][:],
                                                     a2t[:], start=True, stop=True))
        wt("vector", "T", t_mmF)
        op("vector", lambda e: e.tensor_copy(out=a2x[:], in_=psumP[:, 1792:1794]))
        t_t0e = st("vector", "V", lambda e: e.tensor_copy(out=tot0e[0:8, :], in_=psumP[0:8, 1794:1796]))
        wt("tensor", "V", t_t0e)
        t_mmG = st("tensor", "T", lambda e: e.matmul(psumP[:, 1796:1797], CT["mod8tp"][:],
                                                     tot0e[:, 0:1], start=True, stop=True))
        wt("vector", "T", t_mmG)
        op("vector", lambda e: e.tensor_copy(out=tot0b[:], in_=psumP[:, 1796:1797]))
        op("vector", lambda e: e.tensor_add(out=a2x[:, 1:2], in0=a2x[:, 1:2], in1=tot0b[:]))
        t_rhs3 = st("vector", "V", lambda e: e.tensor_tensor(
            out=rhs3[:].rearrange("p (h a ee) -> p h a ee", a=16, ee=8),
            in0=a2x[:].rearrange("p (h a ee) -> p h a ee", a=1, ee=1).broadcast_to((128, 2, 16, 8)),
            in1=CT["m128pat"][:].rearrange("p (h a ee) -> p h a ee", a=16, ee=8), op=ALU.mult))
        wt("tensor", "V", t_rhs3)
        t_mmH = st("tensor", "T", lambda e: e.matmul(psumP[:, 1536:1792], CT["ones128sq"][:],
                                                     rhs3[:], start=True, stop=True))
        wt("vector", "T", t_mmH)
        op("vector", lambda e: e.tensor_copy(out=offs2B[:], in_=psumP[:, 1536:1792]))
        op("vector", lambda e: e.tensor_tensor(out=selbuf[:], in0=oh1[:], in1=offs2B[:], op=ALU.mult))
        op("vector", lambda e: e.tensor_reduce(out=rcol(R["t1"][:]), in_=r3(selbuf[:]),
                                               axis=AX.X, op=ALU.add))
        op("vector", lambda e: e.tensor_add(out=R["vr1"][:], in0=R["vr1"][:], in1=R["t1"][:]))
        op("vector", lambda e: e.tensor_tensor(out=selbuf[:], in0=oh2[:], in1=offs2B[:], op=ALU.mult))
        op("vector", lambda e: e.tensor_reduce(out=rcol(R["t2"][:]), in_=r3(selbuf[:]),
                                               axis=AX.X, op=ALU.add))
        op("vector", lambda e: e.tensor_add(out=R["vr2"][:], in0=R["vr2"][:], in1=R["t2"][:]))
        op("vector", lambda e: e.scalar_tensor_tensor(out=R["sx1"][:], in0=R["i1"][:],
                                                      scalar=float(CAP), in1=R["vr1"][:],
                                                      op0=ALU.mult, op1=ALU.add))
        op("vector", lambda e: e.tensor_scalar(out=R["sx1"][:], in0=R["sx1"][:],
                                               scalar1=float(-DUMMY), scalar2=None, op0=ALU.add))
        op("vector", lambda e: e.tensor_tensor(out=R["sx1"][:], in0=R["v1"][:], in1=R["sx1"][:], op=ALU.mult))
        op("vector", lambda e: e.tensor_scalar(out=R["sx1"][:], in0=R["sx1"][:],
                                               scalar1=float(DUMMY), scalar2=None, op0=ALU.add))
        op("vector", lambda e: e.scalar_tensor_tensor(out=R["sx2"][:], in0=R["i2"][:],
                                                      scalar=float(CAP), in1=R["vr2"][:],
                                                      op0=ALU.mult, op1=ALU.add))
        op("vector", lambda e: e.tensor_scalar(out=R["sx2"][:], in0=R["sx2"][:],
                                               scalar1=float(-DUMMY), scalar2=None, op0=ALU.add))
        op("vector", lambda e: e.tensor_tensor(out=R["sx2"][:], in0=R["v2"][:], in1=R["sx2"][:], op=ALU.mult))
        op("vector", lambda e: e.tensor_scalar(out=R["sx2"][:], in0=R["sx2"][:],
                                               scalar1=float(DUMMY), scalar2=None, op0=ALU.add))
        op("vector", lambda e: e.tensor_tensor(out=R["wm1"][:], in0=R["w1"][:], in1=R["v1"][:], op=ALU.mult))
        op("vector", lambda e: e.tensor_tensor(out=R["wm2"][:], in0=R["w2"][:], in1=R["v2"][:], op=ALU.mult))
        op("vector", lambda e: e.memset(R["csum"][:], 0.0))
        op("vector", lambda e: e.memset(R["acc"][:], 0.0))
        for c_ in range(1, 8):
            op("vector", lambda e, c_=c_: e.tensor_scalar(
                out=R["t1"][:], in0=CT["siota"][:], scalar1=mycum[:, c_:c_ + 1],
                scalar2=None, op0=ALU.is_ge))
            op("vector", lambda e: e.tensor_add(out=R["csum"][:], in0=R["csum"][:], in1=R["t1"][:]))
            op("vector", lambda e, c_=c_: e.scalar_tensor_tensor(
                out=R["t2"][:], in0=R["t1"][:], scalar=myvc[:, c_ - 1:c_],
                in1=R["acc"][:], op0=ALU.mult, op1=ALU.add))
            op("vector", lambda e: e.tensor_copy(out=R["acc"][:], in_=R["t2"][:]))
        op("vector", lambda e: e.tensor_sub(out=R["t1"][:], in0=CT["siota"][:], in1=R["acc"][:]))
        op("vector", lambda e: e.scalar_tensor_tensor(out=R["slotv"][:], in0=R["csum"][:],
                                                      scalar=float(CAP), in1=R["t1"][:],
                                                      op0=ALU.mult, op1=ALU.add))
        op("vector", lambda e: e.tensor_scalar(out=R["t1"][:], in0=CT["siota"][:],
                                               scalar1=mytot[:, 0:1], scalar2=None, op0=ALU.is_ge))
        op("vector", lambda e: e.tensor_scalar(out=R["t2"][:], in0=R["slotv"][:],
                                               scalar1=-1.0, scalar2=float(DUMMY),
                                               op0=ALU.mult, op1=ALU.add))
        op("vector", lambda e: e.tensor_tensor(out=R["t2"][:], in0=R["t1"][:], in1=R["t2"][:], op=ALU.mult))
        t_slotv = st("vector", "V", lambda e: e.tensor_add(out=R["slotv"][:], in0=R["slotv"][:], in1=R["t2"][:]))

        if stop_after == "p3d":
            early_out(); off[0] = True
        def fold(src, dst, tag):
            for g in range(8):
                pslot = 1024 + (g % 2) * 64
                t_fm_ = st("tensor", "T", lambda e, g=g, src=src, pslot=pslot: e.matmul(
                    psumP[0:32, pslot:pslot + 32], CT[f"foldsel{g}"][:], src[:],
                    start=True, stop=True))
                wt("vector", "T", t_fm_)
                t_fc = st("vector", "V", lambda e, g=g, dst=dst, pslot=pslot: e.tensor_copy(
                    out=dst[0:32, :].rearrange("p (mm gg) -> p mm gg", gg=8)[:, :, g:g + 1],
                    in_=psumP[0:32, pslot:pslot + 32].rearrange("p (a b) -> p a b", b=1)))
                if g >= 1:
                    wt("tensor", "V", t_fc)
            return cnt["V"]

        wt("tensor", "V", t_slotv)
        fold(R["sx1"], Lk0, "k0")
        fold(R["sx2"], Lk1, "k1")
        t_folds = fold(R["slotv"], Lslot, "sl")

        if debug:
            wt("gpsimd", "V", t_folds)
            dma("gpsimd", "sDBG", dbg["d_r1"], pk[:])
            for i, k in enumerate(["v1", "v2", "vr1", "vr2"]):
                op("vector", lambda e, i=i, k=k: e.tensor_copy(out=pk[:, i * 32:(i + 1) * 32], in_=R[k][:]))
            t_dbg2 = st("vector", "V", lambda e: e.tensor_copy(out=ex8[:, 0:64], in_=cumB[:]))
            wt("gpsimd", "V", t_dbg2)
            dma("gpsimd", "sDBG", dbg["d_r2"], pk[:])
            dma("gpsimd", "sDBG", dbg["d_r3"][:, 0:64], ex8[0:128, 0:64])
            for i, k in enumerate(["sx1", "sx2", "slotv"]):
                op("vector", lambda e, i=i, k=k: e.tensor_copy(out=tmp8[:, i * 32:(i + 1) * 32], in_=R[k][:]))
            t_dbg4 = st("vector", "V", lambda e: e.tensor_copy(out=tmp8[:, 95:96], in_=mytot[:]))
            wt("gpsimd", "V", t_dbg4)
            dma("gpsimd", "sDBG", dbg["d_r4"], tmp8[:, 0:96])

        class _Stop(Exception):
            pass

        try:
            if stop_after == "p1":
                early_out()
                raise _Stop()
            wt("gpsimd", "V", t_folds)
            wt("gpsimd", "V", tV_tile[NT - 1])
            wt("gpsimd", "sZ", t_zero)
            t_sc1 = None
            for kk, L in ((0, Lk0), (1, Lk1)):
                for jj in range(4):
                    if t_sc1:
                        wt("gpsimd", "sSC", t_sc1)
                    t_sc1 = bump("sSC", 16)
                    emit("gpsimd", lambda e, L=L, jj=jj: e.dma_scatter_add(
                        out_ap=send1.ap(),
                        in_ap=xn_tm[:, jj * 8192:(jj + 1) * 8192].rearrange("p (m d) -> p m d", d=D),
                        idxs_ap=L[:, jj * 64:(jj + 1) * 64], num_idxs=1024,
                        num_idxs_reg=1024, elem_size=D).then_inc(sem["sSC"], 16))
            for dc in range(8):
                dma("gpsimd", "sWF", cfc_sb[:, dc * DFF:(dc + 1) * DFF], cfc[dc * 128:(dc + 1) * 128, :])
            for j in range(12):
                t_wffn = dma("gpsimd", "sWF", cpr_sb[:, j * D:(j + 1) * D], cpr[j * 128:(j + 1) * 128, :])
            wt("gpsimd", "sSC", t_sc1)
            t_a2a1 = st("gpsimd", "CC", lambda e: e.collective_compute(
                "AllToAll", ALU.bypass, replica_groups=[list(range(NCORE))],
                ins=[send1.ap()[0:DUMMY, :].opt()], outs=[recv1.ap()[0:DUMMY, :].opt()]))
            wt("gpsimd", "CC", t_a2a1)

            exph = [xn_tm[:, hh_ * 4096:(hh_ + 1) * 4096].rearrange("p (b t) -> p b t", b=8)
                    for hh_ in range(8)]
            t_gath = [0] * 8
            for h in range(8):
                kg = f"sEG{h % 2}"
                if cnt[kg] > 0:
                    wt("gpsimd", kg, cnt[kg])
                t_gath[h] = bump(kg, 16)
                emit("gpsimd", lambda e, h=h, kg=kg: e.dma_gather(
                    out_ap=exph[h], in_ap=recv1.ap(), idxs_ap=Lslot[:, h * 32:(h + 1) * 32],
                    num_idxs=512, num_idxs_reg=512, elem_size=D,
                    transpose=True).then_inc(sem[kg], 16))
            wt("tensor", "sWF", t_wffn)
            if stop_after == "a2a1":
                wt("gpsimd", "sEG0", cnt["sEG0"])
                wt("gpsimd", "sEG1", cnt["sEG1"])
                early_out()
                raise _Stop()

            t_expo = [0] * 8
            t_lastmult = None
            for s_ in range(8):
                wt("tensor", f"sEG{s_ % 2}", t_gath[s_])
                for j in range(12):
                    pa = psumP[:, (j % 2) * 1024:(j % 2) * 1024 + 512]
                    pg = psumP[:, (j % 2) * 1024 + 512:(j % 2) * 1024 + 1024]
                    if s_ > 0 or j >= 2:
                        wt("tensor", "V", cnt["V"])
                    for dc in range(8):
                        op("tensor", lambda e, j=j, dc=dc, s_=s_, pa=pa: e.matmul(
                            pa, cfc_sb[:, dc * DFF + j * 128:dc * DFF + (j + 1) * 128],
                            exph[s_][:, dc, :],
                            start=(dc == 0), stop=(dc == 7)))
                    t_mm_g = None
                    for dc in range(8):
                        fn = (lambda e, j=j, dc=dc, s_=s_, pg=pg: e.matmul(
                            pg, cfc_sb[:, dc * DFF + (j + 12) * 128:dc * DFF + (j + 13) * 128],
                            exph[s_][:, dc, :],
                            start=(dc == 0), stop=(dc == 7)))
                        if dc == 7:
                            t_mm_g = st("tensor", "T", fn)
                        else:
                            op("tensor", fn)
                    wt("scalar", "T", t_mm_g)
                    if t_lastmult:
                        wt("scalar", "V", t_lastmult)
                    t_silu = st("scalar", "A", lambda e, pa=pa: e.activation(out=sa, in_=pa, func=AF.Silu))
                    wt("vector", "A", t_silu)
                    t_lastmult = st("vector", "V", lambda e, j=j, pg=pg: e.tensor_tensor(
                        out=hh[:, j * 512:(j + 1) * 512], in0=sa, in1=pg, op=ALU.mult))
                t_hh = cnt["V"]
                wt("tensor", "V", t_hh)
                for sub in range(4):
                    t_o = None
                    for nh in range(2):
                        for j in range(12):
                            fn = (lambda e, sub=sub, nh=nh, j=j: e.matmul(
                                p_z[:, nh * 512:(nh + 1) * 512],
                                hh[:, j * 512 + sub * 128:j * 512 + (sub + 1) * 128],
                                cpr_sb[:, j * D + nh * 512:j * D + (nh + 1) * 512],
                                start=(j == 0), stop=(j == 11)))
                            if nh == 1 and j == 11:
                                t_o = st("tensor", "T", fn)
                            else:
                                op("tensor", fn)
                    wt("vector", "T", t_o)
                    t_ev = st("vector", "V", lambda e, sub=sub: e.tensor_copy(
                        out=expo[:, sub * D:(sub + 1) * D], in_=p_z[:]))
                    wt("tensor", "V", t_ev)
                t_evall = cnt["V"]
                wt("gpsimd", "V", t_evall)
                keo = "sEO0"
                if cnt[keo] > 0:
                    wt("gpsimd", keo, cnt[keo])
                t_expo[s_] = bump(keo, 16)
                emit("gpsimd", lambda e, s_=s_, keo=keo: e.dma_scatter_add(
                    out_ap=send2.ap(), in_ap=expo[:].rearrange("p (a d) -> p a d", d=D),
                    idxs_ap=Lslot[:, s_ * 32:(s_ + 1) * 32], num_idxs=512,
                    num_idxs_reg=512, elem_size=D).then_inc(sem[keo], 16))
                if s_ < 7:
                    wt("vector", keo, t_expo[s_])

            wt("gpsimd", "sEO0", cnt["sEO0"])
            t_a2a2 = st("gpsimd", "CC", lambda e: e.collective_compute(
                "AllToAll", ALU.bypass, replica_groups=[list(range(NCORE))],
                ins=[send2.ap()[0:DUMMY, :].opt()], outs=[recv2.ap()[0:DUMMY, :].opt()]))
            wt("gpsimd", "CC", t_a2a2)
            if stop_after == "ffn":
                early_out()
                raise _Stop()
            gb0 = cfc_sb[:, 0:8192].rearrange("p (b d) -> p b d", d=D)
            gb1 = cfc_sb[:, 8192:16384].rearrange("p (b d) -> p b d", d=D)
            wt("sync", "A", cnt["A"])
            wt("sync", "V", cnt["V"])
            wt("vector", "sP0", cnt["sP0"])
            wt("vector", "sP1", cnt["sP1"])
            t_out = None
            for cj in range(4):
                if cnt["sOG0"] > 0:
                    wt("gpsimd", "sOG0", cnt["sOG0"])
                    wt("gpsimd", "sOG1", cnt["sOG1"])
                tg0 = bump("sOG0", 16)
                emit("gpsimd", lambda e, cj=cj: e.dma_gather(
                    out_ap=gb0, in_ap=recv2.ap(), idxs_ap=Lk0[:, cj * 64:(cj + 1) * 64],
                    num_idxs=1024, num_idxs_reg=1024, elem_size=D).then_inc(sem["sOG0"], 16))
                tg1 = bump("sOG1", 16)
                emit("gpsimd", lambda e, cj=cj: e.dma_gather(
                    out_ap=gb1, in_ap=recv2.ap(), idxs_ap=Lk1[:, cj * 64:(cj + 1) * 64],
                    num_idxs=1024, num_idxs_reg=1024, elem_size=D).then_inc(sem["sOG1"], 16))
                wt("vector", "sOG0", tg0)
                wt("vector", "sOG1", tg1)
                for b in range(8):
                    mloc = cj * 8 + b
                    pb = pt8[b % 2]
                    ob = ot8[b % 2]
                    if cj > 0 or b >= 2:
                        wt("sync", "V", cnt["V"])
                    kpl = f"sPL{b % 2}"
                    t_pl = dma("sync", kpl, pb[:], partial.ap()[mloc], pre_wait=True)
                    wt("vector", kpl, t_pl)
                    op("vector", lambda e, b=b, mloc=mloc, pb=pb, ob=ob: e.scalar_tensor_tensor(
                        out=ob[:], in0=gb0[:, b, :], scalar=R["wm1"][:, mloc:mloc + 1],
                        in1=pb[:], op0=ALU.mult, op1=ALU.add))
                    t_cmb = st("vector", "V", lambda e, b=b, mloc=mloc, ob=ob: e.scalar_tensor_tensor(
                        out=ob[:], in0=gb1[:, b, :], scalar=R["wm2"][:, mloc:mloc + 1],
                        in1=ob[:], op0=ALU.mult, op1=ALU.add))
                    wt("sync", "V", t_cmb)
                    kst = f"sST{b % 2}"
                    t_out = dma("sync", kst, out[mloc], ob[:], pre_wait=True)
                    if not (cj == 3 and b == 7):
                        wt("vector", kst, t_out)
                if cj < 3:
                    wt("gpsimd", "V", cnt["V"])
            wt("sync", "sST0", cnt["sST0"])
            wt("sync", "sST1", cnt["sST1"])
            if debug:
                wt("sync", "sDBG", cnt["sDBG"])

        except _Stop:
            pass

        with allow_ici_all_to_all(), nc.Block() as block:
            @block.sync
            def _(en):
                for f in plans["sync"]:
                    f(en)

            @block.scalar
            def _(en):
                for f in plans["scalar"]:
                    f(en)

            @block.vector
            def _(en):
                for f in plans["vector"]:
                    f(en)

            @block.tensor
            def _(en):
                for f in plans["tensor"]:
                    f(en)

            @block.gpsimd
            def _(en):
                for f in plans["gpsimd"]:
                    f(en)

    lower_extended_insts(nc)
    return nc


def prepare_in_maps(inputs):
    x = np.ascontiguousarray(np.asarray(inputs["L0_attn_x"], np.float32)).reshape(NCOL, 128, D)
    W1 = np.ascontiguousarray(np.asarray(inputs["L0_attn_W1"], np.float32))
    b1 = np.asarray(inputs["L0_attn_b1"], np.float32).reshape(1, D)
    W2 = np.asarray(inputs["L0_attn_W2"], np.float32)
    b2 = np.asarray(inputs["L0_attn_b2"], np.float32)
    W3 = np.asarray(inputs["L0_attn_W3"], np.float32)
    b3 = np.asarray(inputs["L0_attn_b3"], np.float32)
    wg = np.asarray(inputs["L0_ffn_w_gate"], np.float32)
    cfc = np.asarray(inputs["L0_ffn_c_fc"], np.float32)
    cpr = np.asarray(inputs["L0_ffn_c_proj"], np.float32)
    Wbcg = np.ascontiguousarray(np.concatenate([W2, W3, wg], axis=1))
    brow = np.ascontiguousarray(np.concatenate([b2, b3, np.zeros(8, np.float32)])).reshape(1, 40)

    in_maps = []
    for c in range(NCORE):
        m = {
            "xsh": np.ascontiguousarray(x[c::8]),
            "W1": W1, "Wbcg": Wbcg, "brow": brow, "b1row": b1,
            "cfc": np.ascontiguousarray(cfc[c]),
            "cpr": np.ascontiguousarray(cpr[c]),
        }
        for k, v in host_constants(c).items():
            m[k] = np.ascontiguousarray(v)
        in_maps.append(m)
    return in_maps


def kernel(**inputs):
    in_maps = prepare_in_maps(inputs)
    nc = build_nc(DEBUG)
    res = run_bass_kernel_spmd(nc, in_maps, core_ids=list(range(NCORE)))
    full = np.zeros((NCOL, 128, D), np.float32)
    for c in range(NCORE):
        full[c::8] = res.results[c]["out"]
    if DEBUG:
        kernel._debug = [{k: v for k, v in r.items() if k.startswith("d_")} for r in res.results]
    return full.reshape(8, 4096, 1024)


# version 05
# speedup vs baseline: 1.0601x; 1.0601x over previous
import numpy as np
import ml_dtypes
import concourse.bass as bass
import concourse.mybir as mybir
from concourse.bass_utils import run_bass_kernel_spmd
from concourse import library_config
from concourse.aot_env import allow_ici_all_to_all
from concourse.library_overlay import lower_extended_insts
from contextlib import ExitStack

F32 = mybir.dt.float32
BF16 = mybir.dt.bfloat16
I16 = mybir.dt.int16
U8 = mybir.dt.uint8
AF = mybir.ActivationFunctionType
ALU = mybir.AluOpType
AX = mybir.AxisListType

NCORE = 8
NT = 32
NCOL = 256
D = 1024
NEXP = 8
CAPN = 4096
CAP = 768
DFF = 3072
DH = DFF // 2
DUMMY = NEXP * CAP

DEBUG = False
STOP_AFTER = "full"
BFNP = ml_dtypes.bfloat16


def host_constants(core):
    c = core
    H = {}
    H["ltri128"] = np.triu(np.ones((128, 128), np.float32), 1)
    q = np.arange(128)
    H["ltri16x8"] = ((q[:, None] % 8 == q[None, :] % 8)
                     & (q[:, None] // 8 < q[None, :] // 8)).astype(np.float32)
    H["ident_bf"] = np.eye(128, dtype=np.float32).astype(BFNP)
    H["ident_f32"] = np.eye(128, dtype=np.float32)
    H["onescol"] = np.ones((128, 1), np.float32)
    H["ones1_bf"] = np.ones((1, 128), np.float32).astype(BFNP)
    o16p = np.zeros((32, 128), np.float32); o16p[0:16, :] = 1.0
    H["ones16p"] = o16p
    H["ones64"] = np.ones((64, 128), np.float32)
    H["ones128sq"] = np.ones((128, 128), np.float32)
    so = np.zeros((128, 32), np.float32)
    for half in range(2):
        for e in range(8):
            so[((c + 8 * half) % 16) * 8 + e, half * 8 + e] = 1.0
    H["selown16"] = so
    m16 = np.zeros((16, 256), np.float32)
    for mloc in range(32):
        for e in range(8):
            m16[(mloc % 2) * 8 + e, mloc * 8 + e] = 1.0
    H["m16pat"] = m16
    lt16 = np.zeros((32, 32), np.float32)
    lt16[:16, :16] = np.triu(np.ones((16, 16), np.float32), 1)
    H["ltri16p"] = lt16
    m16b = np.zeros((16, 256), np.float32)
    for mloc in range(32):
        for e in range(8):
            m16b[mloc // 2, mloc * 8 + e] = 1.0
    H["m16bpat"] = m16b
    m8t = (np.arange(128)[None, :] % 8 == np.arange(8)[:, None]).astype(np.float32)
    H["mod8tp"] = np.concatenate([m8t, np.zeros((24, 128), np.float32)], 0)
    H["mod8p"] = np.concatenate([m8t.T, np.zeros((128, 24), np.float32)], 1)
    sc8 = np.zeros((128, 64), np.float32)
    for qq in range(128):
        sc8[qq, ((qq // 8) % 8) * 8 + (qq % 8)] = 1.0
    H["selc8"] = sc8
    cm = np.arange(64)
    ce_c, ce_e = cm // 8, cm % 8
    H["mask64x"] = ((ce_e[:, None] == ce_e[None, :]) & (ce_c[:, None] < ce_c[None, :])).astype(np.float32)
    H["mask64i"] = ((ce_e[:, None] == ce_e[None, :]) & (ce_c[:, None] <= ce_c[None, :])).astype(np.float32)
    H["i64"] = np.eye(64, dtype=np.float32)
    H["memask"] = np.tile((np.arange(64) % 8 == c).astype(np.float32)[None, :], (128, 1))
    H["memask_tot"] = np.tile((np.arange(64) == 56 + c).astype(np.float32)[None, :], (128, 1))
    mp = np.zeros((128, 256), np.float32)
    for mloc in range(32):
        for e in range(8):
            mp[(mloc % 16) * 8 + e, mloc * 8 + e] = 1.0
    H["m128pat"] = mp
    for g in range(8):
        fs = np.zeros((128, 128), np.float32)
        for mrow in range(128):
            fs[16 * g + (mrow % 16), mrow] = 1.0
        H[f"foldsel{g}"] = fs
    H["siota"] = (np.arange(32)[None, :] * 128 + np.arange(128)[:, None]).astype(np.float32)
    u32 = np.zeros((32, 16), np.float32)
    u32[:16, :] = (np.arange(16)[:, None] < np.arange(16)[None, :]).astype(np.float32)
    H["ultri32"] = u32
    return H




def build_nc(debug=DEBUG, stop_after=None):
    stop_after = stop_after or STOP_AFTER
    nc = bass.Bass("TRN2", target_bir_lowering=False)
    AP_IN = {}

    def din(name, shape, dtype=F32):
        AP_IN[name] = nc.dram_tensor(name, list(shape), dtype, kind="ExternalInput").ap()
        return AP_IN[name]

    xsh = din("xsh", (NT, 128, D))
    W1 = din("W1", (D, D))
    Wbcg = din("Wbcg", (D, 40))
    brow = din("brow", (1, 40))
    b1row = din("b1row", (1, D))
    cfc = din("cfc", (D, DFF))
    cpr = din("cpr", (DH, D))
    consts = host_constants(0)
    for k, v in consts.items():
        din(k, v.shape, BF16 if v.dtype == BFNP else F32)

    partial = nc.dram_tensor("partial", [NT, 128, D], F32, kind="Internal")
    send1 = nc.dram_tensor("send1", [DUMMY + 1, D], BF16, kind="Internal")
    recv1 = nc.dram_tensor("recv1", [DUMMY + 1, D], BF16, kind="Internal")
    send2 = nc.dram_tensor("send2", [DUMMY + 1, D], BF16, kind="Internal")
    recv2 = nc.dram_tensor("recv2", [DUMMY + 1, D], BF16, kind="Internal")
    ag_in = nc.dram_tensor("ag_in", [128, 128], F32, kind="Internal")
    ag_out = nc.dram_tensor("ag_out", [NCORE * 128, 128], F32, kind="Internal")

    out = nc.dram_tensor("out", [NT, 128, D], F32, kind="ExternalOutput").ap()
    dbg = {}
    if debug:
        for nm, sh in [("d_r1", (128, 128)), ("d_r2", (128, 128)),
                       ("d_r3", (128, 128)), ("d_r4", (128, 96))]:
            dbg[nm] = nc.dram_tensor(nm, list(sh), F32, kind="ExternalOutput").ap()

    with ExitStack() as ctx:
        e_ = ctx.enter_context

        def sb(name, shape, dtype):
            return e_(nc.sbuf_tensor(name, list(shape), dtype))

        CT = {k: sb("c_" + k, v.shape, BF16 if v.dtype == BFNP else F32)
              for k, v in consts.items()}
        xt = [sb("xt0", (128, D), F32), sb("xt1", (128, D), F32)]
        sa = None
        s0 = sb("s0", (128, D), F32)
        s1 = sb("s1", (128, D), F32)
        u1 = sb("u1", (128, 4096), BF16)
        s2 = u1[:, 0:2048].bitcast(F32)
        fmt = [u1[:, 2048:3072], u1[:, 3072:4096]]
        lnst = sb("lnst", (128, 16), F32)
        mnst = sb("mnst", (128, NT), F32)
        rsst = sb("rsst", (128, NT), F32)
        svst = sb("svst", (128, NT), F32)
        bc16 = sb("bc16", (128, 16), F32)
        bc40 = sb("bc40", (128, 40), F32)
        xn_tm = sb("xn_tm", (128, NT * D), BF16)
        wbcg_sb = sb("wbcg_sb", (128, 8 * 40), BF16)
        brow_sb = sb("brow_sb", (1, 40), BF16)
        b1_sb = sb("b1_sb", (1, D), BF16)
        logits = sb("logits", (128, NT * 8), F32)
        R = {k: sb("r_" + k, (128, NT), F32) for k in
             ["i1", "i2", "w1", "w2", "m1", "m2", "den", "v1", "v2",
              "vr1", "vr2", "sx1", "sx2", "wm1", "wm2", "slotv", "csum",
              "acc", "t1", "t2"]}
        tmp8 = sb("tmp8", (128, NT * 8), F32)
        ex8 = sb("ex8", (128, NT * 8), F32)
        cbig = sb("cbig", (128, NT * 8), F32)
        eio8 = sb("eio8", (128, NT * 8), F32)
        pk = sb("pk", (128, 128), F32)
        idx1g = sb("idx1g", (128, NCOL), F32)
        idx2g = sb("idx2g", (128, NCOL), F32)

        t16 = sb("t16", (128, 16), F32)
        a1 = sb("a1", (128, 16), F32)
        g16 = sb("g16", (128, 32), F32)
        g16t = sb("g16t", (32, 16), F32)
        g16x = sb("g16x", (128, 32), F32)
        g16xT = sb("g16xT", (32, 16), F32)
        o16 = sb("o16", (16, 16), F32)
        rhsA = sb("rhsA", (32, 256), F32)
        rhsB = sb("rhsB", (32, 256), F32)
        ownoff = sb("ownoff", (128, 256), F32)
        offgT = sb("offgT", (128, 16), F32)
        keptT = sb("keptT", (128, 16), F32)
        kscr = sb("kscr", (128, 16), F32)
        vc64 = sb("vc64", (64, 16), F32)
        vc64r = sb("vc64r", (64, 1), F32)
        cum64 = sb("cum64", (64, 1), F32)
        inc64 = sb("inc64", (64, 1), F32)

        cumB = pk[:, 0:64]
        incB = pk[:, 64:128]
        scr64 = sb("scr64", (128, 64), F32)
        mycum = sb("mycum", (128, 8), F32)
        myinc = sb("myinc", (128, 8), F32)
        myvc = sb("myvc", (128, 8), F32)
        mytot = sb("mytot", (128, 1), F32)
        oh1 = sb("oh1", (128, NT * 8), F32)
        oh2 = idx2g
        ohs = sb("ohs", (128, NT * 8), F32)
        ecsU = logits
        ecsV = eio8
        ohv1 = tmp8
        ohv2 = ex8
        selbuf = cbig
        csrowV = idx1g
        a2t = sb("a2t", (128, 2), F32)
        a2x = sb("a2x", (128, 2), F32)
        tot0e = sb("tot0e", (32, 2), F32)
        tot0b = sb("tot0b", (128, 1), F32)
        rhs3 = ownoff
        offs2B = sb("offs2B", (128, 256), F32)
        Lk0 = sb("Lk0", (128, 256), I16)
        Lk1 = sb("Lk1", (128, 256), I16)
        Lslot = sb("Lslot", (128, 256), I16)
        cfc_sb = sb("cfc_sb", (128, 8 * DFF), BF16)
        w1_sb = cfc_sb.tensor if hasattr(cfc_sb, "tensor") else cfc_sb
        w1_sb = cfc_sb
        cpr_sb = sb("cpr_sb", (128, 12 * D), BF16)
        G = cpr_sb[:, 0:4096].bitcast(F32)
        srow = cpr_sb[:, 4096:8192].bitcast(F32)
        hh = sb("hh", (128, 12 * 512), BF16)
        expo = u1
        sa = xt[0][:, 0:512]

        zeros_bf = hh
        pt8 = xt
        ot8 = [s0, s1]

        psumP = e_(nc.psum_tensor("psumP", [128, 2048], F32))
        p_z = e_(nc.psum_tensor("p_z", [128, D], F32))
        p_bc = e_(nc.psum_tensor("p_bc", [128, 64], F32))
        p_tr = psumP[:, 0:1024]
        p_trbs = [psumP[:, 0:512].bitcast(BF16), psumP[:, 512:1024].bitcast(BF16)]
        p_zs = [p_z[:, 0:1024], psumP[:, 1024:2048]]
        p_bcs = [p_bc[:, 0:40], p_bc[:, 0:40]]
        p_row = psumP[0:1, 0:2048]

        sem = {nm: e_(nc.semaphore(nm)) for nm in
               ["sC", "sX0", "sX1", "sP0", "sP1", "sW1", "sWF", "sAG", "sSC",
                "sEG0", "sEG1", "sEO0", "sEO1", "sOG0", "sOG1", "sPL0", "sPL1",
                "sST0", "sST1", "sZ", "sDBG", "T", "V", "A", "CC"]}
        cnt = {k: 0 for k in sem}
        plans = {k: [] for k in ["sync", "scalar", "vector", "tensor", "gpsimd"]}
        off = [False]

        def emit(eng, fn):
            if off[0]:
                return
            plans[eng].append(fn)

        def bump(key, n=1):
            cnt[key] += n
            return cnt[key]

        def wt(eng, key, val):
            emit(eng, lambda e, k=key, v=val: e.wait_ge(sem[k], v))

        def dma(eng, key, out_ap, in_ap, pre_wait=False):
            pre = cnt[key]
            if pre_wait and pre > 0:
                wt(eng, key, pre)
            tgt = bump(key, 16)
            emit(eng, lambda e, o=out_ap, i=in_ap, k=key: e.dma_start(out=o, in_=i).then_inc(sem[k], 16))
            return tgt

        def st(eng, key, fn):
            if eng in ("vector", "scalar"):
                return op(eng, fn)
            tgt = bump(key, 1)
            emit(eng, lambda e, f=fn, k=key: f(e).then_inc(sem[k], 1))
            return tgt

        def op(eng, fn):
            if eng in ("vector", "scalar"):
                key = "V" if eng == "vector" else "A"
                pre = cnt[key]
                tgt = bump(key, 1)

                def run(e, f=fn, pre=pre, key=key):
                    ins = f(e).then_inc(sem[key], 1)
                    if pre > 0:
                        ins._wait_ge(sem[key], pre)
                emit(eng, run)
                return tgt
            emit(eng, lambda e, f=fn: f(e))
            return None

        def r3(ap, inner=8):
            return ap.rearrange("p (c e) -> p c e", e=inner)

        def rcol(ap):
            return ap.rearrange("p (c o) -> p c o", o=1)

        for k in consts:
            dma("sync", "sC", CT[k][:], AP_IN[k])
        t_const = cnt["sC"]
        op("gpsimd", lambda e: e.load_library(library_config.mlp))
        for j in range(8):
            dma("gpsimd", "sW1", w1_sb[:, j * D:(j + 1) * D], W1[j * 128:(j + 1) * 128, :])
        dma("gpsimd", "sW1", wbcg_sb[:].rearrange("p (a b) -> p a b", a=8),
            Wbcg.rearrange("(a p) b -> p a b", p=128))
        dma("gpsimd", "sW1", brow_sb[:], brow)
        dma("gpsimd", "sW1", b1_sb[:], b1row)
        t_wsmall = cnt["sW1"]

        z_v = st("vector", "V", lambda e: e.memset(zeros_bf[:, 0:2048], 0.0))
        wt("scalar", "V", z_v)
        for base in range(0, DUMMY, 256):
            dma("scalar", "sZ", send1.ap()[base:base + 256, :].rearrange("(a p) d -> p a d", p=128),
                zeros_bf[:, 0:2048].rearrange("p (a d) -> p a d", d=D))
            dma("scalar", "sZ", send2.ap()[base:base + 256, :].rearrange("(a p) d -> p a d", p=128),
                zeros_bf[:, 0:2048].rearrange("p (a d) -> p a d", d=D))
        dma("scalar", "sZ", send1.ap()[DUMMY:DUMMY + 1, :], zeros_bf[0:1, 0:1024])
        dma("scalar", "sZ", send2.ap()[DUMMY:DUMMY + 1, :], zeros_bf[0:1, 0:1024])
        dma("scalar", "sZ", recv2.ap()[DUMMY:DUMMY + 1, :], zeros_bf[0:1, 0:1024])
        t_zero = cnt["sZ"]
        for e8 in range(8):
            op("vector", lambda e, i=e8: e.memset(
                r3(eio8[:])[:, :, i:i + 1], float(i)))
        op("vector", lambda e: e.memset(g16[:], 0.0))
        op("vector", lambda e: e.memset(g16t[:], 0.0))
        op("vector", lambda e: e.memset(g16x[:], 0.0))
        op("vector", lambda e: e.memset(g16xT[:], 0.0))
        op("vector", lambda e: e.memset(rhsA[:], 0.0))
        op("vector", lambda e: e.memset(rhsB[:], 0.0))
        op("vector", lambda e: e.memset(tot0e[:], 0.0))
        op("vector", lambda e: e.memset(Lk0[:], 0))
        op("vector", lambda e: e.memset(Lk1[:], 0))
        op("vector", lambda e: e.memset(Lslot[:], 0))

        wt("tensor", "sC", t_const)
        wt("tensor", "sW1", t_wsmall)
        wt("vector", "sC", t_const)
        t_xnb = [0] * NT
        t_evac = [0] * NT
        t_bcv = [0] * NT

        def stageA_early(m):
            xb = xt[m % 2]
            if m >= 2:
                wt("sync", "V", t_xnb[m - 2])
            kx = f"sX{m % 2}"
            t_x = dma("sync", kx, xb[:], xsh[m], pre_wait=True)
            L = lnst[:, (m % 2) * 8:(m % 2) * 8 + 8]
            wt("vector", kx, t_x)
            op("vector", lambda e, xb=xb, L=L: e.reduce_sum(
                out=L[:, 0:1].rearrange("p (a b) -> p a b", b=1),
                in_=xb[:].rearrange("p (a b) -> p a b", a=1), axis=AX.X))
            op("vector", lambda e, xb=xb: e.tensor_tensor(out=s1[:], in0=xb[:], in1=xb[:], op=ALU.mult))
            op("vector", lambda e, L=L: e.reduce_sum(
                out=L[:, 1:2].rearrange("p (a b) -> p a b", b=1),
                in_=s1[:].rearrange("p (a b) -> p a b", a=1), axis=AX.X))
            op("vector", lambda e, L=L, m=m: e.tensor_scalar(out=mnst[:, m:m + 1], in0=L[:, 0:1],
                                                             scalar1=1.0 / D, scalar2=None, op0=ALU.mult))
            op("vector", lambda e, L=L: e.tensor_scalar(out=L[:, 3:4], in0=L[:, 1:2],
                                                        scalar1=1.0 / D, scalar2=None, op0=ALU.mult))
            op("vector", lambda e, L=L, m=m: e.tensor_tensor(out=L[:, 4:5], in0=mnst[:, m:m + 1],
                                                             in1=mnst[:, m:m + 1], op=ALU.mult))
            t_var = st("vector", "V", lambda e, L=L: e.scalar_tensor_tensor(
                out=L[:, 5:6], in0=L[:, 3:4], scalar=1e-5,
                in1=L[:, 4:5], op0=ALU.add, op1=ALU.subtract))
            wt("scalar", "V", t_var)
            op("scalar", lambda e, L=L: e.activation(out=L[:, 6:7], in_=L[:, 5:6], func=AF.Ln))
            op("scalar", lambda e, L=L: e.mul(out=L[:, 6:7], in_=L[:, 6:7], mul=-0.5))
            t_rstd = st("scalar", "A", lambda e, L=L, m=m: e.activation(
                out=rsst[:, m:m + 1], in_=L[:, 6:7], func=AF.Exp))
            wt("vector", "A", t_rstd)
            t_xnb[m] = st("vector", "V", lambda e, m=m, xb=xb: e.tensor_scalar(
                out=xn_tm[:, m * D:(m + 1) * D], in0=xb[:],
                scalar1=mnst[:, m:m + 1], scalar2=rsst[:, m:m + 1],
                op0=ALU.subtract, op1=ALU.mult))

        def stageA_late(m):
            fm = fmt[m % 2]
            ptr = p_trbs[m % 2]
            wt("tensor", "V", t_xnb[m])
            if m >= 2:
                wt("tensor", "V", t_evac[m - 2])
            t_tr = None
            for j in range(8):
                fn = (lambda e, m=m, j=j, ptr=ptr: e.transpose(
                    ptr[:, j * 128:(j + 1) * 128],
                    xn_tm[:, m * D + j * 128:m * D + (j + 1) * 128],
                    CT["ident_bf"][:]))
                if j == 7:
                    t_tr = st("tensor", "T", fn)
                else:
                    op("tensor", fn)
            wt("vector", "T", t_tr)
            t_evac[m] = st("vector", "V", lambda e, fm=fm, ptr=ptr: e.tensor_copy(out=fm, in_=ptr[:]))
            wt("tensor", "V", t_evac[m])
            if m >= 2:
                wt("tensor", "V", t_bcv[m - 2])
            pb = p_bcs[m % 2]
            for j in range(8):
                op("tensor", lambda e, fm=fm, j=j, pb=pb: e.matmul(
                    pb, fm[:, j * 128:(j + 1) * 128],
                    wbcg_sb[:, j * 40:(j + 1) * 40], start=(j == 0), stop=False))
            t_bc = st("tensor", "T", lambda e, pb=pb: e.matmul(
                pb, CT["ones1_bf"][:], brow_sb[:], start=False, stop=True))
            wt("vector", "T", t_bc)
            op("vector", lambda e, pb=pb: e.tensor_copy(out=bc40[:], in_=pb))
            op("vector", lambda e: e.tensor_tensor(out=bc16[:], in0=bc40[:, 0:16],
                                                   in1=bc40[:, 16:32], op=ALU.mult))
            op("vector", lambda e, m=m: e.reduce_sum(
                out=svst[:, m:m + 1].rearrange("p (a b) -> p a b", b=1),
                in_=bc16[:].rearrange("p (a b) -> p a b", a=1), axis=AX.X))
            t_bcv[m] = st("vector", "V", lambda e, m=m: e.tensor_copy(
                out=logits[:, m * 8:(m + 1) * 8], in_=bc40[:, 32:40]))

        for m in range(NT):
            stageA_early(m)
            if m >= 1:
                stageA_late(m - 1)
        stageA_late(NT - 1)
        t_xn_last = t_xnb[NT - 1]

        if stop_after == "p1a":
            early_out()
            off[0] = True
        z3 = r3(logits[:])
        eio3 = r3(eio8[:])
        op("vector", lambda e: e.reduce_max(out=rcol(R["m1"][:]), in_=z3, axis=AX.X))
        op("vector", lambda e: e.tensor_tensor(
            out=r3(ohs[:]), in0=z3,
            in1=rcol(R["m1"][:]).broadcast_to((128, NT, 8)), op=ALU.is_equal))
        op("vector", lambda e: e.tensor_tensor(out=tmp8[:], in0=ohs[:], in1=eio8[:], op=ALU.mult))
        op("vector", lambda e: e.scalar_tensor_tensor(out=tmp8[:], in0=ohs[:], scalar=-99.0,
                                                      in1=tmp8[:], op0=ALU.mult, op1=ALU.add))
        op("vector", lambda e: e.tensor_scalar(out=tmp8[:], in0=tmp8[:], scalar1=99.0,
                                               scalar2=None, op0=ALU.add))
        op("vector", lambda e: e.tensor_reduce(out=rcol(R["i1"][:]), in_=r3(tmp8[:]),
                                               axis=AX.X, op=ALU.min))
        op("vector", lambda e: e.tensor_tensor(
            out=r3(ohs[:]), in0=eio3,
            in1=rcol(R["i1"][:]).broadcast_to((128, NT, 8)), op=ALU.is_equal))
        op("vector", lambda e: e.scalar_tensor_tensor(out=tmp8[:], in0=ohs[:], scalar=-1e30,
                                                      in1=logits[:], op0=ALU.mult, op1=ALU.add))
        op("vector", lambda e: e.tensor_reduce(out=rcol(R["m2"][:]), in_=r3(tmp8[:]),
                                               axis=AX.X, op=ALU.max))
        op("vector", lambda e: e.tensor_tensor(
            out=r3(ohs[:]), in0=r3(tmp8[:]),
            in1=rcol(R["m2"][:]).broadcast_to((128, NT, 8)), op=ALU.is_equal))
        op("vector", lambda e: e.tensor_tensor(out=tmp8[:], in0=ohs[:], in1=eio8[:], op=ALU.mult))
        op("vector", lambda e: e.scalar_tensor_tensor(out=tmp8[:], in0=ohs[:], scalar=-99.0,
                                                      in1=tmp8[:], op0=ALU.mult, op1=ALU.add))
        op("vector", lambda e: e.tensor_scalar(out=tmp8[:], in0=tmp8[:], scalar1=99.0,
                                               scalar2=None, op0=ALU.add))
        op("vector", lambda e: e.tensor_reduce(out=rcol(R["i2"][:]), in_=r3(tmp8[:]),
                                               axis=AX.X, op=ALU.min))
        t_sm0 = st("vector", "V", lambda e: e.tensor_tensor(
            out=r3(tmp8[:]), in0=z3,
            in1=rcol(R["m1"][:]).broadcast_to((128, NT, 8)), op=ALU.subtract))
        wt("scalar", "V", t_sm0)
        t_ex = st("scalar", "A", lambda e: e.activation(out=ex8[:], in_=tmp8[:], func=AF.Exp))
        wt("vector", "A", t_ex)
        op("vector", lambda e: e.tensor_reduce(out=rcol(R["den"][:]), in_=r3(ex8[:]),
                                               axis=AX.X, op=ALU.add))
        op("vector", lambda e: e.reciprocal(out=R["w1"][:], in_=R["den"][:]))
        t_d2 = st("vector", "V", lambda e: e.tensor_sub(out=R["m2"][:], in0=R["m2"][:], in1=R["m1"][:]))
        wt("scalar", "V", t_d2)
        t_e2 = st("scalar", "A", lambda e: e.activation(out=R["den"][:], in_=R["m2"][:], func=AF.Exp))
        wt("vector", "A", t_e2)
        op("vector", lambda e: e.tensor_mul(out=R["w2"][:], in0=R["den"][:], in1=R["w1"][:]))
        t_pk = None
        for i, k in enumerate(["i1", "i2", "w1", "w2"]):
            fn = (lambda e, i=i, k=k: e.tensor_copy(out=pk[:, i * 32:(i + 1) * 32], in_=R[k][:]))
            if i == 3:
                t_pk = st("vector", "V", fn)
            else:
                op("vector", fn)
        wt("gpsimd", "V", t_pk)
        t_agin = dma("gpsimd", "sAG", ag_in.ap(), pk[:])
        wt("gpsimd", "sAG", t_agin)
        t_ag = st("gpsimd", "CC", lambda e: e.collective_compute(
            "AllGather", ALU.bypass, replica_groups=[list(range(NCORE))],
            ins=[ag_in.ap().opt()], outs=[ag_out.ap().opt()]))
        wt("gpsimd", "CC", t_ag)
        for cc in range(NCORE):
            dma("gpsimd", "sAG", tmp8[:, cc * 32:(cc + 1) * 32],
                ag_out.ap()[cc * 128:(cc + 1) * 128, 0:32])
            t_idxg = dma("gpsimd", "sAG", ex8[:, cc * 32:(cc + 1) * 32],
                         ag_out.ap()[cc * 128:(cc + 1) * 128, 32:64])
        wt("vector", "sAG", t_idxg)
        op("vector", lambda e: e.tensor_copy(
            out=idx1g[:].rearrange("p (f cc) -> p f cc", cc=8),
            in_=tmp8[:].rearrange("p (cc f) -> p f cc", cc=8)))
        op("vector", lambda e: e.tensor_copy(
            out=idx2g[:].rearrange("p (f cc) -> p f cc", cc=8),
            in_=ex8[:].rearrange("p (cc f) -> p f cc", cc=8)))

        if stop_after == "p1b":
            early_out()
            off[0] = True
        for e8 in range(8):
            op("vector", lambda e, i=e8: e.tensor_scalar(
                out=r3(G, 8)[:, :, i:i + 1],
                in0=idx1g[:].rearrange("p (c o) -> p c o", o=1),
                scalar1=float(i), scalar2=None, op0=ALU.is_equal))
            op("vector", lambda e, i=e8: e.tensor_scalar(
                out=r3(srow, 8)[:, :, i:i + 1],
                in0=idx2g[:].rearrange("p (c o) -> p c o", o=1),
                scalar1=float(i), scalar2=None, op0=ALU.is_equal))
        t_G = st("vector", "V", lambda e: e.tensor_add(out=G, in0=G, in1=srow))
        if stop_after == "p3a":
            early_out(); off[0] = True
        t_sz = st("vector", "V", lambda e: e.memset(srow, 0.0))
        wt("tensor", "V", t_sz)
        t_cs = None
        for qq in range(4):
            fn = (lambda e, q=qq: e.matmul(p_row[:, q * 512:(q + 1) * 512], CT["onescol"][:],
                                           G[:, q * 512:(q + 1) * 512], start=True, stop=True))
            if qq == 3:
                t_cs = st("tensor", "T", fn)
            else:
                op("tensor", fn)
        wt("vector", "T", t_cs)
        t_srow = st("vector", "V", lambda e: e.tensor_copy(out=srow[0:1, :], in_=p_row[:]))
        wt("tensor", "V", t_srow)
        for rnd in range(2):
            t_trr = None
            for j8 in range(8):
                j = rnd * 8 + j8
                fn = (lambda e, j=j, j8=j8: e.matmul(
                    psumP[:, 1024 + j8 * 128:1024 + (j8 + 1) * 128],
                    srow[:, j * 128:(j + 1) * 128], CT["ident_f32"][:],
                    start=True, stop=True))
                if j8 == 7:
                    t_trr = st("tensor", "T", fn)
                else:
                    op("tensor", fn)
            wt("vector", "T", t_trr)
            t_t16 = None
            for j8 in range(8):
                j = rnd * 8 + j8
                fn = (lambda e, j=j, j8=j8: e.tensor_copy(
                    out=t16[:, j:j + 1], in_=psumP[:, 1024 + j8 * 128:1024 + j8 * 128 + 1]))
                if j8 == 7:
                    t_t16 = st("vector", "V", fn)
                else:
                    op("vector", fn)
            wt("tensor", "V", t_t16)
        if stop_after == "p3b":
            early_out(); off[0] = True
        op("tensor", lambda e: e.matmul(psumP[:, 1024:1040], CT["ltri16x8"][:], t16[:],
                                        start=True, stop=True))
        t_mm1 = st("tensor", "T", lambda e: e.matmul(psumP[0:32, 1040:1056], CT["mod8p"][:], t16[:],
                                                     start=True, stop=True))
        wt("vector", "T", t_mm1)
        op("vector", lambda e: e.tensor_copy(out=a1[:], in_=psumP[:, 1024:1040]))
        t_g16 = st("vector", "V", lambda e: e.tensor_copy(out=g16[0:8, 0:16], in_=psumP[0:8, 1040:1056]))
        wt("tensor", "V", t_g16)
        t_mm2 = st("tensor", "T", lambda e: e.matmul(psumP[0:32, 1056:1184], g16[:, 0:16].bitcast(F32) if False else g16[:], CT["ident_f32"][:], start=True, stop=True))
        wt("vector", "T", t_mm2)
        t_g16t = st("vector", "V", lambda e: e.tensor_copy(out=g16t[0:16, :], in_=psumP[0:16, 1056:1072]))
        wt("tensor", "V", t_g16t)
        t_mm3 = st("tensor", "T", lambda e: e.matmul(psumP[0:32, 1184:1192], CT["ltri16p"][:],
                                                     g16t[:, 0:8], start=True, stop=True))
        wt("vector", "T", t_mm3)
        t_g16x = st("vector", "V", lambda e: e.tensor_copy(out=g16x[0:16, 0:8], in_=psumP[0:16, 1184:1192]))
        if stop_after == "p3b2":
            early_out(); off[0] = True
        wt("tensor", "V", t_g16x)
        t_mm4 = st("tensor", "T", lambda e: e.matmul(psumP[0:32, 1192:1208], CT["selown16"][:],
                                                     a1[:], start=True, stop=True))
        wt("vector", "T", t_mm4)
        t_o16 = st("vector", "V", lambda e: e.tensor_copy(out=o16[:], in_=psumP[0:16, 1192:1208]))
        op("vector", lambda e: e.tensor_tensor(
            out=rhsA[0:16, :].rearrange("r (j h ee) -> r j h ee", h=2, ee=8),
            in0=o16[:].rearrange("r (j h ee) -> r j h ee", h=1, ee=1).broadcast_to((16, 16, 2, 8)),
            in1=CT["m16pat"][:].rearrange("r (j h ee) -> r j h ee", h=2, ee=8), op=ALU.mult))
        op("vector", lambda e: e.tensor_tensor(
            out=rhsB[0:16, :].rearrange("r (j h ee) -> r j h ee", h=2, ee=8),
            in0=g16x[0:16, 0:8].rearrange("r (j h ee) -> r j h ee", j=1, h=1).broadcast_to((16, 16, 2, 8)),
            in1=CT["m16bpat"][:].rearrange("r (j h ee) -> r j h ee", h=2, ee=8), op=ALU.mult))
        t_rhs = st("vector", "V", lambda e: e.tensor_add(out=rhsA[:], in0=rhsA[:], in1=rhsB[:]))
        wt("tensor", "V", t_rhs)
        t_mm5 = st("tensor", "T", lambda e: e.matmul(psumP[:, 1024:1280], CT["ones16p"][:],
                                                     rhsA[:], start=True, stop=True))
        wt("vector", "T", t_mm5)
        t_ownoff = st("vector", "V", lambda e: e.tensor_copy(out=ownoff[:], in_=psumP[:, 1024:1280]))
        if stop_after == "p3b3":
            early_out(); off[0] = True
        wt("tensor", "V", t_g16x)
        t_mm6 = st("tensor", "T", lambda e: e.matmul(psumP[0:16, 1280:1296], g16t[:],
                                                     CT["ultri32"][:], start=True, stop=True))
        wt("vector", "T", t_mm6)
        t_gxt = st("vector", "V", lambda e: e.tensor_copy(out=g16xT[0:8, :], in_=psumP[0:8, 1280:1296]))
        if stop_after == "p3b4a":
            early_out(); off[0] = True
        wt("tensor", "V", t_gxt)
        t_mm7 = st("tensor", "T", lambda e: e.matmul(psumP[:, 1408:1424], CT["mod8tp"][:],
                                                     g16xT[:], start=True, stop=True))
        wt("vector", "T", t_mm7)
        op("vector", lambda e: e.tensor_copy(out=offgT[:], in_=psumP[:, 1408:1424]))
        if stop_after == "p3b4b":
            early_out(); off[0] = True
        op("vector", lambda e: e.tensor_add(out=kscr[:], in0=a1[:], in1=offgT[:]))
        op("vector", lambda e: e.tensor_scalar(out=kscr[:], in0=kscr[:], scalar1=-1.0,
                                               scalar2=float(CAPN), op0=ALU.mult, op1=ALU.add))
        op("vector", lambda e: e.tensor_scalar(out=kscr[:], in0=kscr[:], scalar1=0.0,
                                               scalar2=None, op0=ALU.max))
        t_kept = st("vector", "V", lambda e: e.tensor_tensor(out=keptT[:], in0=kscr[:],
                                                             in1=t16[:], op=ALU.min))
        if stop_after == "p3b4":
            early_out(); off[0] = True
        wt("tensor", "V", t_kept)
        t_mm8 = st("tensor", "T", lambda e: e.matmul(psumP[0:64, 1424:1440], CT["selc8"][:],
                                                     keptT[:], start=True, stop=True))
        wt("vector", "T", t_mm8)
        op("vector", lambda e: e.tensor_copy(out=vc64[:], in_=psumP[0:64, 1424:1440]))
        t_vcr = st("vector", "V", lambda e: e.tensor_reduce(
            out=vc64r[:].rearrange("p (a b) -> p a b", b=1),
            in_=vc64[:].rearrange("p (a b) -> p a b", a=1), axis=AX.X, op=ALU.add))
        wt("tensor", "V", t_vcr)
        op("tensor", lambda e: e.matmul(psumP[0:64, 1440:1441], CT["mask64x"][:],
                                        vc64r[:], start=True, stop=True))
        t_mm9 = st("tensor", "T", lambda e: e.matmul(psumP[0:64, 1441:1442], CT["mask64i"][:],
                                                     vc64r[:], start=True, stop=True))
        wt("vector", "T", t_mm9)
        op("vector", lambda e: e.tensor_copy(out=cum64[:], in_=psumP[0:64, 1440:1441]))
        t_c64 = st("vector", "V", lambda e: e.tensor_copy(out=inc64[:], in_=psumP[0:64, 1441:1442]))
        if stop_after == "p3b5":
            early_out(); off[0] = True
        t_dg1 = st("vector", "V", lambda e: e.tensor_scalar(out=scr64[0:64, :], in0=CT["i64"][:],
                                                            scalar1=cum64[:, 0:1], scalar2=None, op0=ALU.mult))
        wt("tensor", "V", t_dg1)
        t_mmA = st("tensor", "T", lambda e: e.matmul(psumP[:, 1442:1506], CT["ones64"][:],
                                                     scr64[0:64, :], start=True, stop=True))
        wt("vector", "T", t_mmA)
        t_cb = st("vector", "V", lambda e: e.tensor_copy(out=cumB[:], in_=psumP[:, 1442:1506]))
        t_dg2 = st("vector", "V", lambda e: e.tensor_scalar(out=scr64[0:64, :], in0=CT["i64"][:],
                                                            scalar1=inc64[:, 0:1], scalar2=None, op0=ALU.mult))
        wt("tensor", "V", t_dg2)
        t_mmB = st("tensor", "T", lambda e: e.matmul(psumP[:, 1536:1600], CT["ones64"][:],
                                                     scr64[0:64, :], start=True, stop=True))
        wt("vector", "T", t_mmB)
        op("vector", lambda e: e.tensor_copy(out=incB[:], in_=psumP[:, 1536:1600]))
        op("vector", lambda e: e.tensor_tensor(out=scr64[:], in0=cumB[:], in1=CT["memask"][:], op=ALU.mult))
        op("vector", lambda e: e.tensor_reduce(
            out=mycum[:].rearrange("p (c o) -> p c o", o=1),
            in_=scr64[:].rearrange("p (c e) -> p c e", e=8), axis=AX.X, op=ALU.add))
        op("vector", lambda e: e.tensor_tensor(out=scr64[:], in0=incB[:], in1=CT["memask"][:], op=ALU.mult))
        op("vector", lambda e: e.tensor_reduce(
            out=myinc[:].rearrange("p (c o) -> p c o", o=1),
            in_=scr64[:].rearrange("p (c e) -> p c e", e=8), axis=AX.X, op=ALU.add))
        op("vector", lambda e: e.tensor_sub(out=myvc[:], in0=myinc[:], in1=mycum[:]))
        op("vector", lambda e: e.tensor_tensor(out=scr64[:], in0=incB[:], in1=CT["memask_tot"][:], op=ALU.mult))
        op("vector", lambda e: e.tensor_reduce(
            out=mytot[:].rearrange("p (a b) -> p a b", b=1),
            in_=scr64[:].rearrange("p (a b) -> p a b", a=1), axis=AX.X, op=ALU.add))
        if stop_after == "p3c":
            early_out(); off[0] = True
        for e8 in range(8):
            op("vector", lambda e, i=e8: e.tensor_scalar(
                out=r3(oh1[:])[:, :, i:i + 1], in0=rcol(R["i1"][:]),
                scalar1=float(i), scalar2=None, op0=ALU.is_equal))
            op("vector", lambda e, i=e8: e.tensor_scalar(
                out=r3(oh2[:])[:, :, i:i + 1], in0=rcol(R["i2"][:]),
                scalar1=float(i), scalar2=None, op0=ALU.is_equal))
        t_ohs = st("vector", "V", lambda e: e.tensor_add(out=ohs[:], in0=oh1[:], in1=oh2[:]))
        wt("tensor", "V", t_ohs)
        t_mmC = st("tensor", "T", lambda e: e.matmul(psumP[:, 1024:1280], CT["ltri128"][:],
                                                     ohs[:], start=True, stop=True))
        wt("vector", "T", t_mmC)
        op("vector", lambda e: e.tensor_copy(out=ecsU[:], in_=psumP[:, 1024:1280]))
        op("vector", lambda e: e.tensor_tensor(out=selbuf[:], in0=oh1[:], in1=ecsU[:], op=ALU.mult))
        op("vector", lambda e: e.tensor_reduce(out=rcol(R["vr1"][:]), in_=r3(selbuf[:]),
                                               axis=AX.X, op=ALU.add))
        op("vector", lambda e: e.tensor_tensor(out=selbuf[:], in0=oh1[:], in1=ownoff[:], op=ALU.mult))
        op("vector", lambda e: e.tensor_reduce(out=rcol(R["t1"][:]), in_=r3(selbuf[:]),
                                               axis=AX.X, op=ALU.add))
        op("vector", lambda e: e.tensor_add(out=R["t1"][:], in0=R["t1"][:], in1=R["vr1"][:]))
        op("vector", lambda e: e.tensor_scalar(out=R["v1"][:], in0=R["t1"][:],
                                               scalar1=float(CAPN), scalar2=None, op0=ALU.is_lt))
        op("vector", lambda e: e.tensor_add(out=selbuf[:], in0=ecsU[:], in1=oh1[:]))
        op("vector", lambda e: e.tensor_tensor(out=selbuf[:], in0=oh2[:], in1=selbuf[:], op=ALU.mult))
        op("vector", lambda e: e.tensor_reduce(out=rcol(R["vr2"][:]), in_=r3(selbuf[:]),
                                               axis=AX.X, op=ALU.add))
        op("vector", lambda e: e.tensor_tensor(out=selbuf[:], in0=oh2[:], in1=ownoff[:], op=ALU.mult))
        op("vector", lambda e: e.tensor_reduce(out=rcol(R["t2"][:]), in_=r3(selbuf[:]),
                                               axis=AX.X, op=ALU.add))
        op("vector", lambda e: e.tensor_add(out=R["t2"][:], in0=R["t2"][:], in1=R["vr2"][:]))
        op("vector", lambda e: e.tensor_scalar(out=R["v2"][:], in0=R["t2"][:],
                                               scalar1=float(CAPN), scalar2=None, op0=ALU.is_lt))
        op("vector", lambda e: e.tensor_tensor(
            out=r3(ohv1[:]), in0=r3(oh1[:]),
            in1=rcol(R["v1"][:]).broadcast_to((128, NT, 8)), op=ALU.mult))
        op("vector", lambda e: e.tensor_tensor(
            out=r3(ohv2[:]), in0=r3(oh2[:]),
            in1=rcol(R["v2"][:]).broadcast_to((128, NT, 8)), op=ALU.mult))
        t_ohv = st("vector", "V", lambda e: e.tensor_add(out=ohs[:], in0=ohv1[:], in1=ohv2[:]))
        wt("tensor", "V", t_ohv)
        t_mmD = st("tensor", "T", lambda e: e.matmul(psumP[:, 1280:1536], CT["ltri128"][:],
                                                     ohs[:], start=True, stop=True))
        wt("vector", "T", t_mmD)
        op("vector", lambda e: e.tensor_copy(out=ecsV[:], in_=psumP[:, 1280:1536]))
        op("vector", lambda e: e.tensor_tensor(out=selbuf[:], in0=ohv1[:], in1=ecsV[:], op=ALU.mult))
        op("vector", lambda e: e.tensor_reduce(out=rcol(R["vr1"][:]), in_=r3(selbuf[:]),
                                               axis=AX.X, op=ALU.add))
        op("vector", lambda e: e.tensor_add(out=selbuf[:], in0=ecsV[:], in1=ohv1[:]))
        op("vector", lambda e: e.tensor_tensor(out=selbuf[:], in0=ohv2[:], in1=selbuf[:], op=ALU.mult))
        op("vector", lambda e: e.tensor_reduce(out=rcol(R["vr2"][:]), in_=r3(selbuf[:]),
                                               axis=AX.X, op=ALU.add))
        t_csz = st("vector", "V", lambda e: e.memset(csrowV[:], 0.0))
        wt("tensor", "V", max(t_ohv, t_csz))
        t_mmE = st("tensor", "T", lambda e: e.matmul(psumP[0:1, 1536:1792], CT["onescol"][:],
                                                     ohs[:], start=True, stop=True))
        wt("vector", "T", t_mmE)
        t_csr = st("vector", "V", lambda e: e.tensor_copy(out=csrowV[0:1, :], in_=psumP[0:1, 1536:1792]))
        wt("tensor", "V", t_csr)
        t_tr2 = None
        for half in range(2):
            fn = (lambda e, half=half: e.matmul(
                psumP[:, 1536 + half * 128:1536 + (half + 1) * 128],
                csrowV[:, half * 128:(half + 1) * 128], CT["ident_f32"][:],
                start=True, stop=True))
            if half == 1:
                t_tr2 = st("tensor", "T", fn)
            else:
                op("tensor", fn)
        wt("vector", "T", t_tr2)
        op("vector", lambda e: e.tensor_copy(out=a2t[:, 0:1], in_=psumP[:, 1536:1537]))
        t_a2t = st("vector", "V", lambda e: e.tensor_copy(out=a2t[:, 1:2], in_=psumP[:, 1664:1665]))
        wt("tensor", "V", t_a2t)
        op("tensor", lambda e: e.matmul(psumP[:, 1792:1794], CT["ltri16x8"][:], a2t[:],
                                        start=True, stop=True))
        t_mmF = st("tensor", "T", lambda e: e.matmul(psumP[0:32, 1794:1796], CT["mod8p"][:],
                                                     a2t[:], start=True, stop=True))
        wt("vector", "T", t_mmF)
        op("vector", lambda e: e.tensor_copy(out=a2x[:], in_=psumP[:, 1792:1794]))
        t_t0e = st("vector", "V", lambda e: e.tensor_copy(out=tot0e[0:8, :], in_=psumP[0:8, 1794:1796]))
        wt("tensor", "V", t_t0e)
        t_mmG = st("tensor", "T", lambda e: e.matmul(psumP[:, 1796:1797], CT["mod8tp"][:],
                                                     tot0e[:, 0:1], start=True, stop=True))
        wt("vector", "T", t_mmG)
        op("vector", lambda e: e.tensor_copy(out=tot0b[:], in_=psumP[:, 1796:1797]))
        op("vector", lambda e: e.tensor_add(out=a2x[:, 1:2], in0=a2x[:, 1:2], in1=tot0b[:]))
        t_rhs3 = st("vector", "V", lambda e: e.tensor_tensor(
            out=rhs3[:].rearrange("p (h a ee) -> p h a ee", a=16, ee=8),
            in0=a2x[:].rearrange("p (h a ee) -> p h a ee", a=1, ee=1).broadcast_to((128, 2, 16, 8)),
            in1=CT["m128pat"][:].rearrange("p (h a ee) -> p h a ee", a=16, ee=8), op=ALU.mult))
        wt("tensor", "V", t_rhs3)
        t_mmH = st("tensor", "T", lambda e: e.matmul(psumP[:, 1536:1792], CT["ones128sq"][:],
                                                     rhs3[:], start=True, stop=True))
        wt("vector", "T", t_mmH)
        op("vector", lambda e: e.tensor_copy(out=offs2B[:], in_=psumP[:, 1536:1792]))
        op("vector", lambda e: e.tensor_tensor(out=selbuf[:], in0=oh1[:], in1=offs2B[:], op=ALU.mult))
        op("vector", lambda e: e.tensor_reduce(out=rcol(R["t1"][:]), in_=r3(selbuf[:]),
                                               axis=AX.X, op=ALU.add))
        op("vector", lambda e: e.tensor_add(out=R["vr1"][:], in0=R["vr1"][:], in1=R["t1"][:]))
        op("vector", lambda e: e.tensor_tensor(out=selbuf[:], in0=oh2[:], in1=offs2B[:], op=ALU.mult))
        op("vector", lambda e: e.tensor_reduce(out=rcol(R["t2"][:]), in_=r3(selbuf[:]),
                                               axis=AX.X, op=ALU.add))
        op("vector", lambda e: e.tensor_add(out=R["vr2"][:], in0=R["vr2"][:], in1=R["t2"][:]))
        op("vector", lambda e: e.scalar_tensor_tensor(out=R["sx1"][:], in0=R["i1"][:],
                                                      scalar=float(CAP), in1=R["vr1"][:],
                                                      op0=ALU.mult, op1=ALU.add))
        op("vector", lambda e: e.tensor_scalar(out=R["sx1"][:], in0=R["sx1"][:],
                                               scalar1=float(-DUMMY), scalar2=None, op0=ALU.add))
        op("vector", lambda e: e.tensor_tensor(out=R["sx1"][:], in0=R["v1"][:], in1=R["sx1"][:], op=ALU.mult))
        op("vector", lambda e: e.tensor_scalar(out=R["sx1"][:], in0=R["sx1"][:],
                                               scalar1=float(DUMMY), scalar2=None, op0=ALU.add))
        op("vector", lambda e: e.scalar_tensor_tensor(out=R["sx2"][:], in0=R["i2"][:],
                                                      scalar=float(CAP), in1=R["vr2"][:],
                                                      op0=ALU.mult, op1=ALU.add))
        op("vector", lambda e: e.tensor_scalar(out=R["sx2"][:], in0=R["sx2"][:],
                                               scalar1=float(-DUMMY), scalar2=None, op0=ALU.add))
        op("vector", lambda e: e.tensor_tensor(out=R["sx2"][:], in0=R["v2"][:], in1=R["sx2"][:], op=ALU.mult))
        op("vector", lambda e: e.tensor_scalar(out=R["sx2"][:], in0=R["sx2"][:],
                                               scalar1=float(DUMMY), scalar2=None, op0=ALU.add))
        op("vector", lambda e: e.tensor_tensor(out=R["wm1"][:], in0=R["w1"][:], in1=R["v1"][:], op=ALU.mult))
        op("vector", lambda e: e.tensor_tensor(out=R["wm2"][:], in0=R["w2"][:], in1=R["v2"][:], op=ALU.mult))
        op("vector", lambda e: e.memset(R["csum"][:], 0.0))
        op("vector", lambda e: e.memset(R["acc"][:], 0.0))
        for c_ in range(1, 8):
            op("vector", lambda e, c_=c_: e.tensor_scalar(
                out=R["t1"][:], in0=CT["siota"][:], scalar1=mycum[:, c_:c_ + 1],
                scalar2=None, op0=ALU.is_ge))
            op("vector", lambda e: e.tensor_add(out=R["csum"][:], in0=R["csum"][:], in1=R["t1"][:]))
            op("vector", lambda e, c_=c_: e.scalar_tensor_tensor(
                out=R["t2"][:], in0=R["t1"][:], scalar=myvc[:, c_ - 1:c_],
                in1=R["acc"][:], op0=ALU.mult, op1=ALU.add))
            op("vector", lambda e: e.tensor_copy(out=R["acc"][:], in_=R["t2"][:]))
        op("vector", lambda e: e.tensor_sub(out=R["t1"][:], in0=CT["siota"][:], in1=R["acc"][:]))
        op("vector", lambda e: e.scalar_tensor_tensor(out=R["slotv"][:], in0=R["csum"][:],
                                                      scalar=float(CAP), in1=R["t1"][:],
                                                      op0=ALU.mult, op1=ALU.add))
        op("vector", lambda e: e.tensor_scalar(out=R["t1"][:], in0=CT["siota"][:],
                                               scalar1=mytot[:, 0:1], scalar2=None, op0=ALU.is_ge))
        op("vector", lambda e: e.tensor_scalar(out=R["t2"][:], in0=R["slotv"][:],
                                               scalar1=-1.0, scalar2=float(DUMMY),
                                               op0=ALU.mult, op1=ALU.add))
        op("vector", lambda e: e.tensor_tensor(out=R["t2"][:], in0=R["t1"][:], in1=R["t2"][:], op=ALU.mult))
        t_slotv = st("vector", "V", lambda e: e.tensor_add(out=R["slotv"][:], in0=R["slotv"][:], in1=R["t2"][:]))

        if stop_after == "p3d":
            early_out(); off[0] = True
        def fold(src, dst, tag):
            for g in range(8):
                pslot = 1024 + (g % 2) * 64
                t_fm_ = st("tensor", "T", lambda e, g=g, src=src, pslot=pslot: e.matmul(
                    psumP[:, pslot:pslot + 32], CT[f"foldsel{g}"][:], src[:],
                    start=True, stop=True))
                wt("vector", "T", t_fm_)
                t_fc = st("vector", "V", lambda e, g=g, dst=dst, pslot=pslot: e.tensor_copy(
                    out=dst[:].rearrange("p (mm gg) -> p mm gg", gg=8)[:, :, g:g + 1],
                    in_=psumP[:, pslot:pslot + 32].rearrange("p (a b) -> p a b", b=1)))
                if g >= 1:
                    wt("tensor", "V", t_fc)
            return cnt["V"]

        wt("tensor", "V", t_slotv)
        fold(R["sx1"], Lk0, "k0")
        fold(R["sx2"], Lk1, "k1")
        t_folds = fold(R["slotv"], Lslot, "sl")

        wt("tensor", "V", t_folds)
        wt("sync", "V", t_folds)
        t_y = [0] * NT
        t_ev2 = [0] * NT
        t_zf = [0] * NT

        def stageB(m):
            xb = xt[m % 2]
            fm = fmt[m % 2]
            ptr = p_trbs[m % 2]
            pz = p_zs[m % 2]
            kx = f"sX{m % 2}"
            if m >= 2:
                wt("sync", "V", t_y[m - 2])
            t_x = dma("sync", kx, xb[:], xsh[m], pre_wait=True)
            if m >= 2:
                wt("tensor", "V", t_ev2[m - 2])
            t_tr = None
            for j in range(8):
                fn = (lambda e, m=m, j=j, ptr=ptr: e.transpose(
                    ptr[:, j * 128:(j + 1) * 128],
                    xn_tm[:, m * D + j * 128:m * D + (j + 1) * 128],
                    CT["ident_bf"][:]))
                if j == 7:
                    t_tr = st("tensor", "T", fn)
                else:
                    op("tensor", fn)
            wt("vector", "T", t_tr)
            t_ev2[m] = st("vector", "V", lambda e, fm=fm, ptr=ptr: e.tensor_copy(out=fm, in_=ptr[:]))
            wt("tensor", "V", t_ev2[m])
            if m >= 2:
                wt("tensor", "A", t_zf[m - 2])
            t_zmm = None
            for o in range(2):
                for j in range(8):
                    op("tensor", lambda e, fm=fm, o=o, j=j, pz=pz: e.matmul(
                        pz[:, o * 512:(o + 1) * 512], fm[:, j * 128:(j + 1) * 128],
                        w1_sb[:, j * D + o * 512:j * D + (o + 1) * 512],
                        start=(j == 0), stop=False))
                fnb = (lambda e, o=o, pz=pz: e.matmul(
                    pz[:, o * 512:(o + 1) * 512], CT["ones1_bf"][:],
                    b1_sb[:, o * 512:(o + 1) * 512], start=False, stop=True))
                if o == 1:
                    t_zmm = st("tensor", "T", fnb)
                else:
                    op("tensor", fnb)
            wt("scalar", "T", t_zmm)
            if m >= 1:
                wt("scalar", "V", t_y[m - 1])
            op("scalar", lambda e, pz=pz: e.activation(out=s1[:], in_=pz, func=AF.Exp))
            t_zf[m] = st("scalar", "A", lambda e: e.activation(out=s2, in_=s1[:], func=AF.Ln, bias=1.0))
            wt("vector", "A", t_zf[m])
            wt("vector", kx, t_x)
            op("vector", lambda e, m=m: e.scalar_tensor_tensor(
                out=s0[:], in0=xn_tm[:, m * D:(m + 1) * D], scalar=svst[:, m:m + 1],
                in1=s2, op0=ALU.mult, op1=ALU.mult))
            t_y[m] = st("vector", "V", lambda e, xb=xb: e.tensor_add(out=s0[:], in0=xb[:], in1=s0[:]))
            wt("sync", "V", t_y[m])
            kp = f"sP{m % 2}"
            t_pst = dma("sync", kp, partial.ap()[m], s0[:], pre_wait=True)
            if m < NT - 1:
                wt("vector", kp, t_pst)

        for m in range(NT):
            stageB(m)
        t_zB_last = cnt["T"]

        if debug:
            wt("gpsimd", "V", t_folds)
            dma("gpsimd", "sDBG", dbg["d_r1"], pk[:])
            for i, k in enumerate(["v1", "v2", "vr1", "vr2"]):
                op("vector", lambda e, i=i, k=k: e.tensor_copy(out=pk[:, i * 32:(i + 1) * 32], in_=R[k][:]))
            t_dbg2 = st("vector", "V", lambda e: e.tensor_copy(out=ex8[:, 0:64], in_=cumB[:]))
            wt("gpsimd", "V", t_dbg2)
            dma("gpsimd", "sDBG", dbg["d_r2"], pk[:])
            dma("gpsimd", "sDBG", dbg["d_r3"][:, 0:64], ex8[0:128, 0:64])
            for i, k in enumerate(["sx1", "sx2", "slotv"]):
                op("vector", lambda e, i=i, k=k: e.tensor_copy(out=tmp8[:, i * 32:(i + 1) * 32], in_=R[k][:]))
            t_dbg4 = st("vector", "V", lambda e: e.tensor_copy(out=tmp8[:, 95:96], in_=mytot[:]))
            wt("gpsimd", "V", t_dbg4)
            dma("gpsimd", "sDBG", dbg["d_r4"], tmp8[:, 0:96])

        class _Stop(Exception):
            pass

        try:
            if stop_after == "p1":
                early_out()
                raise _Stop()
            wt("gpsimd", "V", t_folds)
            wt("gpsimd", "V", t_xn_last)
            wt("gpsimd", "sZ", t_zero)
            t_sc1 = None
            for kk, L in ((0, Lk0), (1, Lk1)):
                for jj in range(4):
                    if t_sc1:
                        wt("gpsimd", "sSC", t_sc1)
                    t_sc1 = bump("sSC", 16)
                    emit("gpsimd", lambda e, L=L, jj=jj: e.dma_scatter_add(
                        out_ap=send1.ap(),
                        in_ap=xn_tm[:, jj * 8192:(jj + 1) * 8192].rearrange("p (m d) -> p m d", d=D),
                        idxs_ap=L[:, jj * 64:(jj + 1) * 64], num_idxs=1024,
                        num_idxs_reg=1024, elem_size=D).then_inc(sem["sSC"], 16))
            wt("gpsimd", "sSC", t_sc1)
            t_a2a1 = st("gpsimd", "CC", lambda e: e.collective_compute(
                "AllToAll", ALU.bypass, replica_groups=[list(range(NCORE))],
                ins=[send1.ap()[0:DUMMY, :].opt()], outs=[recv1.ap()[0:DUMMY, :].opt()]))
            wt("gpsimd", "T", t_zB_last)
            for dc in range(8):
                dma("gpsimd", "sWF", cfc_sb[:, dc * DFF:(dc + 1) * DFF], cfc[dc * 128:(dc + 1) * 128, :])
            for j in range(12):
                t_wffn = dma("gpsimd", "sWF", cpr_sb[:, j * D:(j + 1) * D], cpr[j * 128:(j + 1) * 128, :])
            wt("gpsimd", "CC", t_a2a1)

            exph = [xn_tm[:, hh_ * 4096:(hh_ + 1) * 4096].rearrange("p (b t) -> p b t", b=8)
                    for hh_ in range(8)]
            t_gath = [0] * 8
            wt("gpsimd", "V", t_y[NT - 1])
            for h in range(8):
                kg = f"sEG{h % 2}"
                if cnt[kg] > 0:
                    wt("gpsimd", kg, cnt[kg])
                t_gath[h] = bump(kg, 16)
                emit("gpsimd", lambda e, h=h, kg=kg: e.dma_gather(
                    out_ap=exph[h], in_ap=recv1.ap(), idxs_ap=Lslot[:, h * 32:(h + 1) * 32],
                    num_idxs=512, num_idxs_reg=512, elem_size=D,
                    transpose=True).then_inc(sem[kg], 16))
            wt("tensor", "sWF", t_wffn)
            wt("tensor", "A", t_zf[NT - 1])
            if stop_after == "a2a1":
                wt("gpsimd", "sEG0", cnt["sEG0"])
                wt("gpsimd", "sEG1", cnt["sEG1"])
                early_out()
                raise _Stop()

            t_expo = [0] * 8
            t_lastmult = None
            for s_ in range(8):
                wt("tensor", f"sEG{s_ % 2}", t_gath[s_])
                for j in range(12):
                    pa = psumP[:, (j % 2) * 1024:(j % 2) * 1024 + 512]
                    pg = psumP[:, (j % 2) * 1024 + 512:(j % 2) * 1024 + 1024]
                    if s_ > 0 or j >= 2:
                        wt("tensor", "V", cnt["V"])
                    for dc in range(8):
                        op("tensor", lambda e, j=j, dc=dc, s_=s_, pa=pa: e.matmul(
                            pa, cfc_sb[:, dc * DFF + j * 128:dc * DFF + (j + 1) * 128],
                            exph[s_][:, dc, :],
                            start=(dc == 0), stop=(dc == 7)))
                    t_mm_g = None
                    for dc in range(8):
                        fn = (lambda e, j=j, dc=dc, s_=s_, pg=pg: e.matmul(
                            pg, cfc_sb[:, dc * DFF + (j + 12) * 128:dc * DFF + (j + 13) * 128],
                            exph[s_][:, dc, :],
                            start=(dc == 0), stop=(dc == 7)))
                        if dc == 7:
                            t_mm_g = st("tensor", "T", fn)
                        else:
                            op("tensor", fn)
                    wt("scalar", "T", t_mm_g)
                    if t_lastmult:
                        wt("scalar", "V", t_lastmult)
                    t_silu = st("scalar", "A", lambda e, pa=pa: e.activation(out=sa, in_=pa, func=AF.Silu))
                    wt("vector", "A", t_silu)
                    t_lastmult = st("vector", "V", lambda e, j=j, pg=pg: e.tensor_tensor(
                        out=hh[:, j * 512:(j + 1) * 512], in0=sa, in1=pg, op=ALU.mult))
                t_hh = cnt["V"]
                wt("tensor", "V", t_hh)
                for sub in range(4):
                    t_o = None
                    for nh in range(2):
                        for j in range(12):
                            fn = (lambda e, sub=sub, nh=nh, j=j: e.matmul(
                                p_z[:, nh * 512:(nh + 1) * 512],
                                hh[:, j * 512 + sub * 128:j * 512 + (sub + 1) * 128],
                                cpr_sb[:, j * D + nh * 512:j * D + (nh + 1) * 512],
                                start=(j == 0), stop=(j == 11)))
                            if nh == 1 and j == 11:
                                t_o = st("tensor", "T", fn)
                            else:
                                op("tensor", fn)
                    wt("vector", "T", t_o)
                    t_ev = st("vector", "V", lambda e, sub=sub: e.tensor_copy(
                        out=expo[:, sub * D:(sub + 1) * D], in_=p_z[:]))
                    wt("tensor", "V", t_ev)
                t_evall = cnt["V"]
                wt("gpsimd", "V", t_evall)
                keo = "sEO0"
                if cnt[keo] > 0:
                    wt("gpsimd", keo, cnt[keo])
                t_expo[s_] = bump(keo, 16)
                emit("gpsimd", lambda e, s_=s_, keo=keo: e.dma_scatter_add(
                    out_ap=send2.ap(), in_ap=expo[:].rearrange("p (a d) -> p a d", d=D),
                    idxs_ap=Lslot[:, s_ * 32:(s_ + 1) * 32], num_idxs=512,
                    num_idxs_reg=512, elem_size=D).then_inc(sem[keo], 16))
                if s_ < 7:
                    wt("vector", keo, t_expo[s_])

            wt("gpsimd", "sEO0", cnt["sEO0"])
            t_a2a2 = st("gpsimd", "CC", lambda e: e.collective_compute(
                "AllToAll", ALU.bypass, replica_groups=[list(range(NCORE))],
                ins=[send2.ap()[0:DUMMY, :].opt()], outs=[recv2.ap()[0:DUMMY, :].opt()]))
            wt("gpsimd", "CC", t_a2a2)
            if stop_after == "ffn":
                early_out()
                raise _Stop()
            gb0 = cfc_sb[:, 0:8192].rearrange("p (b d) -> p b d", d=D)
            gb1 = cfc_sb[:, 8192:16384].rearrange("p (b d) -> p b d", d=D)
            wt("sync", "A", cnt["A"])
            wt("sync", "V", cnt["V"])
            wt("vector", "sP0", cnt["sP0"])
            wt("vector", "sP1", cnt["sP1"])
            t_out = None
            for cj in range(4):
                if cnt["sOG0"] > 0:
                    wt("gpsimd", "sOG0", cnt["sOG0"])
                    wt("gpsimd", "sOG1", cnt["sOG1"])
                tg0 = bump("sOG0", 16)
                emit("gpsimd", lambda e, cj=cj: e.dma_gather(
                    out_ap=gb0, in_ap=recv2.ap(), idxs_ap=Lk0[:, cj * 64:(cj + 1) * 64],
                    num_idxs=1024, num_idxs_reg=1024, elem_size=D).then_inc(sem["sOG0"], 16))
                tg1 = bump("sOG1", 16)
                emit("gpsimd", lambda e, cj=cj: e.dma_gather(
                    out_ap=gb1, in_ap=recv2.ap(), idxs_ap=Lk1[:, cj * 64:(cj + 1) * 64],
                    num_idxs=1024, num_idxs_reg=1024, elem_size=D).then_inc(sem["sOG1"], 16))
                wt("vector", "sOG0", tg0)
                wt("vector", "sOG1", tg1)
                for b in range(8):
                    mloc = cj * 8 + b
                    pb = pt8[b % 2]
                    ob = ot8[b % 2]
                    if cj > 0 or b >= 2:
                        wt("sync", "V", cnt["V"])
                    kpl = f"sPL{b % 2}"
                    t_pl = dma("sync", kpl, pb[:], partial.ap()[mloc], pre_wait=True)
                    wt("vector", kpl, t_pl)
                    op("vector", lambda e, b=b, mloc=mloc, pb=pb, ob=ob: e.scalar_tensor_tensor(
                        out=ob[:], in0=gb0[:, b, :], scalar=R["wm1"][:, mloc:mloc + 1],
                        in1=pb[:], op0=ALU.mult, op1=ALU.add))
                    t_cmb = st("vector", "V", lambda e, b=b, mloc=mloc, ob=ob: e.scalar_tensor_tensor(
                        out=ob[:], in0=gb1[:, b, :], scalar=R["wm2"][:, mloc:mloc + 1],
                        in1=ob[:], op0=ALU.mult, op1=ALU.add))
                    wt("sync", "V", t_cmb)
                    kst = f"sST{b % 2}"
                    t_out = dma("sync", kst, out[mloc], ob[:], pre_wait=True)
                    if not (cj == 3 and b == 7):
                        wt("vector", kst, t_out)
                if cj < 3:
                    wt("gpsimd", "V", cnt["V"])
            wt("sync", "sST0", cnt["sST0"])
            wt("sync", "sST1", cnt["sST1"])
            if debug:
                wt("sync", "sDBG", cnt["sDBG"])

        except _Stop:
            pass

        with allow_ici_all_to_all(), nc.Block() as block:
            @block.sync
            def _(en):
                for f in plans["sync"]:
                    f(en)

            @block.scalar
            def _(en):
                for f in plans["scalar"]:
                    f(en)

            @block.vector
            def _(en):
                for f in plans["vector"]:
                    f(en)

            @block.tensor
            def _(en):
                for f in plans["tensor"]:
                    f(en)

            @block.gpsimd
            def _(en):
                for f in plans["gpsimd"]:
                    f(en)

    lower_extended_insts(nc)
    return nc


def prepare_in_maps(inputs):
    x = np.ascontiguousarray(np.asarray(inputs["L0_attn_x"], np.float32)).reshape(NCOL, 128, D)
    W1 = np.ascontiguousarray(np.asarray(inputs["L0_attn_W1"], np.float32))
    b1 = np.asarray(inputs["L0_attn_b1"], np.float32).reshape(1, D)
    W2 = np.asarray(inputs["L0_attn_W2"], np.float32)
    b2 = np.asarray(inputs["L0_attn_b2"], np.float32)
    W3 = np.asarray(inputs["L0_attn_W3"], np.float32)
    b3 = np.asarray(inputs["L0_attn_b3"], np.float32)
    wg = np.asarray(inputs["L0_ffn_w_gate"], np.float32)
    cfc = np.asarray(inputs["L0_ffn_c_fc"], np.float32)
    cpr = np.asarray(inputs["L0_ffn_c_proj"], np.float32)
    Wbcg = np.ascontiguousarray(np.concatenate([W2, W3, wg], axis=1))
    brow = np.ascontiguousarray(np.concatenate([b2, b3, np.zeros(8, np.float32)])).reshape(1, 40)

    in_maps = []
    for c in range(NCORE):
        m = {
            "xsh": np.ascontiguousarray(x[c::8]),
            "W1": W1, "Wbcg": Wbcg, "brow": brow, "b1row": b1,
            "cfc": np.ascontiguousarray(cfc[c]),
            "cpr": np.ascontiguousarray(cpr[c]),
        }
        for k, v in host_constants(c).items():
            m[k] = np.ascontiguousarray(v)
        in_maps.append(m)
    return in_maps


def kernel(**inputs):
    in_maps = prepare_in_maps(inputs)
    nc = build_nc(DEBUG)
    res = run_bass_kernel_spmd(nc, in_maps, core_ids=list(range(NCORE)))
    full = np.zeros((NCOL, 128, D), np.float32)
    for c in range(NCORE):
        full[c::8] = res.results[c]["out"]
    if DEBUG:
        kernel._debug = [{k: v for k, v in r.items() if k.startswith("d_")} for r in res.results]
    return full.reshape(8, 4096, 1024)


# version 06
# speedup vs baseline: 1.1106x; 1.0477x over previous
import numpy as np
import ml_dtypes
import concourse.bass as bass
import concourse.mybir as mybir
from concourse.bass_utils import run_bass_kernel_spmd
from concourse import library_config
from concourse.aot_env import allow_ici_all_to_all
from concourse.library_overlay import lower_extended_insts
from contextlib import ExitStack

F32 = mybir.dt.float32
BF16 = mybir.dt.bfloat16
I16 = mybir.dt.int16
U8 = mybir.dt.uint8
AF = mybir.ActivationFunctionType
ALU = mybir.AluOpType
AX = mybir.AxisListType

NCORE = 8
NT = 32
NCOL = 256
D = 1024
NEXP = 8
CAPN = 4096
CAP = 768
DFF = 3072
DH = DFF // 2
DUMMY = NEXP * CAP

DEBUG = False
STOP_AFTER = "full"
BFNP = ml_dtypes.bfloat16


def host_constants(core):
    c = core
    H = {}
    H["ltri128"] = np.triu(np.ones((128, 128), np.float32), 1)
    q = np.arange(128)
    H["ltri16x8"] = ((q[:, None] % 8 == q[None, :] % 8)
                     & (q[:, None] // 8 < q[None, :] // 8)).astype(np.float32)
    H["ident_bf"] = np.eye(128, dtype=np.float32).astype(BFNP)
    H["ident_f32"] = np.eye(128, dtype=np.float32)
    H["onescol"] = np.ones((128, 1), np.float32)
    H["ones1_bf"] = np.ones((1, 128), np.float32).astype(BFNP)
    o16p = np.zeros((32, 128), np.float32); o16p[0:16, :] = 1.0
    H["ones16p"] = o16p
    H["ones64"] = np.ones((64, 128), np.float32)
    H["ones128sq"] = np.ones((128, 128), np.float32)
    so = np.zeros((128, 32), np.float32)
    for half in range(2):
        for e in range(8):
            so[((c + 8 * half) % 16) * 8 + e, half * 8 + e] = 1.0
    H["selown16"] = so
    m16 = np.zeros((16, 256), np.float32)
    for mloc in range(32):
        for e in range(8):
            m16[(mloc % 2) * 8 + e, mloc * 8 + e] = 1.0
    H["m16pat"] = m16
    lt16 = np.zeros((32, 32), np.float32)
    lt16[:16, :16] = np.triu(np.ones((16, 16), np.float32), 1)
    H["ltri16p"] = lt16
    m16b = np.zeros((16, 256), np.float32)
    for mloc in range(32):
        for e in range(8):
            m16b[mloc // 2, mloc * 8 + e] = 1.0
    H["m16bpat"] = m16b
    m8t = (np.arange(128)[None, :] % 8 == np.arange(8)[:, None]).astype(np.float32)
    H["mod8tp"] = np.concatenate([m8t, np.zeros((24, 128), np.float32)], 0)
    H["mod8p"] = np.concatenate([m8t.T, np.zeros((128, 24), np.float32)], 1)
    sc8 = np.zeros((128, 64), np.float32)
    for qq in range(128):
        sc8[qq, ((qq // 8) % 8) * 8 + (qq % 8)] = 1.0
    H["selc8"] = sc8
    cm = np.arange(64)
    ce_c, ce_e = cm // 8, cm % 8
    H["mask64x"] = ((ce_e[:, None] == ce_e[None, :]) & (ce_c[:, None] < ce_c[None, :])).astype(np.float32)
    H["mask64i"] = ((ce_e[:, None] == ce_e[None, :]) & (ce_c[:, None] <= ce_c[None, :])).astype(np.float32)
    H["i64"] = np.eye(64, dtype=np.float32)
    H["memask"] = np.tile((np.arange(64) % 8 == c).astype(np.float32)[None, :], (128, 1))
    H["memask_tot"] = np.tile((np.arange(64) == 56 + c).astype(np.float32)[None, :], (128, 1))
    mp = np.zeros((128, 256), np.float32)
    for mloc in range(32):
        for e in range(8):
            mp[(mloc % 16) * 8 + e, mloc * 8 + e] = 1.0
    H["m128pat"] = mp
    for g in range(8):
        fs = np.zeros((128, 128), np.float32)
        for mrow in range(128):
            fs[16 * g + (mrow % 16), mrow] = 1.0
        H[f"foldsel{g}"] = fs
    H["siota"] = (np.arange(32)[None, :] * 128 + np.arange(128)[:, None]).astype(np.float32)
    H["dumoff"] = np.tile((np.arange(32) // 8).astype(np.float32)[None, :], (128, 1))
    u32 = np.zeros((32, 16), np.float32)
    u32[:16, :] = (np.arange(16)[:, None] < np.arange(16)[None, :]).astype(np.float32)
    H["ultri32"] = u32
    return H




def build_nc(debug=DEBUG, stop_after=None):
    stop_after = stop_after or STOP_AFTER
    nc = bass.Bass("TRN2", target_bir_lowering=False)
    AP_IN = {}

    def din(name, shape, dtype=F32):
        AP_IN[name] = nc.dram_tensor(name, list(shape), dtype, kind="ExternalInput").ap()
        return AP_IN[name]

    xsh = din("xsh", (NT, 128, D))
    W1 = din("W1", (D, D))
    Wbcg = din("Wbcg", (D, 40))
    brow = din("brow", (1, 40))
    b1row = din("b1row", (1, D))
    cfc = din("cfc", (D, DFF))
    cpr = din("cpr", (DH, D))
    consts = host_constants(0)
    for k, v in consts.items():
        din(k, v.shape, BF16 if v.dtype == BFNP else F32)

    partial = nc.dram_tensor("partial", [NT, 128, D], BF16, kind="Internal")
    send1 = nc.dram_tensor("send1", [DUMMY + 8, D], BF16, kind="Internal")
    recv1 = nc.dram_tensor("recv1", [DUMMY + 8, D], BF16, kind="Internal")
    send2 = nc.dram_tensor("send2", [DUMMY + 8, D], BF16, kind="Internal")
    recv2 = nc.dram_tensor("recv2", [DUMMY + 8, D], BF16, kind="Internal")
    ag_in = nc.dram_tensor("ag_in", [128, 128], F32, kind="Internal")
    ag_out = nc.dram_tensor("ag_out", [NCORE * 128, 128], F32, kind="Internal")

    out = nc.dram_tensor("out", [NT, 128, D], F32, kind="ExternalOutput").ap()
    dbg = {}
    if debug:
        for nm, sh in [("d_r1", (128, 128)), ("d_r2", (128, 128)),
                       ("d_r3", (128, 128)), ("d_r4", (128, 96))]:
            dbg[nm] = nc.dram_tensor(nm, list(sh), F32, kind="ExternalOutput").ap()

    with ExitStack() as ctx:
        e_ = ctx.enter_context

        def sb(name, shape, dtype):
            return e_(nc.sbuf_tensor(name, list(shape), dtype))

        CT = {k: sb("c_" + k, v.shape, BF16 if v.dtype == BFNP else F32)
              for k, v in consts.items()}
        xt = [sb("xt0", (128, D), F32), sb("xt1", (128, D), F32)]
        sa = None
        s0 = sb("s0", (128, D), F32)
        s1 = sb("s1", (128, D), F32)
        u1 = sb("u1", (128, 4096), BF16)
        s2 = u1[:, 0:2048].bitcast(F32)
        fmt = [u1[:, 2048:3072], u1[:, 3072:4096]]
        lnst = sb("lnst", (128, 16), F32)
        mnst = sb("mnst", (128, NT), F32)
        rsst = sb("rsst", (128, NT), F32)
        svst = sb("svst", (128, NT), F32)
        bc16 = sb("bc16", (128, 16), F32)
        bc40 = sb("bc40", (128, 40), F32)
        xn_tm = sb("xn_tm", (128, NT * D), BF16)
        wbcg_sb = sb("wbcg_sb", (128, 8 * 40), BF16)
        brow_sb = sb("brow_sb", (1, 40), BF16)
        b1_sb = sb("b1_sb", (1, D), BF16)
        logits = sb("logits", (128, NT * 8), F32)
        R = {k: sb("r_" + k, (128, NT), F32) for k in
             ["i1", "i2", "w1", "w2", "m1", "m2", "den", "v1", "v2",
              "vr1", "vr2", "sx1", "sx2", "wm1", "wm2", "slotv", "csum",
              "acc", "t1", "t2"]}
        tmp8 = sb("tmp8", (128, NT * 8), F32)
        ex8 = sb("ex8", (128, NT * 8), F32)
        cbig = sb("cbig", (128, NT * 8), F32)
        eio8 = sb("eio8", (128, NT * 8), F32)
        pk = sb("pk", (128, 128), F32)
        idx1g = sb("idx1g", (128, NCOL), F32)
        idx2g = sb("idx2g", (128, NCOL), F32)

        t16 = sb("t16", (128, 16), F32)
        a1 = sb("a1", (128, 16), F32)
        g16 = sb("g16", (128, 32), F32)
        g16t = sb("g16t", (32, 16), F32)
        g16x = sb("g16x", (128, 32), F32)
        g16xT = sb("g16xT", (32, 16), F32)
        o16 = sb("o16", (16, 16), F32)
        rhsA = sb("rhsA", (32, 256), F32)
        rhsB = sb("rhsB", (32, 256), F32)
        ownoff = sb("ownoff", (128, 256), F32)
        offgT = sb("offgT", (128, 16), F32)
        keptT = sb("keptT", (128, 16), F32)
        kscr = sb("kscr", (128, 16), F32)
        vc64 = sb("vc64", (64, 16), F32)
        vc64r = sb("vc64r", (64, 1), F32)
        cum64 = sb("cum64", (64, 1), F32)
        inc64 = sb("inc64", (64, 1), F32)

        cumB = pk[:, 0:64]
        incB = pk[:, 64:128]
        scr64 = sb("scr64", (128, 64), F32)
        mycum = sb("mycum", (128, 8), F32)
        myinc = sb("myinc", (128, 8), F32)
        myvc = sb("myvc", (128, 8), F32)
        mytot = sb("mytot", (128, 1), F32)
        oh1 = sb("oh1", (128, NT * 8), F32)
        oh2 = idx2g
        ohs = sb("ohs", (128, NT * 8), F32)
        ecsU = logits
        ecsV = eio8
        ohv1 = tmp8
        ohv2 = ex8
        selbuf = cbig
        csrowV = idx1g
        a2t = sb("a2t", (128, 2), F32)
        a2x = sb("a2x", (128, 2), F32)
        tot0e = sb("tot0e", (32, 2), F32)
        tot0b = sb("tot0b", (128, 1), F32)
        rhs3 = ownoff
        offs2B = sb("offs2B", (128, 256), F32)
        Lk0 = sb("Lk0", (128, 256), I16)
        Lk1 = sb("Lk1", (128, 256), I16)
        Lslot = sb("Lslot", (128, 256), I16)
        cfc_sb = sb("cfc_sb", (128, 8 * DFF), BF16)
        w1_sb = cfc_sb.tensor if hasattr(cfc_sb, "tensor") else cfc_sb
        w1_sb = cfc_sb
        cpr_sb = sb("cpr_sb", (128, 12 * D), BF16)
        G = cpr_sb[:, 0:4096].bitcast(F32)
        srow = cpr_sb[:, 4096:8192].bitcast(F32)
        hh = sb("hh", (128, 12 * 512), BF16)
        expo = u1
        sa = xt[0][:, 0:512]

        zeros_bf = hh
        pt8 = xt
        ot8 = [s0, s1]
        y_bf = s1[:, 0:512].bitcast(BF16)
        yt8 = [hh[:, 0:1024], hh[:, 1024:2048]]

        psumP = e_(nc.psum_tensor("psumP", [128, 2048], F32))
        p_z = e_(nc.psum_tensor("p_z", [128, D], F32))
        p_bc = e_(nc.psum_tensor("p_bc", [128, 64], F32))
        p_tr = psumP[:, 0:1024]
        p_trbs = [psumP[:, 0:512].bitcast(BF16), psumP[:, 512:1024].bitcast(BF16)]
        p_zs = [p_z[:, 0:1024], psumP[:, 1024:2048]]
        p_bcs = [p_bc[:, 0:40], p_bc[:, 0:40]]
        p_row = psumP[0:1, 0:2048]

        sem = {nm: e_(nc.semaphore(nm)) for nm in
               ["sC", "sX0", "sX1", "sP0", "sP1", "sW1", "sWF", "sAG", "sSC",
                "sEG0", "sEG1", "sEO0", "sEO1", "sEO2", "sEO3", "sOG0", "sOG1",
                "sPL0", "sPL1", "sST0", "sST1", "sZ", "sDBG", "T", "V", "A", "CC"]}
        cnt = {k: 0 for k in sem}
        plans = {k: [] for k in ["sync", "scalar", "vector", "tensor", "gpsimd"]}
        off = [False]

        def emit(eng, fn):
            if off[0]:
                return
            plans[eng].append(fn)

        def bump(key, n=1):
            cnt[key] += n
            return cnt[key]

        def wt(eng, key, val):
            emit(eng, lambda e, k=key, v=val: e.wait_ge(sem[k], v))

        def dma(eng, key, out_ap, in_ap, pre_wait=False):
            pre = cnt[key]
            if pre_wait and pre > 0:
                wt(eng, key, pre)
            tgt = bump(key, 16)
            emit(eng, lambda e, o=out_ap, i=in_ap, k=key: e.dma_start(out=o, in_=i).then_inc(sem[k], 16))
            return tgt

        def st(eng, key, fn):
            if eng in ("vector", "scalar"):
                return op(eng, fn)
            tgt = bump(key, 1)
            emit(eng, lambda e, f=fn, k=key: f(e).then_inc(sem[k], 1))
            return tgt

        def op(eng, fn):
            if eng in ("vector", "scalar"):
                key = "V" if eng == "vector" else "A"
                pre = cnt[key]
                tgt = bump(key, 1)

                def run(e, f=fn, pre=pre, key=key):
                    ins = f(e).then_inc(sem[key], 1)
                    if pre > 0:
                        ins._wait_ge(sem[key], pre)
                emit(eng, run)
                return tgt
            emit(eng, lambda e, f=fn: f(e))
            return None

        def r3(ap, inner=8):
            return ap.rearrange("p (c e) -> p c e", e=inner)

        def rcol(ap):
            return ap.rearrange("p (c o) -> p c o", o=1)

        for k in consts:
            dma("sync", "sC", CT[k][:], AP_IN[k])
        t_const = cnt["sC"]
        op("gpsimd", lambda e: e.load_library(library_config.mlp))
        for j in range(8):
            dma("gpsimd", "sW1", w1_sb[:, j * D:(j + 1) * D], W1[j * 128:(j + 1) * 128, :])
        dma("gpsimd", "sW1", wbcg_sb[:].rearrange("p (a b) -> p a b", a=8),
            Wbcg.rearrange("(a p) b -> p a b", p=128))
        dma("gpsimd", "sW1", brow_sb[:], brow)
        dma("gpsimd", "sW1", b1_sb[:], b1row)
        t_wsmall = cnt["sW1"]

        z_v = st("vector", "V", lambda e: e.memset(zeros_bf[:, 0:2048], 0.0))
        wt("scalar", "V", z_v)
        for base in range(0, DUMMY, 256):
            dma("scalar", "sZ", send1.ap()[base:base + 256, :].rearrange("(a p) d -> p a d", p=128),
                zeros_bf[:, 0:2048].rearrange("p (a d) -> p a d", d=D))
            dma("scalar", "sZ", send2.ap()[base:base + 256, :].rearrange("(a p) d -> p a d", p=128),
                zeros_bf[:, 0:2048].rearrange("p (a d) -> p a d", d=D))
        dma("scalar", "sZ", send1.ap()[DUMMY:DUMMY + 1, :], zeros_bf[0:1, 0:1024])
        dma("scalar", "sZ", send2.ap()[DUMMY:DUMMY + 1, :], zeros_bf[0:1, 0:1024])
        dma("scalar", "sZ", recv2.ap()[DUMMY:DUMMY + 8, :], zeros_bf[0:8, 0:1024])
        t_zero = cnt["sZ"]
        for e8 in range(8):
            op("vector", lambda e, i=e8: e.memset(
                r3(eio8[:])[:, :, i:i + 1], float(i)))
        op("vector", lambda e: e.memset(g16[:], 0.0))
        op("vector", lambda e: e.memset(g16t[:], 0.0))
        op("vector", lambda e: e.memset(g16x[:], 0.0))
        op("vector", lambda e: e.memset(g16xT[:], 0.0))
        op("vector", lambda e: e.memset(rhsA[:], 0.0))
        op("vector", lambda e: e.memset(rhsB[:], 0.0))
        op("vector", lambda e: e.memset(tot0e[:], 0.0))
        op("vector", lambda e: e.memset(Lk0[:], 0))
        op("vector", lambda e: e.memset(Lk1[:], 0))
        op("vector", lambda e: e.memset(Lslot[:], 0))

        wt("tensor", "sC", t_const)
        wt("tensor", "sW1", t_wsmall)
        wt("vector", "sC", t_const)
        t_xnb = [0] * NT
        t_evac = [0] * NT
        t_bcv = [0] * NT

        def stageA_early(m):
            xb = xt[m % 2]
            if m >= 2:
                wt("sync", "V", t_xnb[m - 2])
            kx = f"sX{m % 2}"
            t_x = dma("sync", kx, xb[:], xsh[m], pre_wait=True)
            L = lnst[:, (m % 2) * 8:(m % 2) * 8 + 8]
            wt("vector", kx, t_x)
            op("vector", lambda e, xb=xb, L=L: e.reduce_sum(
                out=L[:, 0:1].rearrange("p (a b) -> p a b", b=1),
                in_=xb[:].rearrange("p (a b) -> p a b", a=1), axis=AX.X))
            op("vector", lambda e, xb=xb: e.tensor_tensor(out=s1[:], in0=xb[:], in1=xb[:], op=ALU.mult))
            op("vector", lambda e, L=L: e.reduce_sum(
                out=L[:, 1:2].rearrange("p (a b) -> p a b", b=1),
                in_=s1[:].rearrange("p (a b) -> p a b", a=1), axis=AX.X))
            op("vector", lambda e, L=L, m=m: e.tensor_scalar(out=mnst[:, m:m + 1], in0=L[:, 0:1],
                                                             scalar1=1.0 / D, scalar2=None, op0=ALU.mult))
            op("vector", lambda e, L=L: e.tensor_scalar(out=L[:, 3:4], in0=L[:, 1:2],
                                                        scalar1=1.0 / D, scalar2=None, op0=ALU.mult))
            op("vector", lambda e, L=L, m=m: e.tensor_tensor(out=L[:, 4:5], in0=mnst[:, m:m + 1],
                                                             in1=mnst[:, m:m + 1], op=ALU.mult))
            t_var = st("vector", "V", lambda e, L=L: e.scalar_tensor_tensor(
                out=L[:, 5:6], in0=L[:, 3:4], scalar=1e-5,
                in1=L[:, 4:5], op0=ALU.add, op1=ALU.subtract))
            wt("scalar", "V", t_var)
            op("scalar", lambda e, L=L: e.activation(out=L[:, 6:7], in_=L[:, 5:6], func=AF.Ln))
            op("scalar", lambda e, L=L: e.mul(out=L[:, 6:7], in_=L[:, 6:7], mul=-0.5))
            t_rstd = st("scalar", "A", lambda e, L=L, m=m: e.activation(
                out=rsst[:, m:m + 1], in_=L[:, 6:7], func=AF.Exp))
            wt("vector", "A", t_rstd)
            t_xnb[m] = st("vector", "V", lambda e, m=m, xb=xb: e.tensor_scalar(
                out=xn_tm[:, m * D:(m + 1) * D], in0=xb[:],
                scalar1=mnst[:, m:m + 1], scalar2=rsst[:, m:m + 1],
                op0=ALU.subtract, op1=ALU.mult))

        def stageA_late(m):
            fm = fmt[m % 2]
            ptr = p_trbs[m % 2]
            wt("tensor", "V", t_xnb[m])
            if m >= 2:
                wt("tensor", "V", t_evac[m - 2])
            t_tr = None
            for j in range(8):
                fn = (lambda e, m=m, j=j, ptr=ptr: e.transpose(
                    ptr[:, j * 128:(j + 1) * 128],
                    xn_tm[:, m * D + j * 128:m * D + (j + 1) * 128],
                    CT["ident_bf"][:]))
                if j == 7:
                    t_tr = st("tensor", "T", fn)
                else:
                    op("tensor", fn)
            wt("vector", "T", t_tr)
            t_evac[m] = st("vector", "V", lambda e, fm=fm, ptr=ptr: e.tensor_copy(out=fm, in_=ptr[:]))
            wt("tensor", "V", t_evac[m])
            if m >= 2:
                wt("tensor", "V", t_bcv[m - 2])
            pb = p_bcs[m % 2]
            for j in range(8):
                op("tensor", lambda e, fm=fm, j=j, pb=pb: e.matmul(
                    pb, fm[:, j * 128:(j + 1) * 128],
                    wbcg_sb[:, j * 40:(j + 1) * 40], start=(j == 0), stop=False))
            t_bc = st("tensor", "T", lambda e, pb=pb: e.matmul(
                pb, CT["ones1_bf"][:], brow_sb[:], start=False, stop=True))
            wt("vector", "T", t_bc)
            op("vector", lambda e, pb=pb: e.tensor_copy(out=bc40[:], in_=pb))
            op("vector", lambda e: e.tensor_tensor(out=bc16[:], in0=bc40[:, 0:16],
                                                   in1=bc40[:, 16:32], op=ALU.mult))
            op("vector", lambda e, m=m: e.reduce_sum(
                out=svst[:, m:m + 1].rearrange("p (a b) -> p a b", b=1),
                in_=bc16[:].rearrange("p (a b) -> p a b", a=1), axis=AX.X))
            t_bcv[m] = st("vector", "V", lambda e, m=m: e.tensor_copy(
                out=logits[:, m * 8:(m + 1) * 8], in_=bc40[:, 32:40]))

        for m in range(NT):
            stageA_early(m)
            if m >= 1:
                stageA_late(m - 1)
        stageA_late(NT - 1)
        t_xn_last = t_xnb[NT - 1]

        if stop_after == "p1a":
            early_out()
            off[0] = True
        z3 = r3(logits[:])
        eio3 = r3(eio8[:])
        op("vector", lambda e: e.reduce_max(out=rcol(R["m1"][:]), in_=z3, axis=AX.X))
        op("vector", lambda e: e.tensor_tensor(
            out=r3(ohs[:]), in0=z3,
            in1=rcol(R["m1"][:]).broadcast_to((128, NT, 8)), op=ALU.is_equal))
        op("vector", lambda e: e.tensor_tensor(out=tmp8[:], in0=ohs[:], in1=eio8[:], op=ALU.mult))
        op("vector", lambda e: e.scalar_tensor_tensor(out=tmp8[:], in0=ohs[:], scalar=-99.0,
                                                      in1=tmp8[:], op0=ALU.mult, op1=ALU.add))
        op("vector", lambda e: e.tensor_scalar(out=tmp8[:], in0=tmp8[:], scalar1=99.0,
                                               scalar2=None, op0=ALU.add))
        op("vector", lambda e: e.tensor_reduce(out=rcol(R["i1"][:]), in_=r3(tmp8[:]),
                                               axis=AX.X, op=ALU.min))
        op("vector", lambda e: e.tensor_tensor(
            out=r3(ohs[:]), in0=eio3,
            in1=rcol(R["i1"][:]).broadcast_to((128, NT, 8)), op=ALU.is_equal))
        op("vector", lambda e: e.scalar_tensor_tensor(out=tmp8[:], in0=ohs[:], scalar=-1e30,
                                                      in1=logits[:], op0=ALU.mult, op1=ALU.add))
        op("vector", lambda e: e.tensor_reduce(out=rcol(R["m2"][:]), in_=r3(tmp8[:]),
                                               axis=AX.X, op=ALU.max))
        op("vector", lambda e: e.tensor_tensor(
            out=r3(ohs[:]), in0=r3(tmp8[:]),
            in1=rcol(R["m2"][:]).broadcast_to((128, NT, 8)), op=ALU.is_equal))
        op("vector", lambda e: e.tensor_tensor(out=tmp8[:], in0=ohs[:], in1=eio8[:], op=ALU.mult))
        op("vector", lambda e: e.scalar_tensor_tensor(out=tmp8[:], in0=ohs[:], scalar=-99.0,
                                                      in1=tmp8[:], op0=ALU.mult, op1=ALU.add))
        op("vector", lambda e: e.tensor_scalar(out=tmp8[:], in0=tmp8[:], scalar1=99.0,
                                               scalar2=None, op0=ALU.add))
        op("vector", lambda e: e.tensor_reduce(out=rcol(R["i2"][:]), in_=r3(tmp8[:]),
                                               axis=AX.X, op=ALU.min))
        t_sm0 = st("vector", "V", lambda e: e.tensor_tensor(
            out=r3(tmp8[:]), in0=z3,
            in1=rcol(R["m1"][:]).broadcast_to((128, NT, 8)), op=ALU.subtract))
        wt("scalar", "V", t_sm0)
        t_ex = st("scalar", "A", lambda e: e.activation(out=ex8[:], in_=tmp8[:], func=AF.Exp))
        wt("vector", "A", t_ex)
        op("vector", lambda e: e.tensor_reduce(out=rcol(R["den"][:]), in_=r3(ex8[:]),
                                               axis=AX.X, op=ALU.add))
        op("vector", lambda e: e.reciprocal(out=R["w1"][:], in_=R["den"][:]))
        t_d2 = st("vector", "V", lambda e: e.tensor_sub(out=R["m2"][:], in0=R["m2"][:], in1=R["m1"][:]))
        wt("scalar", "V", t_d2)
        t_e2 = st("scalar", "A", lambda e: e.activation(out=R["den"][:], in_=R["m2"][:], func=AF.Exp))
        wt("vector", "A", t_e2)
        op("vector", lambda e: e.tensor_mul(out=R["w2"][:], in0=R["den"][:], in1=R["w1"][:]))
        t_pk = None
        for i, k in enumerate(["i1", "i2", "w1", "w2"]):
            fn = (lambda e, i=i, k=k: e.tensor_copy(out=pk[:, i * 32:(i + 1) * 32], in_=R[k][:]))
            if i == 3:
                t_pk = st("vector", "V", fn)
            else:
                op("vector", fn)
        wt("gpsimd", "V", t_pk)
        t_agin = dma("gpsimd", "sAG", ag_in.ap(), pk[:])
        wt("gpsimd", "sAG", t_agin)
        t_ag = st("gpsimd", "CC", lambda e: e.collective_compute(
            "AllGather", ALU.bypass, replica_groups=[list(range(NCORE))],
            ins=[ag_in.ap().opt()], outs=[ag_out.ap().opt()]))
        wt("gpsimd", "CC", t_ag)
        for cc in range(NCORE):
            dma("gpsimd", "sAG", tmp8[:, cc * 32:(cc + 1) * 32],
                ag_out.ap()[cc * 128:(cc + 1) * 128, 0:32])
            t_idxg = dma("gpsimd", "sAG", ex8[:, cc * 32:(cc + 1) * 32],
                         ag_out.ap()[cc * 128:(cc + 1) * 128, 32:64])
        wt("vector", "sAG", t_idxg)
        op("vector", lambda e: e.tensor_copy(
            out=idx1g[:].rearrange("p (f cc) -> p f cc", cc=8),
            in_=tmp8[:].rearrange("p (cc f) -> p f cc", cc=8)))
        op("vector", lambda e: e.tensor_copy(
            out=idx2g[:].rearrange("p (f cc) -> p f cc", cc=8),
            in_=ex8[:].rearrange("p (cc f) -> p f cc", cc=8)))

        if stop_after == "p1b":
            early_out()
            off[0] = True
        for e8 in range(8):
            op("vector", lambda e, i=e8: e.tensor_scalar(
                out=r3(G, 8)[:, :, i:i + 1],
                in0=idx1g[:].rearrange("p (c o) -> p c o", o=1),
                scalar1=float(i), scalar2=None, op0=ALU.is_equal))
            op("vector", lambda e, i=e8: e.tensor_scalar(
                out=r3(srow, 8)[:, :, i:i + 1],
                in0=idx2g[:].rearrange("p (c o) -> p c o", o=1),
                scalar1=float(i), scalar2=None, op0=ALU.is_equal))
        t_G = st("vector", "V", lambda e: e.tensor_add(out=G, in0=G, in1=srow))
        if stop_after == "p3a":
            early_out(); off[0] = True
        t_sz = st("vector", "V", lambda e: e.memset(srow, 0.0))
        wt("tensor", "V", t_sz)
        t_cs = None
        for qq in range(4):
            fn = (lambda e, q=qq: e.matmul(p_row[:, q * 512:(q + 1) * 512], CT["onescol"][:],
                                           G[:, q * 512:(q + 1) * 512], start=True, stop=True))
            if qq == 3:
                t_cs = st("tensor", "T", fn)
            else:
                op("tensor", fn)
        wt("vector", "T", t_cs)
        t_srow = st("vector", "V", lambda e: e.tensor_copy(out=srow[0:1, :], in_=p_row[:]))
        wt("tensor", "V", t_srow)
        for rnd in range(2):
            t_trr = None
            for j8 in range(8):
                j = rnd * 8 + j8
                fn = (lambda e, j=j, j8=j8: e.matmul(
                    psumP[:, 1024 + j8 * 128:1024 + (j8 + 1) * 128],
                    srow[:, j * 128:(j + 1) * 128], CT["ident_f32"][:],
                    start=True, stop=True))
                if j8 == 7:
                    t_trr = st("tensor", "T", fn)
                else:
                    op("tensor", fn)
            wt("vector", "T", t_trr)
            t_t16 = None
            for j8 in range(8):
                j = rnd * 8 + j8
                fn = (lambda e, j=j, j8=j8: e.tensor_copy(
                    out=t16[:, j:j + 1], in_=psumP[:, 1024 + j8 * 128:1024 + j8 * 128 + 1]))
                if j8 == 7:
                    t_t16 = st("vector", "V", fn)
                else:
                    op("vector", fn)
            wt("tensor", "V", t_t16)
        if stop_after == "p3b":
            early_out(); off[0] = True
        op("tensor", lambda e: e.matmul(psumP[:, 1024:1040], CT["ltri16x8"][:], t16[:],
                                        start=True, stop=True))
        t_mm1 = st("tensor", "T", lambda e: e.matmul(psumP[0:32, 1040:1056], CT["mod8p"][:], t16[:],
                                                     start=True, stop=True))
        wt("vector", "T", t_mm1)
        op("vector", lambda e: e.tensor_copy(out=a1[:], in_=psumP[:, 1024:1040]))
        t_g16 = st("vector", "V", lambda e: e.tensor_copy(out=g16[0:8, 0:16], in_=psumP[0:8, 1040:1056]))
        wt("tensor", "V", t_g16)
        t_mm2 = st("tensor", "T", lambda e: e.matmul(psumP[0:32, 1056:1184], g16[:, 0:16].bitcast(F32) if False else g16[:], CT["ident_f32"][:], start=True, stop=True))
        wt("vector", "T", t_mm2)
        t_g16t = st("vector", "V", lambda e: e.tensor_copy(out=g16t[0:16, :], in_=psumP[0:16, 1056:1072]))
        wt("tensor", "V", t_g16t)
        t_mm3 = st("tensor", "T", lambda e: e.matmul(psumP[0:32, 1184:1192], CT["ltri16p"][:],
                                                     g16t[:, 0:8], start=True, stop=True))
        wt("vector", "T", t_mm3)
        t_g16x = st("vector", "V", lambda e: e.tensor_copy(out=g16x[0:16, 0:8], in_=psumP[0:16, 1184:1192]))
        if stop_after == "p3b2":
            early_out(); off[0] = True
        wt("tensor", "V", t_g16x)
        t_mm4 = st("tensor", "T", lambda e: e.matmul(psumP[0:32, 1192:1208], CT["selown16"][:],
                                                     a1[:], start=True, stop=True))
        wt("vector", "T", t_mm4)
        t_o16 = st("vector", "V", lambda e: e.tensor_copy(out=o16[:], in_=psumP[0:16, 1192:1208]))
        op("vector", lambda e: e.tensor_tensor(
            out=rhsA[0:16, :].rearrange("r (j h ee) -> r j h ee", h=2, ee=8),
            in0=o16[:].rearrange("r (j h ee) -> r j h ee", h=1, ee=1).broadcast_to((16, 16, 2, 8)),
            in1=CT["m16pat"][:].rearrange("r (j h ee) -> r j h ee", h=2, ee=8), op=ALU.mult))
        op("vector", lambda e: e.tensor_tensor(
            out=rhsB[0:16, :].rearrange("r (j h ee) -> r j h ee", h=2, ee=8),
            in0=g16x[0:16, 0:8].rearrange("r (j h ee) -> r j h ee", j=1, h=1).broadcast_to((16, 16, 2, 8)),
            in1=CT["m16bpat"][:].rearrange("r (j h ee) -> r j h ee", h=2, ee=8), op=ALU.mult))
        t_rhs = st("vector", "V", lambda e: e.tensor_add(out=rhsA[:], in0=rhsA[:], in1=rhsB[:]))
        wt("tensor", "V", t_rhs)
        t_mm5 = st("tensor", "T", lambda e: e.matmul(psumP[:, 1024:1280], CT["ones16p"][:],
                                                     rhsA[:], start=True, stop=True))
        wt("vector", "T", t_mm5)
        t_ownoff = st("vector", "V", lambda e: e.tensor_copy(out=ownoff[:], in_=psumP[:, 1024:1280]))
        if stop_after == "p3b3":
            early_out(); off[0] = True
        wt("tensor", "V", t_g16x)
        t_mm6 = st("tensor", "T", lambda e: e.matmul(psumP[0:16, 1280:1296], g16t[:],
                                                     CT["ultri32"][:], start=True, stop=True))
        wt("vector", "T", t_mm6)
        t_gxt = st("vector", "V", lambda e: e.tensor_copy(out=g16xT[0:8, :], in_=psumP[0:8, 1280:1296]))
        if stop_after == "p3b4a":
            early_out(); off[0] = True
        wt("tensor", "V", t_gxt)
        t_mm7 = st("tensor", "T", lambda e: e.matmul(psumP[:, 1408:1424], CT["mod8tp"][:],
                                                     g16xT[:], start=True, stop=True))
        wt("vector", "T", t_mm7)
        op("vector", lambda e: e.tensor_copy(out=offgT[:], in_=psumP[:, 1408:1424]))
        if stop_after == "p3b4b":
            early_out(); off[0] = True
        op("vector", lambda e: e.tensor_add(out=kscr[:], in0=a1[:], in1=offgT[:]))
        op("vector", lambda e: e.tensor_scalar(out=kscr[:], in0=kscr[:], scalar1=-1.0,
                                               scalar2=float(CAPN), op0=ALU.mult, op1=ALU.add))
        op("vector", lambda e: e.tensor_scalar(out=kscr[:], in0=kscr[:], scalar1=0.0,
                                               scalar2=None, op0=ALU.max))
        t_kept = st("vector", "V", lambda e: e.tensor_tensor(out=keptT[:], in0=kscr[:],
                                                             in1=t16[:], op=ALU.min))
        if stop_after == "p3b4":
            early_out(); off[0] = True
        wt("tensor", "V", t_kept)
        t_mm8 = st("tensor", "T", lambda e: e.matmul(psumP[0:64, 1424:1440], CT["selc8"][:],
                                                     keptT[:], start=True, stop=True))
        wt("vector", "T", t_mm8)
        op("vector", lambda e: e.tensor_copy(out=vc64[:], in_=psumP[0:64, 1424:1440]))
        t_vcr = st("vector", "V", lambda e: e.tensor_reduce(
            out=vc64r[:].rearrange("p (a b) -> p a b", b=1),
            in_=vc64[:].rearrange("p (a b) -> p a b", a=1), axis=AX.X, op=ALU.add))
        wt("tensor", "V", t_vcr)
        op("tensor", lambda e: e.matmul(psumP[0:64, 1440:1441], CT["mask64x"][:],
                                        vc64r[:], start=True, stop=True))
        t_mm9 = st("tensor", "T", lambda e: e.matmul(psumP[0:64, 1441:1442], CT["mask64i"][:],
                                                     vc64r[:], start=True, stop=True))
        wt("vector", "T", t_mm9)
        op("vector", lambda e: e.tensor_copy(out=cum64[:], in_=psumP[0:64, 1440:1441]))
        t_c64 = st("vector", "V", lambda e: e.tensor_copy(out=inc64[:], in_=psumP[0:64, 1441:1442]))
        if stop_after == "p3b5":
            early_out(); off[0] = True
        t_dg1 = st("vector", "V", lambda e: e.tensor_scalar(out=scr64[0:64, :], in0=CT["i64"][:],
                                                            scalar1=cum64[:, 0:1], scalar2=None, op0=ALU.mult))
        wt("tensor", "V", t_dg1)
        t_mmA = st("tensor", "T", lambda e: e.matmul(psumP[:, 1442:1506], CT["ones64"][:],
                                                     scr64[0:64, :], start=True, stop=True))
        wt("vector", "T", t_mmA)
        t_cb = st("vector", "V", lambda e: e.tensor_copy(out=cumB[:], in_=psumP[:, 1442:1506]))
        t_dg2 = st("vector", "V", lambda e: e.tensor_scalar(out=scr64[0:64, :], in0=CT["i64"][:],
                                                            scalar1=inc64[:, 0:1], scalar2=None, op0=ALU.mult))
        wt("tensor", "V", t_dg2)
        t_mmB = st("tensor", "T", lambda e: e.matmul(psumP[:, 1536:1600], CT["ones64"][:],
                                                     scr64[0:64, :], start=True, stop=True))
        wt("vector", "T", t_mmB)
        op("vector", lambda e: e.tensor_copy(out=incB[:], in_=psumP[:, 1536:1600]))
        op("vector", lambda e: e.tensor_tensor(out=scr64[:], in0=cumB[:], in1=CT["memask"][:], op=ALU.mult))
        op("vector", lambda e: e.tensor_reduce(
            out=mycum[:].rearrange("p (c o) -> p c o", o=1),
            in_=scr64[:].rearrange("p (c e) -> p c e", e=8), axis=AX.X, op=ALU.add))
        op("vector", lambda e: e.tensor_tensor(out=scr64[:], in0=incB[:], in1=CT["memask"][:], op=ALU.mult))
        op("vector", lambda e: e.tensor_reduce(
            out=myinc[:].rearrange("p (c o) -> p c o", o=1),
            in_=scr64[:].rearrange("p (c e) -> p c e", e=8), axis=AX.X, op=ALU.add))
        op("vector", lambda e: e.tensor_sub(out=myvc[:], in0=myinc[:], in1=mycum[:]))
        op("vector", lambda e: e.tensor_tensor(out=scr64[:], in0=incB[:], in1=CT["memask_tot"][:], op=ALU.mult))
        op("vector", lambda e: e.tensor_reduce(
            out=mytot[:].rearrange("p (a b) -> p a b", b=1),
            in_=scr64[:].rearrange("p (a b) -> p a b", a=1), axis=AX.X, op=ALU.add))
        if stop_after == "p3c":
            early_out(); off[0] = True
        for e8 in range(8):
            op("vector", lambda e, i=e8: e.tensor_scalar(
                out=r3(oh1[:])[:, :, i:i + 1], in0=rcol(R["i1"][:]),
                scalar1=float(i), scalar2=None, op0=ALU.is_equal))
            op("vector", lambda e, i=e8: e.tensor_scalar(
                out=r3(oh2[:])[:, :, i:i + 1], in0=rcol(R["i2"][:]),
                scalar1=float(i), scalar2=None, op0=ALU.is_equal))
        t_ohs = st("vector", "V", lambda e: e.tensor_add(out=ohs[:], in0=oh1[:], in1=oh2[:]))
        wt("tensor", "V", t_ohs)
        t_mmC = st("tensor", "T", lambda e: e.matmul(psumP[:, 1024:1280], CT["ltri128"][:],
                                                     ohs[:], start=True, stop=True))
        wt("vector", "T", t_mmC)
        op("vector", lambda e: e.tensor_copy(out=ecsU[:], in_=psumP[:, 1024:1280]))
        op("vector", lambda e: e.tensor_tensor(out=selbuf[:], in0=oh1[:], in1=ecsU[:], op=ALU.mult))
        op("vector", lambda e: e.tensor_reduce(out=rcol(R["vr1"][:]), in_=r3(selbuf[:]),
                                               axis=AX.X, op=ALU.add))
        op("vector", lambda e: e.tensor_tensor(out=selbuf[:], in0=oh1[:], in1=ownoff[:], op=ALU.mult))
        op("vector", lambda e: e.tensor_reduce(out=rcol(R["t1"][:]), in_=r3(selbuf[:]),
                                               axis=AX.X, op=ALU.add))
        op("vector", lambda e: e.tensor_add(out=R["t1"][:], in0=R["t1"][:], in1=R["vr1"][:]))
        op("vector", lambda e: e.tensor_scalar(out=R["v1"][:], in0=R["t1"][:],
                                               scalar1=float(CAPN), scalar2=None, op0=ALU.is_lt))
        op("vector", lambda e: e.tensor_add(out=selbuf[:], in0=ecsU[:], in1=oh1[:]))
        op("vector", lambda e: e.tensor_tensor(out=selbuf[:], in0=oh2[:], in1=selbuf[:], op=ALU.mult))
        op("vector", lambda e: e.tensor_reduce(out=rcol(R["vr2"][:]), in_=r3(selbuf[:]),
                                               axis=AX.X, op=ALU.add))
        op("vector", lambda e: e.tensor_tensor(out=selbuf[:], in0=oh2[:], in1=ownoff[:], op=ALU.mult))
        op("vector", lambda e: e.tensor_reduce(out=rcol(R["t2"][:]), in_=r3(selbuf[:]),
                                               axis=AX.X, op=ALU.add))
        op("vector", lambda e: e.tensor_add(out=R["t2"][:], in0=R["t2"][:], in1=R["vr2"][:]))
        op("vector", lambda e: e.tensor_scalar(out=R["v2"][:], in0=R["t2"][:],
                                               scalar1=float(CAPN), scalar2=None, op0=ALU.is_lt))
        op("vector", lambda e: e.tensor_tensor(
            out=r3(ohv1[:]), in0=r3(oh1[:]),
            in1=rcol(R["v1"][:]).broadcast_to((128, NT, 8)), op=ALU.mult))
        op("vector", lambda e: e.tensor_tensor(
            out=r3(ohv2[:]), in0=r3(oh2[:]),
            in1=rcol(R["v2"][:]).broadcast_to((128, NT, 8)), op=ALU.mult))
        t_ohv = st("vector", "V", lambda e: e.tensor_add(out=ohs[:], in0=ohv1[:], in1=ohv2[:]))
        wt("tensor", "V", t_ohv)
        t_mmD = st("tensor", "T", lambda e: e.matmul(psumP[:, 1280:1536], CT["ltri128"][:],
                                                     ohs[:], start=True, stop=True))
        wt("vector", "T", t_mmD)
        op("vector", lambda e: e.tensor_copy(out=ecsV[:], in_=psumP[:, 1280:1536]))
        op("vector", lambda e: e.tensor_tensor(out=selbuf[:], in0=ohv1[:], in1=ecsV[:], op=ALU.mult))
        op("vector", lambda e: e.tensor_reduce(out=rcol(R["vr1"][:]), in_=r3(selbuf[:]),
                                               axis=AX.X, op=ALU.add))
        op("vector", lambda e: e.tensor_add(out=selbuf[:], in0=ecsV[:], in1=ohv1[:]))
        op("vector", lambda e: e.tensor_tensor(out=selbuf[:], in0=ohv2[:], in1=selbuf[:], op=ALU.mult))
        op("vector", lambda e: e.tensor_reduce(out=rcol(R["vr2"][:]), in_=r3(selbuf[:]),
                                               axis=AX.X, op=ALU.add))
        t_csz = st("vector", "V", lambda e: e.memset(csrowV[:], 0.0))
        wt("tensor", "V", max(t_ohv, t_csz))
        t_mmE = st("tensor", "T", lambda e: e.matmul(psumP[0:1, 1536:1792], CT["onescol"][:],
                                                     ohs[:], start=True, stop=True))
        wt("vector", "T", t_mmE)
        t_csr = st("vector", "V", lambda e: e.tensor_copy(out=csrowV[0:1, :], in_=psumP[0:1, 1536:1792]))
        wt("tensor", "V", t_csr)
        t_tr2 = None
        for half in range(2):
            fn = (lambda e, half=half: e.matmul(
                psumP[:, 1536 + half * 128:1536 + (half + 1) * 128],
                csrowV[:, half * 128:(half + 1) * 128], CT["ident_f32"][:],
                start=True, stop=True))
            if half == 1:
                t_tr2 = st("tensor", "T", fn)
            else:
                op("tensor", fn)
        wt("vector", "T", t_tr2)
        op("vector", lambda e: e.tensor_copy(out=a2t[:, 0:1], in_=psumP[:, 1536:1537]))
        t_a2t = st("vector", "V", lambda e: e.tensor_copy(out=a2t[:, 1:2], in_=psumP[:, 1664:1665]))
        wt("tensor", "V", t_a2t)
        op("tensor", lambda e: e.matmul(psumP[:, 1792:1794], CT["ltri16x8"][:], a2t[:],
                                        start=True, stop=True))
        t_mmF = st("tensor", "T", lambda e: e.matmul(psumP[0:32, 1794:1796], CT["mod8p"][:],
                                                     a2t[:], start=True, stop=True))
        wt("vector", "T", t_mmF)
        op("vector", lambda e: e.tensor_copy(out=a2x[:], in_=psumP[:, 1792:1794]))
        t_t0e = st("vector", "V", lambda e: e.tensor_copy(out=tot0e[0:8, :], in_=psumP[0:8, 1794:1796]))
        wt("tensor", "V", t_t0e)
        t_mmG = st("tensor", "T", lambda e: e.matmul(psumP[:, 1796:1797], CT["mod8tp"][:],
                                                     tot0e[:, 0:1], start=True, stop=True))
        wt("vector", "T", t_mmG)
        op("vector", lambda e: e.tensor_copy(out=tot0b[:], in_=psumP[:, 1796:1797]))
        op("vector", lambda e: e.tensor_add(out=a2x[:, 1:2], in0=a2x[:, 1:2], in1=tot0b[:]))
        t_rhs3 = st("vector", "V", lambda e: e.tensor_tensor(
            out=rhs3[:].rearrange("p (h a ee) -> p h a ee", a=16, ee=8),
            in0=a2x[:].rearrange("p (h a ee) -> p h a ee", a=1, ee=1).broadcast_to((128, 2, 16, 8)),
            in1=CT["m128pat"][:].rearrange("p (h a ee) -> p h a ee", a=16, ee=8), op=ALU.mult))
        wt("tensor", "V", t_rhs3)
        t_mmH = st("tensor", "T", lambda e: e.matmul(psumP[:, 1536:1792], CT["ones128sq"][:],
                                                     rhs3[:], start=True, stop=True))
        wt("vector", "T", t_mmH)
        op("vector", lambda e: e.tensor_copy(out=offs2B[:], in_=psumP[:, 1536:1792]))
        op("vector", lambda e: e.tensor_tensor(out=selbuf[:], in0=oh1[:], in1=offs2B[:], op=ALU.mult))
        op("vector", lambda e: e.tensor_reduce(out=rcol(R["t1"][:]), in_=r3(selbuf[:]),
                                               axis=AX.X, op=ALU.add))
        op("vector", lambda e: e.tensor_add(out=R["vr1"][:], in0=R["vr1"][:], in1=R["t1"][:]))
        op("vector", lambda e: e.tensor_tensor(out=selbuf[:], in0=oh2[:], in1=offs2B[:], op=ALU.mult))
        op("vector", lambda e: e.tensor_reduce(out=rcol(R["t2"][:]), in_=r3(selbuf[:]),
                                               axis=AX.X, op=ALU.add))
        op("vector", lambda e: e.tensor_add(out=R["vr2"][:], in0=R["vr2"][:], in1=R["t2"][:]))
        op("vector", lambda e: e.scalar_tensor_tensor(out=R["sx1"][:], in0=R["i1"][:],
                                                      scalar=float(CAP), in1=R["vr1"][:],
                                                      op0=ALU.mult, op1=ALU.add))
        op("vector", lambda e: e.tensor_scalar(out=R["t1"][:], in0=CT["dumoff"][:],
                                               scalar1=float(DUMMY), scalar2=None, op0=ALU.add))
        op("vector", lambda e: e.tensor_sub(out=R["sx1"][:], in0=R["sx1"][:], in1=R["t1"][:]))
        op("vector", lambda e: e.tensor_tensor(out=R["sx1"][:], in0=R["v1"][:], in1=R["sx1"][:], op=ALU.mult))
        op("vector", lambda e: e.tensor_add(out=R["sx1"][:], in0=R["sx1"][:], in1=R["t1"][:]))
        op("vector", lambda e: e.scalar_tensor_tensor(out=R["sx2"][:], in0=R["i2"][:],
                                                      scalar=float(CAP), in1=R["vr2"][:],
                                                      op0=ALU.mult, op1=ALU.add))
        op("vector", lambda e: e.tensor_scalar(out=R["t2"][:], in0=R["t1"][:],
                                               scalar1=4.0, scalar2=None, op0=ALU.add))
        op("vector", lambda e: e.tensor_sub(out=R["sx2"][:], in0=R["sx2"][:], in1=R["t2"][:]))
        op("vector", lambda e: e.tensor_tensor(out=R["sx2"][:], in0=R["v2"][:], in1=R["sx2"][:], op=ALU.mult))
        op("vector", lambda e: e.tensor_add(out=R["sx2"][:], in0=R["sx2"][:], in1=R["t2"][:]))
        op("vector", lambda e: e.tensor_tensor(out=R["wm1"][:], in0=R["w1"][:], in1=R["v1"][:], op=ALU.mult))
        op("vector", lambda e: e.tensor_tensor(out=R["wm2"][:], in0=R["w2"][:], in1=R["v2"][:], op=ALU.mult))
        op("vector", lambda e: e.memset(R["csum"][:], 0.0))
        op("vector", lambda e: e.memset(R["acc"][:], 0.0))
        for c_ in range(1, 8):
            op("vector", lambda e, c_=c_: e.tensor_scalar(
                out=R["t1"][:], in0=CT["siota"][:], scalar1=mycum[:, c_:c_ + 1],
                scalar2=None, op0=ALU.is_ge))
            op("vector", lambda e: e.tensor_add(out=R["csum"][:], in0=R["csum"][:], in1=R["t1"][:]))
            op("vector", lambda e, c_=c_: e.scalar_tensor_tensor(
                out=R["t2"][:], in0=R["t1"][:], scalar=myvc[:, c_ - 1:c_],
                in1=R["acc"][:], op0=ALU.mult, op1=ALU.add))
            op("vector", lambda e: e.tensor_copy(out=R["acc"][:], in_=R["t2"][:]))
        op("vector", lambda e: e.tensor_sub(out=R["t1"][:], in0=CT["siota"][:], in1=R["acc"][:]))
        op("vector", lambda e: e.scalar_tensor_tensor(out=R["slotv"][:], in0=R["csum"][:],
                                                      scalar=float(CAP), in1=R["t1"][:],
                                                      op0=ALU.mult, op1=ALU.add))
        op("vector", lambda e: e.tensor_scalar(out=R["t1"][:], in0=CT["siota"][:],
                                               scalar1=mytot[:, 0:1], scalar2=None, op0=ALU.is_ge))
        op("vector", lambda e: e.tensor_scalar(out=R["t2"][:], in0=R["slotv"][:],
                                               scalar1=-1.0, scalar2=float(DUMMY),
                                               op0=ALU.mult, op1=ALU.add))
        op("vector", lambda e: e.tensor_tensor(out=R["t2"][:], in0=R["t1"][:], in1=R["t2"][:], op=ALU.mult))
        t_slotv = st("vector", "V", lambda e: e.tensor_add(out=R["slotv"][:], in0=R["slotv"][:], in1=R["t2"][:]))

        if stop_after == "p3d":
            early_out(); off[0] = True
        def fold(src, dst, tag):
            for g in range(8):
                pslot = 1024 + (g % 2) * 64
                t_fm_ = st("tensor", "T", lambda e, g=g, src=src, pslot=pslot: e.matmul(
                    psumP[:, pslot:pslot + 32], CT[f"foldsel{g}"][:], src[:],
                    start=True, stop=True))
                wt("vector", "T", t_fm_)
                t_fc = st("vector", "V", lambda e, g=g, dst=dst, pslot=pslot: e.tensor_copy(
                    out=dst[:].rearrange("p (mm gg) -> p mm gg", gg=8)[:, :, g:g + 1],
                    in_=psumP[:, pslot:pslot + 32].rearrange("p (a b) -> p a b", b=1)))
                if g >= 1:
                    wt("tensor", "V", t_fc)
            return cnt["V"]

        wt("tensor", "V", t_slotv)
        fold(R["sx1"], Lk0, "k0")
        fold(R["sx2"], Lk1, "k1")
        t_folds = fold(R["slotv"], Lslot, "sl")

        wt("tensor", "V", t_folds)
        wt("sync", "V", t_folds)
        t_y = [0] * NT
        t_ev2 = [0] * NT
        t_zf = [0] * NT

        t_pstl = [0] * NT

        def stageB(m):
            fm = fmt[m % 2]
            ptr = p_trbs[m % 2]
            pz = p_zs[m % 2]
            if m >= 2:
                wt("tensor", "V", t_ev2[m - 2])
            t_tr = None
            for j in range(8):
                fn = (lambda e, m=m, j=j, ptr=ptr: e.transpose(
                    ptr[:, j * 128:(j + 1) * 128],
                    xn_tm[:, m * D + j * 128:m * D + (j + 1) * 128],
                    CT["ident_bf"][:]))
                if j == 7:
                    t_tr = st("tensor", "T", fn)
                else:
                    op("tensor", fn)
            wt("vector", "T", t_tr)
            t_ev2[m] = st("vector", "V", lambda e, fm=fm, ptr=ptr: e.tensor_copy(out=fm, in_=ptr[:]))
            wt("tensor", "V", t_ev2[m])
            if m >= 2:
                wt("tensor", "A", t_zf[m - 2])
            t_zmm = None
            for o in range(2):
                for j in range(8):
                    op("tensor", lambda e, fm=fm, o=o, j=j, pz=pz: e.matmul(
                        pz[:, o * 512:(o + 1) * 512], fm[:, j * 128:(j + 1) * 128],
                        w1_sb[:, j * D + o * 512:j * D + (o + 1) * 512],
                        start=(j == 0), stop=False))
                fnb = (lambda e, o=o, pz=pz: e.matmul(
                    pz[:, o * 512:(o + 1) * 512], CT["ones1_bf"][:],
                    b1_sb[:, o * 512:(o + 1) * 512], start=False, stop=True))
                if o == 1:
                    t_zmm = st("tensor", "T", fnb)
                else:
                    op("tensor", fnb)
            wt("scalar", "T", t_zmm)
            if m >= 1:
                wt("scalar", f"sP{(m - 1) % 2}", t_pstl[m - 1])
            op("scalar", lambda e, pz=pz: e.activation(out=s1[:], in_=pz, func=AF.Exp))
            t_zf[m] = st("scalar", "A", lambda e: e.activation(out=s2, in_=s1[:], func=AF.Ln, bias=1.0))
            wt("vector", "A", t_zf[m])
            t_y[m] = st("vector", "V", lambda e, m=m: e.scalar_tensor_tensor(
                out=s0[:], in0=xn_tm[:, m * D:(m + 1) * D], scalar=svst[:, m:m + 1],
                in1=s2, op0=ALU.mult, op1=ALU.mult))
            wt("scalar", "V", t_y[m])
            t_cpy = st("scalar", "A", lambda e: e.activation(out=y_bf, in_=s0[:], func=AF.Copy))
            wt("vector", "A", t_cpy)
            wt("sync", "A", t_cpy)
            kp = f"sP{m % 2}"
            t_pstl[m] = dma("sync", kp, partial.ap()[m], y_bf, pre_wait=True)

        for m in range(NT):
            stageB(m)
        t_zB_last = cnt["T"]

        if debug:
            wt("gpsimd", "V", t_folds)
            dma("gpsimd", "sDBG", dbg["d_r1"], pk[:])
            for i, k in enumerate(["v1", "v2", "vr1", "vr2"]):
                op("vector", lambda e, i=i, k=k: e.tensor_copy(out=pk[:, i * 32:(i + 1) * 32], in_=R[k][:]))
            t_dbg2 = st("vector", "V", lambda e: e.tensor_copy(out=ex8[:, 0:64], in_=cumB[:]))
            wt("gpsimd", "V", t_dbg2)
            dma("gpsimd", "sDBG", dbg["d_r2"], pk[:])
            dma("gpsimd", "sDBG", dbg["d_r3"][:, 0:64], ex8[0:128, 0:64])
            for i, k in enumerate(["sx1", "sx2", "slotv"]):
                op("vector", lambda e, i=i, k=k: e.tensor_copy(out=tmp8[:, i * 32:(i + 1) * 32], in_=R[k][:]))
            t_dbg4 = st("vector", "V", lambda e: e.tensor_copy(out=tmp8[:, 95:96], in_=mytot[:]))
            wt("gpsimd", "V", t_dbg4)
            dma("gpsimd", "sDBG", dbg["d_r4"], tmp8[:, 0:96])

        class _Stop(Exception):
            pass

        try:
            if stop_after == "p1":
                early_out()
                raise _Stop()
            wt("gpsimd", "V", t_folds)
            wt("gpsimd", "V", t_xn_last)
            wt("gpsimd", "sZ", t_zero)
            t_sc1 = None
            for kk, L in ((0, Lk0), (1, Lk1)):
                for jj in range(4):
                    t_sc1 = bump("sSC", 16)
                    emit("gpsimd", lambda e, L=L, jj=jj: e.dma_scatter_add(
                        out_ap=send1.ap(),
                        in_ap=xn_tm[:, jj * 8192:(jj + 1) * 8192].rearrange("p (m d) -> p m d", d=D),
                        idxs_ap=L[:, jj * 64:(jj + 1) * 64], num_idxs=1024,
                        num_idxs_reg=1024, elem_size=D).then_inc(sem["sSC"], 16))
            wt("gpsimd", "sSC", t_sc1)
            t_a2a1 = st("gpsimd", "CC", lambda e: e.collective_compute(
                "AllToAll", ALU.bypass, replica_groups=[list(range(NCORE))],
                ins=[send1.ap()[0:DUMMY, :].opt()], outs=[recv1.ap()[0:DUMMY, :].opt()]))
            wt("gpsimd", "T", t_zB_last)
            for dc in range(8):
                dma("gpsimd", "sWF", cfc_sb[:, dc * DFF:(dc + 1) * DFF], cfc[dc * 128:(dc + 1) * 128, :])
            for j in range(12):
                t_wffn = dma("gpsimd", "sWF", cpr_sb[:, j * D:(j + 1) * D], cpr[j * 128:(j + 1) * 128, :])
            wt("gpsimd", "CC", t_a2a1)

            exph = [xn_tm[:, hh_ * 4096:(hh_ + 1) * 4096].rearrange("p (b t) -> p b t", b=8)
                    for hh_ in range(8)]
            t_gath = [0] * 8
            wt("gpsimd", "V", t_y[NT - 1])
            for h in range(8):
                kg = f"sEG{h % 2}"
                if cnt[kg] > 0:
                    wt("gpsimd", kg, cnt[kg])
                t_gath[h] = bump(kg, 16)
                emit("gpsimd", lambda e, h=h, kg=kg: e.dma_gather(
                    out_ap=exph[h], in_ap=recv1.ap(), idxs_ap=Lslot[:, h * 32:(h + 1) * 32],
                    num_idxs=512, num_idxs_reg=512, elem_size=D,
                    transpose=True).then_inc(sem[kg], 16))
            wt("tensor", "sWF", t_wffn)
            wt("tensor", "A", t_zf[NT - 1])
            if stop_after == "a2a1":
                wt("gpsimd", "sEG0", cnt["sEG0"])
                wt("gpsimd", "sEG1", cnt["sEG1"])
                early_out()
                raise _Stop()

            t_expo = [0] * 8
            t_lastmult = None
            for s_ in range(8):
                wt("tensor", f"sEG{s_ % 2}", t_gath[s_])
                for j in range(12):
                    pa = psumP[:, (j % 2) * 1024:(j % 2) * 1024 + 512]
                    pg = psumP[:, (j % 2) * 1024 + 512:(j % 2) * 1024 + 1024]
                    if s_ > 0 or j >= 2:
                        wt("tensor", "V", cnt["V"])
                    for dc in range(8):
                        op("tensor", lambda e, j=j, dc=dc, s_=s_, pa=pa: e.matmul(
                            pa, cfc_sb[:, dc * DFF + j * 128:dc * DFF + (j + 1) * 128],
                            exph[s_][:, dc, :],
                            start=(dc == 0), stop=(dc == 7)))
                    t_mm_g = None
                    for dc in range(8):
                        fn = (lambda e, j=j, dc=dc, s_=s_, pg=pg: e.matmul(
                            pg, cfc_sb[:, dc * DFF + (j + 12) * 128:dc * DFF + (j + 13) * 128],
                            exph[s_][:, dc, :],
                            start=(dc == 0), stop=(dc == 7)))
                        if dc == 7:
                            t_mm_g = st("tensor", "T", fn)
                        else:
                            op("tensor", fn)
                    wt("scalar", "T", t_mm_g)
                    if t_lastmult:
                        wt("scalar", "V", t_lastmult)
                    t_silu = st("scalar", "A", lambda e, pa=pa: e.activation(out=sa, in_=pa, func=AF.Silu))
                    wt("vector", "A", t_silu)
                    t_lastmult = st("vector", "V", lambda e, j=j, pg=pg: e.tensor_tensor(
                        out=hh[:, j * 512:(j + 1) * 512], in0=sa, in1=pg, op=ALU.mult))
                t_hh = cnt["V"]
                wt("tensor", "V", t_hh)
                for sub in range(4):
                    t_o = None
                    for nh in range(2):
                        for j in range(12):
                            fn = (lambda e, sub=sub, nh=nh, j=j: e.matmul(
                                p_z[:, nh * 512:(nh + 1) * 512],
                                hh[:, j * 512 + sub * 128:j * 512 + (sub + 1) * 128],
                                cpr_sb[:, j * D + nh * 512:j * D + (nh + 1) * 512],
                                start=(j == 0), stop=(j == 11)))
                            if nh == 1 and j == 11:
                                t_o = st("tensor", "T", fn)
                            else:
                                op("tensor", fn)
                    wt("vector", "T", t_o)
                    t_ev = st("vector", "V", lambda e, sub=sub: e.tensor_copy(
                        out=expo[:, sub * D:(sub + 1) * D], in_=p_z[:]))
                    wt("tensor", "V", t_ev)
                t_evall = cnt["V"]
                wt("gpsimd", "V", t_evall)
                keo = f"sEO{s_ % 4}"
                if cnt[keo] > 0:
                    wt("gpsimd", keo, cnt[keo])
                t_expo[s_] = bump(keo, 16)
                emit("gpsimd", lambda e, s_=s_, keo=keo: e.dma_scatter_add(
                    out_ap=send2.ap(), in_ap=expo[:].rearrange("p (a d) -> p a d", d=D),
                    idxs_ap=Lslot[:, s_ * 32:(s_ + 1) * 32], num_idxs=512,
                    num_idxs_reg=512, elem_size=D).then_inc(sem[keo], 16))
                if s_ < 7:
                    wt("vector", keo, t_expo[s_])

            for q_ in range(4):
                if cnt[f"sEO{q_}"] > 0:
                    wt("gpsimd", f"sEO{q_}", cnt[f"sEO{q_}"])
            t_a2a2 = st("gpsimd", "CC", lambda e: e.collective_compute(
                "AllToAll", ALU.bypass, replica_groups=[list(range(NCORE))],
                ins=[send2.ap()[0:DUMMY, :].opt()], outs=[recv2.ap()[0:DUMMY, :].opt()]))
            wt("gpsimd", "CC", t_a2a2)
            if stop_after == "ffn":
                early_out()
                raise _Stop()
            gb0 = cfc_sb[:, 0:8192].rearrange("p (b d) -> p b d", d=D)
            gb1 = cfc_sb[:, 8192:16384].rearrange("p (b d) -> p b d", d=D)
            wt("sync", "A", cnt["A"])
            wt("sync", "V", cnt["V"])
            wt("vector", "sP0", cnt["sP0"])
            wt("vector", "sP1", cnt["sP1"])
            t_out = None
            for cj in range(4):
                if cnt["sOG0"] > 0:
                    wt("gpsimd", "sOG0", cnt["sOG0"])
                    wt("gpsimd", "sOG1", cnt["sOG1"])
                tg0 = bump("sOG0", 16)
                emit("gpsimd", lambda e, cj=cj: e.dma_gather(
                    out_ap=gb0, in_ap=recv2.ap(), idxs_ap=Lk0[:, cj * 64:(cj + 1) * 64],
                    num_idxs=1024, num_idxs_reg=1024, elem_size=D).then_inc(sem["sOG0"], 16))
                tg1 = bump("sOG1", 16)
                emit("gpsimd", lambda e, cj=cj: e.dma_gather(
                    out_ap=gb1, in_ap=recv2.ap(), idxs_ap=Lk1[:, cj * 64:(cj + 1) * 64],
                    num_idxs=1024, num_idxs_reg=1024, elem_size=D).then_inc(sem["sOG1"], 16))
                wt("vector", "sOG0", tg0)
                wt("vector", "sOG1", tg1)
                for b in range(8):
                    mloc = cj * 8 + b
                    pb = pt8[b % 2]
                    yb = yt8[b % 2]
                    ob = ot8[b % 2]
                    if cj > 0 or b >= 2:
                        wt("sync", "V", cnt["V"])
                    kpl = f"sPL{b % 2}"
                    t_pl = dma("sync", kpl, pb[:], xsh[mloc], pre_wait=True)
                    kyl = f"sX{b % 2}"
                    t_yl = dma("sync", kyl, yb, partial.ap()[mloc], pre_wait=True)
                    wt("vector", kpl, t_pl)
                    wt("vector", kyl, t_yl)
                    op("vector", lambda e, b=b, mloc=mloc, pb=pb, ob=ob: e.scalar_tensor_tensor(
                        out=ob[:], in0=gb0[:, b, :], scalar=R["wm1"][:, mloc:mloc + 1],
                        in1=pb[:], op0=ALU.mult, op1=ALU.add))
                    op("vector", lambda e, b=b, mloc=mloc, ob=ob: e.scalar_tensor_tensor(
                        out=ob[:], in0=gb1[:, b, :], scalar=R["wm2"][:, mloc:mloc + 1],
                        in1=ob[:], op0=ALU.mult, op1=ALU.add))
                    t_cmb = st("vector", "V", lambda e, yb=yb, ob=ob: e.tensor_tensor(
                        out=ob[:], in0=yb, in1=ob[:], op=ALU.add))
                    wt("sync", "V", t_cmb)
                    kst = f"sST{b % 2}"
                    t_out = dma("sync", kst, out[mloc], ob[:], pre_wait=True)
                    if not (cj == 3 and b == 7):
                        wt("vector", kst, t_out)
                if cj < 3:
                    wt("gpsimd", "V", cnt["V"])
            wt("sync", "sST0", cnt["sST0"])
            wt("sync", "sST1", cnt["sST1"])
            if debug:
                wt("sync", "sDBG", cnt["sDBG"])

        except _Stop:
            pass

        with allow_ici_all_to_all(), nc.Block() as block:
            @block.sync
            def _(en):
                for f in plans["sync"]:
                    f(en)

            @block.scalar
            def _(en):
                for f in plans["scalar"]:
                    f(en)

            @block.vector
            def _(en):
                for f in plans["vector"]:
                    f(en)

            @block.tensor
            def _(en):
                for f in plans["tensor"]:
                    f(en)

            @block.gpsimd
            def _(en):
                for f in plans["gpsimd"]:
                    f(en)

    lower_extended_insts(nc)
    return nc


def prepare_in_maps(inputs):
    x = np.ascontiguousarray(np.asarray(inputs["L0_attn_x"], np.float32)).reshape(NCOL, 128, D)
    W1 = np.ascontiguousarray(np.asarray(inputs["L0_attn_W1"], np.float32))
    b1 = np.asarray(inputs["L0_attn_b1"], np.float32).reshape(1, D)
    W2 = np.asarray(inputs["L0_attn_W2"], np.float32)
    b2 = np.asarray(inputs["L0_attn_b2"], np.float32)
    W3 = np.asarray(inputs["L0_attn_W3"], np.float32)
    b3 = np.asarray(inputs["L0_attn_b3"], np.float32)
    wg = np.asarray(inputs["L0_ffn_w_gate"], np.float32)
    cfc = np.asarray(inputs["L0_ffn_c_fc"], np.float32)
    cpr = np.asarray(inputs["L0_ffn_c_proj"], np.float32)
    Wbcg = np.ascontiguousarray(np.concatenate([W2, W3, wg], axis=1))
    brow = np.ascontiguousarray(np.concatenate([b2, b3, np.zeros(8, np.float32)])).reshape(1, 40)

    in_maps = []
    for c in range(NCORE):
        m = {
            "xsh": np.ascontiguousarray(x[c::8]),
            "W1": W1, "Wbcg": Wbcg, "brow": brow, "b1row": b1,
            "cfc": np.ascontiguousarray(cfc[c]),
            "cpr": np.ascontiguousarray(cpr[c]),
        }
        for k, v in host_constants(c).items():
            m[k] = np.ascontiguousarray(v)
        in_maps.append(m)
    return in_maps


def kernel(**inputs):
    in_maps = prepare_in_maps(inputs)
    nc = build_nc(DEBUG)
    res = run_bass_kernel_spmd(nc, in_maps, core_ids=list(range(NCORE)))
    full = np.zeros((NCOL, 128, D), np.float32)
    for c in range(NCORE):
        full[c::8] = res.results[c]["out"]
    if DEBUG:
        kernel._debug = [{k: v for k, v in r.items() if k.startswith("d_")} for r in res.results]
    return full.reshape(8, 4096, 1024)


# version 07
# speedup vs baseline: 1.2091x; 1.0886x over previous
import numpy as np
import ml_dtypes
import concourse.bass as bass
import concourse.mybir as mybir
from concourse.bass_utils import run_bass_kernel_spmd
from concourse import library_config
from concourse.aot_env import allow_ici_all_to_all
from concourse.library_overlay import lower_extended_insts
from contextlib import ExitStack

F32 = mybir.dt.float32
BF16 = mybir.dt.bfloat16
I16 = mybir.dt.int16
U8 = mybir.dt.uint8
AF = mybir.ActivationFunctionType
ALU = mybir.AluOpType
AX = mybir.AxisListType

NCORE = 8
NT = 32
NCOL = 256
D = 1024
NEXP = 8
CAPN = 4096
CAP = 768
DFF = 3072
DH = DFF // 2
DUMMY = NEXP * CAP

DEBUG = False
STOP_AFTER = "full"
BFNP = ml_dtypes.bfloat16


def host_constants(core):
    c = core
    H = {}
    H["ltri128"] = np.triu(np.ones((128, 128), np.float32), 1)
    q = np.arange(128)
    H["ltri16x8"] = ((q[:, None] % 8 == q[None, :] % 8)
                     & (q[:, None] // 8 < q[None, :] // 8)).astype(np.float32)
    H["ident_bf"] = np.eye(128, dtype=np.float32).astype(BFNP)
    H["ident_f32"] = np.eye(128, dtype=np.float32)
    H["onescol"] = np.ones((128, 1), np.float32)
    H["ones1_bf"] = np.ones((1, 128), np.float32).astype(BFNP)
    o16p = np.zeros((32, 128), np.float32); o16p[0:16, :] = 1.0
    H["ones16p"] = o16p
    H["ones64"] = np.ones((64, 128), np.float32)
    H["ones128sq"] = np.ones((128, 128), np.float32)
    so = np.zeros((128, 32), np.float32)
    for half in range(2):
        for e in range(8):
            so[((c + 8 * half) % 16) * 8 + e, half * 8 + e] = 1.0
    H["selown16"] = so
    m16 = np.zeros((16, 256), np.float32)
    for mloc in range(32):
        for e in range(8):
            m16[(mloc % 2) * 8 + e, mloc * 8 + e] = 1.0
    H["m16pat"] = m16
    lt16 = np.zeros((32, 32), np.float32)
    lt16[:16, :16] = np.triu(np.ones((16, 16), np.float32), 1)
    H["ltri16p"] = lt16
    m16b = np.zeros((16, 256), np.float32)
    for mloc in range(32):
        for e in range(8):
            m16b[mloc // 2, mloc * 8 + e] = 1.0
    H["m16bpat"] = m16b
    m8t = (np.arange(128)[None, :] % 8 == np.arange(8)[:, None]).astype(np.float32)
    H["mod8tp"] = np.concatenate([m8t, np.zeros((24, 128), np.float32)], 0)
    H["mod8p"] = np.concatenate([m8t.T, np.zeros((128, 24), np.float32)], 1)
    sc8 = np.zeros((128, 64), np.float32)
    for qq in range(128):
        sc8[qq, ((qq // 8) % 8) * 8 + (qq % 8)] = 1.0
    H["selc8"] = sc8
    cm = np.arange(64)
    ce_c, ce_e = cm // 8, cm % 8
    H["mask64x"] = ((ce_e[:, None] == ce_e[None, :]) & (ce_c[:, None] < ce_c[None, :])).astype(np.float32)
    H["mask64i"] = ((ce_e[:, None] == ce_e[None, :]) & (ce_c[:, None] <= ce_c[None, :])).astype(np.float32)
    H["i64"] = np.eye(64, dtype=np.float32)
    H["memask"] = np.tile((np.arange(64) % 8 == c).astype(np.float32)[None, :], (128, 1))
    H["memask_tot"] = np.tile((np.arange(64) == 56 + c).astype(np.float32)[None, :], (128, 1))
    mp = np.zeros((128, 256), np.float32)
    for mloc in range(32):
        for e in range(8):
            mp[(mloc % 16) * 8 + e, mloc * 8 + e] = 1.0
    H["m128pat"] = mp
    for g in range(8):
        fs = np.zeros((128, 128), np.float32)
        for mrow in range(128):
            fs[16 * g + (mrow % 16), mrow] = 1.0
        H[f"foldsel{g}"] = fs
    H["siota"] = (np.arange(32)[None, :] * 128 + np.arange(128)[:, None]).astype(np.float32)
    H["dumoff"] = np.tile((np.arange(32) // 8).astype(np.float32)[None, :], (128, 1))
    u32 = np.zeros((32, 16), np.float32)
    u32[:16, :] = (np.arange(16)[:, None] < np.arange(16)[None, :]).astype(np.float32)
    H["ultri32"] = u32
    return H




def build_nc(debug=DEBUG, stop_after=None):
    stop_after = stop_after or STOP_AFTER
    nc = bass.Bass("TRN2", target_bir_lowering=False)
    AP_IN = {}

    def din(name, shape, dtype=F32):
        AP_IN[name] = nc.dram_tensor(name, list(shape), dtype, kind="ExternalInput").ap()
        return AP_IN[name]

    xsh = din("xsh", (NT, 128, D))
    W1 = din("W1", (D, D))
    Wbcg = din("Wbcg", (D, 40))
    brow = din("brow", (1, 40))
    b1row = din("b1row", (1, D))
    cfc = din("cfc", (D, DFF))
    cpr = din("cpr", (DH, D))
    consts = host_constants(0)
    for k, v in consts.items():
        din(k, v.shape, BF16 if v.dtype == BFNP else F32)

    partial = nc.dram_tensor("partial", [NT, 128, D], BF16, kind="Internal")
    send1 = nc.dram_tensor("send1", [DUMMY + 8, D], BF16, kind="Internal")
    recv1 = nc.dram_tensor("recv1", [DUMMY + 8, D], BF16, kind="Internal")
    send2 = nc.dram_tensor("send2", [DUMMY + 8, D], BF16, kind="Internal")
    recv2 = nc.dram_tensor("recv2", [DUMMY + 8, D], BF16, kind="Internal")
    ag_in = nc.dram_tensor("ag_in", [128, 128], F32, kind="Internal")
    ag_out = nc.dram_tensor("ag_out", [NCORE * 128, 128], F32, kind="Internal")

    out = nc.dram_tensor("out", [NT, 128, D], F32, kind="ExternalOutput").ap()
    dbg = {}
    if debug:
        for nm, sh in [("d_r1", (128, 128)), ("d_r2", (128, 128)),
                       ("d_r3", (128, 128)), ("d_r4", (128, 96))]:
            dbg[nm] = nc.dram_tensor(nm, list(sh), F32, kind="ExternalOutput").ap()

    with ExitStack() as ctx:
        e_ = ctx.enter_context

        def sb(name, shape, dtype):
            return e_(nc.sbuf_tensor(name, list(shape), dtype))

        CT = {k: sb("c_" + k, v.shape, BF16 if v.dtype == BFNP else F32)
              for k, v in consts.items()}
        xt = [sb("xt0", (128, D), F32), sb("xt1", (128, D), F32)]
        sa = None
        s0 = sb("s0", (128, D), F32)
        s1 = sb("s1", (128, D), F32)
        u1 = sb("u1", (128, 4096), BF16)
        s2 = u1[:, 0:2048].bitcast(F32)
        fmt = [u1[:, 2048:3072], u1[:, 3072:4096]]
        lnst = sb("lnst", (128, 16), F32)
        mnst = sb("mnst", (128, NT), F32)
        rsst = sb("rsst", (128, NT), F32)
        svst = sb("svst", (128, NT), F32)
        bc16 = sb("bc16", (128, 16), F32)
        bc40 = sb("bc40", (128, 40), F32)
        xn_tm = sb("xn_tm", (128, NT * D), BF16)
        wbcg_sb = sb("wbcg_sb", (128, 8 * 40), BF16)
        brow_sb = sb("brow_sb", (1, 40), BF16)
        b1_sb = sb("b1_sb", (1, D), BF16)
        logits = sb("logits", (128, NT * 8), F32)
        R = {k: sb("r_" + k, (128, NT), F32) for k in
             ["i1", "i2", "w1", "w2", "m1", "m2", "den", "v1", "v2",
              "vr1", "vr2", "sx1", "sx2", "wm1", "wm2", "slotv", "csum",
              "acc", "t1", "t2"]}
        tmp8 = sb("tmp8", (128, NT * 8), F32)
        ex8 = sb("ex8", (128, NT * 8), F32)
        cbig = sb("cbig", (128, NT * 8), F32)
        eio8 = sb("eio8", (128, NT * 8), F32)
        pk = sb("pk", (128, 128), F32)
        idx1g = sb("idx1g", (128, NCOL), F32)
        idx2g = sb("idx2g", (128, NCOL), F32)

        t16 = sb("t16", (128, 16), F32)
        a1 = sb("a1", (128, 16), F32)
        g16 = sb("g16", (128, 32), F32)
        g16t = sb("g16t", (32, 16), F32)
        g16x = sb("g16x", (128, 32), F32)
        g16xT = sb("g16xT", (32, 16), F32)
        o16 = sb("o16", (16, 16), F32)
        rhsA = sb("rhsA", (32, 256), F32)
        rhsB = sb("rhsB", (32, 256), F32)
        ownoff = sb("ownoff", (128, 256), F32)
        offgT = sb("offgT", (128, 16), F32)
        keptT = sb("keptT", (128, 16), F32)
        kscr = sb("kscr", (128, 16), F32)
        vc64 = sb("vc64", (64, 16), F32)
        vc64r = sb("vc64r", (64, 1), F32)
        cum64 = sb("cum64", (64, 1), F32)
        inc64 = sb("inc64", (64, 1), F32)

        cumB = pk[:, 0:64]
        incB = pk[:, 64:128]
        scr64 = sb("scr64", (128, 64), F32)
        mycum = sb("mycum", (128, 8), F32)
        myinc = sb("myinc", (128, 8), F32)
        myvc = sb("myvc", (128, 8), F32)
        mytot = sb("mytot", (128, 1), F32)
        oh1 = sb("oh1", (128, NT * 8), F32)
        oh2 = idx2g
        ohs = sb("ohs", (128, NT * 8), F32)
        ecsU = logits
        ecsV = eio8
        ohv1 = tmp8
        ohv2 = ex8
        selbuf = cbig
        csrowV = idx1g
        a2t = sb("a2t", (128, 2), F32)
        a2x = sb("a2x", (128, 2), F32)
        tot0e = sb("tot0e", (32, 2), F32)
        tot0b = sb("tot0b", (128, 1), F32)
        rhs3 = ownoff
        offs2B = sb("offs2B", (128, 256), F32)
        Lk0 = sb("Lk0", (128, 256), I16)
        Lk1 = sb("Lk1", (128, 256), I16)
        Lslot = sb("Lslot", (128, 256), I16)
        cfc_sb = sb("cfc_sb", (128, 8 * DFF), BF16)
        w1_sb = cfc_sb.tensor if hasattr(cfc_sb, "tensor") else cfc_sb
        w1_sb = cfc_sb
        cpr_sb = sb("cpr_sb", (128, 12 * D), BF16)
        G = cpr_sb[:, 0:4096].bitcast(F32)
        srow = cpr_sb[:, 4096:8192].bitcast(F32)
        hh = sb("hh", (128, 12 * 512), BF16)
        expo = u1
        sa = xt[0][:, 0:512]

        zeros_bf = hh
        pt8 = xt
        ot8 = [s0, s1]
        y_bf = s1[:, 0:512].bitcast(BF16)
        yt8 = [hh[:, 0:1024], hh[:, 1024:2048]]

        psumP = e_(nc.psum_tensor("psumP", [128, 2048], F32))
        p_z = e_(nc.psum_tensor("p_z", [128, D], F32))
        p_bc = e_(nc.psum_tensor("p_bc", [128, 64], F32))
        p_tr = psumP[:, 0:1024]
        p_trbs = [psumP[:, 0:512].bitcast(BF16), psumP[:, 512:1024].bitcast(BF16)]
        p_zs = [p_z[:, 0:1024], psumP[:, 1024:2048]]
        p_bcs = [p_bc[:, 0:40], p_bc[:, 0:40]]
        p_row = psumP[0:1, 0:2048]

        sem = {nm: e_(nc.semaphore(nm)) for nm in
               ["sC", "sX0", "sX1", "sP0", "sP1", "sW1", "sWF", "sAG", "sSC",
                "sEG0", "sEG1", "sEO0", "sEO1", "sEO2", "sEO3", "sOG0", "sOG1",
                "sPL0", "sPL1", "sST0", "sST1", "sZ", "sDBG", "T", "V", "A", "CC"]}
        cnt = {k: 0 for k in sem}
        plans = {k: [] for k in ["sync", "scalar", "vector", "tensor", "gpsimd"]}
        off = [False]

        def emit(eng, fn):
            if off[0]:
                return
            plans[eng].append(fn)

        def bump(key, n=1):
            cnt[key] += n
            return cnt[key]

        def wt(eng, key, val):
            emit(eng, lambda e, k=key, v=val: e.wait_ge(sem[k], v))

        def dma(eng, key, out_ap, in_ap, pre_wait=False):
            pre = cnt[key]
            if pre_wait and pre > 0:
                wt(eng, key, pre)
            tgt = bump(key, 16)
            emit(eng, lambda e, o=out_ap, i=in_ap, k=key: e.dma_start(out=o, in_=i).then_inc(sem[k], 16))
            return tgt

        def st(eng, key, fn):
            if eng in ("vector", "scalar"):
                return op(eng, fn)
            tgt = bump(key, 1)
            emit(eng, lambda e, f=fn, k=key: f(e).then_inc(sem[k], 1))
            return tgt

        def op(eng, fn):
            if eng in ("vector", "scalar"):
                key = "V" if eng == "vector" else "A"
                pre = cnt[key]
                tgt = bump(key, 1)

                def run(e, f=fn, pre=pre, key=key):
                    ins = f(e).then_inc(sem[key], 1)
                    if pre > 0:
                        ins._wait_ge(sem[key], pre)
                emit(eng, run)
                return tgt
            emit(eng, lambda e, f=fn: f(e))
            return None

        def r3(ap, inner=8):
            return ap.rearrange("p (c e) -> p c e", e=inner)

        def rcol(ap):
            return ap.rearrange("p (c o) -> p c o", o=1)

        for k in consts:
            dma("sync", "sC", CT[k][:], AP_IN[k])
        t_const = cnt["sC"]
        op("gpsimd", lambda e: e.load_library(library_config.mlp))
        for j in range(8):
            dma("gpsimd", "sW1", w1_sb[:, j * D:(j + 1) * D], W1[j * 128:(j + 1) * 128, :])
        dma("gpsimd", "sW1", wbcg_sb[:].rearrange("p (a b) -> p a b", a=8),
            Wbcg.rearrange("(a p) b -> p a b", p=128))
        dma("gpsimd", "sW1", brow_sb[:], brow)
        dma("gpsimd", "sW1", b1_sb[:], b1row)
        t_wsmall = cnt["sW1"]

        z_v = st("vector", "V", lambda e: e.memset(zeros_bf[:, 0:2048], 0.0))
        wt("scalar", "V", z_v)
        for base in range(0, DUMMY, 256):
            dma("scalar", "sZ", send1.ap()[base:base + 256, :].rearrange("(a p) d -> p a d", p=128),
                zeros_bf[:, 0:2048].rearrange("p (a d) -> p a d", d=D))
            dma("scalar", "sZ", send2.ap()[base:base + 256, :].rearrange("(a p) d -> p a d", p=128),
                zeros_bf[:, 0:2048].rearrange("p (a d) -> p a d", d=D))
        dma("scalar", "sZ", send1.ap()[DUMMY:DUMMY + 1, :], zeros_bf[0:1, 0:1024])
        dma("scalar", "sZ", send2.ap()[DUMMY:DUMMY + 1, :], zeros_bf[0:1, 0:1024])
        dma("scalar", "sZ", recv2.ap()[DUMMY:DUMMY + 8, :], zeros_bf[0:8, 0:1024])
        t_zero = cnt["sZ"]
        for e8 in range(8):
            op("vector", lambda e, i=e8: e.memset(
                r3(eio8[:])[:, :, i:i + 1], float(i)))
        op("vector", lambda e: e.memset(g16[:], 0.0))
        op("vector", lambda e: e.memset(g16t[:], 0.0))
        op("vector", lambda e: e.memset(g16x[:], 0.0))
        op("vector", lambda e: e.memset(g16xT[:], 0.0))
        op("vector", lambda e: e.memset(rhsA[:], 0.0))
        op("vector", lambda e: e.memset(rhsB[:], 0.0))
        op("vector", lambda e: e.memset(tot0e[:], 0.0))
        op("vector", lambda e: e.memset(Lk0[:], 0))
        op("vector", lambda e: e.memset(Lk1[:], 0))
        op("vector", lambda e: e.memset(Lslot[:], 0))

        wt("tensor", "sC", t_const)
        wt("tensor", "sW1", t_wsmall)
        wt("vector", "sC", t_const)
        t_xnb = [0] * NT
        t_evac = [0] * NT
        t_bcv = [0] * NT

        def stageA_early(m):
            xb = xt[m % 2]
            if m >= 2:
                wt("sync", "V", t_xnb[m - 2])
            kx = f"sX{m % 2}"
            t_x = dma("sync", kx, xb[:], xsh[m], pre_wait=True)
            L = lnst[:, (m % 2) * 8:(m % 2) * 8 + 8]
            wt("vector", kx, t_x)
            op("vector", lambda e, xb=xb, L=L: e.reduce_sum(
                out=L[:, 0:1].rearrange("p (a b) -> p a b", b=1),
                in_=xb[:].rearrange("p (a b) -> p a b", a=1), axis=AX.X))
            op("vector", lambda e, xb=xb: e.tensor_tensor(out=s1[:], in0=xb[:], in1=xb[:], op=ALU.mult))
            op("vector", lambda e, L=L: e.reduce_sum(
                out=L[:, 1:2].rearrange("p (a b) -> p a b", b=1),
                in_=s1[:].rearrange("p (a b) -> p a b", a=1), axis=AX.X))
            op("vector", lambda e, L=L, m=m: e.tensor_scalar(out=mnst[:, m:m + 1], in0=L[:, 0:1],
                                                             scalar1=1.0 / D, scalar2=None, op0=ALU.mult))
            op("vector", lambda e, L=L: e.tensor_scalar(out=L[:, 3:4], in0=L[:, 1:2],
                                                        scalar1=1.0 / D, scalar2=None, op0=ALU.mult))
            op("vector", lambda e, L=L, m=m: e.tensor_tensor(out=L[:, 4:5], in0=mnst[:, m:m + 1],
                                                             in1=mnst[:, m:m + 1], op=ALU.mult))
            t_var = st("vector", "V", lambda e, L=L: e.scalar_tensor_tensor(
                out=L[:, 5:6], in0=L[:, 3:4], scalar=1e-5,
                in1=L[:, 4:5], op0=ALU.add, op1=ALU.subtract))
            wt("scalar", "V", t_var)
            op("scalar", lambda e, L=L: e.activation(out=L[:, 6:7], in_=L[:, 5:6], func=AF.Ln))
            op("scalar", lambda e, L=L: e.mul(out=L[:, 6:7], in_=L[:, 6:7], mul=-0.5))
            t_rstd = st("scalar", "A", lambda e, L=L, m=m: e.activation(
                out=rsst[:, m:m + 1], in_=L[:, 6:7], func=AF.Exp))
            wt("vector", "A", t_rstd)
            t_xnb[m] = st("vector", "V", lambda e, m=m, xb=xb: e.tensor_scalar(
                out=xn_tm[:, m * D:(m + 1) * D], in0=xb[:],
                scalar1=mnst[:, m:m + 1], scalar2=rsst[:, m:m + 1],
                op0=ALU.subtract, op1=ALU.mult))

        def stageA_late(m):
            fm = fmt[m % 2]
            ptr = p_trbs[m % 2]
            wt("tensor", "V", t_xnb[m])
            if m >= 2:
                wt("tensor", "V", t_evac[m - 2])
            t_tr = None
            for j in range(8):
                fn = (lambda e, m=m, j=j, ptr=ptr: e.transpose(
                    ptr[:, j * 128:(j + 1) * 128],
                    xn_tm[:, m * D + j * 128:m * D + (j + 1) * 128],
                    CT["ident_bf"][:]))
                if j == 7:
                    t_tr = st("tensor", "T", fn)
                else:
                    op("tensor", fn)
            wt("vector", "T", t_tr)
            t_evac[m] = st("vector", "V", lambda e, fm=fm, ptr=ptr: e.tensor_copy(out=fm, in_=ptr[:]))
            wt("tensor", "V", t_evac[m])
            if m >= 2:
                wt("tensor", "V", t_bcv[m - 2])
            pb = p_bcs[m % 2]
            for j in range(8):
                op("tensor", lambda e, fm=fm, j=j, pb=pb: e.matmul(
                    pb, fm[:, j * 128:(j + 1) * 128],
                    wbcg_sb[:, j * 40:(j + 1) * 40], start=(j == 0), stop=False))
            t_bc = st("tensor", "T", lambda e, pb=pb: e.matmul(
                pb, CT["ones1_bf"][:], brow_sb[:], start=False, stop=True))
            wt("vector", "T", t_bc)
            op("vector", lambda e, pb=pb: e.tensor_copy(out=bc40[:], in_=pb))
            op("vector", lambda e: e.tensor_tensor(out=bc16[:], in0=bc40[:, 0:16],
                                                   in1=bc40[:, 16:32], op=ALU.mult))
            op("vector", lambda e, m=m: e.reduce_sum(
                out=svst[:, m:m + 1].rearrange("p (a b) -> p a b", b=1),
                in_=bc16[:].rearrange("p (a b) -> p a b", a=1), axis=AX.X))
            t_bcv[m] = st("vector", "V", lambda e, m=m: e.tensor_copy(
                out=logits[:, m * 8:(m + 1) * 8], in_=bc40[:, 32:40]))

        for m in range(NT):
            stageA_early(m)
            if m >= 1:
                stageA_late(m - 1)
        stageA_late(NT - 1)
        t_xn_last = t_xnb[NT - 1]

        if stop_after == "p1a":
            early_out()
            off[0] = True
        z3 = r3(logits[:])
        eio3 = r3(eio8[:])
        op("vector", lambda e: e.reduce_max(out=rcol(R["m1"][:]), in_=z3, axis=AX.X))
        op("vector", lambda e: e.tensor_tensor(
            out=r3(ohs[:]), in0=z3,
            in1=rcol(R["m1"][:]).broadcast_to((128, NT, 8)), op=ALU.is_equal))
        op("vector", lambda e: e.tensor_tensor(out=tmp8[:], in0=ohs[:], in1=eio8[:], op=ALU.mult))
        op("vector", lambda e: e.scalar_tensor_tensor(out=tmp8[:], in0=ohs[:], scalar=-99.0,
                                                      in1=tmp8[:], op0=ALU.mult, op1=ALU.add))
        op("vector", lambda e: e.tensor_scalar(out=tmp8[:], in0=tmp8[:], scalar1=99.0,
                                               scalar2=None, op0=ALU.add))
        op("vector", lambda e: e.tensor_reduce(out=rcol(R["i1"][:]), in_=r3(tmp8[:]),
                                               axis=AX.X, op=ALU.min))
        op("vector", lambda e: e.tensor_tensor(
            out=r3(ohs[:]), in0=eio3,
            in1=rcol(R["i1"][:]).broadcast_to((128, NT, 8)), op=ALU.is_equal))
        op("vector", lambda e: e.scalar_tensor_tensor(out=tmp8[:], in0=ohs[:], scalar=-1e30,
                                                      in1=logits[:], op0=ALU.mult, op1=ALU.add))
        op("vector", lambda e: e.tensor_reduce(out=rcol(R["m2"][:]), in_=r3(tmp8[:]),
                                               axis=AX.X, op=ALU.max))
        op("vector", lambda e: e.tensor_tensor(
            out=r3(ohs[:]), in0=r3(tmp8[:]),
            in1=rcol(R["m2"][:]).broadcast_to((128, NT, 8)), op=ALU.is_equal))
        op("vector", lambda e: e.tensor_tensor(out=tmp8[:], in0=ohs[:], in1=eio8[:], op=ALU.mult))
        op("vector", lambda e: e.scalar_tensor_tensor(out=tmp8[:], in0=ohs[:], scalar=-99.0,
                                                      in1=tmp8[:], op0=ALU.mult, op1=ALU.add))
        op("vector", lambda e: e.tensor_scalar(out=tmp8[:], in0=tmp8[:], scalar1=99.0,
                                               scalar2=None, op0=ALU.add))
        op("vector", lambda e: e.tensor_reduce(out=rcol(R["i2"][:]), in_=r3(tmp8[:]),
                                               axis=AX.X, op=ALU.min))
        t_sm0 = st("vector", "V", lambda e: e.tensor_tensor(
            out=r3(tmp8[:]), in0=z3,
            in1=rcol(R["m1"][:]).broadcast_to((128, NT, 8)), op=ALU.subtract))
        wt("scalar", "V", t_sm0)
        t_ex = st("scalar", "A", lambda e: e.activation(out=ex8[:], in_=tmp8[:], func=AF.Exp))
        wt("vector", "A", t_ex)
        op("vector", lambda e: e.tensor_reduce(out=rcol(R["den"][:]), in_=r3(ex8[:]),
                                               axis=AX.X, op=ALU.add))
        op("vector", lambda e: e.reciprocal(out=R["w1"][:], in_=R["den"][:]))
        t_d2 = st("vector", "V", lambda e: e.tensor_sub(out=R["m2"][:], in0=R["m2"][:], in1=R["m1"][:]))
        wt("scalar", "V", t_d2)
        t_e2 = st("scalar", "A", lambda e: e.activation(out=R["den"][:], in_=R["m2"][:], func=AF.Exp))
        wt("vector", "A", t_e2)
        op("vector", lambda e: e.tensor_mul(out=R["w2"][:], in0=R["den"][:], in1=R["w1"][:]))
        t_pk = None
        for i, k in enumerate(["i1", "i2", "w1", "w2"]):
            fn = (lambda e, i=i, k=k: e.tensor_copy(out=pk[:, i * 32:(i + 1) * 32], in_=R[k][:]))
            if i == 3:
                t_pk = st("vector", "V", fn)
            else:
                op("vector", fn)
        wt("gpsimd", "V", t_pk)
        t_agin = dma("gpsimd", "sAG", ag_in.ap(), pk[:])
        wt("gpsimd", "sAG", t_agin)
        t_ag = st("gpsimd", "CC", lambda e: e.collective_compute(
            "AllGather", ALU.bypass, replica_groups=[list(range(NCORE))],
            ins=[ag_in.ap().opt()], outs=[ag_out.ap().opt()]))
        wt("gpsimd", "CC", t_ag)
        for cc in range(NCORE):
            dma("gpsimd", "sAG", tmp8[:, cc * 32:(cc + 1) * 32],
                ag_out.ap()[cc * 128:(cc + 1) * 128, 0:32])
            t_idxg = dma("gpsimd", "sAG", ex8[:, cc * 32:(cc + 1) * 32],
                         ag_out.ap()[cc * 128:(cc + 1) * 128, 32:64])
        wt("vector", "sAG", t_idxg)
        op("vector", lambda e: e.tensor_copy(
            out=idx1g[:].rearrange("p (f cc) -> p f cc", cc=8),
            in_=tmp8[:].rearrange("p (cc f) -> p f cc", cc=8)))
        op("vector", lambda e: e.tensor_copy(
            out=idx2g[:].rearrange("p (f cc) -> p f cc", cc=8),
            in_=ex8[:].rearrange("p (cc f) -> p f cc", cc=8)))

        if stop_after == "p1b":
            early_out()
            off[0] = True
        for e8 in range(8):
            op("vector", lambda e, i=e8: e.tensor_scalar(
                out=r3(G, 8)[:, :, i:i + 1],
                in0=idx1g[:].rearrange("p (c o) -> p c o", o=1),
                scalar1=float(i), scalar2=None, op0=ALU.is_equal))
            op("vector", lambda e, i=e8: e.tensor_scalar(
                out=r3(srow, 8)[:, :, i:i + 1],
                in0=idx2g[:].rearrange("p (c o) -> p c o", o=1),
                scalar1=float(i), scalar2=None, op0=ALU.is_equal))
        t_G = st("vector", "V", lambda e: e.tensor_add(out=G, in0=G, in1=srow))
        if stop_after == "p3a":
            early_out(); off[0] = True
        t_sz = st("vector", "V", lambda e: e.memset(srow, 0.0))
        wt("tensor", "V", t_sz)
        t_cs = None
        for qq in range(4):
            fn = (lambda e, q=qq: e.matmul(p_row[:, q * 512:(q + 1) * 512], CT["onescol"][:],
                                           G[:, q * 512:(q + 1) * 512], start=True, stop=True))
            if qq == 3:
                t_cs = st("tensor", "T", fn)
            else:
                op("tensor", fn)
        wt("vector", "T", t_cs)
        t_srow = st("vector", "V", lambda e: e.tensor_copy(out=srow[0:1, :], in_=p_row[:]))
        wt("tensor", "V", t_srow)
        for rnd in range(2):
            t_trr = None
            for j8 in range(8):
                j = rnd * 8 + j8
                fn = (lambda e, j=j, j8=j8: e.matmul(
                    psumP[:, 1024 + j8 * 128:1024 + (j8 + 1) * 128],
                    srow[:, j * 128:(j + 1) * 128], CT["ident_f32"][:],
                    start=True, stop=True))
                if j8 == 7:
                    t_trr = st("tensor", "T", fn)
                else:
                    op("tensor", fn)
            wt("vector", "T", t_trr)
            t_t16 = None
            for j8 in range(8):
                j = rnd * 8 + j8
                fn = (lambda e, j=j, j8=j8: e.tensor_copy(
                    out=t16[:, j:j + 1], in_=psumP[:, 1024 + j8 * 128:1024 + j8 * 128 + 1]))
                if j8 == 7:
                    t_t16 = st("vector", "V", fn)
                else:
                    op("vector", fn)
            wt("tensor", "V", t_t16)
        if stop_after == "p3b":
            early_out(); off[0] = True
        op("tensor", lambda e: e.matmul(psumP[:, 1024:1040], CT["ltri16x8"][:], t16[:],
                                        start=True, stop=True))
        t_mm1 = st("tensor", "T", lambda e: e.matmul(psumP[0:32, 1040:1056], CT["mod8p"][:], t16[:],
                                                     start=True, stop=True))
        wt("vector", "T", t_mm1)
        op("vector", lambda e: e.tensor_copy(out=a1[:], in_=psumP[:, 1024:1040]))
        t_g16 = st("vector", "V", lambda e: e.tensor_copy(out=g16[0:8, 0:16], in_=psumP[0:8, 1040:1056]))
        wt("tensor", "V", t_g16)
        t_mm2 = st("tensor", "T", lambda e: e.matmul(psumP[0:32, 1056:1184], g16[:, 0:16].bitcast(F32) if False else g16[:], CT["ident_f32"][:], start=True, stop=True))
        wt("vector", "T", t_mm2)
        t_g16t = st("vector", "V", lambda e: e.tensor_copy(out=g16t[0:16, :], in_=psumP[0:16, 1056:1072]))
        wt("tensor", "V", t_g16t)
        t_mm3 = st("tensor", "T", lambda e: e.matmul(psumP[0:32, 1184:1192], CT["ltri16p"][:],
                                                     g16t[:, 0:8], start=True, stop=True))
        wt("vector", "T", t_mm3)
        t_g16x = st("vector", "V", lambda e: e.tensor_copy(out=g16x[0:16, 0:8], in_=psumP[0:16, 1184:1192]))
        if stop_after == "p3b2":
            early_out(); off[0] = True
        wt("tensor", "V", t_g16x)
        t_mm4 = st("tensor", "T", lambda e: e.matmul(psumP[0:32, 1192:1208], CT["selown16"][:],
                                                     a1[:], start=True, stop=True))
        wt("vector", "T", t_mm4)
        t_o16 = st("vector", "V", lambda e: e.tensor_copy(out=o16[:], in_=psumP[0:16, 1192:1208]))
        op("vector", lambda e: e.tensor_tensor(
            out=rhsA[0:16, :].rearrange("r (j h ee) -> r j h ee", h=2, ee=8),
            in0=o16[:].rearrange("r (j h ee) -> r j h ee", h=1, ee=1).broadcast_to((16, 16, 2, 8)),
            in1=CT["m16pat"][:].rearrange("r (j h ee) -> r j h ee", h=2, ee=8), op=ALU.mult))
        op("vector", lambda e: e.tensor_tensor(
            out=rhsB[0:16, :].rearrange("r (j h ee) -> r j h ee", h=2, ee=8),
            in0=g16x[0:16, 0:8].rearrange("r (j h ee) -> r j h ee", j=1, h=1).broadcast_to((16, 16, 2, 8)),
            in1=CT["m16bpat"][:].rearrange("r (j h ee) -> r j h ee", h=2, ee=8), op=ALU.mult))
        t_rhs = st("vector", "V", lambda e: e.tensor_add(out=rhsA[:], in0=rhsA[:], in1=rhsB[:]))
        wt("tensor", "V", t_rhs)
        t_mm5 = st("tensor", "T", lambda e: e.matmul(psumP[:, 1024:1280], CT["ones16p"][:],
                                                     rhsA[:], start=True, stop=True))
        wt("vector", "T", t_mm5)
        t_ownoff = st("vector", "V", lambda e: e.tensor_copy(out=ownoff[:], in_=psumP[:, 1024:1280]))
        if stop_after == "p3b3":
            early_out(); off[0] = True
        wt("tensor", "V", t_g16x)
        t_mm6 = st("tensor", "T", lambda e: e.matmul(psumP[0:16, 1280:1296], g16t[:],
                                                     CT["ultri32"][:], start=True, stop=True))
        wt("vector", "T", t_mm6)
        t_gxt = st("vector", "V", lambda e: e.tensor_copy(out=g16xT[0:8, :], in_=psumP[0:8, 1280:1296]))
        if stop_after == "p3b4a":
            early_out(); off[0] = True
        wt("tensor", "V", t_gxt)
        t_mm7 = st("tensor", "T", lambda e: e.matmul(psumP[:, 1408:1424], CT["mod8tp"][:],
                                                     g16xT[:], start=True, stop=True))
        wt("vector", "T", t_mm7)
        op("vector", lambda e: e.tensor_copy(out=offgT[:], in_=psumP[:, 1408:1424]))
        if stop_after == "p3b4b":
            early_out(); off[0] = True
        op("vector", lambda e: e.tensor_add(out=kscr[:], in0=a1[:], in1=offgT[:]))
        op("vector", lambda e: e.tensor_scalar(out=kscr[:], in0=kscr[:], scalar1=-1.0,
                                               scalar2=float(CAPN), op0=ALU.mult, op1=ALU.add))
        op("vector", lambda e: e.tensor_scalar(out=kscr[:], in0=kscr[:], scalar1=0.0,
                                               scalar2=None, op0=ALU.max))
        t_kept = st("vector", "V", lambda e: e.tensor_tensor(out=keptT[:], in0=kscr[:],
                                                             in1=t16[:], op=ALU.min))
        if stop_after == "p3b4":
            early_out(); off[0] = True
        wt("tensor", "V", t_kept)
        t_mm8 = st("tensor", "T", lambda e: e.matmul(psumP[0:64, 1424:1440], CT["selc8"][:],
                                                     keptT[:], start=True, stop=True))
        wt("vector", "T", t_mm8)
        op("vector", lambda e: e.tensor_copy(out=vc64[:], in_=psumP[0:64, 1424:1440]))
        t_vcr = st("vector", "V", lambda e: e.tensor_reduce(
            out=vc64r[:].rearrange("p (a b) -> p a b", b=1),
            in_=vc64[:].rearrange("p (a b) -> p a b", a=1), axis=AX.X, op=ALU.add))
        wt("tensor", "V", t_vcr)
        op("tensor", lambda e: e.matmul(psumP[0:64, 1440:1441], CT["mask64x"][:],
                                        vc64r[:], start=True, stop=True))
        t_mm9 = st("tensor", "T", lambda e: e.matmul(psumP[0:64, 1441:1442], CT["mask64i"][:],
                                                     vc64r[:], start=True, stop=True))
        wt("vector", "T", t_mm9)
        op("vector", lambda e: e.tensor_copy(out=cum64[:], in_=psumP[0:64, 1440:1441]))
        t_c64 = st("vector", "V", lambda e: e.tensor_copy(out=inc64[:], in_=psumP[0:64, 1441:1442]))
        if stop_after == "p3b5":
            early_out(); off[0] = True
        t_dg1 = st("vector", "V", lambda e: e.tensor_scalar(out=scr64[0:64, :], in0=CT["i64"][:],
                                                            scalar1=cum64[:, 0:1], scalar2=None, op0=ALU.mult))
        wt("tensor", "V", t_dg1)
        t_mmA = st("tensor", "T", lambda e: e.matmul(psumP[:, 1442:1506], CT["ones64"][:],
                                                     scr64[0:64, :], start=True, stop=True))
        wt("vector", "T", t_mmA)
        t_cb = st("vector", "V", lambda e: e.tensor_copy(out=cumB[:], in_=psumP[:, 1442:1506]))
        t_dg2 = st("vector", "V", lambda e: e.tensor_scalar(out=scr64[0:64, :], in0=CT["i64"][:],
                                                            scalar1=inc64[:, 0:1], scalar2=None, op0=ALU.mult))
        wt("tensor", "V", t_dg2)
        t_mmB = st("tensor", "T", lambda e: e.matmul(psumP[:, 1536:1600], CT["ones64"][:],
                                                     scr64[0:64, :], start=True, stop=True))
        wt("vector", "T", t_mmB)
        op("vector", lambda e: e.tensor_copy(out=incB[:], in_=psumP[:, 1536:1600]))
        op("vector", lambda e: e.tensor_tensor(out=scr64[:], in0=cumB[:], in1=CT["memask"][:], op=ALU.mult))
        op("vector", lambda e: e.tensor_reduce(
            out=mycum[:].rearrange("p (c o) -> p c o", o=1),
            in_=scr64[:].rearrange("p (c e) -> p c e", e=8), axis=AX.X, op=ALU.add))
        op("vector", lambda e: e.tensor_tensor(out=scr64[:], in0=incB[:], in1=CT["memask"][:], op=ALU.mult))
        op("vector", lambda e: e.tensor_reduce(
            out=myinc[:].rearrange("p (c o) -> p c o", o=1),
            in_=scr64[:].rearrange("p (c e) -> p c e", e=8), axis=AX.X, op=ALU.add))
        op("vector", lambda e: e.tensor_sub(out=myvc[:], in0=myinc[:], in1=mycum[:]))
        op("vector", lambda e: e.tensor_tensor(out=scr64[:], in0=incB[:], in1=CT["memask_tot"][:], op=ALU.mult))
        op("vector", lambda e: e.tensor_reduce(
            out=mytot[:].rearrange("p (a b) -> p a b", b=1),
            in_=scr64[:].rearrange("p (a b) -> p a b", a=1), axis=AX.X, op=ALU.add))
        if stop_after == "p3c":
            early_out(); off[0] = True
        for e8 in range(8):
            op("vector", lambda e, i=e8: e.tensor_scalar(
                out=r3(oh1[:])[:, :, i:i + 1], in0=rcol(R["i1"][:]),
                scalar1=float(i), scalar2=None, op0=ALU.is_equal))
            op("vector", lambda e, i=e8: e.tensor_scalar(
                out=r3(oh2[:])[:, :, i:i + 1], in0=rcol(R["i2"][:]),
                scalar1=float(i), scalar2=None, op0=ALU.is_equal))
        t_ohs = st("vector", "V", lambda e: e.tensor_add(out=ohs[:], in0=oh1[:], in1=oh2[:]))
        wt("tensor", "V", t_ohs)
        t_mmC = st("tensor", "T", lambda e: e.matmul(psumP[:, 1024:1280], CT["ltri128"][:],
                                                     ohs[:], start=True, stop=True))
        wt("vector", "T", t_mmC)
        op("vector", lambda e: e.tensor_copy(out=ecsU[:], in_=psumP[:, 1024:1280]))
        op("vector", lambda e: e.tensor_tensor(out=selbuf[:], in0=oh1[:], in1=ecsU[:], op=ALU.mult))
        op("vector", lambda e: e.tensor_reduce(out=rcol(R["vr1"][:]), in_=r3(selbuf[:]),
                                               axis=AX.X, op=ALU.add))
        op("vector", lambda e: e.tensor_tensor(out=selbuf[:], in0=oh1[:], in1=ownoff[:], op=ALU.mult))
        op("vector", lambda e: e.tensor_reduce(out=rcol(R["t1"][:]), in_=r3(selbuf[:]),
                                               axis=AX.X, op=ALU.add))
        op("vector", lambda e: e.tensor_add(out=R["t1"][:], in0=R["t1"][:], in1=R["vr1"][:]))
        op("vector", lambda e: e.tensor_scalar(out=R["v1"][:], in0=R["t1"][:],
                                               scalar1=float(CAPN), scalar2=None, op0=ALU.is_lt))
        op("vector", lambda e: e.tensor_add(out=selbuf[:], in0=ecsU[:], in1=oh1[:]))
        op("vector", lambda e: e.tensor_tensor(out=selbuf[:], in0=oh2[:], in1=selbuf[:], op=ALU.mult))
        op("vector", lambda e: e.tensor_reduce(out=rcol(R["vr2"][:]), in_=r3(selbuf[:]),
                                               axis=AX.X, op=ALU.add))
        op("vector", lambda e: e.tensor_tensor(out=selbuf[:], in0=oh2[:], in1=ownoff[:], op=ALU.mult))
        op("vector", lambda e: e.tensor_reduce(out=rcol(R["t2"][:]), in_=r3(selbuf[:]),
                                               axis=AX.X, op=ALU.add))
        op("vector", lambda e: e.tensor_add(out=R["t2"][:], in0=R["t2"][:], in1=R["vr2"][:]))
        op("vector", lambda e: e.tensor_scalar(out=R["v2"][:], in0=R["t2"][:],
                                               scalar1=float(CAPN), scalar2=None, op0=ALU.is_lt))
        op("vector", lambda e: e.tensor_tensor(
            out=r3(ohv1[:]), in0=r3(oh1[:]),
            in1=rcol(R["v1"][:]).broadcast_to((128, NT, 8)), op=ALU.mult))
        op("vector", lambda e: e.tensor_tensor(
            out=r3(ohv2[:]), in0=r3(oh2[:]),
            in1=rcol(R["v2"][:]).broadcast_to((128, NT, 8)), op=ALU.mult))
        t_ohv = st("vector", "V", lambda e: e.tensor_add(out=ohs[:], in0=ohv1[:], in1=ohv2[:]))
        wt("tensor", "V", t_ohv)
        t_mmD = st("tensor", "T", lambda e: e.matmul(psumP[:, 1280:1536], CT["ltri128"][:],
                                                     ohs[:], start=True, stop=True))
        wt("vector", "T", t_mmD)
        op("vector", lambda e: e.tensor_copy(out=ecsV[:], in_=psumP[:, 1280:1536]))
        op("vector", lambda e: e.tensor_tensor(out=selbuf[:], in0=ohv1[:], in1=ecsV[:], op=ALU.mult))
        op("vector", lambda e: e.tensor_reduce(out=rcol(R["vr1"][:]), in_=r3(selbuf[:]),
                                               axis=AX.X, op=ALU.add))
        op("vector", lambda e: e.tensor_add(out=selbuf[:], in0=ecsV[:], in1=ohv1[:]))
        op("vector", lambda e: e.tensor_tensor(out=selbuf[:], in0=ohv2[:], in1=selbuf[:], op=ALU.mult))
        op("vector", lambda e: e.tensor_reduce(out=rcol(R["vr2"][:]), in_=r3(selbuf[:]),
                                               axis=AX.X, op=ALU.add))
        t_csz = st("vector", "V", lambda e: e.memset(csrowV[:], 0.0))
        wt("tensor", "V", max(t_ohv, t_csz))
        t_mmE = st("tensor", "T", lambda e: e.matmul(psumP[0:1, 1536:1792], CT["onescol"][:],
                                                     ohs[:], start=True, stop=True))
        wt("vector", "T", t_mmE)
        t_csr = st("vector", "V", lambda e: e.tensor_copy(out=csrowV[0:1, :], in_=psumP[0:1, 1536:1792]))
        wt("tensor", "V", t_csr)
        t_tr2 = None
        for half in range(2):
            fn = (lambda e, half=half: e.matmul(
                psumP[:, 1536 + half * 128:1536 + (half + 1) * 128],
                csrowV[:, half * 128:(half + 1) * 128], CT["ident_f32"][:],
                start=True, stop=True))
            if half == 1:
                t_tr2 = st("tensor", "T", fn)
            else:
                op("tensor", fn)
        wt("vector", "T", t_tr2)
        op("vector", lambda e: e.tensor_copy(out=a2t[:, 0:1], in_=psumP[:, 1536:1537]))
        t_a2t = st("vector", "V", lambda e: e.tensor_copy(out=a2t[:, 1:2], in_=psumP[:, 1664:1665]))
        wt("tensor", "V", t_a2t)
        op("tensor", lambda e: e.matmul(psumP[:, 1792:1794], CT["ltri16x8"][:], a2t[:],
                                        start=True, stop=True))
        t_mmF = st("tensor", "T", lambda e: e.matmul(psumP[0:32, 1794:1796], CT["mod8p"][:],
                                                     a2t[:], start=True, stop=True))
        wt("vector", "T", t_mmF)
        op("vector", lambda e: e.tensor_copy(out=a2x[:], in_=psumP[:, 1792:1794]))
        t_t0e = st("vector", "V", lambda e: e.tensor_copy(out=tot0e[0:8, :], in_=psumP[0:8, 1794:1796]))
        wt("tensor", "V", t_t0e)
        t_mmG = st("tensor", "T", lambda e: e.matmul(psumP[:, 1796:1797], CT["mod8tp"][:],
                                                     tot0e[:, 0:1], start=True, stop=True))
        wt("vector", "T", t_mmG)
        op("vector", lambda e: e.tensor_copy(out=tot0b[:], in_=psumP[:, 1796:1797]))
        op("vector", lambda e: e.tensor_add(out=a2x[:, 1:2], in0=a2x[:, 1:2], in1=tot0b[:]))
        t_rhs3 = st("vector", "V", lambda e: e.tensor_tensor(
            out=rhs3[:].rearrange("p (h a ee) -> p h a ee", a=16, ee=8),
            in0=a2x[:].rearrange("p (h a ee) -> p h a ee", a=1, ee=1).broadcast_to((128, 2, 16, 8)),
            in1=CT["m128pat"][:].rearrange("p (h a ee) -> p h a ee", a=16, ee=8), op=ALU.mult))
        wt("tensor", "V", t_rhs3)
        t_mmH = st("tensor", "T", lambda e: e.matmul(psumP[:, 1536:1792], CT["ones128sq"][:],
                                                     rhs3[:], start=True, stop=True))
        wt("vector", "T", t_mmH)
        op("vector", lambda e: e.tensor_copy(out=offs2B[:], in_=psumP[:, 1536:1792]))
        op("vector", lambda e: e.tensor_tensor(out=selbuf[:], in0=oh1[:], in1=offs2B[:], op=ALU.mult))
        op("vector", lambda e: e.tensor_reduce(out=rcol(R["t1"][:]), in_=r3(selbuf[:]),
                                               axis=AX.X, op=ALU.add))
        op("vector", lambda e: e.tensor_add(out=R["vr1"][:], in0=R["vr1"][:], in1=R["t1"][:]))
        op("vector", lambda e: e.tensor_tensor(out=selbuf[:], in0=oh2[:], in1=offs2B[:], op=ALU.mult))
        op("vector", lambda e: e.tensor_reduce(out=rcol(R["t2"][:]), in_=r3(selbuf[:]),
                                               axis=AX.X, op=ALU.add))
        op("vector", lambda e: e.tensor_add(out=R["vr2"][:], in0=R["vr2"][:], in1=R["t2"][:]))
        op("vector", lambda e: e.scalar_tensor_tensor(out=R["sx1"][:], in0=R["i1"][:],
                                                      scalar=float(CAP), in1=R["vr1"][:],
                                                      op0=ALU.mult, op1=ALU.add))
        op("vector", lambda e: e.tensor_scalar(out=R["t1"][:], in0=CT["dumoff"][:],
                                               scalar1=float(DUMMY), scalar2=None, op0=ALU.add))
        op("vector", lambda e: e.tensor_sub(out=R["sx1"][:], in0=R["sx1"][:], in1=R["t1"][:]))
        op("vector", lambda e: e.tensor_tensor(out=R["sx1"][:], in0=R["v1"][:], in1=R["sx1"][:], op=ALU.mult))
        op("vector", lambda e: e.tensor_add(out=R["sx1"][:], in0=R["sx1"][:], in1=R["t1"][:]))
        op("vector", lambda e: e.scalar_tensor_tensor(out=R["sx2"][:], in0=R["i2"][:],
                                                      scalar=float(CAP), in1=R["vr2"][:],
                                                      op0=ALU.mult, op1=ALU.add))
        op("vector", lambda e: e.tensor_scalar(out=R["t2"][:], in0=R["t1"][:],
                                               scalar1=4.0, scalar2=None, op0=ALU.add))
        op("vector", lambda e: e.tensor_sub(out=R["sx2"][:], in0=R["sx2"][:], in1=R["t2"][:]))
        op("vector", lambda e: e.tensor_tensor(out=R["sx2"][:], in0=R["v2"][:], in1=R["sx2"][:], op=ALU.mult))
        op("vector", lambda e: e.tensor_add(out=R["sx2"][:], in0=R["sx2"][:], in1=R["t2"][:]))
        op("vector", lambda e: e.tensor_tensor(out=R["wm1"][:], in0=R["w1"][:], in1=R["v1"][:], op=ALU.mult))
        op("vector", lambda e: e.tensor_tensor(out=R["wm2"][:], in0=R["w2"][:], in1=R["v2"][:], op=ALU.mult))
        op("vector", lambda e: e.memset(R["csum"][:], 0.0))
        op("vector", lambda e: e.memset(R["acc"][:], 0.0))
        for c_ in range(1, 8):
            op("vector", lambda e, c_=c_: e.tensor_scalar(
                out=R["t1"][:], in0=CT["siota"][:], scalar1=mycum[:, c_:c_ + 1],
                scalar2=None, op0=ALU.is_ge))
            op("vector", lambda e: e.tensor_add(out=R["csum"][:], in0=R["csum"][:], in1=R["t1"][:]))
            op("vector", lambda e, c_=c_: e.scalar_tensor_tensor(
                out=R["t2"][:], in0=R["t1"][:], scalar=myvc[:, c_ - 1:c_],
                in1=R["acc"][:], op0=ALU.mult, op1=ALU.add))
            op("vector", lambda e: e.tensor_copy(out=R["acc"][:], in_=R["t2"][:]))
        op("vector", lambda e: e.tensor_sub(out=R["t1"][:], in0=CT["siota"][:], in1=R["acc"][:]))
        op("vector", lambda e: e.scalar_tensor_tensor(out=R["slotv"][:], in0=R["csum"][:],
                                                      scalar=float(CAP), in1=R["t1"][:],
                                                      op0=ALU.mult, op1=ALU.add))
        op("vector", lambda e: e.tensor_scalar(out=R["t1"][:], in0=CT["siota"][:],
                                               scalar1=mytot[:, 0:1], scalar2=None, op0=ALU.is_ge))
        op("vector", lambda e: e.tensor_scalar(out=R["t2"][:], in0=R["slotv"][:],
                                               scalar1=-1.0, scalar2=float(DUMMY),
                                               op0=ALU.mult, op1=ALU.add))
        op("vector", lambda e: e.tensor_tensor(out=R["t2"][:], in0=R["t1"][:], in1=R["t2"][:], op=ALU.mult))
        t_slotv = st("vector", "V", lambda e: e.tensor_add(out=R["slotv"][:], in0=R["slotv"][:], in1=R["t2"][:]))

        if stop_after == "p3d":
            early_out(); off[0] = True
        def fold(src, dst, tag):
            for g in range(8):
                pslot = 1024 + (g % 2) * 64
                t_fm_ = st("tensor", "T", lambda e, g=g, src=src, pslot=pslot: e.matmul(
                    psumP[:, pslot:pslot + 32], CT[f"foldsel{g}"][:], src[:],
                    start=True, stop=True))
                wt("vector", "T", t_fm_)
                t_fc = st("vector", "V", lambda e, g=g, dst=dst, pslot=pslot: e.tensor_copy(
                    out=dst[:].rearrange("p (mm gg) -> p mm gg", gg=8)[:, :, g:g + 1],
                    in_=psumP[:, pslot:pslot + 32].rearrange("p (a b) -> p a b", b=1)))
                if g >= 1:
                    wt("tensor", "V", t_fc)
            return cnt["V"]

        wt("tensor", "V", t_slotv)
        fold(R["sx1"], Lk0, "k0")
        fold(R["sx2"], Lk1, "k1")
        t_folds = fold(R["slotv"], Lslot, "sl")

        wt("tensor", "V", t_folds)
        wt("sync", "V", t_folds)
        t_y = [0] * NT
        t_ev2 = [0] * NT
        t_zf = [0] * NT

        t_pstl = [0] * NT

        def stageB(m):
            fm = fmt[m % 2]
            ptr = p_trbs[m % 2]
            pz = p_zs[m % 2]
            if m >= 2:
                wt("tensor", "V", t_ev2[m - 2])
            t_tr = None
            for j in range(8):
                fn = (lambda e, m=m, j=j, ptr=ptr: e.transpose(
                    ptr[:, j * 128:(j + 1) * 128],
                    xn_tm[:, m * D + j * 128:m * D + (j + 1) * 128],
                    CT["ident_bf"][:]))
                if j == 7:
                    t_tr = st("tensor", "T", fn)
                else:
                    op("tensor", fn)
            wt("vector", "T", t_tr)
            t_ev2[m] = st("vector", "V", lambda e, fm=fm, ptr=ptr: e.tensor_copy(out=fm, in_=ptr[:]))
            wt("tensor", "V", t_ev2[m])
            if m >= 2:
                wt("tensor", "A", t_zf[m - 2])
            t_zmm = None
            for o in range(2):
                for j in range(8):
                    op("tensor", lambda e, fm=fm, o=o, j=j, pz=pz: e.matmul(
                        pz[:, o * 512:(o + 1) * 512], fm[:, j * 128:(j + 1) * 128],
                        w1_sb[:, j * D + o * 512:j * D + (o + 1) * 512],
                        start=(j == 0), stop=False))
                fnb = (lambda e, o=o, pz=pz: e.matmul(
                    pz[:, o * 512:(o + 1) * 512], CT["ones1_bf"][:],
                    b1_sb[:, o * 512:(o + 1) * 512], start=False, stop=True))
                if o == 1:
                    t_zmm = st("tensor", "T", fnb)
                else:
                    op("tensor", fnb)
            wt("scalar", "T", t_zmm)
            if m >= 1:
                wt("scalar", f"sP{(m - 1) % 2}", t_pstl[m - 1])
            op("scalar", lambda e, pz=pz: e.activation(out=s1[:], in_=pz, func=AF.Exp))
            t_zf[m] = st("scalar", "A", lambda e: e.activation(out=s2, in_=s1[:], func=AF.Ln, bias=1.0))
            wt("vector", "A", t_zf[m])
            t_y[m] = st("vector", "V", lambda e, m=m: e.scalar_tensor_tensor(
                out=s0[:], in0=xn_tm[:, m * D:(m + 1) * D], scalar=svst[:, m:m + 1],
                in1=s2, op0=ALU.mult, op1=ALU.mult))
            wt("scalar", "V", t_y[m])
            t_cpy = st("scalar", "A", lambda e: e.activation(out=y_bf, in_=s0[:], func=AF.Copy))
            wt("vector", "A", t_cpy)
            wt("sync", "A", t_cpy)
            kp = f"sP{m % 2}"
            t_pstl[m] = dma("sync", kp, partial.ap()[m], y_bf, pre_wait=True)

        for m in range(NT):
            stageB(m)
        t_zB_last = cnt["T"]

        if debug:
            wt("gpsimd", "V", t_folds)
            dma("gpsimd", "sDBG", dbg["d_r1"], pk[:])
            for i, k in enumerate(["v1", "v2", "vr1", "vr2"]):
                op("vector", lambda e, i=i, k=k: e.tensor_copy(out=pk[:, i * 32:(i + 1) * 32], in_=R[k][:]))
            t_dbg2 = st("vector", "V", lambda e: e.tensor_copy(out=ex8[:, 0:64], in_=cumB[:]))
            wt("gpsimd", "V", t_dbg2)
            dma("gpsimd", "sDBG", dbg["d_r2"], pk[:])
            dma("gpsimd", "sDBG", dbg["d_r3"][:, 0:64], ex8[0:128, 0:64])
            for i, k in enumerate(["sx1", "sx2", "slotv"]):
                op("vector", lambda e, i=i, k=k: e.tensor_copy(out=tmp8[:, i * 32:(i + 1) * 32], in_=R[k][:]))
            t_dbg4 = st("vector", "V", lambda e: e.tensor_copy(out=tmp8[:, 95:96], in_=mytot[:]))
            wt("gpsimd", "V", t_dbg4)
            dma("gpsimd", "sDBG", dbg["d_r4"], tmp8[:, 0:96])

        class _Stop(Exception):
            pass

        try:
            if stop_after == "p1":
                early_out()
                raise _Stop()
            wt("gpsimd", "V", t_folds)
            wt("gpsimd", "V", t_xn_last)
            wt("gpsimd", "sZ", t_zero)
            t_sc1 = None
            for kk, L in ((0, Lk0), (1, Lk1)):
                for jj in range(4):
                    t_sc1 = bump("sSC", 16)
                    emit("gpsimd", lambda e, L=L, jj=jj: e.dma_scatter_add(
                        out_ap=send1.ap(),
                        in_ap=xn_tm[:, jj * 8192:(jj + 1) * 8192].rearrange("p (m d) -> p m d", d=D),
                        idxs_ap=L[:, jj * 64:(jj + 1) * 64], num_idxs=1024,
                        num_idxs_reg=1024, elem_size=D).then_inc(sem["sSC"], 16))
            wt("gpsimd", "sSC", t_sc1)
            t_a2a1 = st("gpsimd", "CC", lambda e: e.collective_compute(
                "AllToAll", ALU.bypass, replica_groups=[list(range(NCORE))],
                ins=[send1.ap()[0:DUMMY, :].opt()], outs=[recv1.ap()[0:DUMMY, :].opt()]))
            wt("gpsimd", "T", t_zB_last)
            for dc in range(8):
                dma("gpsimd", "sWF", cfc_sb[:, dc * DFF:(dc + 1) * DFF], cfc[dc * 128:(dc + 1) * 128, :])
            for j in range(12):
                t_wffn = dma("gpsimd", "sWF", cpr_sb[:, j * D:(j + 1) * D], cpr[j * 128:(j + 1) * 128, :])
            wt("gpsimd", "CC", t_a2a1)

            exph = [xn_tm[:, hh_ * 4096:(hh_ + 1) * 4096].rearrange("p (b t) -> p b t", b=8)
                    for hh_ in range(8)]
            t_gath = [0] * 8
            wt("gpsimd", "V", t_y[NT - 1])
            for h in range(8):
                kg = f"sEG{h % 2}"
                if cnt[kg] > 0:
                    wt("gpsimd", kg, cnt[kg])
                t_gath[h] = bump(kg, 16)
                emit("gpsimd", lambda e, h=h, kg=kg: e.dma_gather(
                    out_ap=exph[h], in_ap=recv1.ap(), idxs_ap=Lslot[:, h * 32:(h + 1) * 32],
                    num_idxs=512, num_idxs_reg=512, elem_size=D,
                    transpose=True).then_inc(sem[kg], 16))
            wt("tensor", "sWF", t_wffn)
            wt("tensor", "A", t_zf[NT - 1])
            if stop_after == "a2a1":
                wt("gpsimd", "sEG0", cnt["sEG0"])
                wt("gpsimd", "sEG1", cnt["sEG1"])
                early_out()
                raise _Stop()

            wt("vector", f"sP{(NT - 1) % 2}", t_pstl[NT - 1])
            wt("vector", f"sP{(NT - 2) % 2}", t_pstl[NT - 2])
            t_expo2 = {}
            pfree = {0: 0, 1: 0, 2: 0}
            ex_bufs = [
                [expo[:, 0:2048], expo[:, 2048:4096]],
                [s0[:].bitcast(BF16), s1[:].bitcast(BF16)],
            ]
            t_lastmult = None
            for s_ in range(8):
                wt("tensor", f"sEG{s_ % 2}", t_gath[s_])
                bufs = ex_bufs[s_ % 2]
                if s_ >= 2:
                    for h in range(2):
                        wt("vector", f"sEO{(s_ % 2) * 2 + h}", t_expo2[(s_ - 2, h)])
                t_mult = [0] * 12
                for j in range(12):
                    pa = psumP[:, (j % 2) * 1024:(j % 2) * 1024 + 512]
                    pg = psumP[:, (j % 2) * 1024 + 512:(j % 2) * 1024 + 1024]
                    if pfree[j % 2]:
                        wt("tensor", "V", pfree[j % 2])
                    t_mm_a = None
                    for dc in range(8):
                        fn = (lambda e, j=j, dc=dc, s_=s_, pa=pa: e.matmul(
                            pa, cfc_sb[:, dc * DFF + j * 128:dc * DFF + (j + 1) * 128],
                            exph[s_][:, dc, :],
                            start=(dc == 0), stop=(dc == 7)))
                        if dc == 7:
                            t_mm_a = st("tensor", "T", fn)
                        else:
                            op("tensor", fn)
                    t_mm_g = None
                    for dc in range(8):
                        fn = (lambda e, j=j, dc=dc, s_=s_, pg=pg: e.matmul(
                            pg, cfc_sb[:, dc * DFF + (j + 12) * 128:dc * DFF + (j + 13) * 128],
                            exph[s_][:, dc, :],
                            start=(dc == 0), stop=(dc == 7)))
                        if dc == 7:
                            t_mm_g = st("tensor", "T", fn)
                        else:
                            op("tensor", fn)
                    wt("scalar", "T", t_mm_a)
                    if t_lastmult:
                        wt("scalar", "V", t_lastmult)
                    t_silu = st("scalar", "A", lambda e, pa=pa: e.activation(out=sa, in_=pa, func=AF.Silu))
                    wt("vector", "A", t_silu)
                    wt("vector", "T", t_mm_g)
                    t_lastmult = t_mult[j] = st("vector", "V", lambda e, j=j, pg=pg: e.tensor_tensor(
                        out=hh[:, j * 512:(j + 1) * 512], in0=sa, in1=pg, op=ALU.mult))
                    pfree[j % 2] = t_mult[j]
                t_ev = [0] * 4
                for sub in range(4):
                    reg = 2 if sub % 2 == 0 else 1
                    pz2 = p_z if sub % 2 == 0 else psumP[:, 1024:2048]
                    if pfree[reg]:
                        wt("tensor", "V", pfree[reg])
                    if sub == 0:
                        wt("tensor", "V", t_mult[11])
                    t_o = None
                    for nh in range(2):
                        for j in range(12):
                            fn = (lambda e, sub=sub, nh=nh, j=j, pz2=pz2: e.matmul(
                                pz2[:, nh * 512:(nh + 1) * 512],
                                hh[:, j * 512 + sub * 128:j * 512 + (sub + 1) * 128],
                                cpr_sb[:, j * D + nh * 512:j * D + (nh + 1) * 512],
                                start=(j == 0), stop=(j == 11)))
                            if nh == 1 and j == 11:
                                t_o = st("tensor", "T", fn)
                            else:
                                op("tensor", fn)
                    wt("vector", "T", t_o)
                    t_ev[sub] = st("vector", "V", lambda e, sub=sub, bufs=bufs, pz2=pz2: e.tensor_copy(
                        out=bufs[sub // 2][:, (sub % 2) * 1024:(sub % 2 + 1) * 1024], in_=pz2[:]))
                    pfree[reg] = t_ev[sub]
                for h in range(2):
                    keo = f"sEO{(s_ % 2) * 2 + h}"
                    wt("gpsimd", "V", t_ev[2 * h + 1])
                    if cnt[keo] > 0:
                        wt("gpsimd", keo, cnt[keo])
                    t_expo2[(s_, h)] = bump(keo, 16)
                    emit("gpsimd", lambda e, s_=s_, h=h, bufs=bufs, keo=keo: e.dma_scatter_add(
                        out_ap=send2.ap(),
                        in_ap=bufs[h].rearrange("p (a d) -> p a d", d=D),
                        idxs_ap=Lslot[:, s_ * 32 + h * 16:s_ * 32 + (h + 1) * 16],
                        num_idxs=256, num_idxs_reg=256, elem_size=D).then_inc(sem[keo], 16))

            for q_ in range(4):
                if cnt[f"sEO{q_}"] > 0:
                    wt("gpsimd", f"sEO{q_}", cnt[f"sEO{q_}"])
            t_a2a2 = st("gpsimd", "CC", lambda e: e.collective_compute(
                "AllToAll", ALU.bypass, replica_groups=[list(range(NCORE))],
                ins=[send2.ap()[0:DUMMY, :].opt()], outs=[recv2.ap()[0:DUMMY, :].opt()]))
            wt("gpsimd", "CC", t_a2a2)
            if stop_after == "ffn":
                early_out()
                raise _Stop()
            gb0 = cfc_sb[:, 0:8192].rearrange("p (b d) -> p b d", d=D)
            gb1 = cfc_sb[:, 8192:16384].rearrange("p (b d) -> p b d", d=D)
            wt("sync", "A", cnt["A"])
            wt("sync", "V", cnt["V"])
            wt("vector", "sP0", cnt["sP0"])
            wt("vector", "sP1", cnt["sP1"])
            t_out = None
            for cj in range(4):
                if cnt["sOG0"] > 0:
                    wt("gpsimd", "sOG0", cnt["sOG0"])
                    wt("gpsimd", "sOG1", cnt["sOG1"])
                tg0 = bump("sOG0", 16)
                emit("gpsimd", lambda e, cj=cj: e.dma_gather(
                    out_ap=gb0, in_ap=recv2.ap(), idxs_ap=Lk0[:, cj * 64:(cj + 1) * 64],
                    num_idxs=1024, num_idxs_reg=1024, elem_size=D).then_inc(sem["sOG0"], 16))
                tg1 = bump("sOG1", 16)
                emit("gpsimd", lambda e, cj=cj: e.dma_gather(
                    out_ap=gb1, in_ap=recv2.ap(), idxs_ap=Lk1[:, cj * 64:(cj + 1) * 64],
                    num_idxs=1024, num_idxs_reg=1024, elem_size=D).then_inc(sem["sOG1"], 16))
                wt("vector", "sOG0", tg0)
                wt("vector", "sOG1", tg1)
                for b in range(8):
                    mloc = cj * 8 + b
                    pb = pt8[b % 2]
                    yb = yt8[b % 2]
                    ob = ot8[b % 2]
                    if cj > 0 or b >= 2:
                        wt("sync", "V", cnt["V"])
                    kpl = f"sPL{b % 2}"
                    t_pl = dma("sync", kpl, pb[:], xsh[mloc], pre_wait=True)
                    kyl = f"sX{b % 2}"
                    t_yl = dma("sync", kyl, yb, partial.ap()[mloc], pre_wait=True)
                    wt("vector", kpl, t_pl)
                    wt("vector", kyl, t_yl)
                    op("vector", lambda e, b=b, mloc=mloc, pb=pb, ob=ob: e.scalar_tensor_tensor(
                        out=ob[:], in0=gb0[:, b, :], scalar=R["wm1"][:, mloc:mloc + 1],
                        in1=pb[:], op0=ALU.mult, op1=ALU.add))
                    op("vector", lambda e, b=b, mloc=mloc, ob=ob: e.scalar_tensor_tensor(
                        out=ob[:], in0=gb1[:, b, :], scalar=R["wm2"][:, mloc:mloc + 1],
                        in1=ob[:], op0=ALU.mult, op1=ALU.add))
                    t_cmb = st("vector", "V", lambda e, yb=yb, ob=ob: e.tensor_tensor(
                        out=ob[:], in0=yb, in1=ob[:], op=ALU.add))
                    wt("sync", "V", t_cmb)
                    kst = f"sST{b % 2}"
                    t_out = dma("sync", kst, out[mloc], ob[:], pre_wait=True)
                    if not (cj == 3 and b == 7):
                        wt("vector", kst, t_out)
                if cj < 3:
                    wt("gpsimd", "V", cnt["V"])
            wt("sync", "sST0", cnt["sST0"])
            wt("sync", "sST1", cnt["sST1"])
            if debug:
                wt("sync", "sDBG", cnt["sDBG"])

        except _Stop:
            pass

        with allow_ici_all_to_all(), nc.Block() as block:
            @block.sync
            def _(en):
                for f in plans["sync"]:
                    f(en)

            @block.scalar
            def _(en):
                for f in plans["scalar"]:
                    f(en)

            @block.vector
            def _(en):
                for f in plans["vector"]:
                    f(en)

            @block.tensor
            def _(en):
                for f in plans["tensor"]:
                    f(en)

            @block.gpsimd
            def _(en):
                for f in plans["gpsimd"]:
                    f(en)

    lower_extended_insts(nc)
    return nc


def prepare_in_maps(inputs):
    x = np.ascontiguousarray(np.asarray(inputs["L0_attn_x"], np.float32)).reshape(NCOL, 128, D)
    W1 = np.ascontiguousarray(np.asarray(inputs["L0_attn_W1"], np.float32))
    b1 = np.asarray(inputs["L0_attn_b1"], np.float32).reshape(1, D)
    W2 = np.asarray(inputs["L0_attn_W2"], np.float32)
    b2 = np.asarray(inputs["L0_attn_b2"], np.float32)
    W3 = np.asarray(inputs["L0_attn_W3"], np.float32)
    b3 = np.asarray(inputs["L0_attn_b3"], np.float32)
    wg = np.asarray(inputs["L0_ffn_w_gate"], np.float32)
    cfc = np.asarray(inputs["L0_ffn_c_fc"], np.float32)
    cpr = np.asarray(inputs["L0_ffn_c_proj"], np.float32)
    Wbcg = np.ascontiguousarray(np.concatenate([W2, W3, wg], axis=1))
    brow = np.ascontiguousarray(np.concatenate([b2, b3, np.zeros(8, np.float32)])).reshape(1, 40)

    in_maps = []
    for c in range(NCORE):
        m = {
            "xsh": np.ascontiguousarray(x[c::8]),
            "W1": W1, "Wbcg": Wbcg, "brow": brow, "b1row": b1,
            "cfc": np.ascontiguousarray(cfc[c]),
            "cpr": np.ascontiguousarray(cpr[c]),
        }
        for k, v in host_constants(c).items():
            m[k] = np.ascontiguousarray(v)
        in_maps.append(m)
    return in_maps


def kernel(**inputs):
    in_maps = prepare_in_maps(inputs)
    nc = build_nc(DEBUG)
    res = run_bass_kernel_spmd(nc, in_maps, core_ids=list(range(NCORE)))
    full = np.zeros((NCOL, 128, D), np.float32)
    for c in range(NCORE):
        full[c::8] = res.results[c]["out"]
    if DEBUG:
        kernel._debug = [{k: v for k, v in r.items() if k.startswith("d_")} for r in res.results]
    return full.reshape(8, 4096, 1024)


# version 10
# speedup vs baseline: 1.2494x; 1.0334x over previous
import numpy as np
import ml_dtypes
import concourse.bass as bass
import concourse.mybir as mybir
from concourse.bass_utils import run_bass_kernel_spmd
from concourse import library_config
from concourse.aot_env import allow_ici_all_to_all
from concourse.library_overlay import lower_extended_insts
from contextlib import ExitStack

F32 = mybir.dt.float32
BF16 = mybir.dt.bfloat16
I16 = mybir.dt.int16
U8 = mybir.dt.uint8
AF = mybir.ActivationFunctionType
ALU = mybir.AluOpType
AX = mybir.AxisListType

NCORE = 8
NT = 32
NCOL = 256
D = 1024
NEXP = 8
CAPN = 4096
CAP = 768
DFF = 3072
DH = DFF // 2
DUMMY = NEXP * CAP

DEBUG = False
STOP_AFTER = "full"
BFNP = ml_dtypes.bfloat16


def host_constants(core):
    c = core
    H = {}
    H["ltri128"] = np.triu(np.ones((128, 128), np.float32), 1)
    q = np.arange(128)
    H["ltri16x8"] = ((q[:, None] % 8 == q[None, :] % 8)
                     & (q[:, None] // 8 < q[None, :] // 8)).astype(np.float32)
    H["ident_bf"] = np.eye(128, dtype=np.float32).astype(BFNP)
    H["ident_f32"] = np.eye(128, dtype=np.float32)
    H["onescol"] = np.ones((128, 1), np.float32)
    H["ones1_bf"] = np.ones((1, 128), np.float32).astype(BFNP)
    o16p = np.zeros((32, 128), np.float32); o16p[0:16, :] = 1.0
    H["ones16p"] = o16p
    H["ones64"] = np.ones((64, 128), np.float32)
    H["ones128sq"] = np.ones((128, 128), np.float32)
    so = np.zeros((128, 32), np.float32)
    for half in range(2):
        for e in range(8):
            so[((c + 8 * half) % 16) * 8 + e, half * 8 + e] = 1.0
    H["selown16"] = so
    m16 = np.zeros((16, 256), np.float32)
    for mloc in range(32):
        for e in range(8):
            m16[(mloc % 2) * 8 + e, mloc * 8 + e] = 1.0
    H["m16pat"] = m16
    lt16 = np.zeros((32, 32), np.float32)
    lt16[:16, :16] = np.triu(np.ones((16, 16), np.float32), 1)
    H["ltri16p"] = lt16
    m16b = np.zeros((16, 256), np.float32)
    for mloc in range(32):
        for e in range(8):
            m16b[mloc // 2, mloc * 8 + e] = 1.0
    H["m16bpat"] = m16b
    m8t = (np.arange(128)[None, :] % 8 == np.arange(8)[:, None]).astype(np.float32)
    H["mod8tp"] = np.concatenate([m8t, np.zeros((24, 128), np.float32)], 0)
    H["mod8p"] = np.concatenate([m8t.T, np.zeros((128, 24), np.float32)], 1)
    sc8 = np.zeros((128, 64), np.float32)
    for qq in range(128):
        sc8[qq, ((qq // 8) % 8) * 8 + (qq % 8)] = 1.0
    H["selc8"] = sc8
    cm = np.arange(64)
    ce_c, ce_e = cm // 8, cm % 8
    H["mask64x"] = ((ce_e[:, None] == ce_e[None, :]) & (ce_c[:, None] < ce_c[None, :])).astype(np.float32)
    H["mask64i"] = ((ce_e[:, None] == ce_e[None, :]) & (ce_c[:, None] <= ce_c[None, :])).astype(np.float32)
    H["i64"] = np.eye(64, dtype=np.float32)
    H["memask"] = np.tile((np.arange(64) % 8 == c).astype(np.float32)[None, :], (128, 1))
    H["memask_tot"] = np.tile((np.arange(64) == 56 + c).astype(np.float32)[None, :], (128, 1))
    mp = np.zeros((128, 256), np.float32)
    for mloc in range(32):
        for e in range(8):
            mp[(mloc % 16) * 8 + e, mloc * 8 + e] = 1.0
    H["m128pat"] = mp
    for g in range(8):
        fs = np.zeros((128, 128), np.float32)
        for mrow in range(128):
            fs[16 * g + (mrow % 16), mrow] = 1.0
        H[f"foldsel{g}"] = fs
    H["siota"] = (np.arange(32)[None, :] * 128 + np.arange(128)[:, None]).astype(np.float32)
    H["dumoff"] = np.tile((np.arange(32) // 8).astype(np.float32)[None, :], (128, 1))
    u32 = np.zeros((32, 16), np.float32)
    u32[:16, :] = (np.arange(16)[:, None] < np.arange(16)[None, :]).astype(np.float32)
    H["ultri32"] = u32
    return H




def build_nc(debug=DEBUG, stop_after=None):
    stop_after = stop_after or STOP_AFTER
    nc = bass.Bass("TRN2", target_bir_lowering=False)
    AP_IN = {}

    def din(name, shape, dtype=F32):
        AP_IN[name] = nc.dram_tensor(name, list(shape), dtype, kind="ExternalInput").ap()
        return AP_IN[name]

    xsh = din("xsh", (NT, 128, D))
    W1 = din("W1", (D, D))
    Wbcg = din("Wbcg", (D, 40))
    brow = din("brow", (1, 40))
    b1row = din("b1row", (1, D))
    cfc = din("cfc", (D, DFF))
    cpr = din("cpr", (DH, D))
    consts = host_constants(0)
    for k, v in consts.items():
        din(k, v.shape, BF16 if v.dtype == BFNP else F32)

    partial = nc.dram_tensor("partial", [NT, 128, D], BF16, kind="Internal")
    send1 = nc.dram_tensor("send1", [DUMMY + 8, D], BF16, kind="Internal")
    recv1 = nc.dram_tensor("recv1", [DUMMY + 8, D], BF16, kind="Internal")
    send2 = nc.dram_tensor("send2", [DUMMY + 8, D], BF16, kind="Internal")
    recv2 = nc.dram_tensor("recv2", [DUMMY + 8, D], BF16, kind="Internal")
    ag_in = nc.dram_tensor("ag_in", [128, 128], F32, kind="Internal")
    ag_out = nc.dram_tensor("ag_out", [NCORE * 128, 128], F32, kind="Internal")

    out = nc.dram_tensor("out", [NT, 128, D], F32, kind="ExternalOutput").ap()
    dbg = {}
    if debug:
        for nm, sh in [("d_r1", (128, 128)), ("d_r2", (128, 128)),
                       ("d_r3", (128, 128)), ("d_r4", (128, 96))]:
            dbg[nm] = nc.dram_tensor(nm, list(sh), F32, kind="ExternalOutput").ap()

    with ExitStack() as ctx:
        e_ = ctx.enter_context

        def sb(name, shape, dtype):
            return e_(nc.sbuf_tensor(name, list(shape), dtype))

        CT = {k: sb("c_" + k, v.shape, BF16 if v.dtype == BFNP else F32)
              for k, v in consts.items()}
        xt = [sb("xt0", (128, D), F32), sb("xt1", (128, D), F32)]
        sa = None
        s0 = sb("s0", (128, D), F32)
        s1 = sb("s1", (128, D), F32)
        u1 = sb("u1", (128, 4096), BF16)
        s2 = u1[:, 0:2048].bitcast(F32)
        fmt = [u1[:, 2048:3072], u1[:, 3072:4096]]
        lnst = sb("lnst", (128, 16), F32)
        mnst = sb("mnst", (128, NT), F32)
        rsst = sb("rsst", (128, NT), F32)
        svst = sb("svst", (128, NT), F32)
        bc16 = sb("bc16", (128, 16), F32)
        bc40 = sb("bc40", (128, 40), F32)
        xn_tm = sb("xn_tm", (128, NT * D), BF16)
        wbcg_sb = sb("wbcg_sb", (128, 8 * 40), BF16)
        brow_sb = sb("brow_sb", (1, 40), BF16)
        b1_sb = sb("b1_sb", (1, D), BF16)
        logits = sb("logits", (128, NT * 8), F32)
        R = {k: sb("r_" + k, (128, NT), F32) for k in
             ["i1", "i2", "w1", "w2", "m1", "m2", "den", "v1", "v2",
              "vr1", "vr2", "sx1", "sx2", "wm1", "wm2", "slotv", "csum",
              "acc", "t1", "t2"]}
        tmp8 = sb("tmp8", (128, NT * 8), F32)
        ex8 = sb("ex8", (128, NT * 8), F32)
        cbig = sb("cbig", (128, NT * 8), F32)
        eio8 = sb("eio8", (128, NT * 8), F32)
        pk = sb("pk", (128, 128), F32)
        idx1g = sb("idx1g", (128, NCOL), F32)
        idx2g = sb("idx2g", (128, NCOL), F32)

        t16 = sb("t16", (128, 16), F32)
        a1 = sb("a1", (128, 16), F32)
        g16 = sb("g16", (128, 32), F32)
        g16t = sb("g16t", (32, 16), F32)
        g16x = sb("g16x", (128, 32), F32)
        g16xT = sb("g16xT", (32, 16), F32)
        o16 = sb("o16", (16, 16), F32)
        rhsA = sb("rhsA", (32, 256), F32)
        rhsB = sb("rhsB", (32, 256), F32)
        ownoff = sb("ownoff", (128, 256), F32)
        offgT = sb("offgT", (128, 16), F32)
        keptT = sb("keptT", (128, 16), F32)
        kscr = sb("kscr", (128, 16), F32)
        vc64 = sb("vc64", (64, 16), F32)
        vc64r = sb("vc64r", (64, 1), F32)
        cum64 = sb("cum64", (64, 1), F32)
        inc64 = sb("inc64", (64, 1), F32)

        cumB = pk[:, 0:64]
        incB = pk[:, 64:128]
        scr64 = sb("scr64", (128, 64), F32)
        mycum = sb("mycum", (128, 8), F32)
        myinc = sb("myinc", (128, 8), F32)
        myvc = sb("myvc", (128, 8), F32)
        mytot = sb("mytot", (128, 1), F32)
        oh1 = sb("oh1", (128, NT * 8), F32)
        oh2 = idx2g
        ohs = sb("ohs", (128, NT * 8), F32)
        ecsU = logits
        ecsV = eio8
        ohv1 = tmp8
        ohv2 = ex8
        selbuf = cbig
        csrowV = idx1g
        a2t = sb("a2t", (128, 2), F32)
        a2x = sb("a2x", (128, 2), F32)
        tot0e = sb("tot0e", (32, 2), F32)
        tot0b = sb("tot0b", (128, 1), F32)
        rhs3 = ownoff
        offs2B = sb("offs2B", (128, 256), F32)
        Lk0 = sb("Lk0", (128, 256), I16)
        Lk1 = sb("Lk1", (128, 256), I16)
        Lslot = sb("Lslot", (128, 256), I16)
        cfc_sb = sb("cfc_sb", (128, 8 * DFF), BF16)
        w1_sb = cfc_sb.tensor if hasattr(cfc_sb, "tensor") else cfc_sb
        w1_sb = cfc_sb
        cpr_sb = sb("cpr_sb", (128, 12 * D), BF16)
        G = cpr_sb[:, 0:4096].bitcast(F32)
        srow = cpr_sb[:, 4096:8192].bitcast(F32)
        hh = sb("hh", (128, 12 * 512), BF16)
        expo = u1
        sa = xt[0][:, 0:512]

        zeros_bf = hh
        pt8 = xt
        ot8 = [s0, s1]
        y_bf = s1[:, 0:512].bitcast(BF16)
        yt8 = [hh[:, 0:1024], hh[:, 1024:2048]]

        psumP = e_(nc.psum_tensor("psumP", [128, 2048], F32))
        p_z = e_(nc.psum_tensor("p_z", [128, D], F32))
        p_bc = e_(nc.psum_tensor("p_bc", [128, 64], F32))
        p_tr = psumP[:, 0:1024]
        p_trbs = [psumP[:, 0:512].bitcast(BF16), psumP[:, 512:1024].bitcast(BF16)]
        p_zs = [p_z[:, 0:1024], psumP[:, 1024:2048]]
        p_bcs = [p_bc[:, 0:40], p_bc[:, 0:40]]
        p_row = psumP[0:1, 0:2048]

        sem = {nm: e_(nc.semaphore(nm)) for nm in
               ["sC", "sX0", "sX1", "sP0", "sP1", "sW1", "sWF", "sAG", "sSC",
                "sEG0", "sEG1", "sEG2", "sEG3", "sEO0", "sEO1", "sEO2", "sEO3",
                "sOG0", "sOG1", "sOG2", "sOG3",
                "sPL0", "sPL1", "sST0", "sST1", "sZ", "sDBG", "T", "V", "A", "CC"]}
        cnt = {k: 0 for k in sem}
        plans = {k: [] for k in ["sync", "scalar", "vector", "tensor", "gpsimd"]}
        off = [False]

        def emit(eng, fn):
            if off[0]:
                return
            plans[eng].append(fn)

        def bump(key, n=1):
            cnt[key] += n
            return cnt[key]

        def wt(eng, key, val):
            emit(eng, lambda e, k=key, v=val: e.wait_ge(sem[k], v))

        def dma(eng, key, out_ap, in_ap, pre_wait=False):
            pre = cnt[key]
            if pre_wait and pre > 0:
                wt(eng, key, pre)
            tgt = bump(key, 16)
            emit(eng, lambda e, o=out_ap, i=in_ap, k=key: e.dma_start(out=o, in_=i).then_inc(sem[k], 16))
            return tgt

        def st(eng, key, fn):
            if eng in ("vector", "scalar"):
                return op(eng, fn)
            tgt = bump(key, 1)
            emit(eng, lambda e, f=fn, k=key: f(e).then_inc(sem[k], 1))
            return tgt

        def op(eng, fn):
            if eng in ("vector", "scalar"):
                key = "V" if eng == "vector" else "A"
                pre = cnt[key]
                tgt = bump(key, 1)

                def run(e, f=fn, pre=pre, key=key):
                    ins = f(e).then_inc(sem[key], 1)
                    if pre > 0:
                        ins._wait_ge(sem[key], pre)
                emit(eng, run)
                return tgt
            emit(eng, lambda e, f=fn: f(e))
            return None

        def r3(ap, inner=8):
            return ap.rearrange("p (c e) -> p c e", e=inner)

        def rcol(ap):
            return ap.rearrange("p (c o) -> p c o", o=1)

        for k in consts:
            dma("sync", "sC", CT[k][:], AP_IN[k])
        t_const = cnt["sC"]
        op("gpsimd", lambda e: e.load_library(library_config.mlp))
        for j in range(8):
            dma("gpsimd", "sW1", w1_sb[:, j * D:(j + 1) * D], W1[j * 128:(j + 1) * 128, :])
        dma("gpsimd", "sW1", wbcg_sb[:].rearrange("p (a b) -> p a b", a=8),
            Wbcg.rearrange("(a p) b -> p a b", p=128))
        dma("gpsimd", "sW1", brow_sb[:], brow)
        dma("gpsimd", "sW1", b1_sb[:], b1row)
        t_wsmall = cnt["sW1"]

        z_v = st("vector", "V", lambda e: e.memset(zeros_bf[:, 0:2048], 0.0))
        wt("scalar", "V", z_v)
        for base in range(0, DUMMY, 256):
            dma("scalar", "sZ", send1.ap()[base:base + 256, :].rearrange("(a p) d -> p a d", p=128),
                zeros_bf[:, 0:2048].rearrange("p (a d) -> p a d", d=D))
            dma("scalar", "sZ", send2.ap()[base:base + 256, :].rearrange("(a p) d -> p a d", p=128),
                zeros_bf[:, 0:2048].rearrange("p (a d) -> p a d", d=D))
        dma("scalar", "sZ", send1.ap()[DUMMY:DUMMY + 1, :], zeros_bf[0:1, 0:1024])
        dma("scalar", "sZ", send2.ap()[DUMMY:DUMMY + 1, :], zeros_bf[0:1, 0:1024])
        dma("scalar", "sZ", recv2.ap()[DUMMY:DUMMY + 8, :], zeros_bf[0:8, 0:1024])
        t_zero = cnt["sZ"]
        for e8 in range(8):
            op("vector", lambda e, i=e8: e.memset(
                r3(eio8[:])[:, :, i:i + 1], float(i)))
        op("vector", lambda e: e.memset(g16[:], 0.0))
        op("vector", lambda e: e.memset(g16t[:], 0.0))
        op("vector", lambda e: e.memset(g16x[:], 0.0))
        op("vector", lambda e: e.memset(g16xT[:], 0.0))
        op("vector", lambda e: e.memset(rhsA[:], 0.0))
        op("vector", lambda e: e.memset(rhsB[:], 0.0))
        op("vector", lambda e: e.memset(tot0e[:], 0.0))
        op("vector", lambda e: e.memset(Lk0[:], 0))
        op("vector", lambda e: e.memset(Lk1[:], 0))
        op("vector", lambda e: e.memset(Lslot[:], 0))

        wt("tensor", "sC", t_const)
        wt("tensor", "sW1", t_wsmall)
        wt("vector", "sC", t_const)
        t_xnb = [0] * NT
        t_evac = [0] * NT
        t_bcv = [0] * NT

        def stageA_early(m):
            xb = xt[m % 2]
            if m >= 2:
                wt("sync", "V", t_xnb[m - 2])
            kx = f"sX{m % 2}"
            t_x = dma("sync", kx, xb[:], xsh[m], pre_wait=True)
            L = lnst[:, (m % 2) * 8:(m % 2) * 8 + 8]
            wt("vector", kx, t_x)
            op("vector", lambda e, xb=xb, L=L: e.reduce_sum(
                out=L[:, 0:1].rearrange("p (a b) -> p a b", b=1),
                in_=xb[:].rearrange("p (a b) -> p a b", a=1), axis=AX.X))
            op("vector", lambda e, xb=xb: e.tensor_tensor(out=s1[:], in0=xb[:], in1=xb[:], op=ALU.mult))
            op("vector", lambda e, L=L: e.reduce_sum(
                out=L[:, 1:2].rearrange("p (a b) -> p a b", b=1),
                in_=s1[:].rearrange("p (a b) -> p a b", a=1), axis=AX.X))
            op("vector", lambda e, L=L, m=m: e.tensor_scalar(out=mnst[:, m:m + 1], in0=L[:, 0:1],
                                                             scalar1=1.0 / D, scalar2=None, op0=ALU.mult))
            op("vector", lambda e, L=L: e.tensor_scalar(out=L[:, 3:4], in0=L[:, 1:2],
                                                        scalar1=1.0 / D, scalar2=None, op0=ALU.mult))
            op("vector", lambda e, L=L, m=m: e.tensor_tensor(out=L[:, 4:5], in0=mnst[:, m:m + 1],
                                                             in1=mnst[:, m:m + 1], op=ALU.mult))
            t_var = st("vector", "V", lambda e, L=L: e.scalar_tensor_tensor(
                out=L[:, 5:6], in0=L[:, 3:4], scalar=1e-5,
                in1=L[:, 4:5], op0=ALU.add, op1=ALU.subtract))
            wt("scalar", "V", t_var)
            op("scalar", lambda e, L=L: e.activation(out=L[:, 6:7], in_=L[:, 5:6], func=AF.Ln))
            op("scalar", lambda e, L=L: e.mul(out=L[:, 6:7], in_=L[:, 6:7], mul=-0.5))
            t_rstd = st("scalar", "A", lambda e, L=L, m=m: e.activation(
                out=rsst[:, m:m + 1], in_=L[:, 6:7], func=AF.Exp))
            wt("vector", "A", t_rstd)
            t_xnb[m] = st("vector", "V", lambda e, m=m, xb=xb: e.tensor_scalar(
                out=xn_tm[:, m * D:(m + 1) * D], in0=xb[:],
                scalar1=mnst[:, m:m + 1], scalar2=rsst[:, m:m + 1],
                op0=ALU.subtract, op1=ALU.mult))

        def stageA_late(m):
            fm = fmt[m % 2]
            ptr = p_trbs[m % 2]
            wt("tensor", "V", t_xnb[m])
            if m >= 2:
                wt("tensor", "V", t_evac[m - 2])
            t_tr = None
            for j in range(8):
                fn = (lambda e, m=m, j=j, ptr=ptr: e.transpose(
                    ptr[:, j * 128:(j + 1) * 128],
                    xn_tm[:, m * D + j * 128:m * D + (j + 1) * 128],
                    CT["ident_bf"][:]))
                if j == 7:
                    t_tr = st("tensor", "T", fn)
                else:
                    op("tensor", fn)
            wt("vector", "T", t_tr)
            t_evac[m] = st("vector", "V", lambda e, fm=fm, ptr=ptr: e.tensor_copy(out=fm, in_=ptr[:]))
            wt("tensor", "V", t_evac[m])
            if m >= 2:
                wt("tensor", "V", t_bcv[m - 2])
            pb = p_bcs[m % 2]
            for j in range(8):
                op("tensor", lambda e, fm=fm, j=j, pb=pb: e.matmul(
                    pb, fm[:, j * 128:(j + 1) * 128],
                    wbcg_sb[:, j * 40:(j + 1) * 40], start=(j == 0), stop=False))
            t_bc = st("tensor", "T", lambda e, pb=pb: e.matmul(
                pb, CT["ones1_bf"][:], brow_sb[:], start=False, stop=True))
            wt("vector", "T", t_bc)
            op("vector", lambda e, pb=pb: e.tensor_copy(out=bc40[:], in_=pb))
            op("vector", lambda e: e.tensor_tensor(out=bc16[:], in0=bc40[:, 0:16],
                                                   in1=bc40[:, 16:32], op=ALU.mult))
            op("vector", lambda e, m=m: e.reduce_sum(
                out=svst[:, m:m + 1].rearrange("p (a b) -> p a b", b=1),
                in_=bc16[:].rearrange("p (a b) -> p a b", a=1), axis=AX.X))
            t_bcv[m] = st("vector", "V", lambda e, m=m: e.tensor_copy(
                out=logits[:, m * 8:(m + 1) * 8], in_=bc40[:, 32:40]))

        for m in range(NT):
            stageA_early(m)
            if m >= 1:
                stageA_late(m - 1)
        stageA_late(NT - 1)
        t_xn_last = t_xnb[NT - 1]

        if stop_after == "p1a":
            early_out()
            off[0] = True
        z3 = r3(logits[:])
        eio3 = r3(eio8[:])
        op("vector", lambda e: e.reduce_max(out=rcol(R["m1"][:]), in_=z3, axis=AX.X))
        op("vector", lambda e: e.tensor_tensor(
            out=r3(ohs[:]), in0=z3,
            in1=rcol(R["m1"][:]).broadcast_to((128, NT, 8)), op=ALU.is_equal))
        op("vector", lambda e: e.tensor_tensor(out=tmp8[:], in0=ohs[:], in1=eio8[:], op=ALU.mult))
        op("vector", lambda e: e.scalar_tensor_tensor(out=tmp8[:], in0=ohs[:], scalar=-99.0,
                                                      in1=tmp8[:], op0=ALU.mult, op1=ALU.add))
        op("vector", lambda e: e.tensor_scalar(out=tmp8[:], in0=tmp8[:], scalar1=99.0,
                                               scalar2=None, op0=ALU.add))
        op("vector", lambda e: e.tensor_reduce(out=rcol(R["i1"][:]), in_=r3(tmp8[:]),
                                               axis=AX.X, op=ALU.min))
        op("vector", lambda e: e.tensor_tensor(
            out=r3(ohs[:]), in0=eio3,
            in1=rcol(R["i1"][:]).broadcast_to((128, NT, 8)), op=ALU.is_equal))
        op("vector", lambda e: e.scalar_tensor_tensor(out=tmp8[:], in0=ohs[:], scalar=-1e30,
                                                      in1=logits[:], op0=ALU.mult, op1=ALU.add))
        op("vector", lambda e: e.tensor_reduce(out=rcol(R["m2"][:]), in_=r3(tmp8[:]),
                                               axis=AX.X, op=ALU.max))
        op("vector", lambda e: e.tensor_tensor(
            out=r3(ohs[:]), in0=r3(tmp8[:]),
            in1=rcol(R["m2"][:]).broadcast_to((128, NT, 8)), op=ALU.is_equal))
        op("vector", lambda e: e.tensor_tensor(out=tmp8[:], in0=ohs[:], in1=eio8[:], op=ALU.mult))
        op("vector", lambda e: e.scalar_tensor_tensor(out=tmp8[:], in0=ohs[:], scalar=-99.0,
                                                      in1=tmp8[:], op0=ALU.mult, op1=ALU.add))
        op("vector", lambda e: e.tensor_scalar(out=tmp8[:], in0=tmp8[:], scalar1=99.0,
                                               scalar2=None, op0=ALU.add))
        op("vector", lambda e: e.tensor_reduce(out=rcol(R["i2"][:]), in_=r3(tmp8[:]),
                                               axis=AX.X, op=ALU.min))
        t_sm0 = st("vector", "V", lambda e: e.tensor_tensor(
            out=r3(tmp8[:]), in0=z3,
            in1=rcol(R["m1"][:]).broadcast_to((128, NT, 8)), op=ALU.subtract))
        wt("scalar", "V", t_sm0)
        t_ex = st("scalar", "A", lambda e: e.activation(out=ex8[:], in_=tmp8[:], func=AF.Exp))
        wt("vector", "A", t_ex)
        op("vector", lambda e: e.tensor_reduce(out=rcol(R["den"][:]), in_=r3(ex8[:]),
                                               axis=AX.X, op=ALU.add))
        op("vector", lambda e: e.reciprocal(out=R["w1"][:], in_=R["den"][:]))
        t_d2 = st("vector", "V", lambda e: e.tensor_sub(out=R["m2"][:], in0=R["m2"][:], in1=R["m1"][:]))
        wt("scalar", "V", t_d2)
        t_e2 = st("scalar", "A", lambda e: e.activation(out=R["den"][:], in_=R["m2"][:], func=AF.Exp))
        wt("vector", "A", t_e2)
        op("vector", lambda e: e.tensor_mul(out=R["w2"][:], in0=R["den"][:], in1=R["w1"][:]))
        t_pk = None
        for i, k in enumerate(["i1", "i2", "w1", "w2"]):
            fn = (lambda e, i=i, k=k: e.tensor_copy(out=pk[:, i * 32:(i + 1) * 32], in_=R[k][:]))
            if i == 3:
                t_pk = st("vector", "V", fn)
            else:
                op("vector", fn)
        wt("gpsimd", "V", t_pk)
        t_agin = dma("gpsimd", "sAG", ag_in.ap(), pk[:])
        wt("gpsimd", "sAG", t_agin)
        t_ag = st("gpsimd", "CC", lambda e: e.collective_compute(
            "AllGather", ALU.bypass, replica_groups=[list(range(NCORE))],
            ins=[ag_in.ap().opt()], outs=[ag_out.ap().opt()]))
        wt("gpsimd", "CC", t_ag)
        for cc in range(NCORE):
            dma("gpsimd", "sAG", tmp8[:, cc * 32:(cc + 1) * 32],
                ag_out.ap()[cc * 128:(cc + 1) * 128, 0:32])
            t_idxg = dma("gpsimd", "sAG", ex8[:, cc * 32:(cc + 1) * 32],
                         ag_out.ap()[cc * 128:(cc + 1) * 128, 32:64])
        wt("vector", "sAG", t_idxg)
        op("vector", lambda e: e.tensor_copy(
            out=idx1g[:].rearrange("p (f cc) -> p f cc", cc=8),
            in_=tmp8[:].rearrange("p (cc f) -> p f cc", cc=8)))
        op("vector", lambda e: e.tensor_copy(
            out=idx2g[:].rearrange("p (f cc) -> p f cc", cc=8),
            in_=ex8[:].rearrange("p (cc f) -> p f cc", cc=8)))

        if stop_after == "p1b":
            early_out()
            off[0] = True
        for e8 in range(8):
            op("vector", lambda e, i=e8: e.tensor_scalar(
                out=r3(G, 8)[:, :, i:i + 1],
                in0=idx1g[:].rearrange("p (c o) -> p c o", o=1),
                scalar1=float(i), scalar2=None, op0=ALU.is_equal))
            op("vector", lambda e, i=e8: e.tensor_scalar(
                out=r3(srow, 8)[:, :, i:i + 1],
                in0=idx2g[:].rearrange("p (c o) -> p c o", o=1),
                scalar1=float(i), scalar2=None, op0=ALU.is_equal))
        t_G = st("vector", "V", lambda e: e.tensor_add(out=G, in0=G, in1=srow))
        if stop_after == "p3a":
            early_out(); off[0] = True
        t_sz = st("vector", "V", lambda e: e.memset(srow, 0.0))
        wt("tensor", "V", t_sz)
        t_cs = None
        for qq in range(4):
            fn = (lambda e, q=qq: e.matmul(p_row[:, q * 512:(q + 1) * 512], CT["onescol"][:],
                                           G[:, q * 512:(q + 1) * 512], start=True, stop=True))
            if qq == 3:
                t_cs = st("tensor", "T", fn)
            else:
                op("tensor", fn)
        wt("vector", "T", t_cs)
        t_srow = st("vector", "V", lambda e: e.tensor_copy(out=srow[0:1, :], in_=p_row[:]))
        wt("tensor", "V", t_srow)
        for rnd in range(2):
            t_trr = None
            for j8 in range(8):
                j = rnd * 8 + j8
                fn = (lambda e, j=j, j8=j8: e.matmul(
                    psumP[:, 1024 + j8 * 128:1024 + (j8 + 1) * 128],
                    srow[:, j * 128:(j + 1) * 128], CT["ident_f32"][:],
                    start=True, stop=True))
                if j8 == 7:
                    t_trr = st("tensor", "T", fn)
                else:
                    op("tensor", fn)
            wt("vector", "T", t_trr)
            t_t16 = None
            for j8 in range(8):
                j = rnd * 8 + j8
                fn = (lambda e, j=j, j8=j8: e.tensor_copy(
                    out=t16[:, j:j + 1], in_=psumP[:, 1024 + j8 * 128:1024 + j8 * 128 + 1]))
                if j8 == 7:
                    t_t16 = st("vector", "V", fn)
                else:
                    op("vector", fn)
            wt("tensor", "V", t_t16)
        if stop_after == "p3b":
            early_out(); off[0] = True
        op("tensor", lambda e: e.matmul(psumP[:, 1024:1040], CT["ltri16x8"][:], t16[:],
                                        start=True, stop=True))
        t_mm1 = st("tensor", "T", lambda e: e.matmul(psumP[0:32, 1040:1056], CT["mod8p"][:], t16[:],
                                                     start=True, stop=True))
        wt("vector", "T", t_mm1)
        op("vector", lambda e: e.tensor_copy(out=a1[:], in_=psumP[:, 1024:1040]))
        t_g16 = st("vector", "V", lambda e: e.tensor_copy(out=g16[0:8, 0:16], in_=psumP[0:8, 1040:1056]))
        wt("tensor", "V", t_g16)
        t_mm2 = st("tensor", "T", lambda e: e.matmul(psumP[0:32, 1056:1184], g16[:, 0:16].bitcast(F32) if False else g16[:], CT["ident_f32"][:], start=True, stop=True))
        wt("vector", "T", t_mm2)
        t_g16t = st("vector", "V", lambda e: e.tensor_copy(out=g16t[0:16, :], in_=psumP[0:16, 1056:1072]))
        wt("tensor", "V", t_g16t)
        t_mm3 = st("tensor", "T", lambda e: e.matmul(psumP[0:32, 1184:1192], CT["ltri16p"][:],
                                                     g16t[:, 0:8], start=True, stop=True))
        wt("vector", "T", t_mm3)
        t_g16x = st("vector", "V", lambda e: e.tensor_copy(out=g16x[0:16, 0:8], in_=psumP[0:16, 1184:1192]))
        if stop_after == "p3b2":
            early_out(); off[0] = True
        wt("tensor", "V", t_g16x)
        t_mm4 = st("tensor", "T", lambda e: e.matmul(psumP[0:32, 1192:1208], CT["selown16"][:],
                                                     a1[:], start=True, stop=True))
        wt("vector", "T", t_mm4)
        t_o16 = st("vector", "V", lambda e: e.tensor_copy(out=o16[:], in_=psumP[0:16, 1192:1208]))
        op("vector", lambda e: e.tensor_tensor(
            out=rhsA[0:16, :].rearrange("r (j h ee) -> r j h ee", h=2, ee=8),
            in0=o16[:].rearrange("r (j h ee) -> r j h ee", h=1, ee=1).broadcast_to((16, 16, 2, 8)),
            in1=CT["m16pat"][:].rearrange("r (j h ee) -> r j h ee", h=2, ee=8), op=ALU.mult))
        op("vector", lambda e: e.tensor_tensor(
            out=rhsB[0:16, :].rearrange("r (j h ee) -> r j h ee", h=2, ee=8),
            in0=g16x[0:16, 0:8].rearrange("r (j h ee) -> r j h ee", j=1, h=1).broadcast_to((16, 16, 2, 8)),
            in1=CT["m16bpat"][:].rearrange("r (j h ee) -> r j h ee", h=2, ee=8), op=ALU.mult))
        t_rhs = st("vector", "V", lambda e: e.tensor_add(out=rhsA[:], in0=rhsA[:], in1=rhsB[:]))
        wt("tensor", "V", t_rhs)
        t_mm5 = st("tensor", "T", lambda e: e.matmul(psumP[:, 1024:1280], CT["ones16p"][:],
                                                     rhsA[:], start=True, stop=True))
        wt("vector", "T", t_mm5)
        t_ownoff = st("vector", "V", lambda e: e.tensor_copy(out=ownoff[:], in_=psumP[:, 1024:1280]))
        if stop_after == "p3b3":
            early_out(); off[0] = True
        wt("tensor", "V", t_g16x)
        t_mm6 = st("tensor", "T", lambda e: e.matmul(psumP[0:16, 1280:1296], g16t[:],
                                                     CT["ultri32"][:], start=True, stop=True))
        wt("vector", "T", t_mm6)
        t_gxt = st("vector", "V", lambda e: e.tensor_copy(out=g16xT[0:8, :], in_=psumP[0:8, 1280:1296]))
        if stop_after == "p3b4a":
            early_out(); off[0] = True
        wt("tensor", "V", t_gxt)
        t_mm7 = st("tensor", "T", lambda e: e.matmul(psumP[:, 1408:1424], CT["mod8tp"][:],
                                                     g16xT[:], start=True, stop=True))
        wt("vector", "T", t_mm7)
        op("vector", lambda e: e.tensor_copy(out=offgT[:], in_=psumP[:, 1408:1424]))
        if stop_after == "p3b4b":
            early_out(); off[0] = True
        op("vector", lambda e: e.tensor_add(out=kscr[:], in0=a1[:], in1=offgT[:]))
        op("vector", lambda e: e.tensor_scalar(out=kscr[:], in0=kscr[:], scalar1=-1.0,
                                               scalar2=float(CAPN), op0=ALU.mult, op1=ALU.add))
        op("vector", lambda e: e.tensor_scalar(out=kscr[:], in0=kscr[:], scalar1=0.0,
                                               scalar2=None, op0=ALU.max))
        t_kept = st("vector", "V", lambda e: e.tensor_tensor(out=keptT[:], in0=kscr[:],
                                                             in1=t16[:], op=ALU.min))
        if stop_after == "p3b4":
            early_out(); off[0] = True
        wt("tensor", "V", t_kept)
        t_mm8 = st("tensor", "T", lambda e: e.matmul(psumP[0:64, 1424:1440], CT["selc8"][:],
                                                     keptT[:], start=True, stop=True))
        wt("vector", "T", t_mm8)
        op("vector", lambda e: e.tensor_copy(out=vc64[:], in_=psumP[0:64, 1424:1440]))
        t_vcr = st("vector", "V", lambda e: e.tensor_reduce(
            out=vc64r[:].rearrange("p (a b) -> p a b", b=1),
            in_=vc64[:].rearrange("p (a b) -> p a b", a=1), axis=AX.X, op=ALU.add))
        wt("tensor", "V", t_vcr)
        op("tensor", lambda e: e.matmul(psumP[0:64, 1440:1441], CT["mask64x"][:],
                                        vc64r[:], start=True, stop=True))
        t_mm9 = st("tensor", "T", lambda e: e.matmul(psumP[0:64, 1441:1442], CT["mask64i"][:],
                                                     vc64r[:], start=True, stop=True))
        wt("vector", "T", t_mm9)
        op("vector", lambda e: e.tensor_copy(out=cum64[:], in_=psumP[0:64, 1440:1441]))
        t_c64 = st("vector", "V", lambda e: e.tensor_copy(out=inc64[:], in_=psumP[0:64, 1441:1442]))
        if stop_after == "p3b5":
            early_out(); off[0] = True
        t_dg1 = st("vector", "V", lambda e: e.tensor_scalar(out=scr64[0:64, :], in0=CT["i64"][:],
                                                            scalar1=cum64[:, 0:1], scalar2=None, op0=ALU.mult))
        wt("tensor", "V", t_dg1)
        t_mmA = st("tensor", "T", lambda e: e.matmul(psumP[:, 1442:1506], CT["ones64"][:],
                                                     scr64[0:64, :], start=True, stop=True))
        wt("vector", "T", t_mmA)
        t_cb = st("vector", "V", lambda e: e.tensor_copy(out=cumB[:], in_=psumP[:, 1442:1506]))
        t_dg2 = st("vector", "V", lambda e: e.tensor_scalar(out=scr64[0:64, :], in0=CT["i64"][:],
                                                            scalar1=inc64[:, 0:1], scalar2=None, op0=ALU.mult))
        wt("tensor", "V", t_dg2)
        t_mmB = st("tensor", "T", lambda e: e.matmul(psumP[:, 1536:1600], CT["ones64"][:],
                                                     scr64[0:64, :], start=True, stop=True))
        wt("vector", "T", t_mmB)
        op("vector", lambda e: e.tensor_copy(out=incB[:], in_=psumP[:, 1536:1600]))
        op("vector", lambda e: e.tensor_tensor(out=scr64[:], in0=cumB[:], in1=CT["memask"][:], op=ALU.mult))
        op("vector", lambda e: e.tensor_reduce(
            out=mycum[:].rearrange("p (c o) -> p c o", o=1),
            in_=scr64[:].rearrange("p (c e) -> p c e", e=8), axis=AX.X, op=ALU.add))
        op("vector", lambda e: e.tensor_tensor(out=scr64[:], in0=incB[:], in1=CT["memask"][:], op=ALU.mult))
        op("vector", lambda e: e.tensor_reduce(
            out=myinc[:].rearrange("p (c o) -> p c o", o=1),
            in_=scr64[:].rearrange("p (c e) -> p c e", e=8), axis=AX.X, op=ALU.add))
        op("vector", lambda e: e.tensor_sub(out=myvc[:], in0=myinc[:], in1=mycum[:]))
        op("vector", lambda e: e.tensor_tensor(out=scr64[:], in0=incB[:], in1=CT["memask_tot"][:], op=ALU.mult))
        op("vector", lambda e: e.tensor_reduce(
            out=mytot[:].rearrange("p (a b) -> p a b", b=1),
            in_=scr64[:].rearrange("p (a b) -> p a b", a=1), axis=AX.X, op=ALU.add))
        if stop_after == "p3c":
            early_out(); off[0] = True
        for e8 in range(8):
            op("vector", lambda e, i=e8: e.tensor_scalar(
                out=r3(oh1[:])[:, :, i:i + 1], in0=rcol(R["i1"][:]),
                scalar1=float(i), scalar2=None, op0=ALU.is_equal))
            op("vector", lambda e, i=e8: e.tensor_scalar(
                out=r3(oh2[:])[:, :, i:i + 1], in0=rcol(R["i2"][:]),
                scalar1=float(i), scalar2=None, op0=ALU.is_equal))
        t_ohs = st("vector", "V", lambda e: e.tensor_add(out=ohs[:], in0=oh1[:], in1=oh2[:]))
        wt("tensor", "V", t_ohs)
        t_mmC = st("tensor", "T", lambda e: e.matmul(psumP[:, 1024:1280], CT["ltri128"][:],
                                                     ohs[:], start=True, stop=True))
        wt("vector", "T", t_mmC)
        op("vector", lambda e: e.tensor_copy(out=ecsU[:], in_=psumP[:, 1024:1280]))
        op("vector", lambda e: e.tensor_tensor(out=selbuf[:], in0=oh1[:], in1=ecsU[:], op=ALU.mult))
        op("vector", lambda e: e.tensor_reduce(out=rcol(R["vr1"][:]), in_=r3(selbuf[:]),
                                               axis=AX.X, op=ALU.add))
        op("vector", lambda e: e.tensor_tensor(out=selbuf[:], in0=oh1[:], in1=ownoff[:], op=ALU.mult))
        op("vector", lambda e: e.tensor_reduce(out=rcol(R["t1"][:]), in_=r3(selbuf[:]),
                                               axis=AX.X, op=ALU.add))
        op("vector", lambda e: e.tensor_add(out=R["t1"][:], in0=R["t1"][:], in1=R["vr1"][:]))
        op("vector", lambda e: e.tensor_scalar(out=R["v1"][:], in0=R["t1"][:],
                                               scalar1=float(CAPN), scalar2=None, op0=ALU.is_lt))
        op("vector", lambda e: e.tensor_add(out=selbuf[:], in0=ecsU[:], in1=oh1[:]))
        op("vector", lambda e: e.tensor_tensor(out=selbuf[:], in0=oh2[:], in1=selbuf[:], op=ALU.mult))
        op("vector", lambda e: e.tensor_reduce(out=rcol(R["vr2"][:]), in_=r3(selbuf[:]),
                                               axis=AX.X, op=ALU.add))
        op("vector", lambda e: e.tensor_tensor(out=selbuf[:], in0=oh2[:], in1=ownoff[:], op=ALU.mult))
        op("vector", lambda e: e.tensor_reduce(out=rcol(R["t2"][:]), in_=r3(selbuf[:]),
                                               axis=AX.X, op=ALU.add))
        op("vector", lambda e: e.tensor_add(out=R["t2"][:], in0=R["t2"][:], in1=R["vr2"][:]))
        op("vector", lambda e: e.tensor_scalar(out=R["v2"][:], in0=R["t2"][:],
                                               scalar1=float(CAPN), scalar2=None, op0=ALU.is_lt))
        op("vector", lambda e: e.tensor_tensor(
            out=r3(ohv1[:]), in0=r3(oh1[:]),
            in1=rcol(R["v1"][:]).broadcast_to((128, NT, 8)), op=ALU.mult))
        op("vector", lambda e: e.tensor_tensor(
            out=r3(ohv2[:]), in0=r3(oh2[:]),
            in1=rcol(R["v2"][:]).broadcast_to((128, NT, 8)), op=ALU.mult))
        t_ohv = st("vector", "V", lambda e: e.tensor_add(out=ohs[:], in0=ohv1[:], in1=ohv2[:]))
        wt("tensor", "V", t_ohv)
        t_mmD = st("tensor", "T", lambda e: e.matmul(psumP[:, 1280:1536], CT["ltri128"][:],
                                                     ohs[:], start=True, stop=True))
        wt("vector", "T", t_mmD)
        op("vector", lambda e: e.tensor_copy(out=ecsV[:], in_=psumP[:, 1280:1536]))
        op("vector", lambda e: e.tensor_tensor(out=selbuf[:], in0=ohv1[:], in1=ecsV[:], op=ALU.mult))
        op("vector", lambda e: e.tensor_reduce(out=rcol(R["vr1"][:]), in_=r3(selbuf[:]),
                                               axis=AX.X, op=ALU.add))
        op("vector", lambda e: e.tensor_add(out=selbuf[:], in0=ecsV[:], in1=ohv1[:]))
        op("vector", lambda e: e.tensor_tensor(out=selbuf[:], in0=ohv2[:], in1=selbuf[:], op=ALU.mult))
        op("vector", lambda e: e.tensor_reduce(out=rcol(R["vr2"][:]), in_=r3(selbuf[:]),
                                               axis=AX.X, op=ALU.add))
        t_csz = st("vector", "V", lambda e: e.memset(csrowV[:], 0.0))
        wt("tensor", "V", max(t_ohv, t_csz))
        t_mmE = st("tensor", "T", lambda e: e.matmul(psumP[0:1, 1536:1792], CT["onescol"][:],
                                                     ohs[:], start=True, stop=True))
        wt("vector", "T", t_mmE)
        t_csr = st("vector", "V", lambda e: e.tensor_copy(out=csrowV[0:1, :], in_=psumP[0:1, 1536:1792]))
        wt("tensor", "V", t_csr)
        t_tr2 = None
        for half in range(2):
            fn = (lambda e, half=half: e.matmul(
                psumP[:, 1536 + half * 128:1536 + (half + 1) * 128],
                csrowV[:, half * 128:(half + 1) * 128], CT["ident_f32"][:],
                start=True, stop=True))
            if half == 1:
                t_tr2 = st("tensor", "T", fn)
            else:
                op("tensor", fn)
        wt("vector", "T", t_tr2)
        op("vector", lambda e: e.tensor_copy(out=a2t[:, 0:1], in_=psumP[:, 1536:1537]))
        t_a2t = st("vector", "V", lambda e: e.tensor_copy(out=a2t[:, 1:2], in_=psumP[:, 1664:1665]))
        wt("tensor", "V", t_a2t)
        op("tensor", lambda e: e.matmul(psumP[:, 1792:1794], CT["ltri16x8"][:], a2t[:],
                                        start=True, stop=True))
        t_mmF = st("tensor", "T", lambda e: e.matmul(psumP[0:32, 1794:1796], CT["mod8p"][:],
                                                     a2t[:], start=True, stop=True))
        wt("vector", "T", t_mmF)
        op("vector", lambda e: e.tensor_copy(out=a2x[:], in_=psumP[:, 1792:1794]))
        t_t0e = st("vector", "V", lambda e: e.tensor_copy(out=tot0e[0:8, :], in_=psumP[0:8, 1794:1796]))
        wt("tensor", "V", t_t0e)
        t_mmG = st("tensor", "T", lambda e: e.matmul(psumP[:, 1796:1797], CT["mod8tp"][:],
                                                     tot0e[:, 0:1], start=True, stop=True))
        wt("vector", "T", t_mmG)
        op("vector", lambda e: e.tensor_copy(out=tot0b[:], in_=psumP[:, 1796:1797]))
        op("vector", lambda e: e.tensor_add(out=a2x[:, 1:2], in0=a2x[:, 1:2], in1=tot0b[:]))
        t_rhs3 = st("vector", "V", lambda e: e.tensor_tensor(
            out=rhs3[:].rearrange("p (h a ee) -> p h a ee", a=16, ee=8),
            in0=a2x[:].rearrange("p (h a ee) -> p h a ee", a=1, ee=1).broadcast_to((128, 2, 16, 8)),
            in1=CT["m128pat"][:].rearrange("p (h a ee) -> p h a ee", a=16, ee=8), op=ALU.mult))
        wt("tensor", "V", t_rhs3)
        t_mmH = st("tensor", "T", lambda e: e.matmul(psumP[:, 1536:1792], CT["ones128sq"][:],
                                                     rhs3[:], start=True, stop=True))
        wt("vector", "T", t_mmH)
        op("vector", lambda e: e.tensor_copy(out=offs2B[:], in_=psumP[:, 1536:1792]))
        op("vector", lambda e: e.tensor_tensor(out=selbuf[:], in0=oh1[:], in1=offs2B[:], op=ALU.mult))
        op("vector", lambda e: e.tensor_reduce(out=rcol(R["t1"][:]), in_=r3(selbuf[:]),
                                               axis=AX.X, op=ALU.add))
        op("vector", lambda e: e.tensor_add(out=R["vr1"][:], in0=R["vr1"][:], in1=R["t1"][:]))
        op("vector", lambda e: e.tensor_tensor(out=selbuf[:], in0=oh2[:], in1=offs2B[:], op=ALU.mult))
        op("vector", lambda e: e.tensor_reduce(out=rcol(R["t2"][:]), in_=r3(selbuf[:]),
                                               axis=AX.X, op=ALU.add))
        op("vector", lambda e: e.tensor_add(out=R["vr2"][:], in0=R["vr2"][:], in1=R["t2"][:]))
        op("vector", lambda e: e.scalar_tensor_tensor(out=R["sx1"][:], in0=R["i1"][:],
                                                      scalar=float(CAP), in1=R["vr1"][:],
                                                      op0=ALU.mult, op1=ALU.add))
        op("vector", lambda e: e.tensor_scalar(out=R["t1"][:], in0=CT["dumoff"][:],
                                               scalar1=float(DUMMY), scalar2=None, op0=ALU.add))
        op("vector", lambda e: e.tensor_sub(out=R["sx1"][:], in0=R["sx1"][:], in1=R["t1"][:]))
        op("vector", lambda e: e.tensor_tensor(out=R["sx1"][:], in0=R["v1"][:], in1=R["sx1"][:], op=ALU.mult))
        op("vector", lambda e: e.tensor_add(out=R["sx1"][:], in0=R["sx1"][:], in1=R["t1"][:]))
        op("vector", lambda e: e.scalar_tensor_tensor(out=R["sx2"][:], in0=R["i2"][:],
                                                      scalar=float(CAP), in1=R["vr2"][:],
                                                      op0=ALU.mult, op1=ALU.add))
        op("vector", lambda e: e.tensor_scalar(out=R["t2"][:], in0=R["t1"][:],
                                               scalar1=4.0, scalar2=None, op0=ALU.add))
        op("vector", lambda e: e.tensor_sub(out=R["sx2"][:], in0=R["sx2"][:], in1=R["t2"][:]))
        op("vector", lambda e: e.tensor_tensor(out=R["sx2"][:], in0=R["v2"][:], in1=R["sx2"][:], op=ALU.mult))
        op("vector", lambda e: e.tensor_add(out=R["sx2"][:], in0=R["sx2"][:], in1=R["t2"][:]))
        op("vector", lambda e: e.tensor_tensor(out=R["wm1"][:], in0=R["w1"][:], in1=R["v1"][:], op=ALU.mult))
        op("vector", lambda e: e.tensor_tensor(out=R["wm2"][:], in0=R["w2"][:], in1=R["v2"][:], op=ALU.mult))
        op("vector", lambda e: e.memset(R["csum"][:], 0.0))
        op("vector", lambda e: e.memset(R["acc"][:], 0.0))
        for c_ in range(1, 8):
            op("vector", lambda e, c_=c_: e.tensor_scalar(
                out=R["t1"][:], in0=CT["siota"][:], scalar1=mycum[:, c_:c_ + 1],
                scalar2=None, op0=ALU.is_ge))
            op("vector", lambda e: e.tensor_add(out=R["csum"][:], in0=R["csum"][:], in1=R["t1"][:]))
            op("vector", lambda e, c_=c_: e.scalar_tensor_tensor(
                out=R["t2"][:], in0=R["t1"][:], scalar=myvc[:, c_ - 1:c_],
                in1=R["acc"][:], op0=ALU.mult, op1=ALU.add))
            op("vector", lambda e: e.tensor_copy(out=R["acc"][:], in_=R["t2"][:]))
        op("vector", lambda e: e.tensor_sub(out=R["t1"][:], in0=CT["siota"][:], in1=R["acc"][:]))
        op("vector", lambda e: e.scalar_tensor_tensor(out=R["slotv"][:], in0=R["csum"][:],
                                                      scalar=float(CAP), in1=R["t1"][:],
                                                      op0=ALU.mult, op1=ALU.add))
        op("vector", lambda e: e.tensor_scalar(out=R["t1"][:], in0=CT["siota"][:],
                                               scalar1=mytot[:, 0:1], scalar2=None, op0=ALU.is_ge))
        op("vector", lambda e: e.tensor_scalar(out=R["t2"][:], in0=R["slotv"][:],
                                               scalar1=-1.0, scalar2=float(DUMMY),
                                               op0=ALU.mult, op1=ALU.add))
        op("vector", lambda e: e.tensor_tensor(out=R["t2"][:], in0=R["t1"][:], in1=R["t2"][:], op=ALU.mult))
        t_slotv = st("vector", "V", lambda e: e.tensor_add(out=R["slotv"][:], in0=R["slotv"][:], in1=R["t2"][:]))

        if stop_after == "p3d":
            early_out(); off[0] = True
        def fold(src, dst, tag):
            for g in range(8):
                pslot = 1024 + (g % 2) * 64
                t_fm_ = st("tensor", "T", lambda e, g=g, src=src, pslot=pslot: e.matmul(
                    psumP[:, pslot:pslot + 32], CT[f"foldsel{g}"][:], src[:],
                    start=True, stop=True))
                wt("vector", "T", t_fm_)
                t_fc = st("vector", "V", lambda e, g=g, dst=dst, pslot=pslot: e.tensor_copy(
                    out=dst[:].rearrange("p (mm gg) -> p mm gg", gg=8)[:, :, g:g + 1],
                    in_=psumP[:, pslot:pslot + 32].rearrange("p (a b) -> p a b", b=1)))
                if g >= 1:
                    wt("tensor", "V", t_fc)
            return cnt["V"]

        wt("tensor", "V", t_slotv)
        fold(R["sx1"], Lk0, "k0")
        fold(R["sx2"], Lk1, "k1")
        t_folds = fold(R["slotv"], Lslot, "sl")

        wt("tensor", "V", t_folds)
        wt("sync", "V", t_folds)
        t_y = [0] * NT
        t_ev2 = [0] * NT
        t_zf = [0] * NT

        t_pstl = [0] * NT

        def stageB(m):
            fm = fmt[m % 2]
            ptr = p_trbs[m % 2]
            pz = p_zs[m % 2]
            if m >= 2:
                wt("tensor", "V", t_ev2[m - 2])
            t_tr = None
            for j in range(8):
                fn = (lambda e, m=m, j=j, ptr=ptr: e.transpose(
                    ptr[:, j * 128:(j + 1) * 128],
                    xn_tm[:, m * D + j * 128:m * D + (j + 1) * 128],
                    CT["ident_bf"][:]))
                if j == 7:
                    t_tr = st("tensor", "T", fn)
                else:
                    op("tensor", fn)
            wt("vector", "T", t_tr)
            t_ev2[m] = st("vector", "V", lambda e, fm=fm, ptr=ptr: e.tensor_copy(out=fm, in_=ptr[:]))
            wt("tensor", "V", t_ev2[m])
            if m >= 2:
                wt("tensor", "A", t_zf[m - 2])
            t_zmm = None
            for o in range(2):
                for j in range(8):
                    op("tensor", lambda e, fm=fm, o=o, j=j, pz=pz: e.matmul(
                        pz[:, o * 512:(o + 1) * 512], fm[:, j * 128:(j + 1) * 128],
                        w1_sb[:, j * D + o * 512:j * D + (o + 1) * 512],
                        start=(j == 0), stop=False))
                fnb = (lambda e, o=o, pz=pz: e.matmul(
                    pz[:, o * 512:(o + 1) * 512], CT["ones1_bf"][:],
                    b1_sb[:, o * 512:(o + 1) * 512], start=False, stop=True))
                if o == 1:
                    t_zmm = st("tensor", "T", fnb)
                else:
                    op("tensor", fnb)
            wt("scalar", "T", t_zmm)
            if m >= 1:
                wt("scalar", f"sP{(m - 1) % 2}", t_pstl[m - 1])
            op("scalar", lambda e, pz=pz: e.activation(out=s1[:], in_=pz, func=AF.Exp))
            t_zf[m] = st("scalar", "A", lambda e: e.activation(out=s2, in_=s1[:], func=AF.Ln, bias=1.0))
            wt("vector", "A", t_zf[m])
            t_y[m] = st("vector", "V", lambda e, m=m: e.scalar_tensor_tensor(
                out=s0[:], in0=xn_tm[:, m * D:(m + 1) * D], scalar=svst[:, m:m + 1],
                in1=s2, op0=ALU.mult, op1=ALU.mult))
            wt("scalar", "V", t_y[m])
            t_cpy = st("scalar", "A", lambda e: e.activation(out=y_bf, in_=s0[:], func=AF.Copy))
            wt("vector", "A", t_cpy)
            wt("sync", "A", t_cpy)
            kp = f"sP{m % 2}"
            t_pstl[m] = dma("sync", kp, partial.ap()[m], y_bf, pre_wait=True)

        for m in range(NT):
            stageB(m)
        t_zB_last = cnt["T"]

        if debug:
            wt("gpsimd", "V", t_folds)
            dma("gpsimd", "sDBG", dbg["d_r1"], pk[:])
            for i, k in enumerate(["v1", "v2", "vr1", "vr2"]):
                op("vector", lambda e, i=i, k=k: e.tensor_copy(out=pk[:, i * 32:(i + 1) * 32], in_=R[k][:]))
            t_dbg2 = st("vector", "V", lambda e: e.tensor_copy(out=ex8[:, 0:64], in_=cumB[:]))
            wt("gpsimd", "V", t_dbg2)
            dma("gpsimd", "sDBG", dbg["d_r2"], pk[:])
            dma("gpsimd", "sDBG", dbg["d_r3"][:, 0:64], ex8[0:128, 0:64])
            for i, k in enumerate(["sx1", "sx2", "slotv"]):
                op("vector", lambda e, i=i, k=k: e.tensor_copy(out=tmp8[:, i * 32:(i + 1) * 32], in_=R[k][:]))
            t_dbg4 = st("vector", "V", lambda e: e.tensor_copy(out=tmp8[:, 95:96], in_=mytot[:]))
            wt("gpsimd", "V", t_dbg4)
            dma("gpsimd", "sDBG", dbg["d_r4"], tmp8[:, 0:96])

        class _Stop(Exception):
            pass

        try:
            if stop_after == "p1":
                early_out()
                raise _Stop()
            wt("gpsimd", "V", t_folds)
            wt("gpsimd", "V", t_xn_last)
            wt("gpsimd", "sZ", t_zero)
            t_sc1 = None
            for kk, L in ((0, Lk0), (1, Lk1)):
                for jj in range(4):
                    t_sc1 = bump("sSC", 16)
                    emit("gpsimd", lambda e, L=L, jj=jj: e.dma_scatter_add(
                        out_ap=send1.ap(),
                        in_ap=xn_tm[:, jj * 8192:(jj + 1) * 8192].rearrange("p (m d) -> p m d", d=D),
                        idxs_ap=L[:, jj * 64:(jj + 1) * 64], num_idxs=1024,
                        num_idxs_reg=1024, elem_size=D).then_inc(sem["sSC"], 16))
            wt("gpsimd", "sSC", t_sc1)
            t_a2a1 = st("gpsimd", "CC", lambda e: e.collective_compute(
                "AllToAll", ALU.bypass, replica_groups=[list(range(NCORE))],
                ins=[send1.ap()[0:DUMMY, :].opt()], outs=[recv1.ap()[0:DUMMY, :].opt()]))
            wt("gpsimd", "T", t_zB_last)
            for dc in range(8):
                dma("gpsimd", "sWF", cfc_sb[:, dc * DFF:(dc + 1) * DFF], cfc[dc * 128:(dc + 1) * 128, :])
            for j in range(12):
                t_wffn = dma("gpsimd", "sWF", cpr_sb[:, j * D:(j + 1) * D], cpr[j * 128:(j + 1) * 128, :])
            wt("gpsimd", "CC", t_a2a1)

            exph = [xn_tm[:, hh_ * 4096:(hh_ + 1) * 4096].rearrange("p (b t) -> p b t", b=8)
                    for hh_ in range(8)]
            t_gath = [0] * 8
            wt("gpsimd", "V", t_y[NT - 1])
            for h in range(8):
                kg = f"sEG{h % 4}"
                if h >= 4:
                    wt("gpsimd", kg, t_gath[h - 4])
                t_gath[h] = bump(kg, 16)
                emit("gpsimd", lambda e, h=h, kg=kg: e.dma_gather(
                    out_ap=exph[h], in_ap=recv1.ap(), idxs_ap=Lslot[:, h * 32:(h + 1) * 32],
                    num_idxs=512, num_idxs_reg=512, elem_size=D,
                    transpose=True).then_inc(sem[kg], 16))
            wt("tensor", "sWF", t_wffn)
            wt("tensor", "A", t_zf[NT - 1])
            if stop_after == "a2a1":
                for q_ in range(4):
                    wt("gpsimd", f"sEG{q_}", cnt[f"sEG{q_}"])
                early_out()
                raise _Stop()

            wt("vector", f"sP{(NT - 1) % 2}", t_pstl[NT - 1])
            wt("vector", f"sP{(NT - 2) % 2}", t_pstl[NT - 2])
            t_expo2 = {}
            pfree = {0: 0, 1: 0, 2: 0}
            ex_bufs = [
                [expo[:, 0:2048], expo[:, 2048:4096]],
                [s0[:].bitcast(BF16), s1[:].bitcast(BF16)],
            ]
            t_lastmult = None
            for s_ in range(8):
                wt("tensor", f"sEG{s_ % 4}", t_gath[s_])
                bufs = ex_bufs[s_ % 2]
                if s_ >= 2:
                    for h in range(2):
                        wt("vector", f"sEO{(s_ % 2) * 2 + h}", t_expo2[(s_ - 2, h)])
                t_mult = [0] * 12
                for j in range(12):
                    pa = psumP[:, (j % 2) * 1024:(j % 2) * 1024 + 512]
                    pg = psumP[:, (j % 2) * 1024 + 512:(j % 2) * 1024 + 1024]
                    if pfree[j % 2]:
                        wt("tensor", "V", pfree[j % 2])
                    t_mm_a = None
                    for dc in range(8):
                        fn = (lambda e, j=j, dc=dc, s_=s_, pa=pa: e.matmul(
                            pa, cfc_sb[:, dc * DFF + j * 128:dc * DFF + (j + 1) * 128],
                            exph[s_][:, dc, :],
                            start=(dc == 0), stop=(dc == 7)))
                        if dc == 7:
                            t_mm_a = st("tensor", "T", fn)
                        else:
                            op("tensor", fn)
                    t_mm_g = None
                    for dc in range(8):
                        fn = (lambda e, j=j, dc=dc, s_=s_, pg=pg: e.matmul(
                            pg, cfc_sb[:, dc * DFF + (j + 12) * 128:dc * DFF + (j + 13) * 128],
                            exph[s_][:, dc, :],
                            start=(dc == 0), stop=(dc == 7)))
                        if dc == 7:
                            t_mm_g = st("tensor", "T", fn)
                        else:
                            op("tensor", fn)
                    wt("scalar", "T", t_mm_a)
                    if t_lastmult:
                        wt("scalar", "V", t_lastmult)
                    t_silu = st("scalar", "A", lambda e, pa=pa: e.activation(out=sa, in_=pa, func=AF.Silu))
                    wt("vector", "A", t_silu)
                    wt("vector", "T", t_mm_g)
                    t_lastmult = t_mult[j] = st("vector", "V", lambda e, j=j, pg=pg: e.tensor_tensor(
                        out=hh[:, j * 512:(j + 1) * 512], in0=sa, in1=pg, op=ALU.mult))
                    pfree[j % 2] = t_mult[j]
                t_ev = [0] * 4
                for sub in range(4):
                    reg = 2 if sub % 2 == 0 else 1
                    pz2 = p_z if sub % 2 == 0 else psumP[:, 1024:2048]
                    if pfree[reg]:
                        wt("tensor", "V", pfree[reg])
                    if sub == 0:
                        wt("tensor", "V", t_mult[11])
                    t_o = None
                    for nh in range(2):
                        for j in range(12):
                            fn = (lambda e, sub=sub, nh=nh, j=j, pz2=pz2: e.matmul(
                                pz2[:, nh * 512:(nh + 1) * 512],
                                hh[:, j * 512 + sub * 128:j * 512 + (sub + 1) * 128],
                                cpr_sb[:, j * D + nh * 512:j * D + (nh + 1) * 512],
                                start=(j == 0), stop=(j == 11)))
                            if nh == 1 and j == 11:
                                t_o = st("tensor", "T", fn)
                            else:
                                op("tensor", fn)
                    wt("vector", "T", t_o)
                    t_ev[sub] = st("vector", "V", lambda e, sub=sub, bufs=bufs, pz2=pz2: e.tensor_copy(
                        out=bufs[sub // 2][:, (sub % 2) * 1024:(sub % 2 + 1) * 1024], in_=pz2[:]))
                    pfree[reg] = t_ev[sub]
                for h in range(2):
                    keo = f"sEO{(s_ % 2) * 2 + h}"
                    wt("gpsimd", "V", t_ev[2 * h + 1])
                    if cnt[keo] > 0:
                        wt("gpsimd", keo, cnt[keo])
                    t_expo2[(s_, h)] = bump(keo, 16)
                    emit("gpsimd", lambda e, s_=s_, h=h, bufs=bufs, keo=keo: e.dma_scatter_add(
                        out_ap=send2.ap(),
                        in_ap=bufs[h].rearrange("p (a d) -> p a d", d=D),
                        idxs_ap=Lslot[:, s_ * 32 + h * 16:s_ * 32 + (h + 1) * 16],
                        num_idxs=256, num_idxs_reg=256, elem_size=D).then_inc(sem[keo], 16))

            for q_ in range(4):
                if cnt[f"sEO{q_}"] > 0:
                    wt("gpsimd", f"sEO{q_}", cnt[f"sEO{q_}"])
            t_a2a2 = st("gpsimd", "CC", lambda e: e.collective_compute(
                "AllToAll", ALU.bypass, replica_groups=[list(range(NCORE))],
                ins=[send2.ap()[0:DUMMY, :].opt()], outs=[recv2.ap()[0:DUMMY, :].opt()]))
            wt("gpsimd", "CC", t_a2a2)
            if stop_after == "ffn":
                early_out()
                raise _Stop()
            gbs = [[cfc_sb[:, 0:8192].rearrange("p (b d) -> p b d", d=D),
                    cfc_sb[:, 8192:16384].rearrange("p (b d) -> p b d", d=D)],
                   [cfc_sb[:, 16384:24576].rearrange("p (b d) -> p b d", d=D),
                    cpr_sb[:, 0:8192].rearrange("p (b d) -> p b d", d=D)]]
            wt("sync", "A", cnt["A"])
            wt("sync", "V", cnt["V"])
            wt("vector", "sP0", cnt["sP0"])
            wt("vector", "sP1", cnt["sP1"])
            t_out = None
            t_og = {}
            vch = {}
            for cj in range(4):
                for k_, Lx in ((0, Lk0), (1, Lk1)):
                    ko = f"sOG{(cj % 2) * 2 + k_}"
                    if cj >= 2:
                        wt("gpsimd", ko, t_og[(cj - 2, k_)])
                        wt("gpsimd", "V", vch[cj - 2])
                    t_og[(cj, k_)] = bump(ko, 16)
                    emit("gpsimd", lambda e, cj=cj, k_=k_, Lx=Lx, ko=ko: e.dma_gather(
                        out_ap=gbs[cj % 2][k_], in_ap=recv2.ap(),
                        idxs_ap=Lx[:, cj * 64:(cj + 1) * 64],
                        num_idxs=1024, num_idxs_reg=1024, elem_size=D).then_inc(sem[ko], 16))
                wt("vector", f"sOG{(cj % 2) * 2}", t_og[(cj, 0)])
                wt("vector", f"sOG{(cj % 2) * 2 + 1}", t_og[(cj, 1)])
                gb0, gb1 = gbs[cj % 2]
                for b in range(8):
                    mloc = cj * 8 + b
                    pb = pt8[b % 2]
                    yb = yt8[b % 2]
                    ob = ot8[b % 2]
                    if cj > 0 or b >= 2:
                        wt("sync", "V", cnt["V"])
                    kpl = f"sPL{b % 2}"
                    t_pl = dma("sync", kpl, pb[:], xsh[mloc], pre_wait=True)
                    kyl = f"sX{b % 2}"
                    t_yl = dma("sync", kyl, yb, partial.ap()[mloc], pre_wait=True)
                    wt("vector", kpl, t_pl)
                    wt("vector", kyl, t_yl)
                    op("vector", lambda e, b=b, mloc=mloc, pb=pb, ob=ob, gb0=gb0: e.scalar_tensor_tensor(
                        out=ob[:], in0=gb0[:, b, :], scalar=R["wm1"][:, mloc:mloc + 1],
                        in1=pb[:], op0=ALU.mult, op1=ALU.add))
                    op("vector", lambda e, b=b, mloc=mloc, ob=ob, gb1=gb1: e.scalar_tensor_tensor(
                        out=ob[:], in0=gb1[:, b, :], scalar=R["wm2"][:, mloc:mloc + 1],
                        in1=ob[:], op0=ALU.mult, op1=ALU.add))
                    t_cmb = st("vector", "V", lambda e, yb=yb, ob=ob: e.tensor_tensor(
                        out=ob[:], in0=yb, in1=ob[:], op=ALU.add))
                    wt("sync", "V", t_cmb)
                    kst = f"sST{b % 2}"
                    t_out = dma("sync", kst, out[mloc], ob[:], pre_wait=True)
                    if not (cj == 3 and b == 7):
                        wt("vector", kst, t_out)
                vch[cj] = cnt["V"]
            wt("sync", "sST0", cnt["sST0"])
            wt("sync", "sST1", cnt["sST1"])
            if debug:
                wt("sync", "sDBG", cnt["sDBG"])

        except _Stop:
            pass

        with allow_ici_all_to_all(), nc.Block() as block:
            @block.sync
            def _(en):
                for f in plans["sync"]:
                    f(en)

            @block.scalar
            def _(en):
                for f in plans["scalar"]:
                    f(en)

            @block.vector
            def _(en):
                for f in plans["vector"]:
                    f(en)

            @block.tensor
            def _(en):
                for f in plans["tensor"]:
                    f(en)

            @block.gpsimd
            def _(en):
                for f in plans["gpsimd"]:
                    f(en)

    lower_extended_insts(nc)
    return nc


def prepare_in_maps(inputs):
    x = np.ascontiguousarray(np.asarray(inputs["L0_attn_x"], np.float32)).reshape(NCOL, 128, D)
    W1 = np.ascontiguousarray(np.asarray(inputs["L0_attn_W1"], np.float32))
    b1 = np.asarray(inputs["L0_attn_b1"], np.float32).reshape(1, D)
    W2 = np.asarray(inputs["L0_attn_W2"], np.float32)
    b2 = np.asarray(inputs["L0_attn_b2"], np.float32)
    W3 = np.asarray(inputs["L0_attn_W3"], np.float32)
    b3 = np.asarray(inputs["L0_attn_b3"], np.float32)
    wg = np.asarray(inputs["L0_ffn_w_gate"], np.float32)
    cfc = np.asarray(inputs["L0_ffn_c_fc"], np.float32)
    cpr = np.asarray(inputs["L0_ffn_c_proj"], np.float32)
    Wbcg = np.ascontiguousarray(np.concatenate([W2, W3, wg], axis=1))
    brow = np.ascontiguousarray(np.concatenate([b2, b3, np.zeros(8, np.float32)])).reshape(1, 40)

    in_maps = []
    for c in range(NCORE):
        m = {
            "xsh": np.ascontiguousarray(x[c::8]),
            "W1": W1, "Wbcg": Wbcg, "brow": brow, "b1row": b1,
            "cfc": np.ascontiguousarray(cfc[c]),
            "cpr": np.ascontiguousarray(cpr[c]),
        }
        for k, v in host_constants(c).items():
            m[k] = np.ascontiguousarray(v)
        in_maps.append(m)
    return in_maps


def kernel(**inputs):
    in_maps = prepare_in_maps(inputs)
    nc = build_nc(DEBUG)
    res = run_bass_kernel_spmd(nc, in_maps, core_ids=list(range(NCORE)))
    full = np.zeros((NCOL, 128, D), np.float32)
    for c in range(NCORE):
        full[c::8] = res.results[c]["out"]
    if DEBUG:
        kernel._debug = [{k: v for k, v in r.items() if k.startswith("d_")} for r in res.results]
    return full.reshape(8, 4096, 1024)


# version 11
# speedup vs baseline: 1.2715x; 1.0177x over previous
import numpy as np
import ml_dtypes
import concourse.bass as bass
import concourse.mybir as mybir
from concourse.bass_utils import run_bass_kernel_spmd
from concourse import library_config
from concourse.aot_env import allow_ici_all_to_all
from concourse.library_overlay import lower_extended_insts
from contextlib import ExitStack

F32 = mybir.dt.float32
BF16 = mybir.dt.bfloat16
I16 = mybir.dt.int16
U8 = mybir.dt.uint8
AF = mybir.ActivationFunctionType
ALU = mybir.AluOpType
AX = mybir.AxisListType

NCORE = 8
NT = 32
NCOL = 256
D = 1024
NEXP = 8
CAPN = 4096
CAP = 768
DFF = 3072
DH = DFF // 2
DUMMY = NEXP * CAP

DEBUG = False
STOP_AFTER = "full"
BFNP = ml_dtypes.bfloat16


def host_constants(core):
    c = core
    H = {}
    H["ltri128"] = np.triu(np.ones((128, 128), np.float32), 1)
    q = np.arange(128)
    H["ltri16x8"] = ((q[:, None] % 8 == q[None, :] % 8)
                     & (q[:, None] // 8 < q[None, :] // 8)).astype(np.float32)
    H["ident_bf"] = np.eye(128, dtype=np.float32).astype(BFNP)
    H["ident_f32"] = np.eye(128, dtype=np.float32)
    H["onescol"] = np.ones((128, 1), np.float32)
    H["ones1_bf"] = np.ones((1, 128), np.float32).astype(BFNP)
    o16p = np.zeros((32, 128), np.float32); o16p[0:16, :] = 1.0
    H["ones16p"] = o16p
    H["ones64"] = np.ones((64, 128), np.float32)
    H["ones128sq"] = np.ones((128, 128), np.float32)
    so = np.zeros((128, 32), np.float32)
    for half in range(2):
        for e in range(8):
            so[((c + 8 * half) % 16) * 8 + e, half * 8 + e] = 1.0
    H["selown16"] = so
    m16 = np.zeros((16, 256), np.float32)
    for mloc in range(32):
        for e in range(8):
            m16[(mloc % 2) * 8 + e, mloc * 8 + e] = 1.0
    H["m16pat"] = m16
    lt16 = np.zeros((32, 32), np.float32)
    lt16[:16, :16] = np.triu(np.ones((16, 16), np.float32), 1)
    H["ltri16p"] = lt16
    m16b = np.zeros((16, 256), np.float32)
    for mloc in range(32):
        for e in range(8):
            m16b[mloc // 2, mloc * 8 + e] = 1.0
    H["m16bpat"] = m16b
    m8t = (np.arange(128)[None, :] % 8 == np.arange(8)[:, None]).astype(np.float32)
    H["mod8tp"] = np.concatenate([m8t, np.zeros((24, 128), np.float32)], 0)
    H["mod8p"] = np.concatenate([m8t.T, np.zeros((128, 24), np.float32)], 1)
    sc8 = np.zeros((128, 64), np.float32)
    for qq in range(128):
        sc8[qq, ((qq // 8) % 8) * 8 + (qq % 8)] = 1.0
    H["selc8"] = sc8
    cm = np.arange(64)
    ce_c, ce_e = cm // 8, cm % 8
    H["mask64x"] = ((ce_e[:, None] == ce_e[None, :]) & (ce_c[:, None] < ce_c[None, :])).astype(np.float32)
    H["mask64i"] = ((ce_e[:, None] == ce_e[None, :]) & (ce_c[:, None] <= ce_c[None, :])).astype(np.float32)
    H["i64"] = np.eye(64, dtype=np.float32)
    H["memask"] = np.tile((np.arange(64) % 8 == c).astype(np.float32)[None, :], (128, 1))
    H["memask_tot"] = np.tile((np.arange(64) == 56 + c).astype(np.float32)[None, :], (128, 1))
    mp = np.zeros((128, 256), np.float32)
    for mloc in range(32):
        for e in range(8):
            mp[(mloc % 16) * 8 + e, mloc * 8 + e] = 1.0
    H["m128pat"] = mp
    for g in range(8):
        fs = np.zeros((128, 128), np.float32)
        for mrow in range(128):
            fs[16 * g + (mrow % 16), mrow] = 1.0
        H[f"foldsel{g}"] = fs
    H["siota"] = (np.arange(32)[None, :] * 128 + np.arange(128)[:, None]).astype(np.float32)
    H["dumoff"] = np.tile((np.arange(32) // 8).astype(np.float32)[None, :], (128, 1))
    u32 = np.zeros((32, 16), np.float32)
    u32[:16, :] = (np.arange(16)[:, None] < np.arange(16)[None, :]).astype(np.float32)
    H["ultri32"] = u32
    return H




def build_nc(debug=DEBUG, stop_after=None):
    stop_after = stop_after or STOP_AFTER
    nc = bass.Bass("TRN2", target_bir_lowering=False)
    AP_IN = {}

    def din(name, shape, dtype=F32):
        AP_IN[name] = nc.dram_tensor(name, list(shape), dtype, kind="ExternalInput").ap()
        return AP_IN[name]

    xsh = din("xsh", (NT, 128, D))
    W1 = din("W1", (D, D))
    Wbcg = din("Wbcg", (D, 40))
    brow = din("brow", (1, 40))
    b1row = din("b1row", (1, D))
    cfc = din("cfc", (D, DFF))
    cpr = din("cpr", (DH, D))
    consts = host_constants(0)
    for k, v in consts.items():
        din(k, v.shape, BF16 if v.dtype == BFNP else F32)

    partial = nc.dram_tensor("partial", [NT, 128, D], BF16, kind="Internal")
    send1 = nc.dram_tensor("send1", [DUMMY + 8, D], BF16, kind="Internal")
    recv1 = nc.dram_tensor("recv1", [DUMMY + 8, D], BF16, kind="Internal")
    send2 = nc.dram_tensor("send2", [DUMMY + 8, D], BF16, kind="Internal")
    recv2 = nc.dram_tensor("recv2", [DUMMY + 8, D], BF16, kind="Internal")
    ag_in = nc.dram_tensor("ag_in", [128, 128], F32, kind="Internal")
    ag_out = nc.dram_tensor("ag_out", [NCORE * 128, 128], F32, kind="Internal")

    out = nc.dram_tensor("out", [NT, 128, D], F32, kind="ExternalOutput").ap()
    dbg = {}
    if debug:
        for nm, sh in [("d_r1", (128, 128)), ("d_r2", (128, 128)),
                       ("d_r3", (128, 128)), ("d_r4", (128, 96))]:
            dbg[nm] = nc.dram_tensor(nm, list(sh), F32, kind="ExternalOutput").ap()

    with ExitStack() as ctx:
        e_ = ctx.enter_context

        def sb(name, shape, dtype):
            return e_(nc.sbuf_tensor(name, list(shape), dtype))

        CT = {k: sb("c_" + k, v.shape, BF16 if v.dtype == BFNP else F32)
              for k, v in consts.items()}
        xt = [sb("xt0", (128, D), F32), sb("xt1", (128, D), F32)]
        sa = None
        s0 = sb("s0", (128, D), F32)
        s1 = sb("s1", (128, D), F32)
        u1 = sb("u1", (128, 4096), BF16)
        s2 = u1[:, 0:2048].bitcast(F32)
        fmt = [u1[:, 2048:3072], u1[:, 3072:4096]]
        lnst = sb("lnst", (128, 16), F32)
        mnst = sb("mnst", (128, NT), F32)
        rsst = sb("rsst", (128, NT), F32)
        svst = sb("svst", (128, NT), F32)
        bc16 = sb("bc16", (128, 16), F32)
        bc40 = sb("bc40", (128, 40), F32)
        xn_tm = sb("xn_tm", (128, NT * D), BF16)
        wbcg_sb = sb("wbcg_sb", (128, 8 * 40), BF16)
        brow_sb = sb("brow_sb", (1, 40), BF16)
        b1_sb = sb("b1_sb", (1, D), BF16)
        logits = sb("logits", (128, NT * 8), F32)
        R = {k: sb("r_" + k, (128, NT), F32) for k in
             ["i1", "i2", "w1", "w2", "m1", "m2", "den", "v1", "v2",
              "vr1", "vr2", "sx1", "sx2", "wm1", "wm2", "slotv", "csum",
              "acc", "t1", "t2"]}
        tmp8 = sb("tmp8", (128, NT * 8), F32)
        ex8 = sb("ex8", (128, NT * 8), F32)
        cbig = sb("cbig", (128, NT * 8), F32)
        eio8 = sb("eio8", (128, NT * 8), F32)
        pk = sb("pk", (128, 128), F32)
        idx1g = sb("idx1g", (128, NCOL), F32)
        idx2g = sb("idx2g", (128, NCOL), F32)

        t16 = sb("t16", (128, 16), F32)
        a1 = sb("a1", (128, 16), F32)
        g16 = sb("g16", (128, 32), F32)
        g16t = sb("g16t", (32, 16), F32)
        g16x = sb("g16x", (128, 32), F32)
        g16xT = sb("g16xT", (32, 16), F32)
        o16 = sb("o16", (16, 16), F32)
        rhsA = sb("rhsA", (32, 256), F32)
        rhsB = sb("rhsB", (32, 256), F32)
        ownoff = sb("ownoff", (128, 256), F32)
        offgT = sb("offgT", (128, 16), F32)
        keptT = sb("keptT", (128, 16), F32)
        kscr = sb("kscr", (128, 16), F32)
        vc64 = sb("vc64", (64, 16), F32)
        vc64r = sb("vc64r", (64, 1), F32)
        cum64 = sb("cum64", (64, 1), F32)
        inc64 = sb("inc64", (64, 1), F32)

        cumB = pk[:, 0:64]
        incB = pk[:, 64:128]
        scr64 = sb("scr64", (128, 64), F32)
        mycum = sb("mycum", (128, 8), F32)
        myinc = sb("myinc", (128, 8), F32)
        myvc = sb("myvc", (128, 8), F32)
        mytot = sb("mytot", (128, 1), F32)
        oh1 = sb("oh1", (128, NT * 8), F32)
        oh2 = idx2g
        ohs = sb("ohs", (128, NT * 8), F32)
        ecsU = logits
        ecsV = eio8
        ohv1 = tmp8
        ohv2 = ex8
        selbuf = cbig
        csrowV = idx1g
        a2t = sb("a2t", (128, 2), F32)
        a2x = sb("a2x", (128, 2), F32)
        tot0e = sb("tot0e", (32, 2), F32)
        tot0b = sb("tot0b", (128, 1), F32)
        rhs3 = ownoff
        offs2B = sb("offs2B", (128, 256), F32)
        Lk0 = sb("Lk0", (128, 256), I16)
        Lk1 = sb("Lk1", (128, 256), I16)
        Lslot = sb("Lslot", (128, 256), I16)
        cfc_sb = sb("cfc_sb", (128, 8 * DFF), BF16)
        w1_sb = cfc_sb.tensor if hasattr(cfc_sb, "tensor") else cfc_sb
        w1_sb = cfc_sb
        cpr_sb = sb("cpr_sb", (128, 12 * D), BF16)
        G = cpr_sb[:, 0:4096].bitcast(F32)
        srow = cpr_sb[:, 4096:8192].bitcast(F32)
        hh = sb("hh", (128, 12 * 512), BF16)
        expo = u1
        sa = xt[0][:, 0:512]

        zeros_bf = hh
        pt8 = xt
        ot8 = [s0, s1]
        y_bf = s1[:, 0:512].bitcast(BF16)
        yt8 = [hh[:, 0:1024], hh[:, 1024:2048]]

        psumP = e_(nc.psum_tensor("psumP", [128, 2048], F32))
        p_z = e_(nc.psum_tensor("p_z", [128, D], F32))
        p_bc = e_(nc.psum_tensor("p_bc", [128, 64], F32))
        p_tr = psumP[:, 0:1024]
        p_trbs = [psumP[:, 0:512].bitcast(BF16), psumP[:, 512:1024].bitcast(BF16)]
        p_zs = [p_z[:, 0:1024], psumP[:, 1024:2048]]
        p_bcs = [p_bc[:, 0:40], p_bc[:, 0:40]]
        p_row = psumP[0:1, 0:2048]

        sem = {nm: e_(nc.semaphore(nm)) for nm in
               ["sC", "sX0", "sX1", "sP0", "sP1", "sW1", "sWF", "sAG", "sSC",
                "sEG0", "sEG1", "sEO0", "sEO1", "sEO2", "sEO3", "sOG0", "sOG1",
                "sPL0", "sPL1", "sST0", "sST1", "sZ", "sDBG", "T", "V", "A", "CC"]}
        cnt = {k: 0 for k in sem}
        plans = {k: [] for k in ["sync", "scalar", "vector", "tensor", "gpsimd"]}
        off = [False]

        def emit(eng, fn):
            if off[0]:
                return
            plans[eng].append(fn)

        def bump(key, n=1):
            cnt[key] += n
            return cnt[key]

        def wt(eng, key, val):
            emit(eng, lambda e, k=key, v=val: e.wait_ge(sem[k], v))

        def dma(eng, key, out_ap, in_ap, pre_wait=False):
            pre = cnt[key]
            if pre_wait and pre > 0:
                wt(eng, key, pre)
            tgt = bump(key, 16)
            emit(eng, lambda e, o=out_ap, i=in_ap, k=key: e.dma_start(out=o, in_=i).then_inc(sem[k], 16))
            return tgt

        def st(eng, key, fn):
            if eng in ("vector", "scalar"):
                return op(eng, fn)
            tgt = bump(key, 1)
            emit(eng, lambda e, f=fn, k=key: f(e).then_inc(sem[k], 1))
            return tgt

        def op(eng, fn):
            if eng in ("vector", "scalar"):
                key = "V" if eng == "vector" else "A"
                pre = cnt[key]
                tgt = bump(key, 1)

                def run(e, f=fn, pre=pre, key=key):
                    ins = f(e).then_inc(sem[key], 1)
                    if pre > 0:
                        ins._wait_ge(sem[key], pre)
                emit(eng, run)
                return tgt
            emit(eng, lambda e, f=fn: f(e))
            return None

        def r3(ap, inner=8):
            return ap.rearrange("p (c e) -> p c e", e=inner)

        def rcol(ap):
            return ap.rearrange("p (c o) -> p c o", o=1)

        for k in consts:
            dma("sync", "sC", CT[k][:], AP_IN[k])
        t_const = cnt["sC"]
        op("gpsimd", lambda e: e.load_library(library_config.mlp))
        for j in range(8):
            dma("gpsimd", "sW1", w1_sb[:, j * D:(j + 1) * D], W1[j * 128:(j + 1) * 128, :])
        dma("gpsimd", "sW1", wbcg_sb[:].rearrange("p (a b) -> p a b", a=8),
            Wbcg.rearrange("(a p) b -> p a b", p=128))
        dma("gpsimd", "sW1", brow_sb[:], brow)
        dma("gpsimd", "sW1", b1_sb[:], b1row)
        t_wsmall = cnt["sW1"]

        z_v = st("vector", "V", lambda e: e.memset(zeros_bf[:, 0:2048], 0.0))
        wt("scalar", "V", z_v)
        for base in range(0, DUMMY, 256):
            dma("scalar", "sZ", send1.ap()[base:base + 256, :].rearrange("(a p) d -> p a d", p=128),
                zeros_bf[:, 0:2048].rearrange("p (a d) -> p a d", d=D))
            dma("scalar", "sZ", send2.ap()[base:base + 256, :].rearrange("(a p) d -> p a d", p=128),
                zeros_bf[:, 0:2048].rearrange("p (a d) -> p a d", d=D))
        dma("scalar", "sZ", send1.ap()[DUMMY:DUMMY + 1, :], zeros_bf[0:1, 0:1024])
        dma("scalar", "sZ", send2.ap()[DUMMY:DUMMY + 1, :], zeros_bf[0:1, 0:1024])
        dma("scalar", "sZ", recv2.ap()[DUMMY:DUMMY + 8, :], zeros_bf[0:8, 0:1024])
        t_zero = cnt["sZ"]
        for e8 in range(8):
            op("vector", lambda e, i=e8: e.memset(
                r3(eio8[:])[:, :, i:i + 1], float(i)))
        op("vector", lambda e: e.memset(g16[:], 0.0))
        op("vector", lambda e: e.memset(g16t[:], 0.0))
        op("vector", lambda e: e.memset(g16x[:], 0.0))
        op("vector", lambda e: e.memset(g16xT[:], 0.0))
        op("vector", lambda e: e.memset(rhsA[:], 0.0))
        op("vector", lambda e: e.memset(rhsB[:], 0.0))
        op("vector", lambda e: e.memset(tot0e[:], 0.0))
        op("vector", lambda e: e.memset(Lk0[:], 0))
        op("vector", lambda e: e.memset(Lk1[:], 0))
        op("vector", lambda e: e.memset(Lslot[:], 0))

        wt("tensor", "sC", t_const)
        wt("tensor", "sW1", t_wsmall)
        wt("vector", "sC", t_const)
        t_xnb = [0] * NT
        t_evac = [0] * NT
        t_bcv = [0] * NT

        t_red2 = [0] * NT

        def stageA_early(m):
            xb = xt[m % 2]
            if m >= 2:
                wt("sync", "A", t_xnb[m - 2])
            kx = f"sX{m % 2}"
            t_x = dma("sync", kx, xb[:], xsh[m], pre_wait=True)
            L = lnst[:, (m % 2) * 8:(m % 2) * 8 + 8]
            wt("vector", kx, t_x)
            op("vector", lambda e, xb=xb, L=L: e.reduce_sum(
                out=L[:, 0:1].rearrange("p (a b) -> p a b", b=1),
                in_=xb[:].rearrange("p (a b) -> p a b", a=1), axis=AX.X))
            if m >= 1:
                wt("scalar", "V", t_red2[m - 1])
            wt("scalar", kx, t_x)
            t_sq = st("scalar", "A", lambda e, xb=xb: e.activation(out=s1[:], in_=xb[:], func=AF.Square))
            wt("vector", "A", t_sq)
            t_red2[m] = st("vector", "V", lambda e, L=L: e.reduce_sum(
                out=L[:, 1:2].rearrange("p (a b) -> p a b", b=1),
                in_=s1[:].rearrange("p (a b) -> p a b", a=1), axis=AX.X))
            op("vector", lambda e, L=L, m=m: e.tensor_scalar(out=mnst[:, m:m + 1], in0=L[:, 0:1],
                                                             scalar1=1.0 / D, scalar2=None, op0=ALU.mult))
            op("vector", lambda e, L=L: e.tensor_scalar(out=L[:, 3:4], in0=L[:, 1:2],
                                                        scalar1=1.0 / D, scalar2=None, op0=ALU.mult))
            op("vector", lambda e, L=L, m=m: e.tensor_tensor(out=L[:, 4:5], in0=mnst[:, m:m + 1],
                                                             in1=mnst[:, m:m + 1], op=ALU.mult))
            t_var = st("vector", "V", lambda e, L=L: e.scalar_tensor_tensor(
                out=L[:, 5:6], in0=L[:, 3:4], scalar=1e-5,
                in1=L[:, 4:5], op0=ALU.add, op1=ALU.subtract))
            wt("scalar", "V", t_var)
            op("scalar", lambda e, L=L: e.activation(out=L[:, 6:7], in_=L[:, 5:6], func=AF.Ln))
            op("scalar", lambda e, L=L: e.mul(out=L[:, 6:7], in_=L[:, 6:7], mul=-0.5))
            t_rstd = st("scalar", "A", lambda e, L=L, m=m: e.activation(
                out=rsst[:, m:m + 1], in_=L[:, 6:7], func=AF.Exp))
            wt("vector", "A", t_rstd)
            t_nb = st("vector", "V", lambda e, L=L, m=m: e.scalar_tensor_tensor(
                out=L[:, 7:8], in0=mnst[:, m:m + 1], scalar=-1.0,
                in1=rsst[:, m:m + 1], op0=ALU.mult, op1=ALU.mult))
            wt("scalar", "V", t_nb)
            t_xnb[m] = st("scalar", "A", lambda e, m=m, xb=xb, L=L: e.activation(
                out=xn_tm[:, m * D:(m + 1) * D], in_=xb[:], func=AF.Identity,
                bias=L[:, 7:8], scale=rsst[:, m:m + 1]))

        def stageA_late(m):
            fm = fmt[m % 2]
            ptr = p_trbs[m % 2]
            wt("tensor", "A", t_xnb[m])
            if m >= 2:
                wt("tensor", "V", t_evac[m - 2])
            t_tr = None
            for j in range(8):
                fn = (lambda e, m=m, j=j, ptr=ptr: e.transpose(
                    ptr[:, j * 128:(j + 1) * 128],
                    xn_tm[:, m * D + j * 128:m * D + (j + 1) * 128],
                    CT["ident_bf"][:]))
                if j == 7:
                    t_tr = st("tensor", "T", fn)
                else:
                    op("tensor", fn)
            wt("vector", "T", t_tr)
            t_evac[m] = st("vector", "V", lambda e, fm=fm, ptr=ptr: e.tensor_copy(out=fm, in_=ptr[:]))
            wt("tensor", "V", t_evac[m])
            if m >= 2:
                wt("tensor", "V", t_bcv[m - 2])
            pb = p_bcs[m % 2]
            for j in range(8):
                op("tensor", lambda e, fm=fm, j=j, pb=pb: e.matmul(
                    pb, fm[:, j * 128:(j + 1) * 128],
                    wbcg_sb[:, j * 40:(j + 1) * 40], start=(j == 0), stop=False))
            t_bc = st("tensor", "T", lambda e, pb=pb: e.matmul(
                pb, CT["ones1_bf"][:], brow_sb[:], start=False, stop=True))
            wt("vector", "T", t_bc)
            op("vector", lambda e, pb=pb: e.tensor_copy(out=bc40[:], in_=pb))
            op("vector", lambda e: e.tensor_tensor(out=bc16[:], in0=bc40[:, 0:16],
                                                   in1=bc40[:, 16:32], op=ALU.mult))
            op("vector", lambda e, m=m: e.reduce_sum(
                out=svst[:, m:m + 1].rearrange("p (a b) -> p a b", b=1),
                in_=bc16[:].rearrange("p (a b) -> p a b", a=1), axis=AX.X))
            t_bcv[m] = st("vector", "V", lambda e, m=m: e.tensor_copy(
                out=logits[:, m * 8:(m + 1) * 8], in_=bc40[:, 32:40]))

        for m in range(NT):
            stageA_early(m)
            if m >= 1:
                stageA_late(m - 1)
        stageA_late(NT - 1)
        t_xn_last = t_xnb[NT - 1]

        if stop_after == "p1a":
            early_out()
            off[0] = True
        z3 = r3(logits[:])
        eio3 = r3(eio8[:])
        op("vector", lambda e: e.reduce_max(out=rcol(R["m1"][:]), in_=z3, axis=AX.X))
        op("vector", lambda e: e.tensor_tensor(
            out=r3(ohs[:]), in0=z3,
            in1=rcol(R["m1"][:]).broadcast_to((128, NT, 8)), op=ALU.is_equal))
        op("vector", lambda e: e.tensor_tensor(out=tmp8[:], in0=ohs[:], in1=eio8[:], op=ALU.mult))
        op("vector", lambda e: e.scalar_tensor_tensor(out=tmp8[:], in0=ohs[:], scalar=-99.0,
                                                      in1=tmp8[:], op0=ALU.mult, op1=ALU.add))
        op("vector", lambda e: e.tensor_scalar(out=tmp8[:], in0=tmp8[:], scalar1=99.0,
                                               scalar2=None, op0=ALU.add))
        op("vector", lambda e: e.tensor_reduce(out=rcol(R["i1"][:]), in_=r3(tmp8[:]),
                                               axis=AX.X, op=ALU.min))
        op("vector", lambda e: e.tensor_tensor(
            out=r3(ohs[:]), in0=eio3,
            in1=rcol(R["i1"][:]).broadcast_to((128, NT, 8)), op=ALU.is_equal))
        op("vector", lambda e: e.scalar_tensor_tensor(out=tmp8[:], in0=ohs[:], scalar=-1e30,
                                                      in1=logits[:], op0=ALU.mult, op1=ALU.add))
        op("vector", lambda e: e.tensor_reduce(out=rcol(R["m2"][:]), in_=r3(tmp8[:]),
                                               axis=AX.X, op=ALU.max))
        op("vector", lambda e: e.tensor_tensor(
            out=r3(ohs[:]), in0=r3(tmp8[:]),
            in1=rcol(R["m2"][:]).broadcast_to((128, NT, 8)), op=ALU.is_equal))
        op("vector", lambda e: e.tensor_tensor(out=tmp8[:], in0=ohs[:], in1=eio8[:], op=ALU.mult))
        op("vector", lambda e: e.scalar_tensor_tensor(out=tmp8[:], in0=ohs[:], scalar=-99.0,
                                                      in1=tmp8[:], op0=ALU.mult, op1=ALU.add))
        op("vector", lambda e: e.tensor_scalar(out=tmp8[:], in0=tmp8[:], scalar1=99.0,
                                               scalar2=None, op0=ALU.add))
        op("vector", lambda e: e.tensor_reduce(out=rcol(R["i2"][:]), in_=r3(tmp8[:]),
                                               axis=AX.X, op=ALU.min))
        t_sm0 = st("vector", "V", lambda e: e.tensor_tensor(
            out=r3(tmp8[:]), in0=z3,
            in1=rcol(R["m1"][:]).broadcast_to((128, NT, 8)), op=ALU.subtract))
        wt("scalar", "V", t_sm0)
        t_ex = st("scalar", "A", lambda e: e.activation(out=ex8[:], in_=tmp8[:], func=AF.Exp))
        wt("vector", "A", t_ex)
        op("vector", lambda e: e.tensor_reduce(out=rcol(R["den"][:]), in_=r3(ex8[:]),
                                               axis=AX.X, op=ALU.add))
        op("vector", lambda e: e.reciprocal(out=R["w1"][:], in_=R["den"][:]))
        t_d2 = st("vector", "V", lambda e: e.tensor_sub(out=R["m2"][:], in0=R["m2"][:], in1=R["m1"][:]))
        wt("scalar", "V", t_d2)
        t_e2 = st("scalar", "A", lambda e: e.activation(out=R["den"][:], in_=R["m2"][:], func=AF.Exp))
        wt("vector", "A", t_e2)
        op("vector", lambda e: e.tensor_mul(out=R["w2"][:], in0=R["den"][:], in1=R["w1"][:]))
        t_pk = None
        for i, k in enumerate(["i1", "i2", "w1", "w2"]):
            fn = (lambda e, i=i, k=k: e.tensor_copy(out=pk[:, i * 32:(i + 1) * 32], in_=R[k][:]))
            if i == 3:
                t_pk = st("vector", "V", fn)
            else:
                op("vector", fn)
        wt("gpsimd", "V", t_pk)
        t_agin = dma("gpsimd", "sAG", ag_in.ap(), pk[:])
        wt("gpsimd", "sAG", t_agin)
        t_ag = st("gpsimd", "CC", lambda e: e.collective_compute(
            "AllGather", ALU.bypass, replica_groups=[list(range(NCORE))],
            ins=[ag_in.ap().opt()], outs=[ag_out.ap().opt()]))
        wt("gpsimd", "CC", t_ag)
        for cc in range(NCORE):
            dma("gpsimd", "sAG", tmp8[:, cc * 32:(cc + 1) * 32],
                ag_out.ap()[cc * 128:(cc + 1) * 128, 0:32])
            t_idxg = dma("gpsimd", "sAG", ex8[:, cc * 32:(cc + 1) * 32],
                         ag_out.ap()[cc * 128:(cc + 1) * 128, 32:64])
        wt("vector", "sAG", t_idxg)
        op("vector", lambda e: e.tensor_copy(
            out=idx1g[:].rearrange("p (f cc) -> p f cc", cc=8),
            in_=tmp8[:].rearrange("p (cc f) -> p f cc", cc=8)))
        op("vector", lambda e: e.tensor_copy(
            out=idx2g[:].rearrange("p (f cc) -> p f cc", cc=8),
            in_=ex8[:].rearrange("p (cc f) -> p f cc", cc=8)))

        if stop_after == "p1b":
            early_out()
            off[0] = True
        for e8 in range(8):
            op("vector", lambda e, i=e8: e.tensor_scalar(
                out=r3(G, 8)[:, :, i:i + 1],
                in0=idx1g[:].rearrange("p (c o) -> p c o", o=1),
                scalar1=float(i), scalar2=None, op0=ALU.is_equal))
            op("vector", lambda e, i=e8: e.tensor_scalar(
                out=r3(srow, 8)[:, :, i:i + 1],
                in0=idx2g[:].rearrange("p (c o) -> p c o", o=1),
                scalar1=float(i), scalar2=None, op0=ALU.is_equal))
        t_G = st("vector", "V", lambda e: e.tensor_add(out=G, in0=G, in1=srow))
        if stop_after == "p3a":
            early_out(); off[0] = True
        t_sz = st("vector", "V", lambda e: e.memset(srow, 0.0))
        wt("tensor", "V", t_sz)
        t_cs = None
        for qq in range(4):
            fn = (lambda e, q=qq: e.matmul(p_row[:, q * 512:(q + 1) * 512], CT["onescol"][:],
                                           G[:, q * 512:(q + 1) * 512], start=True, stop=True))
            if qq == 3:
                t_cs = st("tensor", "T", fn)
            else:
                op("tensor", fn)
        wt("vector", "T", t_cs)
        t_srow = st("vector", "V", lambda e: e.tensor_copy(out=srow[0:1, :], in_=p_row[:]))
        wt("tensor", "V", t_srow)
        for rnd in range(2):
            t_trr = None
            for j8 in range(8):
                j = rnd * 8 + j8
                fn = (lambda e, j=j, j8=j8: e.matmul(
                    psumP[:, 1024 + j8 * 128:1024 + (j8 + 1) * 128],
                    srow[:, j * 128:(j + 1) * 128], CT["ident_f32"][:],
                    start=True, stop=True))
                if j8 == 7:
                    t_trr = st("tensor", "T", fn)
                else:
                    op("tensor", fn)
            wt("vector", "T", t_trr)
            t_t16 = None
            for j8 in range(8):
                j = rnd * 8 + j8
                fn = (lambda e, j=j, j8=j8: e.tensor_copy(
                    out=t16[:, j:j + 1], in_=psumP[:, 1024 + j8 * 128:1024 + j8 * 128 + 1]))
                if j8 == 7:
                    t_t16 = st("vector", "V", fn)
                else:
                    op("vector", fn)
            wt("tensor", "V", t_t16)
        if stop_after == "p3b":
            early_out(); off[0] = True
        op("tensor", lambda e: e.matmul(psumP[:, 1024:1040], CT["ltri16x8"][:], t16[:],
                                        start=True, stop=True))
        t_mm1 = st("tensor", "T", lambda e: e.matmul(psumP[0:32, 1040:1056], CT["mod8p"][:], t16[:],
                                                     start=True, stop=True))
        wt("vector", "T", t_mm1)
        op("vector", lambda e: e.tensor_copy(out=a1[:], in_=psumP[:, 1024:1040]))
        t_g16 = st("vector", "V", lambda e: e.tensor_copy(out=g16[0:8, 0:16], in_=psumP[0:8, 1040:1056]))
        wt("tensor", "V", t_g16)
        t_mm2 = st("tensor", "T", lambda e: e.matmul(psumP[0:32, 1056:1184], g16[:, 0:16].bitcast(F32) if False else g16[:], CT["ident_f32"][:], start=True, stop=True))
        wt("vector", "T", t_mm2)
        t_g16t = st("vector", "V", lambda e: e.tensor_copy(out=g16t[0:16, :], in_=psumP[0:16, 1056:1072]))
        wt("tensor", "V", t_g16t)
        t_mm3 = st("tensor", "T", lambda e: e.matmul(psumP[0:32, 1184:1192], CT["ltri16p"][:],
                                                     g16t[:, 0:8], start=True, stop=True))
        wt("vector", "T", t_mm3)
        t_g16x = st("vector", "V", lambda e: e.tensor_copy(out=g16x[0:16, 0:8], in_=psumP[0:16, 1184:1192]))
        if stop_after == "p3b2":
            early_out(); off[0] = True
        wt("tensor", "V", t_g16x)
        t_mm4 = st("tensor", "T", lambda e: e.matmul(psumP[0:32, 1192:1208], CT["selown16"][:],
                                                     a1[:], start=True, stop=True))
        wt("vector", "T", t_mm4)
        t_o16 = st("vector", "V", lambda e: e.tensor_copy(out=o16[:], in_=psumP[0:16, 1192:1208]))
        op("vector", lambda e: e.tensor_tensor(
            out=rhsA[0:16, :].rearrange("r (j h ee) -> r j h ee", h=2, ee=8),
            in0=o16[:].rearrange("r (j h ee) -> r j h ee", h=1, ee=1).broadcast_to((16, 16, 2, 8)),
            in1=CT["m16pat"][:].rearrange("r (j h ee) -> r j h ee", h=2, ee=8), op=ALU.mult))
        op("vector", lambda e: e.tensor_tensor(
            out=rhsB[0:16, :].rearrange("r (j h ee) -> r j h ee", h=2, ee=8),
            in0=g16x[0:16, 0:8].rearrange("r (j h ee) -> r j h ee", j=1, h=1).broadcast_to((16, 16, 2, 8)),
            in1=CT["m16bpat"][:].rearrange("r (j h ee) -> r j h ee", h=2, ee=8), op=ALU.mult))
        t_rhs = st("vector", "V", lambda e: e.tensor_add(out=rhsA[:], in0=rhsA[:], in1=rhsB[:]))
        wt("tensor", "V", t_rhs)
        t_mm5 = st("tensor", "T", lambda e: e.matmul(psumP[:, 1024:1280], CT["ones16p"][:],
                                                     rhsA[:], start=True, stop=True))
        wt("vector", "T", t_mm5)
        t_ownoff = st("vector", "V", lambda e: e.tensor_copy(out=ownoff[:], in_=psumP[:, 1024:1280]))
        if stop_after == "p3b3":
            early_out(); off[0] = True
        wt("tensor", "V", t_g16x)
        t_mm6 = st("tensor", "T", lambda e: e.matmul(psumP[0:16, 1280:1296], g16t[:],
                                                     CT["ultri32"][:], start=True, stop=True))
        wt("vector", "T", t_mm6)
        t_gxt = st("vector", "V", lambda e: e.tensor_copy(out=g16xT[0:8, :], in_=psumP[0:8, 1280:1296]))
        if stop_after == "p3b4a":
            early_out(); off[0] = True
        wt("tensor", "V", t_gxt)
        t_mm7 = st("tensor", "T", lambda e: e.matmul(psumP[:, 1408:1424], CT["mod8tp"][:],
                                                     g16xT[:], start=True, stop=True))
        wt("vector", "T", t_mm7)
        op("vector", lambda e: e.tensor_copy(out=offgT[:], in_=psumP[:, 1408:1424]))
        if stop_after == "p3b4b":
            early_out(); off[0] = True
        op("vector", lambda e: e.tensor_add(out=kscr[:], in0=a1[:], in1=offgT[:]))
        op("vector", lambda e: e.tensor_scalar(out=kscr[:], in0=kscr[:], scalar1=-1.0,
                                               scalar2=float(CAPN), op0=ALU.mult, op1=ALU.add))
        op("vector", lambda e: e.tensor_scalar(out=kscr[:], in0=kscr[:], scalar1=0.0,
                                               scalar2=None, op0=ALU.max))
        t_kept = st("vector", "V", lambda e: e.tensor_tensor(out=keptT[:], in0=kscr[:],
                                                             in1=t16[:], op=ALU.min))
        if stop_after == "p3b4":
            early_out(); off[0] = True
        wt("tensor", "V", t_kept)
        t_mm8 = st("tensor", "T", lambda e: e.matmul(psumP[0:64, 1424:1440], CT["selc8"][:],
                                                     keptT[:], start=True, stop=True))
        wt("vector", "T", t_mm8)
        op("vector", lambda e: e.tensor_copy(out=vc64[:], in_=psumP[0:64, 1424:1440]))
        t_vcr = st("vector", "V", lambda e: e.tensor_reduce(
            out=vc64r[:].rearrange("p (a b) -> p a b", b=1),
            in_=vc64[:].rearrange("p (a b) -> p a b", a=1), axis=AX.X, op=ALU.add))
        wt("tensor", "V", t_vcr)
        op("tensor", lambda e: e.matmul(psumP[0:64, 1440:1441], CT["mask64x"][:],
                                        vc64r[:], start=True, stop=True))
        t_mm9 = st("tensor", "T", lambda e: e.matmul(psumP[0:64, 1441:1442], CT["mask64i"][:],
                                                     vc64r[:], start=True, stop=True))
        wt("vector", "T", t_mm9)
        op("vector", lambda e: e.tensor_copy(out=cum64[:], in_=psumP[0:64, 1440:1441]))
        t_c64 = st("vector", "V", lambda e: e.tensor_copy(out=inc64[:], in_=psumP[0:64, 1441:1442]))
        if stop_after == "p3b5":
            early_out(); off[0] = True
        t_dg1 = st("vector", "V", lambda e: e.tensor_scalar(out=scr64[0:64, :], in0=CT["i64"][:],
                                                            scalar1=cum64[:, 0:1], scalar2=None, op0=ALU.mult))
        wt("tensor", "V", t_dg1)
        t_mmA = st("tensor", "T", lambda e: e.matmul(psumP[:, 1442:1506], CT["ones64"][:],
                                                     scr64[0:64, :], start=True, stop=True))
        wt("vector", "T", t_mmA)
        t_cb = st("vector", "V", lambda e: e.tensor_copy(out=cumB[:], in_=psumP[:, 1442:1506]))
        t_dg2 = st("vector", "V", lambda e: e.tensor_scalar(out=scr64[0:64, :], in0=CT["i64"][:],
                                                            scalar1=inc64[:, 0:1], scalar2=None, op0=ALU.mult))
        wt("tensor", "V", t_dg2)
        t_mmB = st("tensor", "T", lambda e: e.matmul(psumP[:, 1536:1600], CT["ones64"][:],
                                                     scr64[0:64, :], start=True, stop=True))
        wt("vector", "T", t_mmB)
        op("vector", lambda e: e.tensor_copy(out=incB[:], in_=psumP[:, 1536:1600]))
        op("vector", lambda e: e.tensor_tensor(out=scr64[:], in0=cumB[:], in1=CT["memask"][:], op=ALU.mult))
        op("vector", lambda e: e.tensor_reduce(
            out=mycum[:].rearrange("p (c o) -> p c o", o=1),
            in_=scr64[:].rearrange("p (c e) -> p c e", e=8), axis=AX.X, op=ALU.add))
        op("vector", lambda e: e.tensor_tensor(out=scr64[:], in0=incB[:], in1=CT["memask"][:], op=ALU.mult))
        op("vector", lambda e: e.tensor_reduce(
            out=myinc[:].rearrange("p (c o) -> p c o", o=1),
            in_=scr64[:].rearrange("p (c e) -> p c e", e=8), axis=AX.X, op=ALU.add))
        op("vector", lambda e: e.tensor_sub(out=myvc[:], in0=myinc[:], in1=mycum[:]))
        op("vector", lambda e: e.tensor_tensor(out=scr64[:], in0=incB[:], in1=CT["memask_tot"][:], op=ALU.mult))
        op("vector", lambda e: e.tensor_reduce(
            out=mytot[:].rearrange("p (a b) -> p a b", b=1),
            in_=scr64[:].rearrange("p (a b) -> p a b", a=1), axis=AX.X, op=ALU.add))
        if stop_after == "p3c":
            early_out(); off[0] = True
        for e8 in range(8):
            op("vector", lambda e, i=e8: e.tensor_scalar(
                out=r3(oh1[:])[:, :, i:i + 1], in0=rcol(R["i1"][:]),
                scalar1=float(i), scalar2=None, op0=ALU.is_equal))
            op("vector", lambda e, i=e8: e.tensor_scalar(
                out=r3(oh2[:])[:, :, i:i + 1], in0=rcol(R["i2"][:]),
                scalar1=float(i), scalar2=None, op0=ALU.is_equal))
        t_ohs = st("vector", "V", lambda e: e.tensor_add(out=ohs[:], in0=oh1[:], in1=oh2[:]))
        wt("tensor", "V", t_ohs)
        t_mmC = st("tensor", "T", lambda e: e.matmul(psumP[:, 1024:1280], CT["ltri128"][:],
                                                     ohs[:], start=True, stop=True))
        wt("vector", "T", t_mmC)
        op("vector", lambda e: e.tensor_copy(out=ecsU[:], in_=psumP[:, 1024:1280]))
        op("vector", lambda e: e.tensor_tensor(out=selbuf[:], in0=oh1[:], in1=ecsU[:], op=ALU.mult))
        op("vector", lambda e: e.tensor_reduce(out=rcol(R["vr1"][:]), in_=r3(selbuf[:]),
                                               axis=AX.X, op=ALU.add))
        op("vector", lambda e: e.tensor_tensor(out=selbuf[:], in0=oh1[:], in1=ownoff[:], op=ALU.mult))
        op("vector", lambda e: e.tensor_reduce(out=rcol(R["t1"][:]), in_=r3(selbuf[:]),
                                               axis=AX.X, op=ALU.add))
        op("vector", lambda e: e.tensor_add(out=R["t1"][:], in0=R["t1"][:], in1=R["vr1"][:]))
        op("vector", lambda e: e.tensor_scalar(out=R["v1"][:], in0=R["t1"][:],
                                               scalar1=float(CAPN), scalar2=None, op0=ALU.is_lt))
        op("vector", lambda e: e.tensor_add(out=selbuf[:], in0=ecsU[:], in1=oh1[:]))
        op("vector", lambda e: e.tensor_tensor(out=selbuf[:], in0=oh2[:], in1=selbuf[:], op=ALU.mult))
        op("vector", lambda e: e.tensor_reduce(out=rcol(R["vr2"][:]), in_=r3(selbuf[:]),
                                               axis=AX.X, op=ALU.add))
        op("vector", lambda e: e.tensor_tensor(out=selbuf[:], in0=oh2[:], in1=ownoff[:], op=ALU.mult))
        op("vector", lambda e: e.tensor_reduce(out=rcol(R["t2"][:]), in_=r3(selbuf[:]),
                                               axis=AX.X, op=ALU.add))
        op("vector", lambda e: e.tensor_add(out=R["t2"][:], in0=R["t2"][:], in1=R["vr2"][:]))
        op("vector", lambda e: e.tensor_scalar(out=R["v2"][:], in0=R["t2"][:],
                                               scalar1=float(CAPN), scalar2=None, op0=ALU.is_lt))
        op("vector", lambda e: e.tensor_tensor(
            out=r3(ohv1[:]), in0=r3(oh1[:]),
            in1=rcol(R["v1"][:]).broadcast_to((128, NT, 8)), op=ALU.mult))
        op("vector", lambda e: e.tensor_tensor(
            out=r3(ohv2[:]), in0=r3(oh2[:]),
            in1=rcol(R["v2"][:]).broadcast_to((128, NT, 8)), op=ALU.mult))
        t_ohv = st("vector", "V", lambda e: e.tensor_add(out=ohs[:], in0=ohv1[:], in1=ohv2[:]))
        wt("tensor", "V", t_ohv)
        t_mmD = st("tensor", "T", lambda e: e.matmul(psumP[:, 1280:1536], CT["ltri128"][:],
                                                     ohs[:], start=True, stop=True))
        wt("vector", "T", t_mmD)
        op("vector", lambda e: e.tensor_copy(out=ecsV[:], in_=psumP[:, 1280:1536]))
        op("vector", lambda e: e.tensor_tensor(out=selbuf[:], in0=ohv1[:], in1=ecsV[:], op=ALU.mult))
        op("vector", lambda e: e.tensor_reduce(out=rcol(R["vr1"][:]), in_=r3(selbuf[:]),
                                               axis=AX.X, op=ALU.add))
        op("vector", lambda e: e.tensor_add(out=selbuf[:], in0=ecsV[:], in1=ohv1[:]))
        op("vector", lambda e: e.tensor_tensor(out=selbuf[:], in0=ohv2[:], in1=selbuf[:], op=ALU.mult))
        op("vector", lambda e: e.tensor_reduce(out=rcol(R["vr2"][:]), in_=r3(selbuf[:]),
                                               axis=AX.X, op=ALU.add))
        t_csz = st("vector", "V", lambda e: e.memset(csrowV[:], 0.0))
        wt("tensor", "V", max(t_ohv, t_csz))
        t_mmE = st("tensor", "T", lambda e: e.matmul(psumP[0:1, 1536:1792], CT["onescol"][:],
                                                     ohs[:], start=True, stop=True))
        wt("vector", "T", t_mmE)
        t_csr = st("vector", "V", lambda e: e.tensor_copy(out=csrowV[0:1, :], in_=psumP[0:1, 1536:1792]))
        wt("tensor", "V", t_csr)
        t_tr2 = None
        for half in range(2):
            fn = (lambda e, half=half: e.matmul(
                psumP[:, 1536 + half * 128:1536 + (half + 1) * 128],
                csrowV[:, half * 128:(half + 1) * 128], CT["ident_f32"][:],
                start=True, stop=True))
            if half == 1:
                t_tr2 = st("tensor", "T", fn)
            else:
                op("tensor", fn)
        wt("vector", "T", t_tr2)
        op("vector", lambda e: e.tensor_copy(out=a2t[:, 0:1], in_=psumP[:, 1536:1537]))
        t_a2t = st("vector", "V", lambda e: e.tensor_copy(out=a2t[:, 1:2], in_=psumP[:, 1664:1665]))
        wt("tensor", "V", t_a2t)
        op("tensor", lambda e: e.matmul(psumP[:, 1792:1794], CT["ltri16x8"][:], a2t[:],
                                        start=True, stop=True))
        t_mmF = st("tensor", "T", lambda e: e.matmul(psumP[0:32, 1794:1796], CT["mod8p"][:],
                                                     a2t[:], start=True, stop=True))
        wt("vector", "T", t_mmF)
        op("vector", lambda e: e.tensor_copy(out=a2x[:], in_=psumP[:, 1792:1794]))
        t_t0e = st("vector", "V", lambda e: e.tensor_copy(out=tot0e[0:8, :], in_=psumP[0:8, 1794:1796]))
        wt("tensor", "V", t_t0e)
        t_mmG = st("tensor", "T", lambda e: e.matmul(psumP[:, 1796:1797], CT["mod8tp"][:],
                                                     tot0e[:, 0:1], start=True, stop=True))
        wt("vector", "T", t_mmG)
        op("vector", lambda e: e.tensor_copy(out=tot0b[:], in_=psumP[:, 1796:1797]))
        op("vector", lambda e: e.tensor_add(out=a2x[:, 1:2], in0=a2x[:, 1:2], in1=tot0b[:]))
        t_rhs3 = st("vector", "V", lambda e: e.tensor_tensor(
            out=rhs3[:].rearrange("p (h a ee) -> p h a ee", a=16, ee=8),
            in0=a2x[:].rearrange("p (h a ee) -> p h a ee", a=1, ee=1).broadcast_to((128, 2, 16, 8)),
            in1=CT["m128pat"][:].rearrange("p (h a ee) -> p h a ee", a=16, ee=8), op=ALU.mult))
        wt("tensor", "V", t_rhs3)
        t_mmH = st("tensor", "T", lambda e: e.matmul(psumP[:, 1536:1792], CT["ones128sq"][:],
                                                     rhs3[:], start=True, stop=True))
        wt("vector", "T", t_mmH)
        op("vector", lambda e: e.tensor_copy(out=offs2B[:], in_=psumP[:, 1536:1792]))
        op("vector", lambda e: e.tensor_tensor(out=selbuf[:], in0=oh1[:], in1=offs2B[:], op=ALU.mult))
        op("vector", lambda e: e.tensor_reduce(out=rcol(R["t1"][:]), in_=r3(selbuf[:]),
                                               axis=AX.X, op=ALU.add))
        op("vector", lambda e: e.tensor_add(out=R["vr1"][:], in0=R["vr1"][:], in1=R["t1"][:]))
        op("vector", lambda e: e.tensor_tensor(out=selbuf[:], in0=oh2[:], in1=offs2B[:], op=ALU.mult))
        op("vector", lambda e: e.tensor_reduce(out=rcol(R["t2"][:]), in_=r3(selbuf[:]),
                                               axis=AX.X, op=ALU.add))
        op("vector", lambda e: e.tensor_add(out=R["vr2"][:], in0=R["vr2"][:], in1=R["t2"][:]))
        op("vector", lambda e: e.scalar_tensor_tensor(out=R["sx1"][:], in0=R["i1"][:],
                                                      scalar=float(CAP), in1=R["vr1"][:],
                                                      op0=ALU.mult, op1=ALU.add))
        op("vector", lambda e: e.tensor_scalar(out=R["t1"][:], in0=CT["dumoff"][:],
                                               scalar1=float(DUMMY), scalar2=None, op0=ALU.add))
        op("vector", lambda e: e.tensor_sub(out=R["sx1"][:], in0=R["sx1"][:], in1=R["t1"][:]))
        op("vector", lambda e: e.tensor_tensor(out=R["sx1"][:], in0=R["v1"][:], in1=R["sx1"][:], op=ALU.mult))
        op("vector", lambda e: e.tensor_add(out=R["sx1"][:], in0=R["sx1"][:], in1=R["t1"][:]))
        op("vector", lambda e: e.scalar_tensor_tensor(out=R["sx2"][:], in0=R["i2"][:],
                                                      scalar=float(CAP), in1=R["vr2"][:],
                                                      op0=ALU.mult, op1=ALU.add))
        op("vector", lambda e: e.tensor_scalar(out=R["t2"][:], in0=R["t1"][:],
                                               scalar1=4.0, scalar2=None, op0=ALU.add))
        op("vector", lambda e: e.tensor_sub(out=R["sx2"][:], in0=R["sx2"][:], in1=R["t2"][:]))
        op("vector", lambda e: e.tensor_tensor(out=R["sx2"][:], in0=R["v2"][:], in1=R["sx2"][:], op=ALU.mult))
        op("vector", lambda e: e.tensor_add(out=R["sx2"][:], in0=R["sx2"][:], in1=R["t2"][:]))
        op("vector", lambda e: e.tensor_tensor(out=R["wm1"][:], in0=R["w1"][:], in1=R["v1"][:], op=ALU.mult))
        op("vector", lambda e: e.tensor_tensor(out=R["wm2"][:], in0=R["w2"][:], in1=R["v2"][:], op=ALU.mult))
        op("vector", lambda e: e.memset(R["csum"][:], 0.0))
        op("vector", lambda e: e.memset(R["acc"][:], 0.0))
        for c_ in range(1, 8):
            op("vector", lambda e, c_=c_: e.tensor_scalar(
                out=R["t1"][:], in0=CT["siota"][:], scalar1=mycum[:, c_:c_ + 1],
                scalar2=None, op0=ALU.is_ge))
            op("vector", lambda e: e.tensor_add(out=R["csum"][:], in0=R["csum"][:], in1=R["t1"][:]))
            op("vector", lambda e, c_=c_: e.scalar_tensor_tensor(
                out=R["t2"][:], in0=R["t1"][:], scalar=myvc[:, c_ - 1:c_],
                in1=R["acc"][:], op0=ALU.mult, op1=ALU.add))
            op("vector", lambda e: e.tensor_copy(out=R["acc"][:], in_=R["t2"][:]))
        op("vector", lambda e: e.tensor_sub(out=R["t1"][:], in0=CT["siota"][:], in1=R["acc"][:]))
        op("vector", lambda e: e.scalar_tensor_tensor(out=R["slotv"][:], in0=R["csum"][:],
                                                      scalar=float(CAP), in1=R["t1"][:],
                                                      op0=ALU.mult, op1=ALU.add))
        op("vector", lambda e: e.tensor_scalar(out=R["t1"][:], in0=CT["siota"][:],
                                               scalar1=mytot[:, 0:1], scalar2=None, op0=ALU.is_ge))
        op("vector", lambda e: e.tensor_scalar(out=R["t2"][:], in0=R["slotv"][:],
                                               scalar1=-1.0, scalar2=float(DUMMY),
                                               op0=ALU.mult, op1=ALU.add))
        op("vector", lambda e: e.tensor_tensor(out=R["t2"][:], in0=R["t1"][:], in1=R["t2"][:], op=ALU.mult))
        t_slotv = st("vector", "V", lambda e: e.tensor_add(out=R["slotv"][:], in0=R["slotv"][:], in1=R["t2"][:]))

        if stop_after == "p3d":
            early_out(); off[0] = True
        def fold(src, dst, tag):
            for g in range(8):
                pslot = 1024 + (g % 2) * 64
                t_fm_ = st("tensor", "T", lambda e, g=g, src=src, pslot=pslot: e.matmul(
                    psumP[:, pslot:pslot + 32], CT[f"foldsel{g}"][:], src[:],
                    start=True, stop=True))
                wt("vector", "T", t_fm_)
                t_fc = st("vector", "V", lambda e, g=g, dst=dst, pslot=pslot: e.tensor_copy(
                    out=dst[:].rearrange("p (mm gg) -> p mm gg", gg=8)[:, :, g:g + 1],
                    in_=psumP[:, pslot:pslot + 32].rearrange("p (a b) -> p a b", b=1)))
                if g >= 1:
                    wt("tensor", "V", t_fc)
            return cnt["V"]

        wt("tensor", "V", t_slotv)
        fold(R["sx1"], Lk0, "k0")
        fold(R["sx2"], Lk1, "k1")
        t_folds = fold(R["slotv"], Lslot, "sl")

        wt("tensor", "V", t_folds)
        wt("sync", "V", t_folds)
        t_y = [0] * NT
        t_ev2 = [0] * NT
        t_zf = [0] * NT

        t_pstl = [0] * NT

        def stageB(m):
            fm = fmt[m % 2]
            ptr = p_trbs[m % 2]
            pz = p_zs[m % 2]
            if m >= 2:
                wt("tensor", "V", t_ev2[m - 2])
            t_tr = None
            for j in range(8):
                fn = (lambda e, m=m, j=j, ptr=ptr: e.transpose(
                    ptr[:, j * 128:(j + 1) * 128],
                    xn_tm[:, m * D + j * 128:m * D + (j + 1) * 128],
                    CT["ident_bf"][:]))
                if j == 7:
                    t_tr = st("tensor", "T", fn)
                else:
                    op("tensor", fn)
            wt("vector", "T", t_tr)
            t_ev2[m] = st("vector", "V", lambda e, fm=fm, ptr=ptr: e.tensor_copy(out=fm, in_=ptr[:]))
            wt("tensor", "V", t_ev2[m])
            if m >= 2:
                wt("tensor", "A", t_zf[m - 2])
            t_zmm = None
            for o in range(2):
                for j in range(8):
                    op("tensor", lambda e, fm=fm, o=o, j=j, pz=pz: e.matmul(
                        pz[:, o * 512:(o + 1) * 512], fm[:, j * 128:(j + 1) * 128],
                        w1_sb[:, j * D + o * 512:j * D + (o + 1) * 512],
                        start=(j == 0), stop=False))
                fnb = (lambda e, o=o, pz=pz: e.matmul(
                    pz[:, o * 512:(o + 1) * 512], CT["ones1_bf"][:],
                    b1_sb[:, o * 512:(o + 1) * 512], start=False, stop=True))
                if o == 1:
                    t_zmm = st("tensor", "T", fnb)
                else:
                    op("tensor", fnb)
            wt("scalar", "T", t_zmm)
            if m >= 1:
                wt("scalar", f"sP{(m - 1) % 2}", t_pstl[m - 1])
            op("scalar", lambda e, pz=pz: e.activation(out=s1[:], in_=pz, func=AF.Exp))
            t_zf[m] = st("scalar", "A", lambda e: e.activation(out=s2, in_=s1[:], func=AF.Ln, bias=1.0))
            wt("vector", "A", t_zf[m])
            t_y[m] = st("vector", "V", lambda e, m=m: e.scalar_tensor_tensor(
                out=s0[:], in0=xn_tm[:, m * D:(m + 1) * D], scalar=svst[:, m:m + 1],
                in1=s2, op0=ALU.mult, op1=ALU.mult))
            wt("scalar", "V", t_y[m])
            t_cpy = st("scalar", "A", lambda e: e.activation(out=y_bf, in_=s0[:], func=AF.Copy))
            wt("vector", "A", t_cpy)
            wt("sync", "A", t_cpy)
            kp = f"sP{m % 2}"
            t_pstl[m] = dma("sync", kp, partial.ap()[m], y_bf, pre_wait=True)

        for m in range(NT):
            stageB(m)
        t_zB_last = cnt["T"]

        if debug:
            wt("gpsimd", "V", t_folds)
            dma("gpsimd", "sDBG", dbg["d_r1"], pk[:])
            for i, k in enumerate(["v1", "v2", "vr1", "vr2"]):
                op("vector", lambda e, i=i, k=k: e.tensor_copy(out=pk[:, i * 32:(i + 1) * 32], in_=R[k][:]))
            t_dbg2 = st("vector", "V", lambda e: e.tensor_copy(out=ex8[:, 0:64], in_=cumB[:]))
            wt("gpsimd", "V", t_dbg2)
            dma("gpsimd", "sDBG", dbg["d_r2"], pk[:])
            dma("gpsimd", "sDBG", dbg["d_r3"][:, 0:64], ex8[0:128, 0:64])
            for i, k in enumerate(["sx1", "sx2", "slotv"]):
                op("vector", lambda e, i=i, k=k: e.tensor_copy(out=tmp8[:, i * 32:(i + 1) * 32], in_=R[k][:]))
            t_dbg4 = st("vector", "V", lambda e: e.tensor_copy(out=tmp8[:, 95:96], in_=mytot[:]))
            wt("gpsimd", "V", t_dbg4)
            dma("gpsimd", "sDBG", dbg["d_r4"], tmp8[:, 0:96])

        class _Stop(Exception):
            pass

        try:
            if stop_after == "p1":
                early_out()
                raise _Stop()
            wt("gpsimd", "V", t_folds)
            wt("gpsimd", "A", t_xn_last)
            wt("gpsimd", "sZ", t_zero)
            t_sc1 = None
            for kk, L in ((0, Lk0), (1, Lk1)):
                for jj in range(4):
                    t_sc1 = bump("sSC", 16)
                    emit("gpsimd", lambda e, L=L, jj=jj: e.dma_scatter_add(
                        out_ap=send1.ap(),
                        in_ap=xn_tm[:, jj * 8192:(jj + 1) * 8192].rearrange("p (m d) -> p m d", d=D),
                        idxs_ap=L[:, jj * 64:(jj + 1) * 64], num_idxs=1024,
                        num_idxs_reg=1024, elem_size=D).then_inc(sem["sSC"], 16))
            wt("gpsimd", "sSC", t_sc1)
            t_a2a1 = st("gpsimd", "CC", lambda e: e.collective_compute(
                "AllToAll", ALU.bypass, replica_groups=[list(range(NCORE))],
                ins=[send1.ap()[0:DUMMY, :].opt()], outs=[recv1.ap()[0:DUMMY, :].opt()]))
            wt("gpsimd", "T", t_zB_last)
            for dc in range(8):
                dma("gpsimd", "sWF", cfc_sb[:, dc * DFF:(dc + 1) * DFF], cfc[dc * 128:(dc + 1) * 128, :])
            for j in range(12):
                t_wffn = dma("gpsimd", "sWF", cpr_sb[:, j * D:(j + 1) * D], cpr[j * 128:(j + 1) * 128, :])
            wt("gpsimd", "CC", t_a2a1)

            exph = [xn_tm[:, hh_ * 4096:(hh_ + 1) * 4096].rearrange("p (b t) -> p b t", b=8)
                    for hh_ in range(8)]
            t_gath = [0] * 8
            wt("gpsimd", "V", t_y[NT - 1])
            for h in range(8):
                kg = f"sEG{h % 2}"
                if cnt[kg] > 0:
                    wt("gpsimd", kg, cnt[kg])
                t_gath[h] = bump(kg, 16)
                emit("gpsimd", lambda e, h=h, kg=kg: e.dma_gather(
                    out_ap=exph[h], in_ap=recv1.ap(), idxs_ap=Lslot[:, h * 32:(h + 1) * 32],
                    num_idxs=512, num_idxs_reg=512, elem_size=D,
                    transpose=True).then_inc(sem[kg], 16))
            wt("tensor", "sWF", t_wffn)
            wt("tensor", "A", t_zf[NT - 1])
            if stop_after == "a2a1":
                wt("gpsimd", "sEG0", cnt["sEG0"])
                wt("gpsimd", "sEG1", cnt["sEG1"])
                early_out()
                raise _Stop()

            wt("vector", f"sP{(NT - 1) % 2}", t_pstl[NT - 1])
            wt("vector", f"sP{(NT - 2) % 2}", t_pstl[NT - 2])
            t_expo2 = {}
            pfree = {0: 0, 1: 0, 2: 0}
            ex_bufs = [
                [expo[:, 0:2048], expo[:, 2048:4096]],
                [s0[:].bitcast(BF16), s1[:].bitcast(BF16)],
            ]
            t_lastmult = None
            for s_ in range(8):
                wt("tensor", f"sEG{s_ % 2}", t_gath[s_])
                bufs = ex_bufs[s_ % 2]
                if s_ >= 2:
                    for h in range(2):
                        wt("vector", f"sEO{(s_ % 2) * 2 + h}", t_expo2[(s_ - 2, h)])
                t_mult = [0] * 12
                for j in range(12):
                    pa = psumP[:, (j % 2) * 1024:(j % 2) * 1024 + 512]
                    pg = psumP[:, (j % 2) * 1024 + 512:(j % 2) * 1024 + 1024]
                    if pfree[j % 2]:
                        wt("tensor", "V", pfree[j % 2])
                    t_mm_a = None
                    for dc in range(8):
                        fn = (lambda e, j=j, dc=dc, s_=s_, pa=pa: e.matmul(
                            pa, cfc_sb[:, dc * DFF + j * 128:dc * DFF + (j + 1) * 128],
                            exph[s_][:, dc, :],
                            start=(dc == 0), stop=(dc == 7)))
                        if dc == 7:
                            t_mm_a = st("tensor", "T", fn)
                        else:
                            op("tensor", fn)
                    t_mm_g = None
                    for dc in range(8):
                        fn = (lambda e, j=j, dc=dc, s_=s_, pg=pg: e.matmul(
                            pg, cfc_sb[:, dc * DFF + (j + 12) * 128:dc * DFF + (j + 13) * 128],
                            exph[s_][:, dc, :],
                            start=(dc == 0), stop=(dc == 7)))
                        if dc == 7:
                            t_mm_g = st("tensor", "T", fn)
                        else:
                            op("tensor", fn)
                    wt("scalar", "T", t_mm_a)
                    if t_lastmult:
                        wt("scalar", "V", t_lastmult)
                    t_silu = st("scalar", "A", lambda e, pa=pa: e.activation(out=sa, in_=pa, func=AF.Silu))
                    wt("vector", "A", t_silu)
                    wt("vector", "T", t_mm_g)
                    t_lastmult = t_mult[j] = st("vector", "V", lambda e, j=j, pg=pg: e.tensor_tensor(
                        out=hh[:, j * 512:(j + 1) * 512], in0=sa, in1=pg, op=ALU.mult))
                    pfree[j % 2] = t_mult[j]
                t_ev = [0] * 4
                for sub in range(4):
                    reg = 2 if sub % 2 == 0 else 1
                    pz2 = p_z if sub % 2 == 0 else psumP[:, 1024:2048]
                    if pfree[reg]:
                        wt("tensor", "V", pfree[reg])
                    if sub == 0:
                        wt("tensor", "V", t_mult[11])
                    t_o = None
                    for nh in range(2):
                        for j in range(12):
                            fn = (lambda e, sub=sub, nh=nh, j=j, pz2=pz2: e.matmul(
                                pz2[:, nh * 512:(nh + 1) * 512],
                                hh[:, j * 512 + sub * 128:j * 512 + (sub + 1) * 128],
                                cpr_sb[:, j * D + nh * 512:j * D + (nh + 1) * 512],
                                start=(j == 0), stop=(j == 11)))
                            if nh == 1 and j == 11:
                                t_o = st("tensor", "T", fn)
                            else:
                                op("tensor", fn)
                    wt("vector", "T", t_o)
                    t_ev[sub] = st("vector", "V", lambda e, sub=sub, bufs=bufs, pz2=pz2: e.tensor_copy(
                        out=bufs[sub // 2][:, (sub % 2) * 1024:(sub % 2 + 1) * 1024], in_=pz2[:]))
                    pfree[reg] = t_ev[sub]
                for h in range(2):
                    keo = f"sEO{(s_ % 2) * 2 + h}"
                    wt("gpsimd", "V", t_ev[2 * h + 1])
                    if cnt[keo] > 0:
                        wt("gpsimd", keo, cnt[keo])
                    t_expo2[(s_, h)] = bump(keo, 16)
                    emit("gpsimd", lambda e, s_=s_, h=h, bufs=bufs, keo=keo: e.dma_scatter_add(
                        out_ap=send2.ap(),
                        in_ap=bufs[h].rearrange("p (a d) -> p a d", d=D),
                        idxs_ap=Lslot[:, s_ * 32 + h * 16:s_ * 32 + (h + 1) * 16],
                        num_idxs=256, num_idxs_reg=256, elem_size=D).then_inc(sem[keo], 16))

            for q_ in range(4):
                if cnt[f"sEO{q_}"] > 0:
                    wt("gpsimd", f"sEO{q_}", cnt[f"sEO{q_}"])
            t_a2a2 = st("gpsimd", "CC", lambda e: e.collective_compute(
                "AllToAll", ALU.bypass, replica_groups=[list(range(NCORE))],
                ins=[send2.ap()[0:DUMMY, :].opt()], outs=[recv2.ap()[0:DUMMY, :].opt()]))
            wt("gpsimd", "CC", t_a2a2)
            if stop_after == "ffn":
                early_out()
                raise _Stop()
            gb0 = cfc_sb[:, 0:8192].rearrange("p (b d) -> p b d", d=D)
            gb1 = cfc_sb[:, 8192:16384].rearrange("p (b d) -> p b d", d=D)
            wt("sync", "A", cnt["A"])
            wt("sync", "V", cnt["V"])
            wt("vector", "sP0", cnt["sP0"])
            wt("vector", "sP1", cnt["sP1"])
            t_out = None
            for cj in range(4):
                if cnt["sOG0"] > 0:
                    wt("gpsimd", "sOG0", cnt["sOG0"])
                    wt("gpsimd", "sOG1", cnt["sOG1"])
                tg0 = bump("sOG0", 16)
                emit("gpsimd", lambda e, cj=cj: e.dma_gather(
                    out_ap=gb0, in_ap=recv2.ap(), idxs_ap=Lk0[:, cj * 64:(cj + 1) * 64],
                    num_idxs=1024, num_idxs_reg=1024, elem_size=D).then_inc(sem["sOG0"], 16))
                tg1 = bump("sOG1", 16)
                emit("gpsimd", lambda e, cj=cj: e.dma_gather(
                    out_ap=gb1, in_ap=recv2.ap(), idxs_ap=Lk1[:, cj * 64:(cj + 1) * 64],
                    num_idxs=1024, num_idxs_reg=1024, elem_size=D).then_inc(sem["sOG1"], 16))
                wt("vector", "sOG0", tg0)
                wt("vector", "sOG1", tg1)
                for b in range(8):
                    mloc = cj * 8 + b
                    pb = pt8[b % 2]
                    yb = yt8[b % 2]
                    ob = ot8[b % 2]
                    if cj > 0 or b >= 2:
                        wt("sync", "V", cnt["V"])
                    kpl = f"sPL{b % 2}"
                    t_pl = dma("sync", kpl, pb[:], xsh[mloc], pre_wait=True)
                    kyl = f"sX{b % 2}"
                    t_yl = dma("sync", kyl, yb, partial.ap()[mloc], pre_wait=True)
                    wt("vector", kpl, t_pl)
                    wt("vector", kyl, t_yl)
                    op("vector", lambda e, b=b, mloc=mloc, pb=pb, ob=ob: e.scalar_tensor_tensor(
                        out=ob[:], in0=gb0[:, b, :], scalar=R["wm1"][:, mloc:mloc + 1],
                        in1=pb[:], op0=ALU.mult, op1=ALU.add))
                    op("vector", lambda e, b=b, mloc=mloc, ob=ob: e.scalar_tensor_tensor(
                        out=ob[:], in0=gb1[:, b, :], scalar=R["wm2"][:, mloc:mloc + 1],
                        in1=ob[:], op0=ALU.mult, op1=ALU.add))
                    t_cmb = st("vector", "V", lambda e, yb=yb, ob=ob: e.tensor_tensor(
                        out=ob[:], in0=yb, in1=ob[:], op=ALU.add))
                    wt("sync", "V", t_cmb)
                    kst = f"sST{b % 2}"
                    t_out = dma("sync", kst, out[mloc], ob[:], pre_wait=True)
                    if not (cj == 3 and b == 7):
                        wt("vector", kst, t_out)
                if cj < 3:
                    wt("gpsimd", "V", cnt["V"])
            wt("sync", "sST0", cnt["sST0"])
            wt("sync", "sST1", cnt["sST1"])
            if debug:
                wt("sync", "sDBG", cnt["sDBG"])

        except _Stop:
            pass

        with allow_ici_all_to_all(), nc.Block() as block:
            @block.sync
            def _(en):
                for f in plans["sync"]:
                    f(en)

            @block.scalar
            def _(en):
                for f in plans["scalar"]:
                    f(en)

            @block.vector
            def _(en):
                for f in plans["vector"]:
                    f(en)

            @block.tensor
            def _(en):
                for f in plans["tensor"]:
                    f(en)

            @block.gpsimd
            def _(en):
                for f in plans["gpsimd"]:
                    f(en)

    lower_extended_insts(nc)
    return nc


def prepare_in_maps(inputs):
    x = np.ascontiguousarray(np.asarray(inputs["L0_attn_x"], np.float32)).reshape(NCOL, 128, D)
    W1 = np.ascontiguousarray(np.asarray(inputs["L0_attn_W1"], np.float32))
    b1 = np.asarray(inputs["L0_attn_b1"], np.float32).reshape(1, D)
    W2 = np.asarray(inputs["L0_attn_W2"], np.float32)
    b2 = np.asarray(inputs["L0_attn_b2"], np.float32)
    W3 = np.asarray(inputs["L0_attn_W3"], np.float32)
    b3 = np.asarray(inputs["L0_attn_b3"], np.float32)
    wg = np.asarray(inputs["L0_ffn_w_gate"], np.float32)
    cfc = np.asarray(inputs["L0_ffn_c_fc"], np.float32)
    cpr = np.asarray(inputs["L0_ffn_c_proj"], np.float32)
    Wbcg = np.ascontiguousarray(np.concatenate([W2, W3, wg], axis=1))
    brow = np.ascontiguousarray(np.concatenate([b2, b3, np.zeros(8, np.float32)])).reshape(1, 40)

    in_maps = []
    for c in range(NCORE):
        m = {
            "xsh": np.ascontiguousarray(x[c::8]),
            "W1": W1, "Wbcg": Wbcg, "brow": brow, "b1row": b1,
            "cfc": np.ascontiguousarray(cfc[c]),
            "cpr": np.ascontiguousarray(cpr[c]),
        }
        for k, v in host_constants(c).items():
            m[k] = np.ascontiguousarray(v)
        in_maps.append(m)
    return in_maps


def kernel(**inputs):
    in_maps = prepare_in_maps(inputs)
    nc = build_nc(DEBUG)
    res = run_bass_kernel_spmd(nc, in_maps, core_ids=list(range(NCORE)))
    full = np.zeros((NCOL, 128, D), np.float32)
    for c in range(NCORE):
        full[c::8] = res.results[c]["out"]
    if DEBUG:
        kernel._debug = [{k: v for k, v in r.items() if k.startswith("d_")} for r in res.results]
    return full.reshape(8, 4096, 1024)
